# version 10
# baseline (speedup 1.0000x reference)
import numpy as np
import ml_dtypes
import concourse.bass as bass
import concourse.mybir as mybir
from concourse.bass_utils import run_bass_kernel_spmd
from contextlib import ExitStack

F32 = mybir.dt.float32
BF16 = mybir.dt.bfloat16
AF = mybir.ActivationFunctionType
ALU = mybir.AxisListType
ALU = mybir.AluOpType
AX = mybir.AxisListType

L = 2048
DM = 2048
HH, PP, NNS, GG = 64, 64, 128, 8
INTER = 4096
CONV = 6144
PROJ = 10304
IFF = 8192
CS = 128
NCH = L // CS
KC = 4
RND = 3.0 * 2.0 ** 22
EPS = 1e-5
DEBUG = False

ENGS = "TVASG"


def _hadamard(n):
    h = np.array([[1.0]], np.float32)
    while h.shape[0] < n:
        h = np.block([[h, h], [h, -h]])
    return h


class Wk:
    def __init__(self, nc, semh):
        self.nc = nc
        self.semh = semh
        self.c = {k: 0 for k in semh}
        self.st = {e: [] for e in ENGS}

    def wait(self, eng, sem, val=None):
        v = self.c[sem] if val is None else val
        if v <= 0:
            return
        h = self.semh[sem]
        self.st[eng].append(lambda e, h=h, v=v: e.wait_ge(h, v))

    def op(self, eng, meth, *args, inc=None, k=1, **kw):
        h = None
        kk = 0
        if inc is not None:
            self.c[inc] += k
            h, kk = self.semh[inc], k

        import traceback
        site = traceback.extract_stack(limit=3)[0]

        def f(e, m=meth, a=args, kw=kw, h=h, kk=kk, site=site):
            try:
                ins = getattr(e, m)(*a, **kw)
            except Exception:
                print("FAILING OP:", m, "emitted at", site.filename, site.lineno,
                      [getattr(x, "shape", None) for x in a])
                raise
            if h is not None:
                ins.then_inc(h, kk)

        self.st[eng].append(f)
        return self.c[inc] if inc is not None else None

    def nop(self, eng, cyc=1024):
        self.st[eng].append(lambda e, c=cyc: e.nop(cycle_cnt=c))

    def fence(self, eng, sem):
        self.c[sem] += 1
        h = self.semh[sem]
        self.st[eng].append(lambda e, h=h: e.nop(cycle_cnt=1024).then_inc(h, 1))
        return self.c[sem]

    def barrier(self):
        snap = dict(self.c)
        for e in ENGS:
            for s, v in snap.items():
                if v > 0:
                    self.wait(e, s, v)


def build_program():
    nc = bass.Bass("TRN2", target_bir_lowering=False)
    dt_ = nc.dram_tensor

    x_in = dt_("x", [L, DM], F32, kind="ExternalInput").ap()
    w_in = dt_("w_in", [PROJ, DM], F32, kind="ExternalInput").ap()
    conv_w = dt_("conv_w", [CONV, KC], F32, kind="ExternalInput").ap()
    conv_b = dt_("conv_b", [CONV], F32, kind="ExternalInput").ap()
    dt_bias = dt_("dt_bias", [HH], F32, kind="ExternalInput").ap()
    a_log = dt_("a_log", [HH], F32, kind="ExternalInput").ap()
    dvec = dt_("dvec", [HH], F32, kind="ExternalInput").ap()
    norm_w = dt_("norm_w", [INTER], F32, kind="ExternalInput").ap()
    w_out = dt_("w_out", [DM, INTER], F32, kind="ExternalInput").ap()
    w_g = dt_("w_g", [IFF, DM], F32, kind="ExternalInput").ap()
    w_u = dt_("w_u", [IFF, DM], F32, kind="ExternalInput").ap()
    w_d = dt_("w_d", [DM, IFF], F32, kind="ExternalInput").ap()
    cb_in = dt_("cb", [128, 1152], BF16, kind="ExternalInput").ap()
    onesf_in = dt_("ones_f", [1, 128], F32, kind="ExternalInput").ap()
    onesc_in = dt_("ones_c", [128, 1], F32, kind="ExternalInput").ap()
    idf_in = dt_("id_f", [128, 128], F32, kind="ExternalInput").ap()

    out_hbm = dt_("out", [L, DM], F32, kind="ExternalOutput").ap()

    s_winT = dt_("s_winT", [DM, PROJ], BF16, kind="Internal").ap()
    s_woutT = dt_("s_woutT", [INTER, DM], BF16, kind="Internal").ap()
    s_wgT = dt_("s_wgT", [DM, IFF], BF16, kind="Internal").ap()
    s_wuT = dt_("s_wuT", [DM, IFF], BF16, kind="Internal").ap()
    s_wdT = dt_("s_wdT", [IFF, DM], BF16, kind="Internal").ap()
    s_za = dt_("s_za", [DM, L], BF16, kind="Internal").ap()
    s_xq = dt_("s_xq", [DM, L], BF16, kind="Internal").ap()
    s_gate = dt_("s_gate", [INTER, L], BF16, kind="Internal").ap()
    s_hbc = dt_("s_hbc", [CONV, L], BF16, kind="Internal").ap()
    s_on = dt_("s_on", [L, INTER], BF16, kind="Internal").ap()
    s_ya = dt_("s_ya", [IFF, L], BF16, kind="Internal").ap()
    s_r2 = dt_("s_r2", [IFF, L], BF16, kind="Internal").ap()

    dbg = {}
    if DEBUG:
        for nm, shp in [("d_za", [DM, L]), ("d_xq", [DM, L]), ("d_gate", [INTER, L]),
                        ("d_hbc", [CONV, L]), ("d_on", [L, INTER]), ("d_ya", [IFF, L]),
                        ("d_r2", [IFF, L])]:
            dbg[nm] = dt_(nm, shp, BF16, kind="ExternalOutput").ap()
        dbg["d_smf"] = dt_("d_smf", [128, 2944], F32, kind="ExternalOutput").ap()

    es = ExitStack()

    def sbuf(name, shape, dty):
        return es.enter_context(nc.sbuf_tensor(name, shape, dty)).ap()

    AR1 = sbuf("AR1", [128, 32768], BF16)
    AR2 = sbuf("AR2", [128, 32768], BF16)
    ARW = sbuf("ARW", [128, 19968], BF16)
    SST = sbuf("SST", [128, 8192], BF16)
    SMB = sbuf("SMB", [128, 4224], BF16)
    SMF = sbuf("SMF", [128, 2944], F32)
    CBT = sbuf("CBT", [128, 1152], BF16)
    IDF = sbuf("IDF", [128, 128], F32)
    ONF = sbuf("ONF", [1, 128], F32)
    ONC = sbuf("ONC", [128, 1], F32)

    pb = [es.enter_context(nc.psum_tensor("pb%d" % i, [128, 512], F32)).ap()
          for i in range(7)]
    pbt = es.enter_context(nc.psum_tensor("pbt", [128, 1024], BF16)).ap()

    sems = {}
    for s in ("dS", "dG", "dW", "dO", "T", "V", "A"):
        sems[s] = es.enter_context(nc.semaphore(s))

    w = Wk(nc, sems)

    def fview(arena, byte_off, parts, cols):
        assert byte_off % 4 == 0
        return arena[0:parts, byte_off // 2: byte_off // 2 + 2 * cols].bitcast(F32)

    H128 = CBT[:, 0:128]
    HB16 = CBT[:, 128:256]
    HB64 = CBT[:, 256:384]
    IDB = CBT[:, 384:512]
    ONB = CBT[0:1, 512:640]
    IR4 = CBT[:, 640:1152]

    smf = SMF
    DTA = smf[0:64, 0:2048]
    RTEN = smf[0:64, 0:2048]
    ABR = smf[64:128, 0:2048]
    ACS = smf[0:64, 2048:2176]
    CW0 = smf[:, 2176:2180]
    CW1 = smf[:, 2180:2184]
    CB0 = smf[:, 2184:2185]
    CB1 = smf[:, 2185:2186]
    COLS = smf[:, 2186:2250]
    M1C = smf[:, 2250:2266]
    M2C = smf[:, 2266:2282]
    M2Q = smf[:, 2282:2298]
    C1R = smf[0:16, 2304:2432]
    NWC = smf[:, 2432:2464]
    VARC = smf[:, 2464:2472]
    RSTC = smf[:, 2472:2480]
    SSCAL = smf[0:1, 2480:2496]
    SCL = smf[:, 2496:2520]
    C14 = smf[:, 2520:2524]
    CI4 = smf[:, 2524:2528]
    MXT = smf[:, 2528:2656]
    C2ROW = smf[0:1, 2656:2784]
    CRW1 = smf[0:1, 0:512]
    CRW2 = smf[0:1, 512:1024]

    def col(i):
        return COLS[:, i:i + 1]

    DTT = SMB[:, 0:1024]
    DST = SMB[:, 1024:2048]
    SDT = SMB[:, 2048:3072]
    CDB = SMB[:, 3072:4096]
    DMAT = SMB[:, 4096:4160]
    CDROW = SMB[0:1, 4160:4224]

    SP0 = SST[:, 0:4096]
    SP1 = SST[:, 4096:8192]

    def load_consts():
        w.op("V", "memset", col(44), 1e-5)
        w.op("V", "memset", col(45), -1.0, inc="V")
        w.op("S", "dma_start", CBT, cb_in, inc="dS", k=16)
        w.op("S", "dma_start", IDF, idf_in, inc="dS", k=16)
        w.op("S", "dma_start", ONF, onesf_in, inc="dS", k=16)
        w.op("S", "dma_start", ONC, onesc_in, inc="dS", k=16)

    def weight_prep():
        slot = [fview(ARW, i * 8192, 128, 2048) for i in range(3)]
        tslot = [ARW[:, 12288 + i * 2048: 12288 + (i + 1) * 2048] for i in range(2)]
        asm = [AR1[:, i * 2048:(i + 1) * 2048] for i in range(2)]
        junk = fview(AR2, 0, 128, 2048)
        run, spot = col(0), col(1)

        w.op("V", "memset", run, 0.0, inc="V")
        a_read = {}
        gi = 0
        for si, wm in enumerate((w_g, w_u, w_d)):
            wf = wm if wm.shape[1] == 2048 else wm.rearrange(
                "a (b c) -> (a b) c", c=2048)
            for t in range(64):
                sl = gi % 3
                if gi >= 3:
                    w.wait("S", "A", a_read[gi - 3])
                w.op("S", "dma_start", slot[sl], wf[t * 128:(t + 1) * 128, :],
                     inc="dS", k=16)
                w.wait("A", "dS", w.c["dS"])
                a_read[gi] = w.op("A", "activation", junk, slot[sl], AF.Abs,
                                  accum_out=spot, inc="A")
                w.wait("V", "A", a_read[gi])
                w.nop("V")
                w.op("V", "tensor_tensor", run, run, spot, ALU.add, inc="V")
                gi += 1
            w.wait("T", "V", w.c["V"])
            w.op("T", "matmul", pb[0][0:1, 0:1], run, ONC, start=True, stop=True,
                 inc="T")
            w.wait("V", "T", w.c["T"])
            sc = SSCAL[:, si:si + 1]
            w.op("V", "tensor_scalar", sc, pb[0][0:1, 0:1], 1.0 / (IFF * DM), 1e-5,
                 ALU.mult, ALU.max)
            w.nop("V")
            w.op("V", "reciprocal", SSCAL[:, 8:9], sc)
            w.nop("V")
            w.op("V", "memset", run, 0.0, inc="V")
            w.wait("T", "V", w.c["V"])
            w.nop("T")
            w.op("T", "matmul", pb[1][:, 0:1], ONF, SSCAL[:, 8:9], start=True,
                 stop=True, inc="T")
            w.wait("V", "T", w.c["T"])
            w.op("V", "tensor_copy", col(2 + si), pb[1][:, 0:1], inc="V")

        for si, tgt in ((0, SCL[:, 0:1]), (1, SCL[:, 1:2]), (2, SCL[:, 4:5])):
            w.wait("T", "V", w.c["V"])
            w.op("T", "matmul", pb[1][:, 1:2], ONF, SSCAL[:, si:si + 1], start=True,
                 stop=True, inc="T")
            w.wait("V", "T", w.c["T"])
            w.op("V", "tensor_copy", tgt, pb[1][:, 1:2], inc="V")
        w.nop("V")
        w.op("V", "tensor_tensor", SCL[:, 2:3], SCL[:, 1:2], SCL[:, 4:5], ALU.mult)
        w.nop("V")
        w.op("V", "tensor_scalar", SCL[:, 2:3], SCL[:, 2:3],
             float(1.0 / (127.0 * np.sqrt(8192.0))), 0.0, ALU.mult, ALU.add)
        w.op("V", "reciprocal", SCL[:, 3:4], SCL[:, 1:2])
        w.nop("V")
        w.op("V", "tensor_scalar", SCL[:, 3:4], SCL[:, 3:4],
             float(np.sqrt(8192.0) * 1e-5), 0.0, ALU.mult, ALU.add, inc="V")

        jobs = []
        for si, wm, dstT in ((0, w_g, s_wgT), (1, w_u, s_wuT)):
            for t in range(64):
                jobs.append(("tern", si, wm[t * 128:(t + 1) * 128, :], 128,
                             dstT[:, t * 128:(t + 1) * 128]
                             .rearrange("(k p) c -> p k c", k=16)))
        for q in range(4):
            for j in range(16):
                jobs.append(("tern", 2,
                             w_d[j * 128:(j + 1) * 128, q * 2048:(q + 1) * 2048], 128,
                             s_wdT[q * 2048:(q + 1) * 2048, j * 128:(j + 1) * 128]
                             .rearrange("(k p) c -> p k c", k=16)))
        for j in range(80):
            jobs.append(("cast", None, w_in[j * 128:(j + 1) * 128, :], 128,
                         s_winT[:, j * 128:(j + 1) * 128]
                         .rearrange("(k p) c -> p k c", k=16)))
        jobs.append(("cast", None, w_in[10240:10304, :], 64,
                     s_winT[:, 10240:10304].rearrange("(k p) c -> p k c", k=16)))
        for hf in range(2):
            for j in range(16):
                jobs.append(("cast", None,
                             w_out[j * 128:(j + 1) * 128, hf * 2048:(hf + 1) * 2048],
                             128,
                             s_woutT[hf * 2048:(hf + 1) * 2048, j * 128:(j + 1) * 128]
                             .rearrange("(k p) c -> p k c", k=16)))

        v_cons = {}
        v_ev = {}
        t_tp = {}
        g_wr = {}
        for t, (kind, si, srcap, rows, dstap) in enumerate(jobs):
            sl = t % 3
            ts = t % 2
            if t >= 3:
                w.wait("S", "V", v_cons[t - 3])
            w.op("S", "dma_start", slot[sl][0:rows, :], srcap, inc="dS", k=16)
            ld = w.c["dS"]
            w.wait("V", "dS", ld)
            if t >= 2:
                w.wait("V", "T", t_tp[t - 2])
            if t == 0:
                w.nop("V")
            sv = slot[sl][0:rows, :]
            tv = tslot[ts][0:rows, :]
            if kind == "tern":
                invc = col(2 + si)[0:rows, :]
                v_cons[t] = w.op("V", "tensor_scalar", junk[0:rows, :], sv, invc,
                                 -1.0, ALU.mult, ALU.max, inc="V")
                w.op("V", "tensor_scalar", junk[0:rows, :], junk[0:rows, :], 1.0,
                     RND, ALU.min, ALU.add)
                w.op("V", "tensor_scalar", tv, junk[0:rows, :], -RND, 0.0, ALU.add,
                     ALU.add, inc="V")
            else:
                v_cons[t] = w.op("V", "tensor_copy", tv, sv, inc="V")
            vt = w.c["V"]
            idn = IDB if rows == 128 else IDB[0:64, 0:64]
            pw = 128 if rows == 128 else 64
            for r in range(2):
                w.wait("T", "V", vt if r == 0 else w.c["V"])
                if t >= 1 or r >= 1:
                    w.wait("T", "V", w.c["V"])
                for k in range(8):
                    w.op("T", "transpose", pbt[:, k * pw:(k + 1) * pw],
                         tv[:, (r * 8 + k) * 128:(r * 8 + k + 1) * 128], idn,
                         inc=("T" if k == 7 else None))
                w.wait("V", "T", w.c["T"])
                if t >= 2:
                    w.wait("V", "dG", g_wr[t - 2])
                w.op("V", "tensor_copy",
                     asm[ts][:, r * 8 * pw:(r + 1) * 8 * pw], pbt[:, 0:8 * pw],
                     inc="V")
            t_tp[t] = w.c["T"]
            v_ev[t] = w.c["V"]
            w.wait("G", "V", v_ev[t])
            g_wr[t] = w.op("G", "dma_start", dstap,
                           asm[ts][:, 0:16 * pw].rearrange("p (k c) -> p k c", k=16),
                           inc="dG", k=16)

    def layernorm():
        xslot = [fview(AR2, 0, 128, 2048), fview(AR2, 8192, 128, 2048)]
        xnn = AR2[:, 8192:10240]
        sqs = fview(ARW, 0, 128, 2048)
        xnT = AR1[:, 0:32768]
        xnTv = xnT.rearrange("p (a c) -> p a c", a=16)
        for tt in range(16):
            sl = tt % 2
            w.wait("S", "V", w.c["V"])
            w.wait("S", "A", w.c["A"])
            w.op("S", "dma_start", xslot[sl], x_in[tt * 128:(tt + 1) * 128, :],
                 inc="dS", k=16)
            sumc, ssq, mc, vc, rc, nb = (col(8 + i) for i in range(6))
            w.wait("V", "dS", w.c["dS"])
            w.op("V", "reduce_sum", sumc, xslot[sl], AX.X)
            w.nop("V")
            w.op("V", "engine_nop", inc="V")
            w.wait("A", "dS", w.c["dS"])
            w.op("A", "activation", sqs, xslot[sl], AF.Square, accum_out=ssq,
                 inc="A")
            w.wait("V", "A", w.c["A"])
            w.nop("V")
            w.op("V", "tensor_scalar", mc, sumc, 1.0 / DM, 0.0, ALU.mult, ALU.add)
            w.op("V", "tensor_scalar", vc, ssq, 1.0 / DM, 0.0, ALU.mult, ALU.add)
            w.nop("V")
            w.op("V", "tensor_tensor", nb, mc, mc, ALU.mult)
            w.nop("V")
            w.op("V", "tensor_tensor", vc, vc, nb, ALU.subtract, inc="V")
            w.wait("A", "V", w.c["V"])
            w.nop("A")
            w.op("A", "activation", vc, vc, AF.Sqrt, bias=col(44), inc="A")
            w.wait("V", "A", w.c["A"])
            w.nop("V")
            w.op("V", "reciprocal", rc, vc)
            w.nop("V")
            w.op("V", "tensor_scalar", xnn, xslot[sl], mc, rc, ALU.subtract,
                 ALU.mult, inc="V")
            w.wait("T", "V", w.c["V"])
            for q in range(4):
                w.wait("T", "V", w.c["V"])
                for j in range(4):
                    d = q * 4 + j
                    w.op("T", "transpose", pbt[:, j * 128:(j + 1) * 128],
                         xnn[:, d * 128:(d + 1) * 128], IDB,
                         inc=("T" if j == 3 else None))
                w.wait("V", "T", w.c["T"])
                w.op("V", "tensor_copy",
                     xnTv[:, q * 4:(q + 1) * 4, tt * 128:(tt + 1) * 128],
                     pbt[:, 0:512].rearrange("p (a c) -> p a c", a=4), inc="V")
        return xnT

    def build_xq(xnT):
        zap = ARW[:, 0:2048]
        z1s = fview(ARW, 6144, 128, 8192)
        mx = fview(ARW, 4096, 128, 512)
        xnv = xnT.rearrange("p (a c) -> p a c", a=16)
        for d in range(16):
            w.wait("V", "dS", w.c["dS"])
            for q in range(4):
                bank = pb[q % 2]
                w.wait("T", "V", w.c["V"])
                w.op("T", "matmul", bank[:, 0:512], H128,
                     xnv[:, d, q * 512:(q + 1) * 512], start=True, stop=True, inc="T")
                w.wait("V", "T", w.c["T"])
                w.op("V", "tensor_copy", zap[:, q * 512:(q + 1) * 512], bank[:, 0:512],
                     inc="V")
            w.wait("S", "V", w.c["V"])
            w.op("S", "dma_start", s_za[d * 128:(d + 1) * 128, :], zap, inc="dS", k=16)
        zb = AR2[:, 0:8192]
        zbv = zb.rearrange("p (a c) -> p a c", a=16)
        xqs = AR2[:, 8192:16384]
        za_r = s_za.rearrange("(dh d3 dl) t -> (dh d3) dl t", dh=16, d3=8, dl=16)
        xq_w = s_xq.rearrange("(dh d3 dl) t -> (dh d3) dl t", dh=16, d3=8, dl=16)
        for tc in range(4):
            w.wait("S", "dS", w.c["dS"])
            w.wait("S", "T", w.c["T"])
            w.op("S", "dma_start", zbv, za_r[:, :, tc * 512:(tc + 1) * 512],
                 inc="dS", k=16)
            w.wait("T", "dS", w.c["dS"])
            for k in range(16):
                bank = pb[2 + (k % 2)]
                w.wait("T", "V", w.c["V"])
                w.op("T", "matmul", bank[:, 0:512], HB16, zb[:, k * 512:(k + 1) * 512],
                     start=True, stop=True, inc="T")
                w.wait("V", "T", w.c["T"])
                w.op("V", "tensor_copy", z1s[:, k * 512:(k + 1) * 512], bank[:, 0:512],
                     inc="V")
            w.op("V", "tensor_reduce", mx,
                 z1s.rearrange("p (a c) -> p c a", a=16), AX.X, ALU.max,
                 apply_absolute_value=True, inc="V")
            w.wait("T", "V", w.c["V"])
            for j in range(4):
                w.op("T", "transpose", pb[4][:, j * 128:(j + 1) * 128],
                     mx[:, j * 128:(j + 1) * 128], IDF, inc=("T" if j == 3 else None))
            w.wait("V", "T", w.c["T"])
            for j in range(4):
                w.op("V", "reduce_max", M1C[:, tc * 4 + j: tc * 4 + j + 1],
                     pb[4][:, j * 128:(j + 1) * 128], AX.X)
            mm4 = M1C[:, tc * 4: tc * 4 + 4]
            w.nop("V")
            w.op("V", "tensor_scalar", mm4, mm4, float(np.sqrt(2048.0) * 1e-5), 0.0,
                 ALU.max, ALU.add)
            w.nop("V")
            w.op("V", "reciprocal", C14, mm4)
            w.nop("V")
            w.op("V", "tensor_scalar", C14, C14, 127.0, 0.0, ALU.mult, ALU.add)
            w.op("V", "tensor_scalar", CI4, mm4, float(1.0 / (127.0 * np.sqrt(2048.0))),
                 0.0, ALU.mult, ALU.add, inc="V")
            w.wait("T", "V", w.c["V"])
            w.nop("T")
            for j in range(4):
                w.op("T", "transpose", pb[4][0:1, j * 128:(j + 1) * 128],
                     C14[:, j:j + 1], IDF)
                w.op("T", "transpose", pb[6][0:1, j * 128:(j + 1) * 128],
                     CI4[:, j:j + 1], IDF, inc=("T" if j == 3 else None))
            w.wait("V", "T", w.c["T"])
            w.op("V", "tensor_copy", CRW1, pb[4][0:1, 0:512])
            w.op("V", "tensor_copy", CRW2, pb[6][0:1, 0:512], inc="V")
            w.wait("T", "V", w.c["V"])
            w.nop("T")
            for j in range(4):
                w.op("T", "matmul", pb[5][:, j * 128:(j + 1) * 128], ONF,
                     CRW1[0:1, j * 128:(j + 1) * 128], start=True, stop=True)
                w.op("T", "matmul", pb[6][:, j * 128:(j + 1) * 128], ONF,
                     CRW2[0:1, j * 128:(j + 1) * 128], start=True, stop=True,
                     inc=("T" if j == 3 else None))
            w.wait("V", "T", w.c["T"])
            w.wait("V", "dS", w.c["dS"])
            z3 = z1s.rearrange("p (a c) -> p a c", a=16)
            xq3 = xqs.rearrange("p (a c) -> p a c", a=16)
            for j in range(4):
                w.op("V", "tensor_tensor", z3[:, :, j * 128:(j + 1) * 128],
                     z3[:, :, j * 128:(j + 1) * 128],
                     pb[5][:, j * 128:(j + 1) * 128].unsqueeze(1)
                     .broadcast_to([128, 16, 128]), ALU.mult)
            w.op("V", "tensor_scalar", z1s, z1s, RND, -RND, ALU.add, ALU.add)
            for j in range(4):
                w.op("V", "tensor_tensor", xq3[:, :, j * 128:(j + 1) * 128],
                     z3[:, :, j * 128:(j + 1) * 128],
                     pb[6][:, j * 128:(j + 1) * 128].unsqueeze(1)
                     .broadcast_to([128, 16, 128]), ALU.mult, inc=("V" if j == 3
                                                                   else None))
            w.wait("S", "V", w.c["V"])
            w.op("S", "dma_start", xq_w[:, :, tc * 512:(tc + 1) * 512],
                 xqs.rearrange("p (a c) -> p a c", a=16), inc="dS", k=16)
        xq = AR2[:, 0:32768]
        w.wait("S", "dS", w.c["dS"])
        xqv = xq.rearrange("p (a c) -> p a c", a=16)
        for k in range(16):
            w.op("S", "dma_start", xqv[:, k, :], s_xq[k * 128:(k + 1) * 128, :],
                 inc="dS", k=16)
        return xq

    def ffn_f1(xq):
        wslot = [ARW[:, 0:4096], ARW[:, 4096:8192]]
        silg = [ARW[:, 8192 + i * 512: 8192 + (i + 1) * 512] for i in range(2)]
        y1b = [ARW[:, 9216 + i * 512: 9216 + (i + 1) * 512] for i in range(2)]
        yab = [ARW[:, 10240 + i * 512: 10240 + (i + 1) * 512] for i in range(2)]
        xqv = xq.rearrange("p (a c) -> p a c", a=16)
        t_mm = {}
        t_sa = {}
        a_done = {}
        v_y1 = {}
        v_evac = {}
        s_store = {}
        t_wcons = {}

        def emit_stage_a(jt):
            pa = pb[4 + (jt % 2)]
            w.wait("T", "V", v_y1[jt])
            if jt >= 2:
                w.wait("T", "V", v_evac[jt - 2])
            t_sa[jt] = w.op("T", "matmul", pa[:, 0:512], H128, y1b[jt % 2],
                            start=True, stop=True, inc="T")
            w.wait("V", "T", t_sa[jt])
            if jt >= 2:
                w.wait("V", "dS", s_store[jt - 2])
            v_evac[jt] = w.op("V", "tensor_copy", yab[jt % 2], pa[:, 0:512], inc="V")
            w.wait("S", "V", v_evac[jt])
            o_, tc_ = jt // 4, jt % 4
            s_store[jt] = w.op("S", "dma_start",
                               s_ya[o_ * 128:(o_ + 1) * 128, tc_ * 512:(tc_ + 1) * 512],
                               yab[jt % 2], inc="dS", k=16)

        for o in range(64):
            sl = o % 2
            if o >= 2:
                w.wait("S", "T", t_wcons[o - 2])
            w.op("S", "dma_start",
                 wslot[sl][:, 0:2048].rearrange("p (k c) -> p k c", k=16),
                 s_wgT[:, o * 128:(o + 1) * 128]
                 .rearrange("(k p) c -> p k c", k=16), inc="dW", k=16)
            w.op("S", "dma_start",
                 wslot[sl][:, 2048:4096].rearrange("p (k c) -> p k c", k=16),
                 s_wuT[:, o * 128:(o + 1) * 128]
                 .rearrange("(k p) c -> p k c", k=16), inc="dW", k=16)
            wz = w.c["dW"]
            for tc in range(4):
                it = o * 4 + tc
                pg = pb[0 + (tc % 2)]
                pu = pb[2 + (tc % 2)]
                if tc == 0:
                    w.wait("T", "dW", wz)
                if it >= 2:
                    w.wait("T", "A", a_done[it - 2])
                    w.wait("T", "V", v_y1[it - 2])
                for k in range(16):
                    w.op("T", "matmul", pg[:, 0:512],
                         wslot[sl][:, k * 128:(k + 1) * 128],
                         xqv[:, k, tc * 512:(tc + 1) * 512], start=(k == 0),
                         stop=(k == 15))
                    w.op("T", "matmul", pu[:, 0:512],
                         wslot[sl][:, 2048 + k * 128: 2048 + (k + 1) * 128],
                         xqv[:, k, tc * 512:(tc + 1) * 512], start=(k == 0),
                         stop=(k == 15), inc=("T" if k == 15 else None))
                t_mm[it] = w.c["T"]
                w.wait("A", "T", t_mm[it])
                if it >= 2:
                    w.wait("A", "V", v_y1[it - 2])
                a_done[it] = w.op("A", "activation", silg[tc % 2], pg[:, 0:512],
                                  AF.Silu, scale=SCL[:, 0:1], inc="A")
                w.wait("V", "A", a_done[it])
                v_y1[it] = w.op("V", "tensor_tensor", y1b[tc % 2], silg[tc % 2],
                                pu[:, 0:512], ALU.mult, inc="V")
                if it >= 1:
                    emit_stage_a(it - 1)
            t_wcons[o] = w.c["T"]
        emit_stage_a(255)

    def ffn_f2a():
        zb = AR2[:, 0:8192]
        zbv = zb.rearrange("p (a c) -> p a c", a=64)
        z2s = fview(ARW, 0, 128, 8192)
        z3 = z2s.rearrange("p (a c) -> p a c", a=64)
        r2b = AR2[:, 8192:16384]
        r2v = r2b.rearrange("p (a c) -> p a c", a=64)
        ya_r = s_ya.rearrange("(oh b6 ol) t -> (oh b6) ol t", oh=64, b6=2, ol=64)
        r2_w = s_r2.rearrange("(oh b6 ol) t -> (oh b6) ol t", oh=64, b6=2, ol=64)
        for tc in range(16):
            w.wait("S", "T", w.c["T"])
            w.wait("S", "dS", w.c["dS"])
            for part in range(4):
                w.op("S", "dma_start", zbv[:, part * 16:(part + 1) * 16, :],
                     ya_r[:, part * 16:(part + 1) * 16, tc * 128:(tc + 1) * 128],
                     inc="dS", k=16)
            w.wait("T", "dS", w.c["dS"])
            for k in range(16):
                bank = pb[k % 2]
                w.wait("T", "V", w.c["V"])
                for j in range(4):
                    w.op("T", "matmul", bank[:, j * 128:(j + 1) * 128], HB64,
                         zb[:, (k * 4 + j) * 128:(k * 4 + j + 1) * 128],
                         start=True, stop=True, inc=("T" if j == 3 else None))
                w.wait("V", "T", w.c["T"])
                w.op("V", "tensor_copy", z2s[:, k * 512:(k + 1) * 512], bank[:, 0:512],
                     inc="V")
            zt = z2s.rearrange("p (a c) -> p c a", a=64)
            w.op("V", "tensor_reduce", MXT, zt, AX.X, ALU.max,
                 apply_absolute_value=True, inc="V")
            w.wait("T", "V", w.c["V"])
            w.op("T", "transpose", pb[2][:, 0:128], MXT, IDF, inc="T")
            w.wait("V", "T", w.c["T"])
            mcol = M2C[:, tc:tc + 1]
            w.op("V", "reduce_max", mcol, pb[2][:, 0:128], AX.X)
            w.nop("V")
            w.op("V", "tensor_tensor", mcol, mcol, SCL[:, 3:4], ALU.max)
            w.nop("V")
            c2c = col(30)
            w.op("V", "reciprocal", c2c, mcol)
            w.nop("V")
            w.op("V", "tensor_scalar", c2c, c2c, 127.0, 0.0, ALU.mult, ALU.add,
                 inc="V")
            w.wait("T", "V", w.c["V"])
            w.nop("T")
            w.op("T", "transpose", pb[2][0:1, 0:128], c2c, IDF, inc="T")
            w.wait("V", "T", w.c["T"])
            w.op("V", "tensor_copy", C2ROW, pb[2][0:1, 0:128], inc="V")
            w.wait("T", "V", w.c["V"])
            w.nop("T")
            w.op("T", "matmul", pb[3][:, 0:128], ONF, C2ROW, start=True,
                 stop=True, inc="T")
            w.wait("V", "T", w.c["T"])
            w.wait("V", "dS", w.c["dS"])
            w.op("V", "tensor_tensor", z3, z3,
                 pb[3][:, 0:128].unsqueeze(1).broadcast_to([128, 64, 128]), ALU.mult)
            w.op("V", "tensor_scalar", r2v, z3, RND, -RND, ALU.add, ALU.add, inc="V")
            w.wait("S", "V", w.c["V"])
            for part in range(4):
                w.op("S", "dma_start",
                     r2_w[:, part * 16:(part + 1) * 16, tc * 128:(tc + 1) * 128],
                     r2v[:, part * 16:(part + 1) * 16, :], inc="dS", k=16)
        w.nop("V")
        w.op("V", "tensor_tensor", M2Q, M2C, SCL[:, 2:3].broadcast_to([128, 16]),
             ALU.mult, inc="V")

    def dt_family():
        dtspb = ARW[:, 12294:14342][0:64, :]
        sdch = ARW[:, 14342:14470][0:64, :]
        dsch = ARW[:, 14470:14598][0:64, :]
        T1 = fview(ARW, 31744, 64, 2048)
        ABR2 = T1
        w.wait("A", "V", w.c["V"])
        w.op("A", "activation", T1, DTA, AF.Abs)
        w.op("A", "activation", T1, T1, AF.Exp, scale=col(45)[0:64, :])
        w.op("A", "activation", T1, T1, AF.Ln, bias=1.0, inc="A")
        w.wait("V", "A", w.c["A"])
        w.op("V", "tensor_scalar", DTA, DTA, 0.0, 0.0, ALU.max, ALU.add)
        w.op("V", "tensor_tensor", DTA, DTA, T1, ALU.add, inc="V")
        w.wait("A", "V", w.c["V"])
        w.op("A", "copy", dtspb, DTA, inc="A")
        w.wait("T", "A", w.c["A"])
        for c in range(NCH):
            if c % 4 == 0:
                w.wait("T", "V", w.c["V"])
            w.op("T", "transpose", pbt[:, (c % 4) * 128:(c % 4) * 128 + 64],
                 dtspb[:, c * 128:(c + 1) * 128], IDB[0:64, 0:64],
                 inc=("T" if c % 4 == 3 else None))
            if c % 4 == 3:
                w.wait("V", "T", w.c["T"])
                w.op("V", "tensor_copy",
                     DTT[:, (c - 3) * 64:(c + 1) * 64].rearrange(
                         "p (a c) -> p a c", a=4),
                     pbt[:, 0:512].rearrange("p (a c) -> p a c", a=4)[:, :, 0:64],
                     inc="V")
        w.wait("A", "dS", w.c["dS"])
        w.wait("A", "V", w.c["V"])
        w.op("A", "activation", col(41)[0:64, :], col(41)[0:64, :], AF.Exp,
             inc="A")
        w.wait("V", "A", w.c["A"])
        w.nop("V")
        w.op("V", "tensor_scalar", col(41)[0:64, :], col(41)[0:64, :], -1.0, 0.0,
             ALU.mult, ALU.add)
        w.nop("V")
        w.op("V", "tensor_scalar", ABR2, DTA, col(41)[0:64, :], 0.0, ALU.mult,
             ALU.add, inc="V")
        zc = col(43)
        w.op("V", "memset", zc, 0.0)
        w.nop("V")
        w.op("V", "engine_nop", inc="V")
        for c in range(NCH):
            w.op("V", "tensor_tensor_scan", ACS, ABR2[:, c * 128:(c + 1) * 128],
                 zc[0:64, :].broadcast_to([64, 128]), 0.0, ALU.add, ALU.add, inc="V")
            w.wait("A", "V", w.c["V"])
            w.nop("A")
            w.op("A", "activation", sdch, ACS, AF.Exp)
            w.op("A", "activation", dsch, ACS, AF.Exp, scale=col(45)[0:64, :],
                 bias=ACS[:, 127:128], inc="A")
            w.wait("T", "A", w.c["A"])
            w.nop("T")
            w.wait("T", "V", w.c["V"])
            w.op("T", "transpose", pbt[:, 0:64], sdch, IDB[0:64, 0:64])
            w.op("T", "transpose", pbt[:, 128:192], dsch, IDB[0:64, 0:64], inc="T")
            w.op("T", "transpose", pbt[0:1, 256:320], sdch[:, 127:128],
                 IDB[0:64, 0:64], inc="T")
            w.wait("V", "T", w.c["T"])
            w.op("V", "tensor_copy", SDT[:, c * 64:(c + 1) * 64], pbt[:, 0:64])
            w.op("V", "tensor_copy", DST[:, c * 64:(c + 1) * 64], pbt[:, 128:192])
            w.op("V", "tensor_copy", CDROW, pbt[0:1, 256:320], inc="V")
            w.wait("T", "V", w.c["V"])
            w.nop("T")
            w.op("T", "matmul", pb[3][:, 0:64], ONB, CDROW, start=True, stop=True,
                 inc="T")
            w.wait("V", "T", w.c["T"])
            w.op("V", "tensor_copy", CDB[:, c * 64:(c + 1) * 64], pb[3][:, 0:64],
                 inc="V")
        w.wait("A", "V", w.c["V"])
        w.op("A", "activation", RTEN, ABR2, AF.Exp, inc="A")
        w.wait("V", "A", w.c["A"])
        for c in range(NCH):
            w.op("V", "memset", RTEN[:, c * 128: c * 128 + 1], 0.0,
                 inc=("V" if c == NCH - 1 else None))
        w.wait("T", "V", w.c["V"])
        w.op("T", "transpose", pb[3][0:1, 0:128], col(42), IDF, inc="T")
        w.wait("V", "T", w.c["T"])
        w.op("V", "tensor_copy", C1R[0:1, 0:128], pb[3][0:1, 0:128], inc="V")
        w.wait("T", "V", w.c["V"])
        w.nop("T")
        w.op("T", "matmul", pb[3][:, 0:64], ONF, C1R[0:1, 0:64], start=True,
             stop=True, inc="T")
        w.wait("V", "T", w.c["T"])
        w.op("V", "tensor_copy", DMAT, pb[3][:, 0:64], inc="V")

    def mixer_m1(xnT):
        wslot = [ARW[:, 0:4096], ARW[:, 4096:8192]]
        cvin = [ARW[:, 8192:10243], ARW[:, 10243:12294]]
        hout = ARW[:, 12294:14342]
        gpiece = ARW[:, 14342:14854]
        xnv = xnT.rearrange("p (a c) -> p a c", a=16)
        w.op("S", "dma_start", col(40)[0:64, :],
             dt_bias.rearrange("(a b) -> a b", b=1), inc="dS", k=16)
        w.op("S", "dma_start", col(41)[0:64, :],
             a_log.rearrange("(a b) -> a b", b=1), inc="dS", k=16)
        w.op("S", "dma_start", col(42)[0:64, :],
             dvec.rearrange("(a b) -> a b", b=1), inc="dS", k=16)
        w.op("V", "memset", cvin[0][:, 0:3], 0.0)
        w.op("V", "memset", cvin[1][:, 0:3], 0.0, inc="V")

        groups = [("dt", [80])]
        groups += [("hbc", list(range(32 + 2 * i, 34 + 2 * i))) for i in range(24)]
        groups += [("gate", list(range(2 * i, 2 * i + 2))) for i in range(16)]

        t_after = {}
        for gi, (kind, js) in enumerate(groups):
            sl = gi % 2
            j0 = js[0]
            rows = 64 if kind == "dt" else 256
            if gi >= 2:
                w.wait("S", "T", t_after[gi - 2])
            w.op("S", "dma_start",
                 wslot[sl][:, 0:4096].rearrange("p (k c) -> p k c", k=16)[:, :, 0:rows],
                 s_winT[:, j0 * 128: j0 * 128 + rows]
                 .rearrange("(k p) c -> p k c", k=16), inc="dW", k=16)
            w.wait("T", "dW", w.c["dW"])
            for jj, j in enumerate(js):
                mrows = 64 if kind == "dt" else 128
                for tc in range(4):
                    bank = pb[tc % 2]
                    w.wait("T", "V", w.c["V"])
                    w.wait("T", "A", w.c["A"])
                    for k in range(16):
                        w.op("T", "matmul", bank[0:mrows, 0:512],
                             wslot[sl][:, k * 256 + jj * 128: k * 256 + jj * 128 + mrows],
                             xnv[:, k, tc * 512:(tc + 1) * 512],
                             start=(k == 0), stop=(k == 15),
                             inc=("T" if k == 15 else None))
                    if kind == "dt":
                        w.wait("V", "T", w.c["T"])
                        w.wait("V", "dS", w.c["dS"])
                        w.op("V", "tensor_scalar", DTA[:, tc * 512:(tc + 1) * 512],
                             bank[0:64, 0:512], col(40)[0:64, :], 0.0, ALU.add,
                             ALU.add, inc="V")
                    elif kind == "hbc":
                        w.wait("A", "T", w.c["T"])
                        w.wait("A", "V", w.c["V"])
                        w.op("A", "copy", cvin[jj % 2][:, 3 + tc * 512: 3 + (tc + 1) * 512],
                             bank[:, 0:512], inc="A")
                    else:
                        w.wait("A", "T", w.c["T"])
                        w.wait("A", "dS", w.c["dS"])
                        w.op("A", "copy", gpiece, bank[:, 0:512], inc="A")
                        w.wait("S", "A", w.c["A"])
                        w.op("S", "dma_start",
                             s_gate[j * 128:(j + 1) * 128, tc * 512:(tc + 1) * 512],
                             gpiece, inc="dS", k=16)
                if kind == "hbc":
                    ci = j - 32
                    cwsp = CW0 if jj % 2 == 0 else CW1
                    cbsp = CB0 if jj % 2 == 0 else CB1
                    w.wait("S", "V", w.c["V"])
                    w.op("S", "dma_start", cwsp, conv_w[ci * 128:(ci + 1) * 128, :],
                         inc="dS", k=16)
                    w.op("S", "dma_start", cbsp,
                         conv_b[ci * 128:(ci + 1) * 128].rearrange("(a b) -> a b", b=1),
                         inc="dS", k=16)
                    w.wait("V", "dS", w.c["dS"])
                    w.wait("V", "A", w.c["A"])
                    src = cvin[jj % 2]
                    w.op("V", "tensor_scalar", hout, src[:, 0:2048], cwsp[:, 0:1],
                         0.0, ALU.mult, ALU.add)
                    for k in range(1, 4):
                        w.op("V", "scalar_tensor_tensor", hout, src[:, k:2048 + k],
                             cwsp[:, k:k + 1], hout, ALU.mult, ALU.add,
                             inc=("V" if k == 3 else None))
                    w.wait("A", "V", w.c["V"])
                    w.op("A", "activation", hout, hout, AF.Silu, bias=cbsp, inc="A")
                    w.wait("S", "A", w.c["A"])
                    w.op("S", "dma_start", s_hbc[ci * 128:(ci + 1) * 128, :], hout,
                         inc="dS", k=16)
            if kind == "dt":
                dt_family()
            t_after[gi] = w.c["T"]

    def ssm():
        XT = AR2[:, 0:4096]
        GT = AR2[:, 4096:8192]
        BT = AR2[:, 8192:9216]
        BCF = AR2[:, 9216:11264]
        XDT = AR2[:, 11264:15360]
        XDD = AR2[:, 15360:19456]
        SILG = AR2[:, 19456:23552]
        GGB = AR2[:, 23552:27648]
        ONT = AR2[:, 27648:31744]
        LMK = AR1[:, 0:1024]
        MPR = AR1[:, 1024:2048]
        YTMP = AR1[:, 2048:2560]
        Y2B = AR1[:, 2560:3072]
        XDB = AR1[:, 4096:8192]
        SQS = fview(ARW, 0, 128, 4096)
        w.op("V", "memset", SP0, 0.0, inc="V")
        for c in range(NCH):
            Sold = SP0 if c % 2 == 0 else SP1
            Snew = SP1 if c % 2 == 0 else SP0
            w.wait("S", "V", w.c["V"])
            w.wait("S", "T", w.c["T"])
            w.wait("S", "A", w.c["A"])
            w.op("S", "dma_start_transpose", XT, s_hbc[0:4096, c * 128:(c + 1) * 128],
                 inc="dS", k=16)
            w.op("S", "dma_start_transpose", GT,
                 s_gate[0:4096, c * 128:(c + 1) * 128], inc="dS", k=16)
            w.op("S", "dma_start_transpose", BT,
                 s_hbc[4096:5120, c * 128:(c + 1) * 128], inc="dS", k=16)
            w.op("S", "dma_start", BCF.rearrange("p (a c) -> p a c", a=16),
                 s_hbc[4096:6144, c * 128:(c + 1) * 128]
                 .rearrange("(a p) c -> p a c", a=16), inc="dS", k=16)
            w.wait("V", "dS", w.c["dS"])
            xtv = XT.rearrange("p (h q) -> p h q", h=64)
            w.op("V", "tensor_tensor", XDT.rearrange("p (h q) -> p h q", h=64), xtv,
                 DTT[:, c * 64:(c + 1) * 64].unsqueeze(2).broadcast_to([128, 64, 64]),
                 ALU.mult)
            w.op("V", "tensor_tensor", XDD.rearrange("p (h q) -> p h q", h=64),
                 XDT.rearrange("p (h q) -> p h q", h=64),
                 DST[:, c * 64:(c + 1) * 64].unsqueeze(2).broadcast_to([128, 64, 64]),
                 ALU.mult)
            w.op("V", "tensor_tensor", XDB.rearrange("p (h q) -> p h q", h=64), xtv,
                 DMAT.unsqueeze(2).broadcast_to([128, 64, 64]), ALU.mult, inc="V")
            w.wait("A", "dS", w.c["dS"])
            w.wait("A", "V", w.c["V"])
            w.op("A", "activation", SILG, GT, AF.Silu, inc="A")
            for g in range(8):
                w.wait("T", "V", w.c["V"])
                w.op("T", "matmul", pb[0][:, 0:128], BCF[:, g * 128:(g + 1) * 128],
                     BCF[:, 1024 + g * 128: 1024 + (g + 1) * 128], start=True,
                     stop=True)
                for i in range(8):
                    bank = pb[1 + i // 4]
                    w.op("T", "matmul", bank[:, (i % 4) * 128:(i % 4 + 1) * 128],
                         IDF[0:64, 8 * g + i: 8 * g + i + 1].broadcast_to([64, 128]),
                         RTEN[:, c * 128:(c + 1) * 128],
                         start=True, stop=True, inc=("T" if i == 7 else None))
                w.wait("V", "T", w.c["T"])
                for half in range(2):
                    w.op("V", "tensor_tensor_scan",
                         LMK[:, half * 512:(half + 1) * 512],
                         pb[1 + half][:, 0:512], IR4, 0.0,
                         ALU.mult, ALU.add)
                for half in range(2):
                    w.op("V", "tensor_tensor",
                         MPR[:, half * 512:(half + 1) * 512]
                         .rearrange("p (a c) -> p a c", a=4),
                         pb[0][:, 0:128].unsqueeze(1).broadcast_to([128, 4, 128]),
                         LMK[:, half * 512:(half + 1) * 512]
                         .rearrange("p (a c) -> p a c", a=4), ALU.mult,
                         inc=("V" if half == 1 else None))
                w.wait("T", "V", w.c["V"])
                for i in range(8):
                    w.op("T", "matmul", pb[3][:, i * 64:(i + 1) * 64],
                         MPR[:, i * 128:(i + 1) * 128],
                         XDT[:, (8 * g + i) * 64:(8 * g + i + 1) * 64],
                         start=True, stop=True)
                w.op("T", "matmul", pb[4][:, 0:512], BT[:, g * 128:(g + 1) * 128],
                     XDD[:, g * 512:(g + 1) * 512], start=True, stop=True)
                w.op("T", "matmul", pb[5][:, 0:512],
                     BCF[:, 1024 + g * 128:1024 + (g + 1) * 128],
                     Sold[:, g * 512:(g + 1) * 512], start=True, stop=True, inc="T")
                w.wait("V", "T", w.c["T"])
                w.op("V", "tensor_tensor",
                     Snew[:, g * 512:(g + 1) * 512].rearrange("p (a c) -> p a c", a=8),
                     Sold[:, g * 512:(g + 1) * 512].rearrange("p (a c) -> p a c", a=8),
                     CDB[:, c * 64 + g * 8: c * 64 + (g + 1) * 8].unsqueeze(2)
                     .broadcast_to([128, 8, 64]), ALU.mult)
                w.op("V", "tensor_tensor", Snew[:, g * 512:(g + 1) * 512],
                     Snew[:, g * 512:(g + 1) * 512], pb[4][:, 0:512], ALU.add)
                w.op("V", "tensor_tensor", YTMP.rearrange("p (a c) -> p a c", a=8),
                     pb[5][:, 0:512].rearrange("p (a c) -> p a c", a=8),
                     SDT[:, c * 64 + g * 8: c * 64 + (g + 1) * 8].unsqueeze(2)
                     .broadcast_to([128, 8, 64]), ALU.mult)
                w.op("V", "scalar_tensor_tensor", Y2B, pb[3][:, 0:512], 1.0, YTMP,
                     ALU.mult, ALU.add)
                w.op("V", "tensor_tensor", Y2B, Y2B, XDB[:, g * 512:(g + 1) * 512],
                     ALU.add)
                w.op("V", "tensor_tensor", GGB[:, g * 512:(g + 1) * 512], Y2B,
                     SILG[:, g * 512:(g + 1) * 512], ALU.mult, inc="V")
            w.wait("A", "V", w.c["V"])
            for g in range(8):
                w.op("A", "activation", SQS[:, g * 512:(g + 1) * 512],
                     GGB[:, g * 512:(g + 1) * 512], AF.Square,
                     accum_out=VARC[:, g:g + 1], inc=("A" if g == 7 else None))
            w.wait("V", "A", w.c["A"])
            w.op("V", "tensor_scalar", VARC, VARC, 1.0 / 512.0, 0.0, ALU.mult,
                 ALU.add, inc="V")
            w.wait("A", "V", w.c["V"])
            w.nop("A")
            w.op("A", "activation", VARC, VARC, AF.Sqrt, bias=col(44), inc="A")
            w.wait("V", "A", w.c["A"])
            w.nop("V")
            w.op("V", "reciprocal", RSTC, VARC)
            w.nop("V")
            w.wait("V", "dS", w.c["dS"])
            w.op("V", "tensor_tensor", ONT.rearrange("p (a c) -> p a c", a=8),
                 GGB.rearrange("p (a c) -> p a c", a=8),
                 RSTC.unsqueeze(2).broadcast_to([128, 8, 512]), ALU.mult, inc="V")
            w.wait("S", "V", w.c["V"])
            w.op("S", "dma_start", s_on[c * 128:(c + 1) * 128, :], ONT, inc="dS",
                 k=16)

    def wout_phase():
        w.op("S", "dma_start", fview(ARW, 0, 32, 128),
             norm_w.rearrange("(a b) -> a b", b=128), inc="dS", k=16)
        w.wait("T", "dS", w.c["dS"])
        w.op("T", "transpose", pb[0][:, 0:32], fview(ARW, 0, 32, 128),
             IDF[0:32, 0:32], inc="T")
        w.wait("V", "T", w.c["T"])
        w.op("V", "tensor_copy", NWC, pb[0][:, 0:32])
        w.nop("V")
        w.op("V", "engine_nop", inc="V")

        onfm = AR1[:, 0:16384]
        onv = onfm.rearrange("p (a c) -> p a c", a=32)
        wc = [AR2[:, 0:8192], AR2[:, 8192:16384]]
        xsl = [fview(ARW, 4096 + i * 1024, 128, 256) for i in range(4)]
        opiece = [fview(ARW, 8192 + i * 1024, 128, 256) for i in range(4)]
        t_on = {}
        t_wc = {}
        v_evac = {}
        o_dma = {}
        v_scale = {}
        for tq in range(4):
            if tq >= 1:
                w.wait("S", "T", t_on[tq - 1])
                w.wait("S", "V", v_scale[tq - 1])
            for i in range(32):
                w.op("S", "dma_start_transpose", onv[:, i, :],
                     s_on[tq * 512:(tq + 1) * 512, i * 128:(i + 1) * 128],
                     inc="dS", k=16)
            w.wait("V", "dS", w.c["dS"])
            for i in range(32):
                w.op("V", "tensor_scalar", onv[:, i, :], onv[:, i, :],
                     NWC[:, i:i + 1], 0.0, ALU.mult, ALU.add,
                     inc=("V" if i == 31 else None))
            v_scale[tq] = w.c["V"]
            for dg in range(8):
                it = tq * 8 + dg
                sl = dg % 2
                if it >= 2:
                    w.wait("S", "T", t_wc[it - 2])
                for half in range(2):
                    w.op("S", "dma_start",
                         wc[sl].rearrange("p (a c) -> p a c", a=32)
                         [:, half * 16:(half + 1) * 16, :],
                         s_woutT[:, dg * 256:(dg + 1) * 256]
                         .rearrange("(k p) c -> p k c", k=32)
                         [:, half * 16:(half + 1) * 16, :], inc="dW", k=16)
                w.wait("T", "dW", w.c["dW"])
                w.wait("T", "V", v_scale[tq])
                wv = wc[sl].rearrange("p (a c) -> p a c", a=32)
                for ts in range(4):
                    pidx = (tq, dg, ts)
                    tsg = tq * 4 + ts
                    bank = pb[ts % 4]
                    prevv = (tq, dg - 1, ts) if dg >= 1 else ((tq - 1, 7, ts) if tq else None)
                    if prevv in o_dma:
                        w.wait("S", "dO", o_dma[prevv])
                    w.op("S", "dma_start", xsl[ts],
                         x_in[tsg * 128:(tsg + 1) * 128, dg * 256:(dg + 1) * 256],
                         inc="dS", k=16)
                    x_ld = w.c["dS"]
                    if prevv in v_evac:
                        w.wait("T", "V", v_evac[prevv])
                    for k in range(32):
                        w.op("T", "matmul", bank[:, 0:256],
                             onv[:, k, ts * 128:(ts + 1) * 128], wv[:, k, :],
                             start=(k == 0), stop=(k == 31),
                             inc=("T" if k == 31 else None))
                    w.wait("V", "T", w.c["T"])
                    w.wait("V", "dS", x_ld)
                    if prevv in o_dma:
                        w.wait("V", "dO", o_dma[prevv])
                    v_evac[pidx] = w.op("V", "tensor_tensor", opiece[ts],
                                        bank[:, 0:256], xsl[ts], ALU.add, inc="V")
                    w.wait("S", "V", v_evac[pidx])
                    o_dma[pidx] = w.op("S", "dma_start",
                                       out_hbm[tsg * 128:(tsg + 1) * 128,
                                               dg * 256:(dg + 1) * 256],
                                       opiece[ts], inc="dO", k=16)
                t_wc[it] = w.c["T"]
            t_on[tq] = w.c["T"]

    def down_phase():
        rres = AR1[:, 0:32768]
        rv = rres.rearrange("p (a c) -> p a c", a=64)
        wc = [AR2[:, 0:16384], AR2[:, 16384:32768]]
        opiece = [fview(ARW, i * 1024, 128, 256) for i in range(4)]
        t_r = {}
        t_wc = {}
        v_evac = {}
        o_dma = {}
        for tq in range(4):
            if tq >= 1:
                w.wait("S", "T", t_r[tq - 1])
            for i in range(64):
                w.op("S", "dma_start", rv[:, i, :],
                     s_r2[i * 128:(i + 1) * 128, tq * 512:(tq + 1) * 512],
                     inc="dS", k=16)
            rq_loaded = w.c["dS"]
            for dg in range(8):
                it = tq * 8 + dg
                sl = dg % 2
                if it >= 2:
                    w.wait("S", "T", t_wc[it - 2])
                for half in range(4):
                    w.op("S", "dma_start",
                         wc[sl].rearrange("p (a c) -> p a c", a=64)
                         [:, half * 16:(half + 1) * 16, :],
                         s_wdT[:, dg * 256:(dg + 1) * 256]
                         .rearrange("(k p) c -> p k c", k=64)
                         [:, half * 16:(half + 1) * 16, :], inc="dW", k=16)
                w.wait("T", "dW", w.c["dW"])
                w.wait("T", "dS", rq_loaded)
                wv = wc[sl].rearrange("p (a c) -> p a c", a=64)
                for ts in range(4):
                    pidx = (tq, dg, ts)
                    bank = pb[ts % 4]
                    prev = (tq, dg - 1, ts) if dg >= 1 else ((tq - 1, 7, ts) if tq else None)
                    if prev in v_evac:
                        w.wait("T", "V", v_evac[prev])
                    for k in range(64):
                        w.op("T", "matmul", bank[:, 0:256],
                             rv[:, k, ts * 128:(ts + 1) * 128], wv[:, k, :],
                             start=(k == 0), stop=(k == 63),
                             inc=("T" if k == 63 else None))
                    w.wait("V", "T", w.c["T"])
                    prev2 = (tq, dg - 1, ts) if dg >= 1 else ((tq - 1, 7, ts) if tq else None)
                    if prev2 in o_dma:
                        w.wait("V", "dO", o_dma[prev2])
                    tsg = tq * 4 + ts
                    v_evac[pidx] = w.op("V", "tensor_scalar", opiece[ts], bank[:, 0:256],
                                        M2Q[:, tsg:tsg + 1], 0.0, ALU.mult, ALU.add,
                                        inc="V")
                    w.wait("G", "V", v_evac[pidx])
                    o_dma[pidx] = w.op("G", "dma_start",
                                       out_hbm[tsg * 128:(tsg + 1) * 128,
                                               dg * 256:(dg + 1) * 256],
                                       opiece[ts], accum_op=ALU.add, inc="dO", k=16)
                t_wc[it] = w.c["T"]
            t_r[tq] = w.c["T"]

    load_consts()
    w.barrier()
    weight_prep()
    w.barrier()
    xnT = layernorm()
    w.barrier()
    xq = build_xq(xnT)
    w.barrier()
    ffn_f1(xq)
    w.barrier()
    ffn_f2a()
    w.barrier()
    mixer_m1(xnT)
    w.barrier()
    ssm()
    w.barrier()
    wout_phase()
    w.barrier()
    down_phase()
    w.barrier()

    if DEBUG:
        for nm, src in [("d_za", s_za), ("d_xq", s_xq), ("d_gate", s_gate),
                        ("d_hbc", s_hbc), ("d_on", s_on), ("d_ya", s_ya),
                        ("d_r2", s_r2)]:
            w.op("G", "dma_start", dbg[nm], src, inc="dO", k=16)
        w.op("G", "dma_start", dbg["d_smf"], SMF, inc="dO", k=16)

    w.wait("G", "dO", w.c["dO"])
    w.wait("G", "dS", w.c["dS"])
    w.wait("G", "dG", w.c["dG"])
    w.wait("G", "dW", w.c["dW"])

    with nc.Block() as block:
        @block.tensor
        def _(t):
            for f in w.st["T"]:
                f(t)

        @block.vector
        def _(v):
            for f in w.st["V"]:
                f(v)

        @block.scalar
        def _(a):
            for f in w.st["A"]:
                f(a)

        @block.sync
        def _(s):
            for f in w.st["S"]:
                f(s)

        @block.gpsimd
        def _(g):
            for f in w.st["G"]:
                f(g)

    es.close()
    return nc


def _consts():
    h128 = _hadamard(128)
    hb16 = np.kron(_hadamard(16), np.eye(8, dtype=np.float32))
    hb64 = np.kron(_hadamard(64), np.eye(2, dtype=np.float32))
    i128 = np.eye(128, dtype=np.float32)
    ones_blk = np.zeros((128, 128), np.float32)
    ones_blk[0, :] = 1.0
    cb = np.concatenate([h128, hb16, hb64, i128, ones_blk,
                         np.tile(i128, (1, 4))], axis=1)
    return {
        "cb": cb.astype(ml_dtypes.bfloat16),
        "ones_f": np.ones((1, 128), np.float32),
        "ones_c": np.ones((128, 1), np.float32),
        "id_f": i128,
    }


def kernel(**inp):
    f32 = lambda a: np.ascontiguousarray(np.asarray(a, np.float32))
    x = f32(inp["L0_attn_hidden_states"])
    shared = {
        "w_in": f32(inp["L0_attn_W_in"]),
        "conv_w": f32(inp["L0_attn_conv_w"]),
        "conv_b": f32(inp["L0_attn_conv_b"]),
        "dt_bias": f32(inp["L0_attn_dt_bias"]),
        "a_log": f32(inp["L0_attn_A_log"]),
        "dvec": f32(inp["L0_attn_D"]),
        "norm_w": f32(inp["L0_attn_norm_weight"]),
        "w_out": f32(inp["L0_attn_W_out"]),
        "w_g": f32(inp["L0_ffn_w_gate"]),
        "w_u": f32(inp["L0_ffn_w_up"]),
        "w_d": f32(inp["L0_ffn_w_down"]),
    }
    shared.update(_consts())
    nc = build_program()
    in_maps = [dict(shared, x=x[b]) for b in range(8)]
    res = run_bass_kernel_spmd(nc, in_maps, core_ids=list(range(8)))
    out = np.stack([np.asarray(res.results[b]["out"]) for b in range(8)], axis=0)
    return out.astype(np.float32)


if __name__ == "__main__":
    nc = build_program()
    print("instructions:", nc.n_instructions())


# version 13
# speedup vs baseline: 1.0906x; 1.0906x over previous
import numpy as np
import ml_dtypes
import concourse.bass as bass
import concourse.mybir as mybir
from concourse.bass_utils import run_bass_kernel_spmd
from contextlib import ExitStack

F32 = mybir.dt.float32
BF16 = mybir.dt.bfloat16
AF = mybir.ActivationFunctionType
ALU = mybir.AxisListType
ALU = mybir.AluOpType
AX = mybir.AxisListType

L = 2048
DM = 2048
HH, PP, NNS, GG = 64, 64, 128, 8
INTER = 4096
CONV = 6144
PROJ = 10304
IFF = 8192
CS = 128
NCH = L // CS
KC = 4
RND = 3.0 * 2.0 ** 22
EPS = 1e-5
DEBUG = False

ENGS = "TVASG"


def _hadamard(n):
    h = np.array([[1.0]], np.float32)
    while h.shape[0] < n:
        h = np.block([[h, h], [h, -h]])
    return h


class Wk:
    def __init__(self, nc, semh):
        self.nc = nc
        self.semh = semh
        self.c = {k: 0 for k in semh}
        self.st = {e: [] for e in ENGS}

    def wait(self, eng, sem, val=None):
        v = self.c[sem] if val is None else val
        if v <= 0:
            return
        h = self.semh[sem]
        self.st[eng].append(lambda e, h=h, v=v: e.wait_ge(h, v))

    def op(self, eng, meth, *args, inc=None, k=1, **kw):
        h = None
        kk = 0
        if inc is not None:
            self.c[inc] += k
            h, kk = self.semh[inc], k

        import traceback
        site = traceback.extract_stack(limit=3)[0]

        def f(e, m=meth, a=args, kw=kw, h=h, kk=kk, site=site):
            try:
                ins = getattr(e, m)(*a, **kw)
            except Exception:
                print("FAILING OP:", m, "emitted at", site.filename, site.lineno,
                      [getattr(x, "shape", None) for x in a])
                raise
            if h is not None:
                ins.then_inc(h, kk)

        self.st[eng].append(f)
        return self.c[inc] if inc is not None else None

    def nop(self, eng, cyc=1024):
        self.st[eng].append(lambda e, c=cyc: e.nop(cycle_cnt=c))

    def fence(self, eng, sem):
        self.c[sem] += 1
        h = self.semh[sem]
        self.st[eng].append(lambda e, h=h: e.nop(cycle_cnt=1024).then_inc(h, 1))
        return self.c[sem]

    def barrier(self):
        snap = dict(self.c)
        for e in ENGS:
            for s, v in snap.items():
                if v > 0:
                    self.wait(e, s, v)


def build_program():
    nc = bass.Bass("TRN2", target_bir_lowering=False)
    dt_ = nc.dram_tensor

    x_in = dt_("x", [L, DM], F32, kind="ExternalInput").ap()
    w_in = dt_("w_in", [PROJ, DM], F32, kind="ExternalInput").ap()
    conv_w = dt_("conv_w", [CONV, KC], F32, kind="ExternalInput").ap()
    conv_b = dt_("conv_b", [CONV], F32, kind="ExternalInput").ap()
    dt_bias = dt_("dt_bias", [HH], F32, kind="ExternalInput").ap()
    a_log = dt_("a_log", [HH], F32, kind="ExternalInput").ap()
    dvec = dt_("dvec", [HH], F32, kind="ExternalInput").ap()
    norm_w = dt_("norm_w", [INTER], F32, kind="ExternalInput").ap()
    w_out = dt_("w_out", [DM, INTER], F32, kind="ExternalInput").ap()
    w_g = dt_("w_g", [IFF, DM], F32, kind="ExternalInput").ap()
    w_u = dt_("w_u", [IFF, DM], F32, kind="ExternalInput").ap()
    w_d = dt_("w_d", [DM, IFF], F32, kind="ExternalInput").ap()
    cb_in = dt_("cb", [128, 1152], BF16, kind="ExternalInput").ap()
    onesf_in = dt_("ones_f", [1, 128], F32, kind="ExternalInput").ap()
    onesc_in = dt_("ones_c", [128, 1], F32, kind="ExternalInput").ap()
    idf_in = dt_("id_f", [128, 128], F32, kind="ExternalInput").ap()

    out_hbm = dt_("out", [L, DM], F32, kind="ExternalOutput").ap()

    s_winT = dt_("s_winT", [DM, PROJ], BF16, kind="Internal").ap()
    s_woutT = dt_("s_woutT", [INTER, DM], BF16, kind="Internal").ap()
    s_wgT = dt_("s_wgT", [DM, IFF], BF16, kind="Internal").ap()
    s_wuT = dt_("s_wuT", [DM, IFF], BF16, kind="Internal").ap()
    s_wdT = dt_("s_wdT", [IFF, DM], BF16, kind="Internal").ap()
    s_za = dt_("s_za", [DM, L], BF16, kind="Internal").ap()
    s_xq = dt_("s_xq", [DM, L], BF16, kind="Internal").ap()
    s_gate = dt_("s_gate", [INTER, L], BF16, kind="Internal").ap()
    s_hbc = dt_("s_hbc", [CONV, L], BF16, kind="Internal").ap()
    s_on = dt_("s_on", [L, INTER], BF16, kind="Internal").ap()
    s_ya = dt_("s_ya", [IFF, L], BF16, kind="Internal").ap()
    s_r2 = dt_("s_r2", [IFF, L], BF16, kind="Internal").ap()

    dbg = {}
    if DEBUG:
        for nm, shp in [("d_za", [DM, L]), ("d_xq", [DM, L]), ("d_gate", [INTER, L]),
                        ("d_hbc", [CONV, L]), ("d_on", [L, INTER]), ("d_ya", [IFF, L]),
                        ("d_r2", [IFF, L])]:
            dbg[nm] = dt_(nm, shp, BF16, kind="ExternalOutput").ap()
        dbg["d_smf"] = dt_("d_smf", [128, 2944], F32, kind="ExternalOutput").ap()

    es = ExitStack()

    def sbuf(name, shape, dty):
        return es.enter_context(nc.sbuf_tensor(name, shape, dty)).ap()

    AR1 = sbuf("AR1", [128, 32768], BF16)
    AR2 = sbuf("AR2", [128, 32768], BF16)
    ARW = sbuf("ARW", [128, 19968], BF16)
    SST = sbuf("SST", [128, 8192], BF16)
    SMB = sbuf("SMB", [128, 4224], BF16)
    SMF = sbuf("SMF", [128, 2944], F32)
    CBT = sbuf("CBT", [128, 1152], BF16)
    IDF = sbuf("IDF", [128, 128], F32)
    ONF = sbuf("ONF", [1, 128], F32)
    ONC = sbuf("ONC", [128, 1], F32)

    pb = [es.enter_context(nc.psum_tensor("pb%d" % i, [128, 512], F32)).ap()
          for i in range(7)]
    pbt = es.enter_context(nc.psum_tensor("pbt", [128, 1024], BF16)).ap()

    sems = {}
    for s in ("dS", "dG", "dW", "dO", "T", "V", "A"):
        sems[s] = es.enter_context(nc.semaphore(s))

    w = Wk(nc, sems)

    def fview(arena, byte_off, parts, cols):
        assert byte_off % 4 == 0
        return arena[0:parts, byte_off // 2: byte_off // 2 + 2 * cols].bitcast(F32)

    H128 = CBT[:, 0:128]
    HB16 = CBT[:, 128:256]
    HB64 = CBT[:, 256:384]
    IDB = CBT[:, 384:512]
    ONB = CBT[0:1, 512:640]
    IR4 = CBT[:, 640:1152]

    smf = SMF
    DTA = smf[0:64, 0:2048]
    RTEN = smf[0:64, 0:2048]
    ABR = smf[64:128, 0:2048]
    ACS = smf[0:64, 2048:2176]
    CW0 = smf[:, 2176:2180]
    CW1 = smf[:, 2180:2184]
    CB0 = smf[:, 2184:2185]
    CB1 = smf[:, 2185:2186]
    COLS = smf[:, 2186:2250]
    M1C = smf[:, 2250:2266]
    M2C = smf[:, 2266:2282]
    M2Q = smf[:, 2282:2298]
    C1R = smf[0:16, 2304:2432]
    NWC = smf[:, 2432:2464]
    VARC = smf[:, 2464:2472]
    RSTC = smf[:, 2472:2480]
    SSCAL = smf[0:1, 2480:2496]
    SCL = smf[:, 2496:2520]
    C14 = smf[:, 2520:2524]
    CI4 = smf[:, 2524:2528]
    MXT = smf[:, 2528:2656]
    C2ROW = smf[0:1, 2656:2784]
    CRW1 = smf[0:1, 0:512]
    CRW2 = smf[0:1, 512:1024]

    def col(i):
        return COLS[:, i:i + 1]

    DTT = SMB[:, 0:1024]
    DST = SMB[:, 1024:2048]
    SDT = SMB[:, 2048:3072]
    CDB = SMB[:, 3072:4096]
    DMAT = SMB[:, 4096:4160]
    CDROW = SMB[0:1, 4160:4224]

    SP0 = SST[:, 0:4096]
    SP1 = SST[:, 4096:8192]

    def load_consts():
        w.op("V", "memset", col(44), 1e-5)
        w.op("V", "memset", col(45), -1.0, inc="V")
        w.op("S", "dma_start", CBT, cb_in, inc="dS", k=16)
        w.op("S", "dma_start", IDF, idf_in, inc="dS", k=16)
        w.op("S", "dma_start", ONF, onesf_in, inc="dS", k=16)
        w.op("S", "dma_start", ONC, onesc_in, inc="dS", k=16)

    def weight_prep():
        slot = [fview(ARW, i * 8192, 128, 2048) for i in range(3)]
        tslot = [ARW[:, 12288 + i * 2048: 12288 + (i + 1) * 2048] for i in range(2)]
        asm = [AR1[:, i * 2048:(i + 1) * 2048] for i in range(2)]
        junk = fview(AR2, 0, 128, 2048)
        run, spot = col(0), col(1)

        w.op("V", "memset", run, 0.0, inc="V")
        a_read = {}
        gi = 0
        for si, wm in enumerate((w_g, w_u, w_d)):
            wf = wm if wm.shape[1] == 2048 else wm.rearrange(
                "a (b c) -> (a b) c", c=2048)
            for t in range(64):
                sl = gi % 3
                if gi >= 3:
                    w.wait("S", "A", a_read[gi - 3])
                w.op("S", "dma_start", slot[sl], wf[t * 128:(t + 1) * 128, :],
                     inc="dS", k=16)
                w.wait("A", "dS", w.c["dS"])
                a_read[gi] = w.op("A", "activation", junk, slot[sl], AF.Abs,
                                  accum_out=spot, inc="A")
                w.wait("V", "A", a_read[gi])
                w.nop("V")
                w.op("V", "tensor_tensor", run, run, spot, ALU.add, inc="V")
                gi += 1
            w.wait("T", "V", w.c["V"])
            w.op("T", "matmul", pb[0][0:1, 0:1], run, ONC, start=True, stop=True,
                 inc="T")
            w.wait("V", "T", w.c["T"])
            sc = SSCAL[:, si:si + 1]
            w.op("V", "tensor_scalar", sc, pb[0][0:1, 0:1], 1.0 / (IFF * DM), 1e-5,
                 ALU.mult, ALU.max)
            w.nop("V")
            w.op("V", "reciprocal", SSCAL[:, 8:9], sc)
            w.nop("V")
            w.op("V", "memset", run, 0.0, inc="V")
            w.wait("T", "V", w.c["V"])
            w.nop("T")
            w.op("T", "matmul", pb[1][:, 0:1], ONF, SSCAL[:, 8:9], start=True,
                 stop=True, inc="T")
            w.wait("V", "T", w.c["T"])
            w.op("V", "tensor_copy", col(2 + si), pb[1][:, 0:1], inc="V")

        for si, tgt in ((0, SCL[:, 0:1]), (1, SCL[:, 1:2]), (2, SCL[:, 4:5])):
            w.wait("T", "V", w.c["V"])
            w.op("T", "matmul", pb[1][:, 1:2], ONF, SSCAL[:, si:si + 1], start=True,
                 stop=True, inc="T")
            w.wait("V", "T", w.c["T"])
            w.op("V", "tensor_copy", tgt, pb[1][:, 1:2], inc="V")
        w.nop("V")
        w.op("V", "tensor_tensor", SCL[:, 2:3], SCL[:, 1:2], SCL[:, 4:5], ALU.mult)
        w.nop("V")
        w.op("V", "tensor_scalar", SCL[:, 2:3], SCL[:, 2:3],
             float(1.0 / (127.0 * np.sqrt(8192.0))), 0.0, ALU.mult, ALU.add)
        w.op("V", "reciprocal", SCL[:, 3:4], SCL[:, 1:2])
        w.nop("V")
        w.op("V", "tensor_scalar", SCL[:, 3:4], SCL[:, 3:4],
             float(np.sqrt(8192.0) * 1e-5), 0.0, ALU.mult, ALU.add, inc="V")

        jobs = []
        for si, wm, dstT in ((0, w_g, s_wgT), (1, w_u, s_wuT)):
            for t in range(64):
                jobs.append(("tern", si, wm[t * 128:(t + 1) * 128, :], 128,
                             dstT[:, t * 128:(t + 1) * 128]
                             .rearrange("(k p) c -> p k c", k=16)))
        for q in range(4):
            for j in range(16):
                jobs.append(("tern", 2,
                             w_d[j * 128:(j + 1) * 128, q * 2048:(q + 1) * 2048], 128,
                             s_wdT[q * 2048:(q + 1) * 2048, j * 128:(j + 1) * 128]
                             .rearrange("(k p) c -> p k c", k=16)))
        for j in range(80):
            jobs.append(("cast", None, w_in[j * 128:(j + 1) * 128, :], 128,
                         s_winT[:, j * 128:(j + 1) * 128]
                         .rearrange("(k p) c -> p k c", k=16)))
        jobs.append(("cast", None, w_in[10240:10304, :], 64,
                     s_winT[:, 10240:10304].rearrange("(k p) c -> p k c", k=16)))
        for hf in range(2):
            for j in range(16):
                jobs.append(("cast", None,
                             w_out[j * 128:(j + 1) * 128, hf * 2048:(hf + 1) * 2048],
                             128,
                             s_woutT[hf * 2048:(hf + 1) * 2048, j * 128:(j + 1) * 128]
                             .rearrange("(k p) c -> p k c", k=16)))

        v_cons = {}
        v_ev = {}
        t_tp = {}
        g_wr = {}
        for t, (kind, si, srcap, rows, dstap) in enumerate(jobs):
            sl = t % 3
            ts = t % 2
            if t >= 3:
                w.wait("S", "V", v_cons[t - 3])
            w.op("S", "dma_start", slot[sl][0:rows, :], srcap, inc="dS", k=16)
            ld = w.c["dS"]
            w.wait("V", "dS", ld)
            if t >= 2:
                w.wait("V", "T", t_tp[t - 2])
            if t == 0:
                w.nop("V")
            sv = slot[sl][0:rows, :]
            tv = tslot[ts][0:rows, :]
            if kind == "tern":
                invc = col(2 + si)[0:rows, :]
                v_cons[t] = w.op("V", "tensor_scalar", junk[0:rows, :], sv, invc,
                                 -1.0, ALU.mult, ALU.max, inc="V")
                w.op("V", "tensor_scalar", junk[0:rows, :], junk[0:rows, :], 1.0,
                     RND, ALU.min, ALU.add)
                w.op("V", "tensor_scalar", tv, junk[0:rows, :], -RND, 0.0, ALU.add,
                     ALU.add, inc="V")
            else:
                v_cons[t] = w.op("V", "tensor_copy", tv, sv, inc="V")
            vt = w.c["V"]
            idn = IDB if rows == 128 else IDB[0:64, 0:64]
            pw = 128 if rows == 128 else 64
            for r in range(2):
                w.wait("T", "V", vt if r == 0 else w.c["V"])
                if t >= 1 or r >= 1:
                    w.wait("T", "V", w.c["V"])
                for k in range(8):
                    w.op("T", "transpose", pbt[:, k * pw:(k + 1) * pw],
                         tv[:, (r * 8 + k) * 128:(r * 8 + k + 1) * 128], idn,
                         inc=("T" if k == 7 else None))
                w.wait("V", "T", w.c["T"])
                if t >= 2:
                    w.wait("V", "dG", g_wr[t - 2])
                w.op("V", "tensor_copy",
                     asm[ts][:, r * 8 * pw:(r + 1) * 8 * pw], pbt[:, 0:8 * pw],
                     inc="V")
            t_tp[t] = w.c["T"]
            v_ev[t] = w.c["V"]
            w.wait("G", "V", v_ev[t])
            g_wr[t] = w.op("G", "dma_start", dstap,
                           asm[ts][:, 0:16 * pw].rearrange("p (k c) -> p k c", k=16),
                           inc="dG", k=16)

    def layernorm():
        xslot = [fview(AR2, 0, 128, 2048), fview(AR2, 8192, 128, 2048)]
        xnn = AR2[:, 8192:10240]
        sqs = fview(ARW, 0, 128, 2048)
        xnT = AR1[:, 0:32768]
        xnTv = xnT.rearrange("p (a c) -> p a c", a=16)
        for tt in range(16):
            sl = tt % 2
            w.wait("S", "V", w.c["V"])
            w.wait("S", "A", w.c["A"])
            w.op("S", "dma_start", xslot[sl], x_in[tt * 128:(tt + 1) * 128, :],
                 inc="dS", k=16)
            sumc, ssq, mc, vc, rc, nb = (col(8 + i) for i in range(6))
            w.wait("V", "dS", w.c["dS"])
            w.op("V", "reduce_sum", sumc, xslot[sl], AX.X)
            w.nop("V")
            w.op("V", "engine_nop", inc="V")
            w.wait("A", "dS", w.c["dS"])
            w.op("A", "activation", sqs, xslot[sl], AF.Square, accum_out=ssq,
                 inc="A")
            w.wait("V", "A", w.c["A"])
            w.nop("V")
            w.op("V", "tensor_scalar", mc, sumc, 1.0 / DM, 0.0, ALU.mult, ALU.add)
            w.op("V", "tensor_scalar", vc, ssq, 1.0 / DM, 0.0, ALU.mult, ALU.add)
            w.nop("V")
            w.op("V", "tensor_tensor", nb, mc, mc, ALU.mult)
            w.nop("V")
            w.op("V", "tensor_tensor", vc, vc, nb, ALU.subtract, inc="V")
            w.wait("A", "V", w.c["V"])
            w.nop("A")
            w.op("A", "activation", vc, vc, AF.Sqrt, bias=col(44), inc="A")
            w.wait("V", "A", w.c["A"])
            w.nop("V")
            w.op("V", "reciprocal", rc, vc)
            w.nop("V")
            w.op("V", "tensor_scalar", xnn, xslot[sl], mc, rc, ALU.subtract,
                 ALU.mult, inc="V")
            w.wait("T", "V", w.c["V"])
            for q in range(4):
                w.wait("T", "V", w.c["V"])
                for j in range(4):
                    d = q * 4 + j
                    w.op("T", "transpose", pbt[:, j * 128:(j + 1) * 128],
                         xnn[:, d * 128:(d + 1) * 128], IDB,
                         inc=("T" if j == 3 else None))
                w.wait("V", "T", w.c["T"])
                w.op("V", "tensor_copy",
                     xnTv[:, q * 4:(q + 1) * 4, tt * 128:(tt + 1) * 128],
                     pbt[:, 0:512].rearrange("p (a c) -> p a c", a=4), inc="V")
        return xnT

    def build_xq(xnT):
        zap = ARW[:, 0:2048]
        z1s = fview(ARW, 6144, 128, 8192)
        mx = fview(ARW, 4096, 128, 512)
        xnv = xnT.rearrange("p (a c) -> p a c", a=16)
        for d in range(16):
            w.wait("V", "dS", w.c["dS"])
            for q in range(4):
                bank = pb[q % 2]
                w.wait("T", "V", w.c["V"])
                w.op("T", "matmul", bank[:, 0:512], H128,
                     xnv[:, d, q * 512:(q + 1) * 512], start=True, stop=True, inc="T")
                w.wait("V", "T", w.c["T"])
                w.op("V", "tensor_copy", zap[:, q * 512:(q + 1) * 512], bank[:, 0:512],
                     inc="V")
            w.wait("G", "V", w.c["V"])
            w.op("G", "dma_start", s_za[d * 128:(d + 1) * 128, :], zap, inc="dS", k=16)
        zb = AR2[:, 0:8192]
        zbv = zb.rearrange("p (a c) -> p a c", a=16)
        xqs = AR2[:, 8192:16384]
        za_r = s_za.rearrange("(dh d3 dl) t -> (dh d3) dl t", dh=16, d3=8, dl=16)
        xq_w = s_xq.rearrange("(dh d3 dl) t -> (dh d3) dl t", dh=16, d3=8, dl=16)
        for tc in range(4):
            w.wait("S", "dS", w.c["dS"])
            w.wait("S", "T", w.c["T"])
            w.op("S", "dma_start", zbv, za_r[:, :, tc * 512:(tc + 1) * 512],
                 inc="dS", k=16)
            w.wait("T", "dS", w.c["dS"])
            for k in range(16):
                bank = pb[2 + (k % 2)]
                w.wait("T", "V", w.c["V"])
                w.op("T", "matmul", bank[:, 0:512], HB16, zb[:, k * 512:(k + 1) * 512],
                     start=True, stop=True, inc="T")
                w.wait("V", "T", w.c["T"])
                w.op("V", "tensor_copy", z1s[:, k * 512:(k + 1) * 512], bank[:, 0:512],
                     inc="V")
            w.op("V", "tensor_reduce", mx,
                 z1s.rearrange("p (a c) -> p c a", a=16), AX.X, ALU.max,
                 apply_absolute_value=True, inc="V")
            w.wait("T", "V", w.c["V"])
            for j in range(4):
                w.op("T", "transpose", pb[4][:, j * 128:(j + 1) * 128],
                     mx[:, j * 128:(j + 1) * 128], IDF, inc=("T" if j == 3 else None))
            w.wait("V", "T", w.c["T"])
            for j in range(4):
                w.op("V", "reduce_max", M1C[:, tc * 4 + j: tc * 4 + j + 1],
                     pb[4][:, j * 128:(j + 1) * 128], AX.X)
            mm4 = M1C[:, tc * 4: tc * 4 + 4]
            w.nop("V")
            w.op("V", "tensor_scalar", mm4, mm4, float(np.sqrt(2048.0) * 1e-5), 0.0,
                 ALU.max, ALU.add)
            w.nop("V")
            w.op("V", "reciprocal", C14, mm4)
            w.nop("V")
            w.op("V", "tensor_scalar", C14, C14, 127.0, 0.0, ALU.mult, ALU.add)
            w.op("V", "tensor_scalar", CI4, mm4, float(1.0 / (127.0 * np.sqrt(2048.0))),
                 0.0, ALU.mult, ALU.add, inc="V")
            w.wait("T", "V", w.c["V"])
            w.nop("T")
            for j in range(4):
                w.op("T", "transpose", pb[4][0:1, j * 128:(j + 1) * 128],
                     C14[:, j:j + 1], IDF)
                w.op("T", "transpose", pb[6][0:1, j * 128:(j + 1) * 128],
                     CI4[:, j:j + 1], IDF, inc=("T" if j == 3 else None))
            w.wait("V", "T", w.c["T"])
            w.op("V", "tensor_copy", CRW1, pb[4][0:1, 0:512])
            w.op("V", "tensor_copy", CRW2, pb[6][0:1, 0:512], inc="V")
            w.wait("T", "V", w.c["V"])
            w.nop("T")
            for j in range(4):
                w.op("T", "matmul", pb[5][:, j * 128:(j + 1) * 128], ONF,
                     CRW1[0:1, j * 128:(j + 1) * 128], start=True, stop=True)
                w.op("T", "matmul", pb[6][:, j * 128:(j + 1) * 128], ONF,
                     CRW2[0:1, j * 128:(j + 1) * 128], start=True, stop=True,
                     inc=("T" if j == 3 else None))
            w.wait("V", "T", w.c["T"])
            w.wait("V", "dS", w.c["dS"])
            z3 = z1s.rearrange("p (a c) -> p a c", a=16)
            xq3 = xqs.rearrange("p (a c) -> p a c", a=16)
            for j in range(4):
                w.op("V", "tensor_tensor", z3[:, :, j * 128:(j + 1) * 128],
                     z3[:, :, j * 128:(j + 1) * 128],
                     pb[5][:, j * 128:(j + 1) * 128].unsqueeze(1)
                     .broadcast_to([128, 16, 128]), ALU.mult)
            w.op("V", "tensor_scalar", z1s, z1s, RND, -RND, ALU.add, ALU.add)
            for j in range(4):
                w.op("V", "tensor_tensor", xq3[:, :, j * 128:(j + 1) * 128],
                     z3[:, :, j * 128:(j + 1) * 128],
                     pb[6][:, j * 128:(j + 1) * 128].unsqueeze(1)
                     .broadcast_to([128, 16, 128]), ALU.mult, inc=("V" if j == 3
                                                                   else None))
            w.wait("G", "V", w.c["V"])
            w.op("G", "dma_start", xq_w[:, :, tc * 512:(tc + 1) * 512],
                 xqs.rearrange("p (a c) -> p a c", a=16), inc="dS", k=16)
        xq = AR2[:, 0:32768]
        w.wait("S", "dS", w.c["dS"])
        xqv = xq.rearrange("p (a c) -> p a c", a=16)
        for k in range(16):
            w.op("S", "dma_start", xqv[:, k, :], s_xq[k * 128:(k + 1) * 128, :],
                 inc="dS", k=16)
        return xq

    def ffn_f1(xq):
        wslot = [ARW[:, 0:4096], ARW[:, 4096:8192]]
        silg = [ARW[:, 8192 + i * 512: 8192 + (i + 1) * 512] for i in range(2)]
        y1b = [ARW[:, 9216 + i * 512: 9216 + (i + 1) * 512] for i in range(2)]
        yab = [ARW[:, 10240 + i * 512: 10240 + (i + 1) * 512] for i in range(2)]
        xqv = xq.rearrange("p (a c) -> p a c", a=16)
        t_mm = {}
        t_sa = {}
        a_done = {}
        v_y1 = {}
        v_evac = {}
        s_store = {}
        t_wcons = {}

        def emit_stage_a(jt):
            pa = pb[4 + (jt % 2)]
            w.wait("T", "V", v_y1[jt])
            if jt >= 2:
                w.wait("T", "V", v_evac[jt - 2])
            t_sa[jt] = w.op("T", "matmul", pa[:, 0:512], H128, y1b[jt % 2],
                            start=True, stop=True, inc="T")
            w.wait("V", "T", t_sa[jt])
            if jt >= 2:
                w.wait("V", "dS", s_store[jt - 2])
            v_evac[jt] = w.op("V", "tensor_copy", yab[jt % 2], pa[:, 0:512], inc="V")
            w.wait("G", "V", v_evac[jt])
            o_, tc_ = jt // 4, jt % 4
            s_store[jt] = w.op("G", "dma_start",
                               s_ya[o_ * 128:(o_ + 1) * 128, tc_ * 512:(tc_ + 1) * 512],
                               yab[jt % 2], inc="dS", k=16)

        for o in range(64):
            sl = o % 2
            if o >= 2:
                w.wait("S", "T", t_wcons[o - 2])
            w.op("S", "dma_start",
                 wslot[sl][:, 0:2048].rearrange("p (k c) -> p k c", k=16),
                 s_wgT[:, o * 128:(o + 1) * 128]
                 .rearrange("(k p) c -> p k c", k=16), inc="dW", k=16)
            w.op("S", "dma_start",
                 wslot[sl][:, 2048:4096].rearrange("p (k c) -> p k c", k=16),
                 s_wuT[:, o * 128:(o + 1) * 128]
                 .rearrange("(k p) c -> p k c", k=16), inc="dW", k=16)
            wz = w.c["dW"]
            for tc in range(4):
                it = o * 4 + tc
                pg = pb[0 + (tc % 2)]
                pu = pb[2 + (tc % 2)]
                if tc == 0:
                    w.wait("T", "dW", wz)
                if it >= 2:
                    w.wait("T", "A", a_done[it - 2])
                    w.wait("T", "V", v_y1[it - 2])
                for k in range(16):
                    w.op("T", "matmul", pg[:, 0:512],
                         wslot[sl][:, k * 128:(k + 1) * 128],
                         xqv[:, k, tc * 512:(tc + 1) * 512], start=(k == 0),
                         stop=(k == 15))
                    w.op("T", "matmul", pu[:, 0:512],
                         wslot[sl][:, 2048 + k * 128: 2048 + (k + 1) * 128],
                         xqv[:, k, tc * 512:(tc + 1) * 512], start=(k == 0),
                         stop=(k == 15), inc=("T" if k == 15 else None))
                t_mm[it] = w.c["T"]
                w.wait("A", "T", t_mm[it])
                if it >= 2:
                    w.wait("A", "V", v_y1[it - 2])
                a_done[it] = w.op("A", "activation", silg[tc % 2], pg[:, 0:512],
                                  AF.Silu, scale=SCL[:, 0:1], inc="A")
                w.wait("V", "A", a_done[it])
                v_y1[it] = w.op("V", "tensor_tensor", y1b[tc % 2], silg[tc % 2],
                                pu[:, 0:512], ALU.mult, inc="V")
                if it >= 1:
                    emit_stage_a(it - 1)
            t_wcons[o] = w.c["T"]
        emit_stage_a(255)

    def ffn_f2a():
        zb = AR2[:, 0:8192]
        zbv = zb.rearrange("p (a c) -> p a c", a=64)
        z2s = fview(ARW, 0, 128, 8192)
        z3 = z2s.rearrange("p (a c) -> p a c", a=64)
        r2b = AR2[:, 8192:16384]
        r2v = r2b.rearrange("p (a c) -> p a c", a=64)
        ya_r = s_ya.rearrange("(oh b6 ol) t -> (oh b6) ol t", oh=64, b6=2, ol=64)
        r2_w = s_r2.rearrange("(oh b6 ol) t -> (oh b6) ol t", oh=64, b6=2, ol=64)
        for tc in range(16):
            w.wait("S", "T", w.c["T"])
            w.wait("S", "dS", w.c["dS"])
            for part in range(4):
                w.op("S", "dma_start", zbv[:, part * 16:(part + 1) * 16, :],
                     ya_r[:, part * 16:(part + 1) * 16, tc * 128:(tc + 1) * 128],
                     inc="dS", k=16)
            w.wait("T", "dS", w.c["dS"])
            for k in range(16):
                bank = pb[k % 2]
                w.wait("T", "V", w.c["V"])
                for j in range(4):
                    w.op("T", "matmul", bank[:, j * 128:(j + 1) * 128], HB64,
                         zb[:, (k * 4 + j) * 128:(k * 4 + j + 1) * 128],
                         start=True, stop=True, inc=("T" if j == 3 else None))
                w.wait("V", "T", w.c["T"])
                w.op("V", "tensor_copy", z2s[:, k * 512:(k + 1) * 512], bank[:, 0:512],
                     inc="V")
            zt = z2s.rearrange("p (a c) -> p c a", a=64)
            w.op("V", "tensor_reduce", MXT, zt, AX.X, ALU.max,
                 apply_absolute_value=True, inc="V")
            w.wait("T", "V", w.c["V"])
            w.op("T", "transpose", pb[2][:, 0:128], MXT, IDF, inc="T")
            w.wait("V", "T", w.c["T"])
            mcol = M2C[:, tc:tc + 1]
            w.op("V", "reduce_max", mcol, pb[2][:, 0:128], AX.X)
            w.nop("V")
            w.op("V", "tensor_tensor", mcol, mcol, SCL[:, 3:4], ALU.max)
            w.nop("V")
            c2c = col(30)
            w.op("V", "reciprocal", c2c, mcol)
            w.nop("V")
            w.op("V", "tensor_scalar", c2c, c2c, 127.0, 0.0, ALU.mult, ALU.add,
                 inc="V")
            w.wait("T", "V", w.c["V"])
            w.nop("T")
            w.op("T", "transpose", pb[2][0:1, 0:128], c2c, IDF, inc="T")
            w.wait("V", "T", w.c["T"])
            w.op("V", "tensor_copy", C2ROW, pb[2][0:1, 0:128], inc="V")
            w.wait("T", "V", w.c["V"])
            w.nop("T")
            w.op("T", "matmul", pb[3][:, 0:128], ONF, C2ROW, start=True,
                 stop=True, inc="T")
            w.wait("V", "T", w.c["T"])
            w.wait("V", "dS", w.c["dS"])
            w.op("V", "tensor_tensor", z3, z3,
                 pb[3][:, 0:128].unsqueeze(1).broadcast_to([128, 64, 128]), ALU.mult)
            w.op("V", "tensor_scalar", r2v, z3, RND, -RND, ALU.add, ALU.add, inc="V")
            w.wait("G", "V", w.c["V"])
            for part in range(4):
                w.op("G", "dma_start",
                     r2_w[:, part * 16:(part + 1) * 16, tc * 128:(tc + 1) * 128],
                     r2v[:, part * 16:(part + 1) * 16, :], inc="dS", k=16)
        w.nop("V")
        w.op("V", "tensor_tensor", M2Q, M2C, SCL[:, 2:3].broadcast_to([128, 16]),
             ALU.mult, inc="V")

    def dt_family():
        dtspb = ARW[:, 12294:14342][0:64, :]
        sdch = ARW[:, 14342:14470][0:64, :]
        dsch = ARW[:, 14470:14598][0:64, :]
        T1 = fview(ARW, 31744, 64, 2048)
        ABR2 = T1
        w.wait("A", "V", w.c["V"])
        w.op("A", "activation", T1, DTA, AF.Abs)
        w.op("A", "activation", T1, T1, AF.Exp, scale=col(45)[0:64, :])
        w.op("A", "activation", T1, T1, AF.Ln, bias=1.0, inc="A")
        w.wait("V", "A", w.c["A"])
        w.op("V", "tensor_scalar", DTA, DTA, 0.0, 0.0, ALU.max, ALU.add)
        w.op("V", "tensor_tensor", DTA, DTA, T1, ALU.add, inc="V")
        w.wait("A", "V", w.c["V"])
        w.op("A", "copy", dtspb, DTA, inc="A")
        w.wait("T", "A", w.c["A"])
        for c in range(NCH):
            if c % 4 == 0:
                w.wait("T", "V", w.c["V"])
            w.op("T", "transpose", pbt[:, (c % 4) * 128:(c % 4) * 128 + 64],
                 dtspb[:, c * 128:(c + 1) * 128], IDB[0:64, 0:64],
                 inc=("T" if c % 4 == 3 else None))
            if c % 4 == 3:
                w.wait("V", "T", w.c["T"])
                w.op("V", "tensor_copy",
                     DTT[:, (c - 3) * 64:(c + 1) * 64].rearrange(
                         "p (a c) -> p a c", a=4),
                     pbt[:, 0:512].rearrange("p (a c) -> p a c", a=4)[:, :, 0:64],
                     inc="V")
        w.wait("A", "dS", w.c["dS"])
        w.wait("A", "V", w.c["V"])
        w.op("A", "activation", col(41)[0:64, :], col(41)[0:64, :], AF.Exp,
             inc="A")
        w.wait("V", "A", w.c["A"])
        w.nop("V")
        w.op("V", "tensor_scalar", col(41)[0:64, :], col(41)[0:64, :], -1.0, 0.0,
             ALU.mult, ALU.add)
        w.nop("V")
        w.op("V", "tensor_scalar", ABR2, DTA, col(41)[0:64, :], 0.0, ALU.mult,
             ALU.add, inc="V")
        zc = col(43)
        w.op("V", "memset", zc, 0.0)
        w.nop("V")
        w.op("V", "engine_nop", inc="V")
        for c in range(NCH):
            w.op("V", "tensor_tensor_scan", ACS, ABR2[:, c * 128:(c + 1) * 128],
                 zc[0:64, :].broadcast_to([64, 128]), 0.0, ALU.add, ALU.add, inc="V")
            w.wait("A", "V", w.c["V"])
            w.nop("A")
            w.op("A", "activation", sdch, ACS, AF.Exp)
            w.op("A", "activation", dsch, ACS, AF.Exp, scale=col(45)[0:64, :],
                 bias=ACS[:, 127:128], inc="A")
            w.wait("T", "A", w.c["A"])
            w.nop("T")
            w.wait("T", "V", w.c["V"])
            w.op("T", "transpose", pbt[:, 0:64], sdch, IDB[0:64, 0:64])
            w.op("T", "transpose", pbt[:, 128:192], dsch, IDB[0:64, 0:64], inc="T")
            w.op("T", "transpose", pbt[0:1, 256:320], sdch[:, 127:128],
                 IDB[0:64, 0:64], inc="T")
            w.wait("V", "T", w.c["T"])
            w.op("V", "tensor_copy", SDT[:, c * 64:(c + 1) * 64], pbt[:, 0:64])
            w.op("V", "tensor_copy", DST[:, c * 64:(c + 1) * 64], pbt[:, 128:192])
            w.op("V", "tensor_copy", CDROW, pbt[0:1, 256:320], inc="V")
            w.wait("T", "V", w.c["V"])
            w.nop("T")
            w.op("T", "matmul", pb[3][:, 0:64], ONB, CDROW, start=True, stop=True,
                 inc="T")
            w.wait("V", "T", w.c["T"])
            w.op("V", "tensor_copy", CDB[:, c * 64:(c + 1) * 64], pb[3][:, 0:64],
                 inc="V")
        w.wait("A", "V", w.c["V"])
        w.op("A", "activation", RTEN, ABR2, AF.Exp, inc="A")
        w.wait("V", "A", w.c["A"])
        for c in range(NCH):
            w.op("V", "memset", RTEN[:, c * 128: c * 128 + 1], 0.0,
                 inc=("V" if c == NCH - 1 else None))
        w.wait("T", "V", w.c["V"])
        w.op("T", "transpose", pb[3][0:1, 0:128], col(42), IDF, inc="T")
        w.wait("V", "T", w.c["T"])
        w.op("V", "tensor_copy", C1R[0:1, 0:128], pb[3][0:1, 0:128], inc="V")
        w.wait("T", "V", w.c["V"])
        w.nop("T")
        w.op("T", "matmul", pb[3][:, 0:64], ONF, C1R[0:1, 0:64], start=True,
             stop=True, inc="T")
        w.wait("V", "T", w.c["T"])
        w.op("V", "tensor_copy", DMAT, pb[3][:, 0:64], inc="V")

    def mixer_m1(xnT):
        wslot = [ARW[:, 0:4096], ARW[:, 4096:8192]]
        cvin = [ARW[:, 8192:10243], ARW[:, 10243:12294]]
        hout = ARW[:, 12294:14342]
        gpiece = ARW[:, 14342:14854]
        xnv = xnT.rearrange("p (a c) -> p a c", a=16)
        w.op("S", "dma_start", col(40)[0:64, :],
             dt_bias.rearrange("(a b) -> a b", b=1), inc="dS", k=16)
        w.op("S", "dma_start", col(41)[0:64, :],
             a_log.rearrange("(a b) -> a b", b=1), inc="dS", k=16)
        w.op("S", "dma_start", col(42)[0:64, :],
             dvec.rearrange("(a b) -> a b", b=1), inc="dS", k=16)
        w.op("V", "memset", cvin[0][:, 0:3], 0.0)
        w.op("V", "memset", cvin[1][:, 0:3], 0.0, inc="V")

        groups = [("dt", [80])]
        groups += [("hbc", list(range(32 + 2 * i, 34 + 2 * i))) for i in range(24)]
        groups += [("gate", list(range(2 * i, 2 * i + 2))) for i in range(16)]

        t_after = {}
        ev = {}
        cnt = 0
        hout2 = [hout, ARW[:, 14854:16902]]
        gp2 = [gpiece, ARW[:, 16902:17414]]
        hb_done = {}
        gp_done = {}
        cv_done = {}
        sl_done = {}
        hi = 0
        gpi = 0
        for gi, (kind, js) in enumerate(groups):
            sl = gi % 2
            j0 = js[0]
            rows = 64 if kind == "dt" else 256
            if gi >= 2:
                w.wait("S", "T", t_after[gi - 2])
            w.op("S", "dma_start",
                 wslot[sl][:, 0:4096].rearrange("p (k c) -> p k c", k=16)[:, :, 0:rows],
                 s_winT[:, j0 * 128: j0 * 128 + rows]
                 .rearrange("(k p) c -> p k c", k=16), inc="dW", k=16)
            w.wait("T", "dW", w.c["dW"])
            for jj, j in enumerate(js):
                mrows = 64 if kind == "dt" else 128
                for tc in range(4):
                    bank = pb[cnt % 2]
                    if cnt >= 2:
                        eng, val = ev[cnt - 2]
                        w.wait("T", eng, val)
                    for k in range(16):
                        w.op("T", "matmul", bank[0:mrows, 0:512],
                             wslot[sl][:, k * 256 + jj * 128: k * 256 + jj * 128 + mrows],
                             xnv[:, k, tc * 512:(tc + 1) * 512],
                             start=(k == 0), stop=(k == 15),
                             inc=("T" if k == 15 else None))
                    tmm = w.c["T"]
                    if kind == "dt":
                        w.wait("V", "T", tmm)
                        w.wait("V", "dS", w.c["dS"])
                        ev[cnt] = ("V", w.op("V", "tensor_scalar",
                                             DTA[:, tc * 512:(tc + 1) * 512],
                                             bank[0:64, 0:512], col(40)[0:64, :], 0.0,
                                             ALU.add, ALU.add, inc="V"))
                    elif kind == "hbc":
                        w.wait("A", "T", tmm)
                        if hi >= 2 and tc == 0:
                            w.wait("A", "V", cv_done[hi - 2])
                        ev[cnt] = ("A", w.op("A", "copy",
                                             cvin[jj % 2][:, 3 + tc * 512: 3 + (tc + 1) * 512],
                                             bank[:, 0:512], inc="A"))
                    else:
                        w.wait("A", "T", tmm)
                        if gpi >= 2:
                            w.wait("A", "dS", gp_done[gpi - 2])
                        ev[cnt] = ("A", w.op("A", "copy", gp2[gpi % 2],
                                             bank[:, 0:512], inc="A"))
                        w.wait("S", "A", w.c["A"])
                        gp_done[gpi] = w.op("S", "dma_start",
                                            s_gate[j * 128:(j + 1) * 128,
                                                   tc * 512:(tc + 1) * 512],
                                            gp2[gpi % 2], inc="dS", k=16)
                        gpi += 1
                    cnt += 1
                if kind == "hbc":
                    ci = j - 32
                    cwsp = CW0 if jj % 2 == 0 else CW1
                    cbsp = CB0 if jj % 2 == 0 else CB1
                    if hi >= 2:
                        w.wait("G", "V", cv_done[hi - 2])
                        w.wait("G", "A", sl_done[hi - 2])
                    w.op("G", "dma_start", cwsp, conv_w[ci * 128:(ci + 1) * 128, :],
                         inc="dG", k=16)
                    w.op("G", "dma_start", cbsp,
                         conv_b[ci * 128:(ci + 1) * 128].rearrange("(a b) -> a b", b=1),
                         inc="dG", k=16)
                    w.wait("V", "dG", w.c["dG"])
                    w.wait("V", "A", w.c["A"])
                    if hi >= 2:
                        w.wait("V", "dS", hb_done[hi - 2])
                    src_ = cvin[jj % 2]
                    ho = hout2[hi % 2]
                    w.op("V", "tensor_scalar", ho, src_[:, 0:2048], cwsp[:, 0:1],
                         0.0, ALU.mult, ALU.add)
                    for k in range(1, 4):
                        w.op("V", "scalar_tensor_tensor", ho, src_[:, k:2048 + k],
                             cwsp[:, k:k + 1], ho, ALU.mult, ALU.add,
                             inc=("V" if k == 3 else None))
                    cv_done[hi] = w.c["V"]
                    w.wait("A", "V", w.c["V"])
                    sl_done[hi] = w.op("A", "activation", ho, ho, AF.Silu, bias=cbsp,
                                       inc="A")
                    w.wait("S", "A", w.c["A"])
                    hb_done[hi] = w.op("S", "dma_start",
                                       s_hbc[ci * 128:(ci + 1) * 128, :], ho,
                                       inc="dS", k=16)
                    hi += 1
            if kind == "dt":
                dt_family()
            t_after[gi] = w.c["T"]

    def ssm():
        XT = AR2[:, 0:4096]
        GT = AR2[:, 4096:8192]
        BT = AR2[:, 8192:9216]
        BCF = AR2[:, 9216:11264]
        XDT = AR2[:, 11264:15360]
        XDD = AR2[:, 15360:19456]
        SILG = AR2[:, 19456:23552]
        GGB = AR2[:, 23552:27648]
        ONT = AR2[:, 27648:31744]
        LMK = AR1[:, 0:1024]
        MPR = AR1[:, 1024:2048]
        YTMP = AR1[:, 2048:2560]
        Y2B = AR1[:, 2560:3072]
        XDB = AR1[:, 4096:8192]
        SQS = fview(ARW, 0, 128, 4096)
        w.op("V", "memset", SP0, 0.0, inc="V")
        for c in range(NCH):
            Sold = SP0 if c % 2 == 0 else SP1
            Snew = SP1 if c % 2 == 0 else SP0
            w.wait("S", "V", w.c["V"])
            w.wait("S", "T", w.c["T"])
            w.wait("S", "A", w.c["A"])
            w.op("S", "dma_start_transpose", XT, s_hbc[0:4096, c * 128:(c + 1) * 128],
                 inc="dS", k=16)
            w.op("S", "dma_start_transpose", GT,
                 s_gate[0:4096, c * 128:(c + 1) * 128], inc="dS", k=16)
            w.op("S", "dma_start_transpose", BT,
                 s_hbc[4096:5120, c * 128:(c + 1) * 128], inc="dS", k=16)
            w.op("S", "dma_start", BCF.rearrange("p (a c) -> p a c", a=16),
                 s_hbc[4096:6144, c * 128:(c + 1) * 128]
                 .rearrange("(a p) c -> p a c", a=16), inc="dS", k=16)
            w.wait("V", "dS", w.c["dS"])
            xtv = XT.rearrange("p (h q) -> p h q", h=64)
            w.op("V", "tensor_tensor", XDT.rearrange("p (h q) -> p h q", h=64), xtv,
                 DTT[:, c * 64:(c + 1) * 64].unsqueeze(2).broadcast_to([128, 64, 64]),
                 ALU.mult)
            w.op("V", "tensor_tensor", XDD.rearrange("p (h q) -> p h q", h=64),
                 XDT.rearrange("p (h q) -> p h q", h=64),
                 DST[:, c * 64:(c + 1) * 64].unsqueeze(2).broadcast_to([128, 64, 64]),
                 ALU.mult)
            w.op("V", "tensor_tensor", XDB.rearrange("p (h q) -> p h q", h=64), xtv,
                 DMAT.unsqueeze(2).broadcast_to([128, 64, 64]), ALU.mult, inc="V")
            w.wait("A", "dS", w.c["dS"])
            w.wait("A", "V", w.c["V"])
            w.op("A", "activation", SILG, GT, AF.Silu, inc="A")
            for g in range(8):
                w.wait("T", "V", w.c["V"])
                w.op("T", "matmul", pb[0][:, 0:128], BCF[:, g * 128:(g + 1) * 128],
                     BCF[:, 1024 + g * 128: 1024 + (g + 1) * 128], start=True,
                     stop=True)
                for i in range(8):
                    bank = pb[1 + i // 4]
                    w.op("T", "matmul", bank[:, (i % 4) * 128:(i % 4 + 1) * 128],
                         IDF[0:64, 8 * g + i: 8 * g + i + 1].broadcast_to([64, 128]),
                         RTEN[:, c * 128:(c + 1) * 128],
                         start=True, stop=True, inc=("T" if i == 7 else None))
                w.wait("V", "T", w.c["T"])
                for half in range(2):
                    w.op("V", "tensor_tensor_scan",
                         LMK[:, half * 512:(half + 1) * 512],
                         pb[1 + half][:, 0:512], IR4, 0.0,
                         ALU.mult, ALU.add)
                for half in range(2):
                    w.op("V", "tensor_tensor",
                         MPR[:, half * 512:(half + 1) * 512]
                         .rearrange("p (a c) -> p a c", a=4),
                         pb[0][:, 0:128].unsqueeze(1).broadcast_to([128, 4, 128]),
                         LMK[:, half * 512:(half + 1) * 512]
                         .rearrange("p (a c) -> p a c", a=4), ALU.mult,
                         inc=("V" if half == 1 else None))
                w.wait("T", "V", w.c["V"])
                for i in range(8):
                    w.op("T", "matmul", pb[3][:, i * 64:(i + 1) * 64],
                         MPR[:, i * 128:(i + 1) * 128],
                         XDT[:, (8 * g + i) * 64:(8 * g + i + 1) * 64],
                         start=True, stop=True)
                w.op("T", "matmul", pb[4][:, 0:512], BT[:, g * 128:(g + 1) * 128],
                     XDD[:, g * 512:(g + 1) * 512], start=True, stop=True)
                w.op("T", "matmul", pb[5][:, 0:512],
                     BCF[:, 1024 + g * 128:1024 + (g + 1) * 128],
                     Sold[:, g * 512:(g + 1) * 512], start=True, stop=True, inc="T")
                w.wait("V", "T", w.c["T"])
                w.op("V", "tensor_tensor",
                     Snew[:, g * 512:(g + 1) * 512].rearrange("p (a c) -> p a c", a=8),
                     Sold[:, g * 512:(g + 1) * 512].rearrange("p (a c) -> p a c", a=8),
                     CDB[:, c * 64 + g * 8: c * 64 + (g + 1) * 8].unsqueeze(2)
                     .broadcast_to([128, 8, 64]), ALU.mult)
                w.op("V", "tensor_tensor", Snew[:, g * 512:(g + 1) * 512],
                     Snew[:, g * 512:(g + 1) * 512], pb[4][:, 0:512], ALU.add)
                w.op("V", "tensor_tensor", YTMP.rearrange("p (a c) -> p a c", a=8),
                     pb[5][:, 0:512].rearrange("p (a c) -> p a c", a=8),
                     SDT[:, c * 64 + g * 8: c * 64 + (g + 1) * 8].unsqueeze(2)
                     .broadcast_to([128, 8, 64]), ALU.mult)
                w.op("V", "scalar_tensor_tensor", Y2B, pb[3][:, 0:512], 1.0, YTMP,
                     ALU.mult, ALU.add)
                w.op("V", "tensor_tensor", Y2B, Y2B, XDB[:, g * 512:(g + 1) * 512],
                     ALU.add)
                w.op("V", "tensor_tensor", GGB[:, g * 512:(g + 1) * 512], Y2B,
                     SILG[:, g * 512:(g + 1) * 512], ALU.mult, inc="V")
            w.wait("A", "V", w.c["V"])
            for g in range(8):
                w.op("A", "activation", SQS[:, g * 512:(g + 1) * 512],
                     GGB[:, g * 512:(g + 1) * 512], AF.Square,
                     accum_out=VARC[:, g:g + 1], inc=("A" if g == 7 else None))
            w.wait("V", "A", w.c["A"])
            w.op("V", "tensor_scalar", VARC, VARC, 1.0 / 512.0, 0.0, ALU.mult,
                 ALU.add, inc="V")
            w.wait("A", "V", w.c["V"])
            w.nop("A")
            w.op("A", "activation", VARC, VARC, AF.Sqrt, bias=col(44), inc="A")
            w.wait("V", "A", w.c["A"])
            w.nop("V")
            w.op("V", "reciprocal", RSTC, VARC)
            w.nop("V")
            w.wait("V", "dS", w.c["dS"])
            w.op("V", "tensor_tensor", ONT.rearrange("p (a c) -> p a c", a=8),
                 GGB.rearrange("p (a c) -> p a c", a=8),
                 RSTC.unsqueeze(2).broadcast_to([128, 8, 512]), ALU.mult, inc="V")
            w.wait("S", "V", w.c["V"])
            w.op("S", "dma_start", s_on[c * 128:(c + 1) * 128, :], ONT, inc="dS",
                 k=16)

    def wout_phase():
        w.op("S", "dma_start", fview(ARW, 0, 32, 128),
             norm_w.rearrange("(a b) -> a b", b=128), inc="dS", k=16)
        w.wait("T", "dS", w.c["dS"])
        w.op("T", "transpose", pb[0][:, 0:32], fview(ARW, 0, 32, 128),
             IDF[0:32, 0:32], inc="T")
        w.wait("V", "T", w.c["T"])
        w.op("V", "tensor_copy", NWC, pb[0][:, 0:32])
        w.nop("V")
        w.op("V", "engine_nop", inc="V")

        onfm = AR1[:, 0:16384]
        onv = onfm.rearrange("p (a c) -> p a c", a=32)
        wc = [AR2[:, 0:8192], AR2[:, 8192:16384]]
        xsl = [fview(ARW, 4096 + i * 1024, 128, 256) for i in range(4)]
        opiece = [fview(ARW, 8192 + i * 1024, 128, 256) for i in range(4)]
        t_on = {}
        t_wc = {}
        v_evac = {}
        o_dma = {}
        v_scale = {}
        for tq in range(4):
            if tq >= 1:
                w.wait("S", "T", t_on[tq - 1])
                w.wait("S", "V", v_scale[tq - 1])
            for i in range(32):
                w.op("S", "dma_start_transpose", onv[:, i, :],
                     s_on[tq * 512:(tq + 1) * 512, i * 128:(i + 1) * 128],
                     inc="dS", k=16)
            w.wait("V", "dS", w.c["dS"])
            for i in range(32):
                w.op("V", "tensor_scalar", onv[:, i, :], onv[:, i, :],
                     NWC[:, i:i + 1], 0.0, ALU.mult, ALU.add,
                     inc=("V" if i == 31 else None))
            v_scale[tq] = w.c["V"]
            for dg in range(8):
                it = tq * 8 + dg
                sl = dg % 2
                if it >= 2:
                    w.wait("S", "T", t_wc[it - 2])
                for half in range(2):
                    w.op("S", "dma_start",
                         wc[sl].rearrange("p (a c) -> p a c", a=32)
                         [:, half * 16:(half + 1) * 16, :],
                         s_woutT[:, dg * 256:(dg + 1) * 256]
                         .rearrange("(k p) c -> p k c", k=32)
                         [:, half * 16:(half + 1) * 16, :], inc="dW", k=16)
                w.wait("T", "dW", w.c["dW"])
                w.wait("T", "V", v_scale[tq])
                wv = wc[sl].rearrange("p (a c) -> p a c", a=32)
                for ts in range(4):
                    pidx = (tq, dg, ts)
                    tsg = tq * 4 + ts
                    bank = pb[ts % 4]
                    prevv = (tq, dg - 1, ts) if dg >= 1 else ((tq - 1, 7, ts) if tq else None)
                    if prevv in o_dma:
                        w.wait("S", "dO", o_dma[prevv])
                    w.op("S", "dma_start", xsl[ts],
                         x_in[tsg * 128:(tsg + 1) * 128, dg * 256:(dg + 1) * 256],
                         inc="dS", k=16)
                    x_ld = w.c["dS"]
                    if prevv in v_evac:
                        w.wait("T", "V", v_evac[prevv])
                    for k in range(32):
                        w.op("T", "matmul", bank[:, 0:256],
                             onv[:, k, ts * 128:(ts + 1) * 128], wv[:, k, :],
                             start=(k == 0), stop=(k == 31),
                             inc=("T" if k == 31 else None))
                    w.wait("V", "T", w.c["T"])
                    w.wait("V", "dS", x_ld)
                    if prevv in o_dma:
                        w.wait("V", "dO", o_dma[prevv])
                    v_evac[pidx] = w.op("V", "tensor_tensor", opiece[ts],
                                        bank[:, 0:256], xsl[ts], ALU.add, inc="V")
                    w.wait("S", "V", v_evac[pidx])
                    o_dma[pidx] = w.op("S", "dma_start",
                                       out_hbm[tsg * 128:(tsg + 1) * 128,
                                               dg * 256:(dg + 1) * 256],
                                       opiece[ts], inc="dO", k=16)
                t_wc[it] = w.c["T"]
            t_on[tq] = w.c["T"]

    def down_phase():
        rres = AR1[:, 0:32768]
        rv = rres.rearrange("p (a c) -> p a c", a=64)
        wc = [AR2[:, 0:16384], AR2[:, 16384:32768]]
        opiece = [fview(ARW, i * 1024, 128, 256) for i in range(4)]
        t_r = {}
        t_wc = {}
        v_evac = {}
        o_dma = {}
        for tq in range(4):
            if tq >= 1:
                w.wait("G", "T", t_r[tq - 1])
            for i in range(64):
                w.op("G", "dma_start", rv[:, i, :],
                     s_r2[i * 128:(i + 1) * 128, tq * 512:(tq + 1) * 512],
                     inc="dS", k=16)
            rq_loaded = w.c["dS"]
            for dg in range(8):
                it = tq * 8 + dg
                sl = dg % 2
                if it >= 2:
                    w.wait("S", "T", t_wc[it - 2])
                for half in range(4):
                    w.op("S", "dma_start",
                         wc[sl].rearrange("p (a c) -> p a c", a=64)
                         [:, half * 16:(half + 1) * 16, :],
                         s_wdT[:, dg * 256:(dg + 1) * 256]
                         .rearrange("(k p) c -> p k c", k=64)
                         [:, half * 16:(half + 1) * 16, :], inc="dW", k=16)
                w.wait("T", "dW", w.c["dW"])
                w.wait("T", "dS", rq_loaded)
                wv = wc[sl].rearrange("p (a c) -> p a c", a=64)
                for ts in range(4):
                    pidx = (tq, dg, ts)
                    bank = pb[ts % 4]
                    prev = (tq, dg - 1, ts) if dg >= 1 else ((tq - 1, 7, ts) if tq else None)
                    if prev in v_evac:
                        w.wait("T", "V", v_evac[prev])
                    for k in range(64):
                        w.op("T", "matmul", bank[:, 0:256],
                             rv[:, k, ts * 128:(ts + 1) * 128], wv[:, k, :],
                             start=(k == 0), stop=(k == 63),
                             inc=("T" if k == 63 else None))
                    w.wait("V", "T", w.c["T"])
                    prev2 = (tq, dg - 1, ts) if dg >= 1 else ((tq - 1, 7, ts) if tq else None)
                    if prev2 in o_dma:
                        w.wait("V", "dO", o_dma[prev2])
                    tsg = tq * 4 + ts
                    v_evac[pidx] = w.op("V", "tensor_scalar", opiece[ts], bank[:, 0:256],
                                        M2Q[:, tsg:tsg + 1], 0.0, ALU.mult, ALU.add,
                                        inc="V")
                    w.wait("G", "V", v_evac[pidx])
                    o_dma[pidx] = w.op("G", "dma_start",
                                       out_hbm[tsg * 128:(tsg + 1) * 128,
                                               dg * 256:(dg + 1) * 256],
                                       opiece[ts], accum_op=ALU.add, inc="dO", k=16)
                t_wc[it] = w.c["T"]
            t_r[tq] = w.c["T"]

    load_consts()
    w.barrier()
    weight_prep()
    w.barrier()
    xnT = layernorm()
    w.barrier()
    xq = build_xq(xnT)
    w.barrier()
    ffn_f1(xq)
    w.barrier()
    ffn_f2a()
    w.barrier()
    mixer_m1(xnT)
    w.barrier()
    ssm()
    w.barrier()
    wout_phase()
    w.barrier()
    down_phase()
    w.barrier()

    if DEBUG:
        for nm, src in [("d_za", s_za), ("d_xq", s_xq), ("d_gate", s_gate),
                        ("d_hbc", s_hbc), ("d_on", s_on), ("d_ya", s_ya),
                        ("d_r2", s_r2)]:
            w.op("G", "dma_start", dbg[nm], src, inc="dO", k=16)
        w.op("G", "dma_start", dbg["d_smf"], SMF, inc="dO", k=16)

    w.wait("G", "dO", w.c["dO"])
    w.wait("G", "dS", w.c["dS"])
    w.wait("G", "dG", w.c["dG"])
    w.wait("G", "dW", w.c["dW"])

    with nc.Block() as block:
        @block.tensor
        def _(t):
            for f in w.st["T"]:
                f(t)

        @block.vector
        def _(v):
            for f in w.st["V"]:
                f(v)

        @block.scalar
        def _(a):
            for f in w.st["A"]:
                f(a)

        @block.sync
        def _(s):
            for f in w.st["S"]:
                f(s)

        @block.gpsimd
        def _(g):
            for f in w.st["G"]:
                f(g)

    es.close()
    return nc


def _consts():
    h128 = _hadamard(128)
    hb16 = np.kron(_hadamard(16), np.eye(8, dtype=np.float32))
    hb64 = np.kron(_hadamard(64), np.eye(2, dtype=np.float32))
    i128 = np.eye(128, dtype=np.float32)
    ones_blk = np.zeros((128, 128), np.float32)
    ones_blk[0, :] = 1.0
    cb = np.concatenate([h128, hb16, hb64, i128, ones_blk,
                         np.tile(i128, (1, 4))], axis=1)
    return {
        "cb": cb.astype(ml_dtypes.bfloat16),
        "ones_f": np.ones((1, 128), np.float32),
        "ones_c": np.ones((128, 1), np.float32),
        "id_f": i128,
    }


def kernel(**inp):
    f32 = lambda a: np.ascontiguousarray(np.asarray(a, np.float32))
    x = f32(inp["L0_attn_hidden_states"])
    shared = {
        "w_in": f32(inp["L0_attn_W_in"]),
        "conv_w": f32(inp["L0_attn_conv_w"]),
        "conv_b": f32(inp["L0_attn_conv_b"]),
        "dt_bias": f32(inp["L0_attn_dt_bias"]),
        "a_log": f32(inp["L0_attn_A_log"]),
        "dvec": f32(inp["L0_attn_D"]),
        "norm_w": f32(inp["L0_attn_norm_weight"]),
        "w_out": f32(inp["L0_attn_W_out"]),
        "w_g": f32(inp["L0_ffn_w_gate"]),
        "w_u": f32(inp["L0_ffn_w_up"]),
        "w_d": f32(inp["L0_ffn_w_down"]),
    }
    shared.update(_consts())
    nc = build_program()
    in_maps = [dict(shared, x=x[b]) for b in range(8)]
    res = run_bass_kernel_spmd(nc, in_maps, core_ids=list(range(8)))
    out = np.stack([np.asarray(res.results[b]["out"]) for b in range(8)], axis=0)
    return out.astype(np.float32)


if __name__ == "__main__":
    nc = build_program()
    print("instructions:", nc.n_instructions())


# version 14
# speedup vs baseline: 1.1187x; 1.0258x over previous
import numpy as np
import ml_dtypes
import concourse.bass as bass
import concourse.mybir as mybir
from concourse.bass_utils import run_bass_kernel_spmd
from contextlib import ExitStack

F32 = mybir.dt.float32
BF16 = mybir.dt.bfloat16
AF = mybir.ActivationFunctionType
ALU = mybir.AxisListType
ALU = mybir.AluOpType
AX = mybir.AxisListType

L = 2048
DM = 2048
HH, PP, NNS, GG = 64, 64, 128, 8
INTER = 4096
CONV = 6144
PROJ = 10304
IFF = 8192
CS = 128
NCH = L // CS
KC = 4
RND = 3.0 * 2.0 ** 22
EPS = 1e-5
DEBUG = False

ENGS = "TVASG"


def _hadamard(n):
    h = np.array([[1.0]], np.float32)
    while h.shape[0] < n:
        h = np.block([[h, h], [h, -h]])
    return h


class Wk:
    def __init__(self, nc, semh):
        self.nc = nc
        self.semh = semh
        self.c = {k: 0 for k in semh}
        self.st = {e: [] for e in ENGS}

    def wait(self, eng, sem, val=None):
        v = self.c[sem] if val is None else val
        if v <= 0:
            return
        h = self.semh[sem]
        self.st[eng].append(lambda e, h=h, v=v: e.wait_ge(h, v))

    def op(self, eng, meth, *args, inc=None, k=1, **kw):
        h = None
        kk = 0
        if inc is not None:
            self.c[inc] += k
            h, kk = self.semh[inc], k

        import traceback
        site = traceback.extract_stack(limit=3)[0]

        def f(e, m=meth, a=args, kw=kw, h=h, kk=kk, site=site):
            try:
                ins = getattr(e, m)(*a, **kw)
            except Exception:
                print("FAILING OP:", m, "emitted at", site.filename, site.lineno,
                      [getattr(x, "shape", None) for x in a])
                raise
            if h is not None:
                ins.then_inc(h, kk)

        self.st[eng].append(f)
        return self.c[inc] if inc is not None else None

    def nop(self, eng, cyc=1024):
        self.st[eng].append(lambda e, c=cyc: e.nop(cycle_cnt=c))

    def fence(self, eng, sem):
        self.c[sem] += 1
        h = self.semh[sem]
        self.st[eng].append(lambda e, h=h: e.nop(cycle_cnt=1024).then_inc(h, 1))
        return self.c[sem]

    def barrier(self):
        snap = dict(self.c)
        for e in ENGS:
            for s, v in snap.items():
                if v > 0:
                    self.wait(e, s, v)


def build_program():
    nc = bass.Bass("TRN2", target_bir_lowering=False)
    dt_ = nc.dram_tensor

    x_in = dt_("x", [L, DM], F32, kind="ExternalInput").ap()
    w_in = dt_("w_in", [PROJ, DM], F32, kind="ExternalInput").ap()
    conv_w = dt_("conv_w", [CONV, KC], F32, kind="ExternalInput").ap()
    conv_b = dt_("conv_b", [CONV], F32, kind="ExternalInput").ap()
    dt_bias = dt_("dt_bias", [HH], F32, kind="ExternalInput").ap()
    a_log = dt_("a_log", [HH], F32, kind="ExternalInput").ap()
    dvec = dt_("dvec", [HH], F32, kind="ExternalInput").ap()
    norm_w = dt_("norm_w", [INTER], F32, kind="ExternalInput").ap()
    w_out = dt_("w_out", [DM, INTER], F32, kind="ExternalInput").ap()
    w_g = dt_("w_g", [IFF, DM], F32, kind="ExternalInput").ap()
    w_u = dt_("w_u", [IFF, DM], F32, kind="ExternalInput").ap()
    w_d = dt_("w_d", [DM, IFF], F32, kind="ExternalInput").ap()
    cb_in = dt_("cb", [128, 1152], BF16, kind="ExternalInput").ap()
    onesf_in = dt_("ones_f", [1, 128], F32, kind="ExternalInput").ap()
    onesc_in = dt_("ones_c", [128, 1], F32, kind="ExternalInput").ap()
    idf_in = dt_("id_f", [128, 128], F32, kind="ExternalInput").ap()

    out_hbm = dt_("out", [L, DM], F32, kind="ExternalOutput").ap()

    s_winT = dt_("s_winT", [DM, PROJ], BF16, kind="Internal").ap()
    s_woutT = dt_("s_woutT", [INTER, DM], BF16, kind="Internal").ap()
    s_wgT = dt_("s_wgT", [DM, IFF], BF16, kind="Internal").ap()
    s_wuT = dt_("s_wuT", [DM, IFF], BF16, kind="Internal").ap()
    s_wdT = dt_("s_wdT", [IFF, DM], BF16, kind="Internal").ap()
    s_za = dt_("s_za", [DM, L], BF16, kind="Internal").ap()
    s_xq = dt_("s_xq", [DM, L], BF16, kind="Internal").ap()
    s_gate = dt_("s_gate", [INTER, L], BF16, kind="Internal").ap()
    s_hbc = dt_("s_hbc", [CONV, L], BF16, kind="Internal").ap()
    s_on = dt_("s_on", [L, INTER], BF16, kind="Internal").ap()
    s_ya = dt_("s_ya", [IFF, L], BF16, kind="Internal").ap()
    s_r2 = dt_("s_r2", [IFF, L], BF16, kind="Internal").ap()

    dbg = {}
    if DEBUG:
        for nm, shp in [("d_za", [DM, L]), ("d_xq", [DM, L]), ("d_gate", [INTER, L]),
                        ("d_hbc", [CONV, L]), ("d_on", [L, INTER]), ("d_ya", [IFF, L]),
                        ("d_r2", [IFF, L])]:
            dbg[nm] = dt_(nm, shp, BF16, kind="ExternalOutput").ap()
        dbg["d_smf"] = dt_("d_smf", [128, 2944], F32, kind="ExternalOutput").ap()

    es = ExitStack()

    def sbuf(name, shape, dty):
        return es.enter_context(nc.sbuf_tensor(name, shape, dty)).ap()

    AR1 = sbuf("AR1", [128, 32768], BF16)
    AR2 = sbuf("AR2", [128, 32768], BF16)
    ARW = sbuf("ARW", [128, 19968], BF16)
    SST = sbuf("SST", [128, 8192], BF16)
    SMB = sbuf("SMB", [128, 4224], BF16)
    SMF = sbuf("SMF", [128, 2944], F32)
    CBT = sbuf("CBT", [128, 1152], BF16)
    IDF = sbuf("IDF", [128, 128], F32)
    ONF = sbuf("ONF", [1, 128], F32)
    ONC = sbuf("ONC", [128, 1], F32)

    pb = [es.enter_context(nc.psum_tensor("pb%d" % i, [128, 512], F32)).ap()
          for i in range(7)]
    pbt = es.enter_context(nc.psum_tensor("pbt", [128, 1024], BF16)).ap()

    sems = {}
    for s in ("dS", "dG", "dW", "dO", "T", "V", "A"):
        sems[s] = es.enter_context(nc.semaphore(s))

    w = Wk(nc, sems)

    def fview(arena, byte_off, parts, cols):
        assert byte_off % 4 == 0
        return arena[0:parts, byte_off // 2: byte_off // 2 + 2 * cols].bitcast(F32)

    H128 = CBT[:, 0:128]
    HB16 = CBT[:, 128:256]
    HB64 = CBT[:, 256:384]
    IDB = CBT[:, 384:512]
    ONB = CBT[0:1, 512:640]
    IR4 = CBT[:, 640:1152]

    smf = SMF
    DTA = smf[0:64, 0:2048]
    RTEN = smf[0:64, 0:2048]
    ABR = smf[64:128, 0:2048]
    ACS = smf[0:64, 2048:2176]
    CW0 = smf[:, 2176:2180]
    CW1 = smf[:, 2180:2184]
    CB0 = smf[:, 2184:2185]
    CB1 = smf[:, 2185:2186]
    COLS = smf[:, 2186:2250]
    M1C = smf[:, 2250:2266]
    M2C = smf[:, 2266:2282]
    M2Q = smf[:, 2282:2298]
    C1R = smf[0:16, 2304:2432]
    NWC = smf[:, 2432:2464]
    VARC = smf[:, 2464:2472]
    RSTC = smf[:, 2472:2480]
    SSCAL = smf[0:1, 2480:2496]
    SCL = smf[:, 2496:2520]
    C14 = smf[:, 2520:2524]
    CI4 = smf[:, 2524:2528]
    MXT = smf[:, 2528:2656]
    C2ROW = smf[0:1, 2656:2784]
    CRW1 = smf[0:1, 0:512]
    CRW2 = smf[0:1, 512:1024]

    def col(i):
        return COLS[:, i:i + 1]

    DTT = SMB[:, 0:1024]
    DST = SMB[:, 1024:2048]
    SDT = SMB[:, 2048:3072]
    CDB = SMB[:, 3072:4096]
    DMAT = SMB[:, 4096:4160]
    CDROW = SMB[0:1, 4160:4224]

    SP0 = SST[:, 0:4096]
    SP1 = SST[:, 4096:8192]

    def load_consts():
        w.op("V", "memset", col(44), 1e-5)
        w.op("V", "memset", col(45), -1.0, inc="V")
        w.op("S", "dma_start", CBT, cb_in, inc="dS", k=16)
        w.op("S", "dma_start", IDF, idf_in, inc="dS", k=16)
        w.op("S", "dma_start", ONF, onesf_in, inc="dS", k=16)
        w.op("S", "dma_start", ONC, onesc_in, inc="dS", k=16)

    def weight_prep():
        slot = [fview(ARW, i * 8192, 128, 2048) for i in range(3)]
        tslot = [ARW[:, 12288 + i * 2048: 12288 + (i + 1) * 2048] for i in range(2)]
        asm = [AR1[:, i * 2048:(i + 1) * 2048] for i in range(2)]
        junk = fview(AR2, 0, 128, 2048)
        run, spot = col(0), col(1)

        w.op("V", "memset", run, 0.0, inc="V")
        a_read = {}
        gi = 0
        for si, wm in enumerate((w_g, w_u, w_d)):
            wf = wm if wm.shape[1] == 2048 else wm.rearrange(
                "a (b c) -> (a b) c", c=2048)
            for t in range(64):
                sl = gi % 3
                if gi >= 3:
                    w.wait("S", "A", a_read[gi - 3])
                w.op("S", "dma_start", slot[sl], wf[t * 128:(t + 1) * 128, :],
                     inc="dS", k=16)
                w.wait("A", "dS", w.c["dS"])
                a_read[gi] = w.op("A", "activation", junk, slot[sl], AF.Abs,
                                  accum_out=spot, inc="A")
                w.wait("V", "A", a_read[gi])
                w.nop("V")
                w.op("V", "tensor_tensor", run, run, spot, ALU.add, inc="V")
                gi += 1
            w.wait("T", "V", w.c["V"])
            w.op("T", "matmul", pb[0][0:1, 0:1], run, ONC, start=True, stop=True,
                 inc="T")
            w.wait("V", "T", w.c["T"])
            sc = SSCAL[:, si:si + 1]
            w.op("V", "tensor_scalar", sc, pb[0][0:1, 0:1], 1.0 / (IFF * DM), 1e-5,
                 ALU.mult, ALU.max)
            w.nop("V")
            w.op("V", "reciprocal", SSCAL[:, 8:9], sc)
            w.nop("V")
            w.op("V", "memset", run, 0.0, inc="V")
            w.wait("T", "V", w.c["V"])
            w.nop("T")
            w.op("T", "matmul", pb[1][:, 0:1], ONF, SSCAL[:, 8:9], start=True,
                 stop=True, inc="T")
            w.wait("V", "T", w.c["T"])
            w.op("V", "tensor_copy", col(2 + si), pb[1][:, 0:1], inc="V")

        for si, tgt in ((0, SCL[:, 0:1]), (1, SCL[:, 1:2]), (2, SCL[:, 4:5])):
            w.wait("T", "V", w.c["V"])
            w.op("T", "matmul", pb[1][:, 1:2], ONF, SSCAL[:, si:si + 1], start=True,
                 stop=True, inc="T")
            w.wait("V", "T", w.c["T"])
            w.op("V", "tensor_copy", tgt, pb[1][:, 1:2], inc="V")
        w.nop("V")
        w.op("V", "tensor_tensor", SCL[:, 2:3], SCL[:, 1:2], SCL[:, 4:5], ALU.mult)
        w.nop("V")
        w.op("V", "tensor_scalar", SCL[:, 2:3], SCL[:, 2:3],
             float(1.0 / (127.0 * np.sqrt(8192.0))), 0.0, ALU.mult, ALU.add)
        w.op("V", "reciprocal", SCL[:, 3:4], SCL[:, 1:2])
        w.nop("V")
        w.op("V", "tensor_scalar", SCL[:, 3:4], SCL[:, 3:4],
             float(np.sqrt(8192.0) * 1e-5), 0.0, ALU.mult, ALU.add, inc="V")

        jobs = []
        for si, wm, dstT in ((0, w_g, s_wgT), (1, w_u, s_wuT)):
            for t in range(64):
                jobs.append(("tern", si, wm[t * 128:(t + 1) * 128, :], 128,
                             dstT[:, t * 128:(t + 1) * 128]
                             .rearrange("(k p) c -> p k c", k=16)))
        for q in range(4):
            for j in range(16):
                jobs.append(("tern", 2,
                             w_d[j * 128:(j + 1) * 128, q * 2048:(q + 1) * 2048], 128,
                             s_wdT[q * 2048:(q + 1) * 2048, j * 128:(j + 1) * 128]
                             .rearrange("(k p) c -> p k c", k=16)))
        for j in range(80):
            jobs.append(("cast", None, w_in[j * 128:(j + 1) * 128, :], 128,
                         s_winT[:, j * 128:(j + 1) * 128]
                         .rearrange("(k p) c -> p k c", k=16)))
        jobs.append(("cast", None, w_in[10240:10304, :], 64,
                     s_winT[:, 10240:10304].rearrange("(k p) c -> p k c", k=16)))
        for hf in range(2):
            for j in range(16):
                jobs.append(("cast", None,
                             w_out[j * 128:(j + 1) * 128, hf * 2048:(hf + 1) * 2048],
                             128,
                             s_woutT[hf * 2048:(hf + 1) * 2048, j * 128:(j + 1) * 128]
                             .rearrange("(k p) c -> p k c", k=16)))

        v_cons = {}
        v_ev = {}
        t_tp = {}
        g_wr = {}
        for t, (kind, si, srcap, rows, dstap) in enumerate(jobs):
            sl = t % 3
            ts = t % 2
            if t >= 3:
                w.wait("S", "V", v_cons[t - 3])
            w.op("S", "dma_start", slot[sl][0:rows, :], srcap, inc="dS", k=16)
            ld = w.c["dS"]
            w.wait("V", "dS", ld)
            if t >= 2:
                w.wait("V", "T", t_tp[t - 2])
            if t == 0:
                w.nop("V")
            sv = slot[sl][0:rows, :]
            tv = tslot[ts][0:rows, :]
            if kind == "tern":
                invc = col(2 + si)[0:rows, :]
                v_cons[t] = w.op("V", "tensor_scalar", junk[0:rows, :], sv, invc,
                                 -1.0, ALU.mult, ALU.max, inc="V")
                w.op("V", "tensor_scalar", junk[0:rows, :], junk[0:rows, :], 1.0,
                     RND, ALU.min, ALU.add)
                w.op("V", "tensor_scalar", tv, junk[0:rows, :], -RND, 0.0, ALU.add,
                     ALU.add, inc="V")
            else:
                v_cons[t] = w.op("V", "tensor_copy", tv, sv, inc="V")
            vt = w.c["V"]
            idn = IDB if rows == 128 else IDB[0:64, 0:64]
            pw = 128 if rows == 128 else 64
            for r in range(2):
                w.wait("T", "V", vt if r == 0 else w.c["V"])
                if t >= 1 or r >= 1:
                    w.wait("T", "V", w.c["V"])
                for k in range(8):
                    w.op("T", "transpose", pbt[:, k * pw:(k + 1) * pw],
                         tv[:, (r * 8 + k) * 128:(r * 8 + k + 1) * 128], idn,
                         inc=("T" if k == 7 else None))
                w.wait("V", "T", w.c["T"])
                if t >= 2:
                    w.wait("V", "dG", g_wr[t - 2])
                w.op("V", "tensor_copy",
                     asm[ts][:, r * 8 * pw:(r + 1) * 8 * pw], pbt[:, 0:8 * pw],
                     inc="V")
            t_tp[t] = w.c["T"]
            v_ev[t] = w.c["V"]
            w.wait("G", "V", v_ev[t])
            g_wr[t] = w.op("G", "dma_start", dstap,
                           asm[ts][:, 0:16 * pw].rearrange("p (k c) -> p k c", k=16),
                           inc="dG", k=16)

    def layernorm():
        xslot = [fview(AR2, 0, 128, 2048), fview(AR2, 8192, 128, 2048)]
        xnn = AR2[:, 8192:10240]
        sqs = fview(ARW, 0, 128, 2048)
        xnT = AR1[:, 0:32768]
        xnTv = xnT.rearrange("p (a c) -> p a c", a=16)
        for tt in range(16):
            sl = tt % 2
            w.wait("S", "V", w.c["V"])
            w.wait("S", "A", w.c["A"])
            w.op("S", "dma_start", xslot[sl], x_in[tt * 128:(tt + 1) * 128, :],
                 inc="dS", k=16)
            sumc, ssq, mc, vc, rc, nb = (col(8 + i) for i in range(6))
            w.wait("V", "dS", w.c["dS"])
            w.op("V", "reduce_sum", sumc, xslot[sl], AX.X)
            w.nop("V")
            w.op("V", "engine_nop", inc="V")
            w.wait("A", "dS", w.c["dS"])
            w.op("A", "activation", sqs, xslot[sl], AF.Square, accum_out=ssq,
                 inc="A")
            w.wait("V", "A", w.c["A"])
            w.nop("V")
            w.op("V", "tensor_scalar", mc, sumc, 1.0 / DM, 0.0, ALU.mult, ALU.add)
            w.op("V", "tensor_scalar", vc, ssq, 1.0 / DM, 0.0, ALU.mult, ALU.add)
            w.nop("V")
            w.op("V", "tensor_tensor", nb, mc, mc, ALU.mult)
            w.nop("V")
            w.op("V", "tensor_tensor", vc, vc, nb, ALU.subtract, inc="V")
            w.wait("A", "V", w.c["V"])
            w.nop("A")
            w.op("A", "activation", vc, vc, AF.Sqrt, bias=col(44), inc="A")
            w.wait("V", "A", w.c["A"])
            w.nop("V")
            w.op("V", "reciprocal", rc, vc)
            w.nop("V")
            w.op("V", "tensor_scalar", xnn, xslot[sl], mc, rc, ALU.subtract,
                 ALU.mult, inc="V")
            w.wait("T", "V", w.c["V"])
            for q in range(4):
                w.wait("T", "V", w.c["V"])
                for j in range(4):
                    d = q * 4 + j
                    w.op("T", "transpose", pbt[:, j * 128:(j + 1) * 128],
                         xnn[:, d * 128:(d + 1) * 128], IDB,
                         inc=("T" if j == 3 else None))
                w.wait("V", "T", w.c["T"])
                w.op("V", "tensor_copy",
                     xnTv[:, q * 4:(q + 1) * 4, tt * 128:(tt + 1) * 128],
                     pbt[:, 0:512].rearrange("p (a c) -> p a c", a=4), inc="V")
        return xnT

    def build_xq(xnT):
        zap = ARW[:, 0:2048]
        z1s = fview(ARW, 6144, 128, 8192)
        mx = fview(ARW, 4096, 128, 512)
        xnv = xnT.rearrange("p (a c) -> p a c", a=16)
        for d in range(16):
            w.wait("V", "dS", w.c["dS"])
            for q in range(4):
                bank = pb[q % 2]
                w.wait("T", "V", w.c["V"])
                w.op("T", "matmul", bank[:, 0:512], H128,
                     xnv[:, d, q * 512:(q + 1) * 512], start=True, stop=True, inc="T")
                w.wait("V", "T", w.c["T"])
                w.op("V", "tensor_copy", zap[:, q * 512:(q + 1) * 512], bank[:, 0:512],
                     inc="V")
            w.wait("G", "V", w.c["V"])
            w.op("G", "dma_start", s_za[d * 128:(d + 1) * 128, :], zap, inc="dS", k=16)
        zb = AR2[:, 0:8192]
        zbv = zb.rearrange("p (a c) -> p a c", a=16)
        xqs = AR2[:, 8192:16384]
        za_r = s_za.rearrange("(dh d3 dl) t -> (dh d3) dl t", dh=16, d3=8, dl=16)
        xq_w = s_xq.rearrange("(dh d3 dl) t -> (dh d3) dl t", dh=16, d3=8, dl=16)
        for tc in range(4):
            w.wait("S", "dS", w.c["dS"])
            w.wait("S", "T", w.c["T"])
            w.op("S", "dma_start", zbv, za_r[:, :, tc * 512:(tc + 1) * 512],
                 inc="dS", k=16)
            w.wait("T", "dS", w.c["dS"])
            for k in range(16):
                bank = pb[2 + (k % 2)]
                w.wait("T", "V", w.c["V"])
                w.op("T", "matmul", bank[:, 0:512], HB16, zb[:, k * 512:(k + 1) * 512],
                     start=True, stop=True, inc="T")
                w.wait("V", "T", w.c["T"])
                w.op("V", "tensor_copy", z1s[:, k * 512:(k + 1) * 512], bank[:, 0:512],
                     inc="V")
            w.op("V", "tensor_reduce", mx,
                 z1s.rearrange("p (a c) -> p c a", a=16), AX.X, ALU.max,
                 apply_absolute_value=True, inc="V")
            w.wait("T", "V", w.c["V"])
            for j in range(4):
                w.op("T", "transpose", pb[4][:, j * 128:(j + 1) * 128],
                     mx[:, j * 128:(j + 1) * 128], IDF, inc=("T" if j == 3 else None))
            w.wait("V", "T", w.c["T"])
            for j in range(4):
                w.op("V", "reduce_max", M1C[:, tc * 4 + j: tc * 4 + j + 1],
                     pb[4][:, j * 128:(j + 1) * 128], AX.X)
            mm4 = M1C[:, tc * 4: tc * 4 + 4]
            w.nop("V")
            w.op("V", "tensor_scalar", mm4, mm4, float(np.sqrt(2048.0) * 1e-5), 0.0,
                 ALU.max, ALU.add)
            w.nop("V")
            w.op("V", "reciprocal", C14, mm4)
            w.nop("V")
            w.op("V", "tensor_scalar", C14, C14, 127.0, 0.0, ALU.mult, ALU.add)
            w.op("V", "tensor_scalar", CI4, mm4, float(1.0 / (127.0 * np.sqrt(2048.0))),
                 0.0, ALU.mult, ALU.add, inc="V")
            w.wait("T", "V", w.c["V"])
            w.nop("T")
            for j in range(4):
                w.op("T", "transpose", pb[4][0:1, j * 128:(j + 1) * 128],
                     C14[:, j:j + 1], IDF)
                w.op("T", "transpose", pb[6][0:1, j * 128:(j + 1) * 128],
                     CI4[:, j:j + 1], IDF, inc=("T" if j == 3 else None))
            w.wait("V", "T", w.c["T"])
            w.op("V", "tensor_copy", CRW1, pb[4][0:1, 0:512])
            w.op("V", "tensor_copy", CRW2, pb[6][0:1, 0:512], inc="V")
            w.wait("T", "V", w.c["V"])
            w.nop("T")
            for j in range(4):
                w.op("T", "matmul", pb[5][:, j * 128:(j + 1) * 128], ONF,
                     CRW1[0:1, j * 128:(j + 1) * 128], start=True, stop=True)
                w.op("T", "matmul", pb[6][:, j * 128:(j + 1) * 128], ONF,
                     CRW2[0:1, j * 128:(j + 1) * 128], start=True, stop=True,
                     inc=("T" if j == 3 else None))
            w.wait("V", "T", w.c["T"])
            w.wait("V", "dS", w.c["dS"])
            z3 = z1s.rearrange("p (a c) -> p a c", a=16)
            xq3 = xqs.rearrange("p (a c) -> p a c", a=16)
            for j in range(4):
                w.op("V", "tensor_tensor", z3[:, :, j * 128:(j + 1) * 128],
                     z3[:, :, j * 128:(j + 1) * 128],
                     pb[5][:, j * 128:(j + 1) * 128].unsqueeze(1)
                     .broadcast_to([128, 16, 128]), ALU.mult)
            w.op("V", "tensor_scalar", z1s, z1s, RND, -RND, ALU.add, ALU.add)
            for j in range(4):
                w.op("V", "tensor_tensor", xq3[:, :, j * 128:(j + 1) * 128],
                     z3[:, :, j * 128:(j + 1) * 128],
                     pb[6][:, j * 128:(j + 1) * 128].unsqueeze(1)
                     .broadcast_to([128, 16, 128]), ALU.mult, inc=("V" if j == 3
                                                                   else None))
            w.wait("G", "V", w.c["V"])
            w.op("G", "dma_start", xq_w[:, :, tc * 512:(tc + 1) * 512],
                 xqs.rearrange("p (a c) -> p a c", a=16), inc="dS", k=16)
        xq = AR2[:, 0:32768]
        w.wait("S", "dS", w.c["dS"])
        xqv = xq.rearrange("p (a c) -> p a c", a=16)
        for k in range(16):
            w.op("S", "dma_start", xqv[:, k, :], s_xq[k * 128:(k + 1) * 128, :],
                 inc="dS", k=16)
        return xq

    def ffn_f1(xq):
        wslot = [ARW[:, 0:4096], ARW[:, 4096:8192]]
        silg = [ARW[:, 8192 + i * 512: 8192 + (i + 1) * 512] for i in range(2)]
        y1b = [ARW[:, 9216 + i * 512: 9216 + (i + 1) * 512] for i in range(2)]
        yab = [ARW[:, 10240 + i * 512: 10240 + (i + 1) * 512] for i in range(2)]
        xqv = xq.rearrange("p (a c) -> p a c", a=16)
        t_mm = {}
        t_sa = {}
        a_done = {}
        v_y1 = {}
        v_evac = {}
        s_store = {}
        t_wcons = {}

        def emit_stage_a(jt):
            pa = pb[4 + (jt % 2)]
            w.wait("T", "V", v_y1[jt])
            if jt >= 2:
                w.wait("T", "V", v_evac[jt - 2])
            t_sa[jt] = w.op("T", "matmul", pa[:, 0:512], H128, y1b[jt % 2],
                            start=True, stop=True, inc="T")
            w.wait("V", "T", t_sa[jt])
            if jt >= 2:
                w.wait("V", "dS", s_store[jt - 2])
            v_evac[jt] = w.op("V", "tensor_copy", yab[jt % 2], pa[:, 0:512], inc="V")
            w.wait("G", "V", v_evac[jt])
            o_, tc_ = jt // 4, jt % 4
            s_store[jt] = w.op("G", "dma_start",
                               s_ya[o_ * 128:(o_ + 1) * 128, tc_ * 512:(tc_ + 1) * 512],
                               yab[jt % 2], inc="dS", k=16)

        for o in range(64):
            sl = o % 2
            if o >= 2:
                w.wait("S", "T", t_wcons[o - 2])
            w.op("S", "dma_start",
                 wslot[sl][:, 0:2048].rearrange("p (k c) -> p k c", k=16),
                 s_wgT[:, o * 128:(o + 1) * 128]
                 .rearrange("(k p) c -> p k c", k=16), inc="dW", k=16)
            w.op("S", "dma_start",
                 wslot[sl][:, 2048:4096].rearrange("p (k c) -> p k c", k=16),
                 s_wuT[:, o * 128:(o + 1) * 128]
                 .rearrange("(k p) c -> p k c", k=16), inc="dW", k=16)
            wz = w.c["dW"]
            for tc in range(4):
                it = o * 4 + tc
                pg = pb[0 + (tc % 2)]
                pu = pb[2 + (tc % 2)]
                if tc == 0:
                    w.wait("T", "dW", wz)
                if it >= 2:
                    w.wait("T", "A", a_done[it - 2])
                    w.wait("T", "V", v_y1[it - 2])
                for k in range(16):
                    w.op("T", "matmul", pg[:, 0:512],
                         wslot[sl][:, k * 128:(k + 1) * 128],
                         xqv[:, k, tc * 512:(tc + 1) * 512], start=(k == 0),
                         stop=(k == 15))
                    w.op("T", "matmul", pu[:, 0:512],
                         wslot[sl][:, 2048 + k * 128: 2048 + (k + 1) * 128],
                         xqv[:, k, tc * 512:(tc + 1) * 512], start=(k == 0),
                         stop=(k == 15), inc=("T" if k == 15 else None))
                t_mm[it] = w.c["T"]
                w.wait("A", "T", t_mm[it])
                if it >= 2:
                    w.wait("A", "V", v_y1[it - 2])
                a_done[it] = w.op("A", "activation", silg[tc % 2], pg[:, 0:512],
                                  AF.Silu, scale=SCL[:, 0:1], inc="A")
                w.wait("V", "A", a_done[it])
                v_y1[it] = w.op("V", "tensor_tensor", y1b[tc % 2], silg[tc % 2],
                                pu[:, 0:512], ALU.mult, inc="V")
                if it >= 1:
                    emit_stage_a(it - 1)
            t_wcons[o] = w.c["T"]
        emit_stage_a(255)

    def ffn_f2a():
        zbs = [AR2[:, 0:8192], SST[:, 0:8192]]
        z2s = [fview(ARW, 0, 128, 8192), fview(AR2, 32768, 128, 8192)]
        r2b = AR2[:, 8192:16384]
        r2v = r2b.rearrange("p (a c) -> p a c", a=64)
        mxts = [MXT, smf[:, 2784:2912]]
        c2cs = [col(30), col(31)]
        c2rows = [C2ROW, smf[0:1, 0:128]]
        ya_r = s_ya.rearrange("(oh b6 ol) t -> (oh b6) ol t", oh=64, b6=2, ol=64)
        r2_w = s_r2.rearrange("(oh b6 ol) t -> (oh b6) ol t", oh=64, b6=2, ol=64)
        t_zb = {}
        q_done = {}
        qa_done = {}
        ld_done = {}

        def mm_part(tc):
            zb = zbs[tc % 2]
            zbv = zb.rearrange("p (a c) -> p a c", a=64)
            zs = z2s[tc % 2]
            if tc >= 2:
                w.wait("S", "T", t_zb[tc - 2])
            for part in range(4):
                w.op("S", "dma_start", zbv[:, part * 16:(part + 1) * 16, :],
                     ya_r[:, part * 16:(part + 1) * 16, tc * 128:(tc + 1) * 128],
                     inc="dS", k=16)
            ld_done[tc] = w.c["dS"]
            w.wait("T", "dS", ld_done[tc])
            if tc >= 2:
                w.wait("T", "V", q_done[tc - 2])
                w.wait("T", "A", qa_done.get(tc - 2, 0))
            for k in range(16):
                bank = pb[(tc % 2) * 2 + (k % 2)]
                if k >= 2:
                    w.wait("T", "V", w.c["V"])
                    w.wait("T", "A", w.c["A"])
                for j in range(4):
                    w.op("T", "matmul", bank[:, j * 128:(j + 1) * 128], HB64,
                         zb[:, (k * 4 + j) * 128:(k * 4 + j + 1) * 128],
                         start=True, stop=True, inc=("T" if j == 3 else None))
                ee = "V" if k % 2 == 0 else "A"
                w.wait(ee, "T", w.c["T"])
                if ee == "V":
                    w.op("V", "tensor_copy", zs[:, k * 512:(k + 1) * 512],
                         bank[:, 0:512], inc="V")
                else:
                    w.op("A", "copy", zs[:, k * 512:(k + 1) * 512], bank[:, 0:512],
                         inc="A")
            t_zb[tc] = w.c["T"]

        def quant_part(tc):
            zs = z2s[tc % 2]
            z3 = zs.rearrange("p (a c) -> p a c", a=64)
            mxt = mxts[tc % 2]
            c2c = c2cs[tc % 2]
            c2r = c2rows[tc % 2]
            tb = pb[4 + (tc % 2)]
            zt = zs.rearrange("p (a c) -> p c a", a=64)
            w.wait("V", "A", w.c["A"])
            w.op("V", "tensor_reduce", mxt, zt, AX.X, ALU.max,
                 apply_absolute_value=True, inc="V")
            w.wait("T", "V", w.c["V"])
            w.op("T", "transpose", tb[:, 0:128], mxt, IDF, inc="T")
            w.wait("V", "T", w.c["T"])
            mcol = M2C[:, tc:tc + 1]
            w.op("V", "reduce_max", mcol, tb[:, 0:128], AX.X)
            w.nop("V")
            w.op("V", "tensor_tensor", mcol, mcol, SCL[:, 3:4], ALU.max)
            w.nop("V")
            w.op("V", "reciprocal", c2c, mcol)
            w.nop("V")
            w.op("V", "tensor_scalar", c2c, c2c, 127.0, 0.0, ALU.mult, ALU.add,
                 inc="V")
            w.wait("T", "V", w.c["V"])
            w.nop("T")
            w.op("T", "transpose", tb[0:1, 0:128], c2c, IDF, inc="T")
            w.wait("V", "T", w.c["T"])
            w.op("V", "tensor_copy", c2r, tb[0:1, 0:128], inc="V")
            w.wait("T", "V", w.c["V"])
            w.nop("T")
            w.op("T", "matmul", tb[:, 128:256], ONF, c2r, start=True, stop=True,
                 inc="T")
            w.wait("V", "T", w.c["T"])
            w.wait("V", "dS", w.c["dS"])
            w.op("V", "tensor_tensor", z3, z3,
                 tb[:, 128:256].unsqueeze(1).broadcast_to([128, 64, 128]), ALU.mult)
            w.op("V", "tensor_scalar", r2v, z3, RND, -RND, ALU.add, ALU.add, inc="V")
            q_done[tc] = w.c["V"]
            qa_done[tc] = w.c["A"]
            w.wait("G", "V", q_done[tc])
            for part in range(4):
                w.op("G", "dma_start",
                     r2_w[:, part * 16:(part + 1) * 16, tc * 128:(tc + 1) * 128],
                     r2v[:, part * 16:(part + 1) * 16, :], inc="dS", k=16)

        for tc in range(16):
            mm_part(tc)
            if tc >= 1:
                quant_part(tc - 1)
        quant_part(15)
        w.nop("V")
        w.op("V", "tensor_tensor", M2Q, M2C, SCL[:, 2:3].broadcast_to([128, 16]),
             ALU.mult, inc="V")

    def dt_family():
        dtspb = ARW[:, 12294:14342][0:64, :]
        sdch = ARW[:, 14342:14470][0:64, :]
        dsch = ARW[:, 14470:14598][0:64, :]
        T1 = fview(ARW, 31744, 64, 2048)
        ABR2 = T1
        w.wait("A", "V", w.c["V"])
        w.op("A", "activation", T1, DTA, AF.Abs)
        w.op("A", "activation", T1, T1, AF.Exp, scale=col(45)[0:64, :])
        w.op("A", "activation", T1, T1, AF.Ln, bias=1.0, inc="A")
        w.wait("V", "A", w.c["A"])
        w.op("V", "tensor_scalar", DTA, DTA, 0.0, 0.0, ALU.max, ALU.add)
        w.op("V", "tensor_tensor", DTA, DTA, T1, ALU.add, inc="V")
        w.wait("A", "V", w.c["V"])
        w.op("A", "copy", dtspb, DTA, inc="A")
        w.wait("T", "A", w.c["A"])
        for c in range(NCH):
            if c % 4 == 0:
                w.wait("T", "V", w.c["V"])
            w.op("T", "transpose", pbt[:, (c % 4) * 128:(c % 4) * 128 + 64],
                 dtspb[:, c * 128:(c + 1) * 128], IDB[0:64, 0:64],
                 inc=("T" if c % 4 == 3 else None))
            if c % 4 == 3:
                w.wait("V", "T", w.c["T"])
                w.op("V", "tensor_copy",
                     DTT[:, (c - 3) * 64:(c + 1) * 64].rearrange(
                         "p (a c) -> p a c", a=4),
                     pbt[:, 0:512].rearrange("p (a c) -> p a c", a=4)[:, :, 0:64],
                     inc="V")
        w.wait("A", "dS", w.c["dS"])
        w.wait("A", "V", w.c["V"])
        w.op("A", "activation", col(41)[0:64, :], col(41)[0:64, :], AF.Exp,
             inc="A")
        w.wait("V", "A", w.c["A"])
        w.nop("V")
        w.op("V", "tensor_scalar", col(41)[0:64, :], col(41)[0:64, :], -1.0, 0.0,
             ALU.mult, ALU.add)
        w.nop("V")
        w.op("V", "tensor_scalar", ABR2, DTA, col(41)[0:64, :], 0.0, ALU.mult,
             ALU.add, inc="V")
        zc = col(43)
        w.op("V", "memset", zc, 0.0)
        w.nop("V")
        w.op("V", "engine_nop", inc="V")
        for c in range(NCH):
            w.op("V", "tensor_tensor_scan", ACS, ABR2[:, c * 128:(c + 1) * 128],
                 zc[0:64, :].broadcast_to([64, 128]), 0.0, ALU.add, ALU.add, inc="V")
            w.wait("A", "V", w.c["V"])
            w.nop("A")
            w.op("A", "activation", sdch, ACS, AF.Exp)
            w.op("A", "activation", dsch, ACS, AF.Exp, scale=col(45)[0:64, :],
                 bias=ACS[:, 127:128], inc="A")
            w.wait("T", "A", w.c["A"])
            w.nop("T")
            w.wait("T", "V", w.c["V"])
            w.op("T", "transpose", pbt[:, 0:64], sdch, IDB[0:64, 0:64])
            w.op("T", "transpose", pbt[:, 128:192], dsch, IDB[0:64, 0:64], inc="T")
            w.op("T", "transpose", pbt[0:1, 256:320], sdch[:, 127:128],
                 IDB[0:64, 0:64], inc="T")
            w.wait("V", "T", w.c["T"])
            w.op("V", "tensor_copy", SDT[:, c * 64:(c + 1) * 64], pbt[:, 0:64])
            w.op("V", "tensor_copy", DST[:, c * 64:(c + 1) * 64], pbt[:, 128:192])
            w.op("V", "tensor_copy", CDROW, pbt[0:1, 256:320], inc="V")
            w.wait("T", "V", w.c["V"])
            w.nop("T")
            w.op("T", "matmul", pb[3][:, 0:64], ONB, CDROW, start=True, stop=True,
                 inc="T")
            w.wait("V", "T", w.c["T"])
            w.op("V", "tensor_copy", CDB[:, c * 64:(c + 1) * 64], pb[3][:, 0:64],
                 inc="V")
        w.wait("A", "V", w.c["V"])
        w.op("A", "activation", RTEN, ABR2, AF.Exp, inc="A")
        w.wait("V", "A", w.c["A"])
        for c in range(NCH):
            w.op("V", "memset", RTEN[:, c * 128: c * 128 + 1], 0.0,
                 inc=("V" if c == NCH - 1 else None))
        w.wait("T", "V", w.c["V"])
        w.op("T", "transpose", pb[3][0:1, 0:128], col(42), IDF, inc="T")
        w.wait("V", "T", w.c["T"])
        w.op("V", "tensor_copy", C1R[0:1, 0:128], pb[3][0:1, 0:128], inc="V")
        w.wait("T", "V", w.c["V"])
        w.nop("T")
        w.op("T", "matmul", pb[3][:, 0:64], ONF, C1R[0:1, 0:64], start=True,
             stop=True, inc="T")
        w.wait("V", "T", w.c["T"])
        w.op("V", "tensor_copy", DMAT, pb[3][:, 0:64], inc="V")

    def mixer_m1(xnT):
        wslot = [ARW[:, 0:4096], ARW[:, 4096:8192]]
        cvin = [ARW[:, 8192:10243], ARW[:, 10243:12294]]
        hout = ARW[:, 12294:14342]
        gpiece = ARW[:, 14342:14854]
        xnv = xnT.rearrange("p (a c) -> p a c", a=16)
        w.op("S", "dma_start", col(40)[0:64, :],
             dt_bias.rearrange("(a b) -> a b", b=1), inc="dS", k=16)
        w.op("S", "dma_start", col(41)[0:64, :],
             a_log.rearrange("(a b) -> a b", b=1), inc="dS", k=16)
        w.op("S", "dma_start", col(42)[0:64, :],
             dvec.rearrange("(a b) -> a b", b=1), inc="dS", k=16)
        w.op("V", "memset", cvin[0][:, 0:3], 0.0)
        w.op("V", "memset", cvin[1][:, 0:3], 0.0, inc="V")

        groups = [("dt", [80])]
        groups += [("hbc", list(range(32 + 2 * i, 34 + 2 * i))) for i in range(24)]
        groups += [("gate", list(range(2 * i, 2 * i + 2))) for i in range(16)]

        t_after = {}
        ev = {}
        cnt = 0
        hout2 = [hout, ARW[:, 14854:16902]]
        gp2 = [gpiece, ARW[:, 16902:17414]]
        hb_done = {}
        gp_done = {}
        cv_done = {}
        sl_done = {}
        hi = 0
        gpi = 0
        for gi, (kind, js) in enumerate(groups):
            sl = gi % 2
            j0 = js[0]
            rows = 64 if kind == "dt" else 256
            if gi >= 2:
                w.wait("S", "T", t_after[gi - 2])
            w.op("S", "dma_start",
                 wslot[sl][:, 0:4096].rearrange("p (k c) -> p k c", k=16)[:, :, 0:rows],
                 s_winT[:, j0 * 128: j0 * 128 + rows]
                 .rearrange("(k p) c -> p k c", k=16), inc="dW", k=16)
            w.wait("T", "dW", w.c["dW"])
            for jj, j in enumerate(js):
                mrows = 64 if kind == "dt" else 128
                for tc in range(4):
                    bank = pb[cnt % 2]
                    if cnt >= 2:
                        eng, val = ev[cnt - 2]
                        w.wait("T", eng, val)
                    for k in range(16):
                        w.op("T", "matmul", bank[0:mrows, 0:512],
                             wslot[sl][:, k * 256 + jj * 128: k * 256 + jj * 128 + mrows],
                             xnv[:, k, tc * 512:(tc + 1) * 512],
                             start=(k == 0), stop=(k == 15),
                             inc=("T" if k == 15 else None))
                    tmm = w.c["T"]
                    if kind == "dt":
                        w.wait("V", "T", tmm)
                        w.wait("V", "dS", w.c["dS"])
                        ev[cnt] = ("V", w.op("V", "tensor_scalar",
                                             DTA[:, tc * 512:(tc + 1) * 512],
                                             bank[0:64, 0:512], col(40)[0:64, :], 0.0,
                                             ALU.add, ALU.add, inc="V"))
                    elif kind == "hbc":
                        w.wait("A", "T", tmm)
                        if hi >= 2 and tc == 0:
                            w.wait("A", "V", cv_done[hi - 2])
                        ev[cnt] = ("A", w.op("A", "copy",
                                             cvin[jj % 2][:, 3 + tc * 512: 3 + (tc + 1) * 512],
                                             bank[:, 0:512], inc="A"))
                    else:
                        w.wait("A", "T", tmm)
                        if gpi >= 2:
                            w.wait("A", "dS", gp_done[gpi - 2])
                        ev[cnt] = ("A", w.op("A", "copy", gp2[gpi % 2],
                                             bank[:, 0:512], inc="A"))
                        w.wait("S", "A", w.c["A"])
                        gp_done[gpi] = w.op("S", "dma_start",
                                            s_gate[j * 128:(j + 1) * 128,
                                                   tc * 512:(tc + 1) * 512],
                                            gp2[gpi % 2], inc="dS", k=16)
                        gpi += 1
                    cnt += 1
                if kind == "hbc":
                    ci = j - 32
                    cwsp = CW0 if jj % 2 == 0 else CW1
                    cbsp = CB0 if jj % 2 == 0 else CB1
                    if hi >= 2:
                        w.wait("G", "V", cv_done[hi - 2])
                        w.wait("G", "A", sl_done[hi - 2])
                    w.op("G", "dma_start", cwsp, conv_w[ci * 128:(ci + 1) * 128, :],
                         inc="dG", k=16)
                    w.op("G", "dma_start", cbsp,
                         conv_b[ci * 128:(ci + 1) * 128].rearrange("(a b) -> a b", b=1),
                         inc="dG", k=16)
                    w.wait("V", "dG", w.c["dG"])
                    w.wait("V", "A", w.c["A"])
                    if hi >= 2:
                        w.wait("V", "dS", hb_done[hi - 2])
                    src_ = cvin[jj % 2]
                    ho = hout2[hi % 2]
                    w.op("V", "tensor_scalar", ho, src_[:, 0:2048], cwsp[:, 0:1],
                         0.0, ALU.mult, ALU.add)
                    for k in range(1, 4):
                        w.op("V", "scalar_tensor_tensor", ho, src_[:, k:2048 + k],
                             cwsp[:, k:k + 1], ho, ALU.mult, ALU.add,
                             inc=("V" if k == 3 else None))
                    cv_done[hi] = w.c["V"]
                    w.wait("A", "V", w.c["V"])
                    sl_done[hi] = w.op("A", "activation", ho, ho, AF.Silu, bias=cbsp,
                                       inc="A")
                    w.wait("S", "A", w.c["A"])
                    hb_done[hi] = w.op("S", "dma_start",
                                       s_hbc[ci * 128:(ci + 1) * 128, :], ho,
                                       inc="dS", k=16)
                    hi += 1
            if kind == "dt":
                dt_family()
            t_after[gi] = w.c["T"]

    def ssm():
        XT = AR2[:, 0:4096]
        GT = AR2[:, 4096:8192]
        BT = AR2[:, 8192:9216]
        BCF = AR2[:, 9216:11264]
        XDT = AR2[:, 11264:15360]
        XDD = AR2[:, 15360:19456]
        SILG = AR2[:, 19456:23552]
        GGB = AR2[:, 23552:27648]
        ONT = AR2[:, 27648:31744]
        LMK = AR1[:, 0:1024]
        MPR = AR1[:, 1024:2048]
        YTMP = AR1[:, 2048:2560]
        Y2B = AR1[:, 2560:3072]
        XDB = AR1[:, 4096:8192]
        SQS = fview(ARW, 0, 128, 4096)
        w.op("V", "memset", SP0, 0.0, inc="V")
        for c in range(NCH):
            Sold = SP0 if c % 2 == 0 else SP1
            Snew = SP1 if c % 2 == 0 else SP0
            w.wait("S", "V", w.c["V"])
            w.wait("S", "T", w.c["T"])
            w.wait("S", "A", w.c["A"])
            w.op("S", "dma_start_transpose", XT, s_hbc[0:4096, c * 128:(c + 1) * 128],
                 inc="dS", k=16)
            w.op("S", "dma_start_transpose", GT,
                 s_gate[0:4096, c * 128:(c + 1) * 128], inc="dS", k=16)
            w.op("S", "dma_start_transpose", BT,
                 s_hbc[4096:5120, c * 128:(c + 1) * 128], inc="dS", k=16)
            w.op("S", "dma_start", BCF.rearrange("p (a c) -> p a c", a=16),
                 s_hbc[4096:6144, c * 128:(c + 1) * 128]
                 .rearrange("(a p) c -> p a c", a=16), inc="dS", k=16)
            w.wait("V", "dS", w.c["dS"])
            xtv = XT.rearrange("p (h q) -> p h q", h=64)
            w.op("V", "tensor_tensor", XDT.rearrange("p (h q) -> p h q", h=64), xtv,
                 DTT[:, c * 64:(c + 1) * 64].unsqueeze(2).broadcast_to([128, 64, 64]),
                 ALU.mult)
            w.op("V", "tensor_tensor", XDD.rearrange("p (h q) -> p h q", h=64),
                 XDT.rearrange("p (h q) -> p h q", h=64),
                 DST[:, c * 64:(c + 1) * 64].unsqueeze(2).broadcast_to([128, 64, 64]),
                 ALU.mult)
            w.op("V", "tensor_tensor", XDB.rearrange("p (h q) -> p h q", h=64), xtv,
                 DMAT.unsqueeze(2).broadcast_to([128, 64, 64]), ALU.mult, inc="V")
            w.wait("A", "dS", w.c["dS"])
            w.wait("A", "V", w.c["V"])
            w.op("A", "activation", SILG, GT, AF.Silu, inc="A")
            for g in range(8):
                w.wait("T", "V", w.c["V"])
                w.op("T", "matmul", pb[0][:, 0:128], BCF[:, g * 128:(g + 1) * 128],
                     BCF[:, 1024 + g * 128: 1024 + (g + 1) * 128], start=True,
                     stop=True)
                for i in range(8):
                    bank = pb[1 + i // 4]
                    w.op("T", "matmul", bank[:, (i % 4) * 128:(i % 4 + 1) * 128],
                         IDF[0:64, 8 * g + i: 8 * g + i + 1].broadcast_to([64, 128]),
                         RTEN[:, c * 128:(c + 1) * 128],
                         start=True, stop=True, inc=("T" if i == 7 else None))
                w.wait("V", "T", w.c["T"])
                for half in range(2):
                    w.op("V", "tensor_tensor_scan",
                         LMK[:, half * 512:(half + 1) * 512],
                         pb[1 + half][:, 0:512], IR4, 0.0,
                         ALU.mult, ALU.add)
                for half in range(2):
                    w.op("V", "tensor_tensor",
                         MPR[:, half * 512:(half + 1) * 512]
                         .rearrange("p (a c) -> p a c", a=4),
                         pb[0][:, 0:128].unsqueeze(1).broadcast_to([128, 4, 128]),
                         LMK[:, half * 512:(half + 1) * 512]
                         .rearrange("p (a c) -> p a c", a=4), ALU.mult,
                         inc=("V" if half == 1 else None))
                w.wait("T", "V", w.c["V"])
                for i in range(8):
                    w.op("T", "matmul", pb[3][:, i * 64:(i + 1) * 64],
                         MPR[:, i * 128:(i + 1) * 128],
                         XDT[:, (8 * g + i) * 64:(8 * g + i + 1) * 64],
                         start=True, stop=True)
                w.op("T", "matmul", pb[4][:, 0:512], BT[:, g * 128:(g + 1) * 128],
                     XDD[:, g * 512:(g + 1) * 512], start=True, stop=True)
                w.op("T", "matmul", pb[5][:, 0:512],
                     BCF[:, 1024 + g * 128:1024 + (g + 1) * 128],
                     Sold[:, g * 512:(g + 1) * 512], start=True, stop=True, inc="T")
                w.wait("V", "T", w.c["T"])
                w.op("V", "tensor_tensor",
                     Snew[:, g * 512:(g + 1) * 512].rearrange("p (a c) -> p a c", a=8),
                     Sold[:, g * 512:(g + 1) * 512].rearrange("p (a c) -> p a c", a=8),
                     CDB[:, c * 64 + g * 8: c * 64 + (g + 1) * 8].unsqueeze(2)
                     .broadcast_to([128, 8, 64]), ALU.mult)
                w.op("V", "tensor_tensor", Snew[:, g * 512:(g + 1) * 512],
                     Snew[:, g * 512:(g + 1) * 512], pb[4][:, 0:512], ALU.add)
                w.op("V", "tensor_tensor", YTMP.rearrange("p (a c) -> p a c", a=8),
                     pb[5][:, 0:512].rearrange("p (a c) -> p a c", a=8),
                     SDT[:, c * 64 + g * 8: c * 64 + (g + 1) * 8].unsqueeze(2)
                     .broadcast_to([128, 8, 64]), ALU.mult)
                w.op("V", "scalar_tensor_tensor", Y2B, pb[3][:, 0:512], 1.0, YTMP,
                     ALU.mult, ALU.add)
                w.op("V", "tensor_tensor", Y2B, Y2B, XDB[:, g * 512:(g + 1) * 512],
                     ALU.add)
                w.op("V", "tensor_tensor", GGB[:, g * 512:(g + 1) * 512], Y2B,
                     SILG[:, g * 512:(g + 1) * 512], ALU.mult, inc="V")
            w.wait("A", "V", w.c["V"])
            for g in range(8):
                w.op("A", "activation", SQS[:, g * 512:(g + 1) * 512],
                     GGB[:, g * 512:(g + 1) * 512], AF.Square,
                     accum_out=VARC[:, g:g + 1], inc=("A" if g == 7 else None))
            w.wait("V", "A", w.c["A"])
            w.op("V", "tensor_scalar", VARC, VARC, 1.0 / 512.0, 0.0, ALU.mult,
                 ALU.add, inc="V")
            w.wait("A", "V", w.c["V"])
            w.nop("A")
            w.op("A", "activation", VARC, VARC, AF.Sqrt, bias=col(44), inc="A")
            w.wait("V", "A", w.c["A"])
            w.nop("V")
            w.op("V", "reciprocal", RSTC, VARC)
            w.nop("V")
            w.wait("V", "dS", w.c["dS"])
            w.op("V", "tensor_tensor", ONT.rearrange("p (a c) -> p a c", a=8),
                 GGB.rearrange("p (a c) -> p a c", a=8),
                 RSTC.unsqueeze(2).broadcast_to([128, 8, 512]), ALU.mult, inc="V")
            w.wait("S", "V", w.c["V"])
            w.op("S", "dma_start", s_on[c * 128:(c + 1) * 128, :], ONT, inc="dS",
                 k=16)

    def wout_phase():
        w.op("S", "dma_start", fview(ARW, 0, 32, 128),
             norm_w.rearrange("(a b) -> a b", b=128), inc="dS", k=16)
        w.wait("T", "dS", w.c["dS"])
        w.op("T", "transpose", pb[0][:, 0:32], fview(ARW, 0, 32, 128),
             IDF[0:32, 0:32], inc="T")
        w.wait("V", "T", w.c["T"])
        w.op("V", "tensor_copy", NWC, pb[0][:, 0:32])
        w.nop("V")
        w.op("V", "engine_nop", inc="V")

        onfm = AR1[:, 0:16384]
        onv = onfm.rearrange("p (a c) -> p a c", a=32)
        wc = [AR2[:, 0:8192], AR2[:, 8192:16384]]
        xsl = [fview(ARW, 4096 + i * 1024, 128, 256) for i in range(4)]
        opiece = [fview(ARW, 8192 + i * 1024, 128, 256) for i in range(4)]
        t_on = {}
        t_wc = {}
        v_evac = {}
        o_dma = {}
        v_scale = {}
        for tq in range(4):
            if tq >= 1:
                w.wait("S", "T", t_on[tq - 1])
                w.wait("S", "V", v_scale[tq - 1])
            for i in range(32):
                w.op("S", "dma_start_transpose", onv[:, i, :],
                     s_on[tq * 512:(tq + 1) * 512, i * 128:(i + 1) * 128],
                     inc="dS", k=16)
            w.wait("V", "dS", w.c["dS"])
            for i in range(32):
                w.op("V", "tensor_scalar", onv[:, i, :], onv[:, i, :],
                     NWC[:, i:i + 1], 0.0, ALU.mult, ALU.add,
                     inc=("V" if i == 31 else None))
            v_scale[tq] = w.c["V"]
            for dg in range(8):
                it = tq * 8 + dg
                sl = dg % 2
                if it >= 2:
                    w.wait("S", "T", t_wc[it - 2])
                for half in range(2):
                    w.op("S", "dma_start",
                         wc[sl].rearrange("p (a c) -> p a c", a=32)
                         [:, half * 16:(half + 1) * 16, :],
                         s_woutT[:, dg * 256:(dg + 1) * 256]
                         .rearrange("(k p) c -> p k c", k=32)
                         [:, half * 16:(half + 1) * 16, :], inc="dW", k=16)
                w.wait("T", "dW", w.c["dW"])
                w.wait("T", "V", v_scale[tq])
                wv = wc[sl].rearrange("p (a c) -> p a c", a=32)
                for ts in range(4):
                    pidx = (tq, dg, ts)
                    tsg = tq * 4 + ts
                    bank = pb[ts % 4]
                    prevv = (tq, dg - 1, ts) if dg >= 1 else ((tq - 1, 7, ts) if tq else None)
                    if prevv in o_dma:
                        w.wait("S", "dO", o_dma[prevv])
                    w.op("S", "dma_start", xsl[ts],
                         x_in[tsg * 128:(tsg + 1) * 128, dg * 256:(dg + 1) * 256],
                         inc="dS", k=16)
                    x_ld = w.c["dS"]
                    if prevv in v_evac:
                        w.wait("T", "V", v_evac[prevv])
                    for k in range(32):
                        w.op("T", "matmul", bank[:, 0:256],
                             onv[:, k, ts * 128:(ts + 1) * 128], wv[:, k, :],
                             start=(k == 0), stop=(k == 31),
                             inc=("T" if k == 31 else None))
                    w.wait("V", "T", w.c["T"])
                    w.wait("V", "dS", x_ld)
                    if prevv in o_dma:
                        w.wait("V", "dO", o_dma[prevv])
                    v_evac[pidx] = w.op("V", "tensor_tensor", opiece[ts],
                                        bank[:, 0:256], xsl[ts], ALU.add, inc="V")
                    w.wait("S", "V", v_evac[pidx])
                    o_dma[pidx] = w.op("S", "dma_start",
                                       out_hbm[tsg * 128:(tsg + 1) * 128,
                                               dg * 256:(dg + 1) * 256],
                                       opiece[ts], inc="dO", k=16)
                t_wc[it] = w.c["T"]
            t_on[tq] = w.c["T"]

    def down_phase():
        rres = AR1[:, 0:32768]
        rv = rres.rearrange("p (a c) -> p a c", a=64)
        wc = [AR2[:, 0:16384], AR2[:, 16384:32768]]
        opiece = [fview(ARW, i * 1024, 128, 256) for i in range(4)]
        t_r = {}
        t_wc = {}
        v_evac = {}
        o_dma = {}
        for tq in range(4):
            if tq >= 1:
                w.wait("G", "T", t_r[tq - 1])
            for i in range(64):
                w.op("G", "dma_start", rv[:, i, :],
                     s_r2[i * 128:(i + 1) * 128, tq * 512:(tq + 1) * 512],
                     inc="dS", k=16)
            rq_loaded = w.c["dS"]
            for dg in range(8):
                it = tq * 8 + dg
                sl = dg % 2
                if it >= 2:
                    w.wait("S", "T", t_wc[it - 2])
                for half in range(4):
                    w.op("S", "dma_start",
                         wc[sl].rearrange("p (a c) -> p a c", a=64)
                         [:, half * 16:(half + 1) * 16, :],
                         s_wdT[:, dg * 256:(dg + 1) * 256]
                         .rearrange("(k p) c -> p k c", k=64)
                         [:, half * 16:(half + 1) * 16, :], inc="dW", k=16)
                w.wait("T", "dW", w.c["dW"])
                w.wait("T", "dS", rq_loaded)
                wv = wc[sl].rearrange("p (a c) -> p a c", a=64)
                for ts in range(4):
                    pidx = (tq, dg, ts)
                    bank = pb[ts % 4]
                    prev = (tq, dg - 1, ts) if dg >= 1 else ((tq - 1, 7, ts) if tq else None)
                    if prev in v_evac:
                        w.wait("T", "V", v_evac[prev])
                    for k in range(64):
                        w.op("T", "matmul", bank[:, 0:256],
                             rv[:, k, ts * 128:(ts + 1) * 128], wv[:, k, :],
                             start=(k == 0), stop=(k == 63),
                             inc=("T" if k == 63 else None))
                    w.wait("V", "T", w.c["T"])
                    prev2 = (tq, dg - 1, ts) if dg >= 1 else ((tq - 1, 7, ts) if tq else None)
                    if prev2 in o_dma:
                        w.wait("V", "dO", o_dma[prev2])
                    tsg = tq * 4 + ts
                    v_evac[pidx] = w.op("V", "tensor_scalar", opiece[ts], bank[:, 0:256],
                                        M2Q[:, tsg:tsg + 1], 0.0, ALU.mult, ALU.add,
                                        inc="V")
                    w.wait("G", "V", v_evac[pidx])
                    o_dma[pidx] = w.op("G", "dma_start",
                                       out_hbm[tsg * 128:(tsg + 1) * 128,
                                               dg * 256:(dg + 1) * 256],
                                       opiece[ts], accum_op=ALU.add, inc="dO", k=16)
                t_wc[it] = w.c["T"]
            t_r[tq] = w.c["T"]

    load_consts()
    w.barrier()
    weight_prep()
    w.barrier()
    xnT = layernorm()
    w.barrier()
    xq = build_xq(xnT)
    w.barrier()
    ffn_f1(xq)
    w.barrier()
    ffn_f2a()
    w.barrier()
    mixer_m1(xnT)
    w.barrier()
    ssm()
    w.barrier()
    wout_phase()
    w.barrier()
    down_phase()
    w.barrier()

    if DEBUG:
        for nm, src in [("d_za", s_za), ("d_xq", s_xq), ("d_gate", s_gate),
                        ("d_hbc", s_hbc), ("d_on", s_on), ("d_ya", s_ya),
                        ("d_r2", s_r2)]:
            w.op("G", "dma_start", dbg[nm], src, inc="dO", k=16)
        w.op("G", "dma_start", dbg["d_smf"], SMF, inc="dO", k=16)

    w.wait("G", "dO", w.c["dO"])
    w.wait("G", "dS", w.c["dS"])
    w.wait("G", "dG", w.c["dG"])
    w.wait("G", "dW", w.c["dW"])

    with nc.Block() as block:
        @block.tensor
        def _(t):
            for f in w.st["T"]:
                f(t)

        @block.vector
        def _(v):
            for f in w.st["V"]:
                f(v)

        @block.scalar
        def _(a):
            for f in w.st["A"]:
                f(a)

        @block.sync
        def _(s):
            for f in w.st["S"]:
                f(s)

        @block.gpsimd
        def _(g):
            for f in w.st["G"]:
                f(g)

    es.close()
    return nc


def _consts():
    h128 = _hadamard(128)
    hb16 = np.kron(_hadamard(16), np.eye(8, dtype=np.float32))
    hb64 = np.kron(_hadamard(64), np.eye(2, dtype=np.float32))
    i128 = np.eye(128, dtype=np.float32)
    ones_blk = np.zeros((128, 128), np.float32)
    ones_blk[0, :] = 1.0
    cb = np.concatenate([h128, hb16, hb64, i128, ones_blk,
                         np.tile(i128, (1, 4))], axis=1)
    return {
        "cb": cb.astype(ml_dtypes.bfloat16),
        "ones_f": np.ones((1, 128), np.float32),
        "ones_c": np.ones((128, 1), np.float32),
        "id_f": i128,
    }


def kernel(**inp):
    f32 = lambda a: np.ascontiguousarray(np.asarray(a, np.float32))
    x = f32(inp["L0_attn_hidden_states"])
    shared = {
        "w_in": f32(inp["L0_attn_W_in"]),
        "conv_w": f32(inp["L0_attn_conv_w"]),
        "conv_b": f32(inp["L0_attn_conv_b"]),
        "dt_bias": f32(inp["L0_attn_dt_bias"]),
        "a_log": f32(inp["L0_attn_A_log"]),
        "dvec": f32(inp["L0_attn_D"]),
        "norm_w": f32(inp["L0_attn_norm_weight"]),
        "w_out": f32(inp["L0_attn_W_out"]),
        "w_g": f32(inp["L0_ffn_w_gate"]),
        "w_u": f32(inp["L0_ffn_w_up"]),
        "w_d": f32(inp["L0_ffn_w_down"]),
    }
    shared.update(_consts())
    nc = build_program()
    in_maps = [dict(shared, x=x[b]) for b in range(8)]
    res = run_bass_kernel_spmd(nc, in_maps, core_ids=list(range(8)))
    out = np.stack([np.asarray(res.results[b]["out"]) for b in range(8)], axis=0)
    return out.astype(np.float32)


if __name__ == "__main__":
    nc = build_program()
    print("instructions:", nc.n_instructions())


# version 17
# speedup vs baseline: 1.1709x; 1.0467x over previous
import numpy as np
import ml_dtypes
import concourse.bass as bass
import concourse.mybir as mybir
from concourse.bass_utils import run_bass_kernel_spmd
from contextlib import ExitStack

F32 = mybir.dt.float32
BF16 = mybir.dt.bfloat16
AF = mybir.ActivationFunctionType
ALU = mybir.AxisListType
ALU = mybir.AluOpType
AX = mybir.AxisListType

L = 2048
DM = 2048
HH, PP, NNS, GG = 64, 64, 128, 8
INTER = 4096
CONV = 6144
PROJ = 10304
IFF = 8192
CS = 128
NCH = L // CS
KC = 4
RND = 3.0 * 2.0 ** 22
EPS = 1e-5
DEBUG = False

ENGS = "TVASG"


def _hadamard(n):
    h = np.array([[1.0]], np.float32)
    while h.shape[0] < n:
        h = np.block([[h, h], [h, -h]])
    return h


class Wk:
    def __init__(self, nc, semh):
        self.nc = nc
        self.semh = semh
        self.c = {k: 0 for k in semh}
        self.st = {e: [] for e in ENGS}

    def wait(self, eng, sem, val=None):
        v = self.c[sem] if val is None else val
        if v <= 0:
            return
        h = self.semh[sem]
        self.st[eng].append(lambda e, h=h, v=v: e.wait_ge(h, v))

    def op(self, eng, meth, *args, inc=None, k=1, **kw):
        h = None
        kk = 0
        if inc is not None:
            self.c[inc] += k
            h, kk = self.semh[inc], k

        import traceback
        site = traceback.extract_stack(limit=3)[0]

        def f(e, m=meth, a=args, kw=kw, h=h, kk=kk, site=site):
            try:
                ins = getattr(e, m)(*a, **kw)
            except Exception:
                print("FAILING OP:", m, "emitted at", site.filename, site.lineno,
                      [getattr(x, "shape", None) for x in a])
                raise
            if h is not None:
                ins.then_inc(h, kk)

        self.st[eng].append(f)
        return self.c[inc] if inc is not None else None

    def nop(self, eng, cyc=1024):
        self.st[eng].append(lambda e, c=cyc: e.nop(cycle_cnt=c))

    def fence(self, eng, sem):
        self.c[sem] += 1
        h = self.semh[sem]
        self.st[eng].append(lambda e, h=h: e.nop(cycle_cnt=1024).then_inc(h, 1))
        return self.c[sem]

    def barrier(self):
        snap = dict(self.c)
        for e in ENGS:
            for s, v in snap.items():
                if v > 0:
                    self.wait(e, s, v)


def build_program():
    nc = bass.Bass("TRN2", target_bir_lowering=False)
    dt_ = nc.dram_tensor

    x_in = dt_("x", [L, DM], F32, kind="ExternalInput").ap()
    w_in = dt_("w_in", [PROJ, DM], F32, kind="ExternalInput").ap()
    conv_w = dt_("conv_w", [CONV, KC], F32, kind="ExternalInput").ap()
    conv_b = dt_("conv_b", [CONV], F32, kind="ExternalInput").ap()
    dt_bias = dt_("dt_bias", [HH], F32, kind="ExternalInput").ap()
    a_log = dt_("a_log", [HH], F32, kind="ExternalInput").ap()
    dvec = dt_("dvec", [HH], F32, kind="ExternalInput").ap()
    norm_w = dt_("norm_w", [INTER], F32, kind="ExternalInput").ap()
    w_out = dt_("w_out", [DM, INTER], F32, kind="ExternalInput").ap()
    w_g = dt_("w_g", [IFF, DM], F32, kind="ExternalInput").ap()
    w_u = dt_("w_u", [IFF, DM], F32, kind="ExternalInput").ap()
    w_d = dt_("w_d", [DM, IFF], F32, kind="ExternalInput").ap()
    cb_in = dt_("cb", [128, 1152], BF16, kind="ExternalInput").ap()
    onesf_in = dt_("ones_f", [1, 128], F32, kind="ExternalInput").ap()
    onesc_in = dt_("ones_c", [128, 1], F32, kind="ExternalInput").ap()
    idf_in = dt_("id_f", [128, 128], F32, kind="ExternalInput").ap()

    out_hbm = dt_("out", [L, DM], F32, kind="ExternalOutput").ap()

    s_winT = dt_("s_winT", [DM, PROJ], BF16, kind="Internal").ap()
    s_woutT = dt_("s_woutT", [INTER, DM], BF16, kind="Internal").ap()
    s_wgT = dt_("s_wgT", [DM, IFF], BF16, kind="Internal").ap()
    s_wuT = dt_("s_wuT", [DM, IFF], BF16, kind="Internal").ap()
    s_wdT = dt_("s_wdT", [IFF, DM], BF16, kind="Internal").ap()
    s_za = dt_("s_za", [DM, L], BF16, kind="Internal").ap()
    s_xq = dt_("s_xq", [DM, L], BF16, kind="Internal").ap()
    s_gate = dt_("s_gate", [INTER, L], BF16, kind="Internal").ap()
    s_hbc = dt_("s_hbc", [CONV, L], BF16, kind="Internal").ap()
    s_on = dt_("s_on", [L, INTER], BF16, kind="Internal").ap()
    s_ya = dt_("s_ya", [IFF, L], BF16, kind="Internal").ap()
    s_r2 = dt_("s_r2", [IFF, L], BF16, kind="Internal").ap()

    dbg = {}
    if DEBUG:
        for nm, shp in [("d_za", [DM, L]), ("d_xq", [DM, L]), ("d_gate", [INTER, L]),
                        ("d_hbc", [CONV, L]), ("d_on", [L, INTER]), ("d_ya", [IFF, L]),
                        ("d_r2", [IFF, L])]:
            dbg[nm] = dt_(nm, shp, BF16, kind="ExternalOutput").ap()
        dbg["d_smf"] = dt_("d_smf", [128, 2944], F32, kind="ExternalOutput").ap()

    es = ExitStack()

    def sbuf(name, shape, dty):
        return es.enter_context(nc.sbuf_tensor(name, shape, dty)).ap()

    AR1 = sbuf("AR1", [128, 32768], BF16)
    AR2 = sbuf("AR2", [128, 32768], BF16)
    ARW = sbuf("ARW", [128, 19968], BF16)
    SST = sbuf("SST", [128, 8192], BF16)
    SMB = sbuf("SMB", [128, 4224], BF16)
    SMF = sbuf("SMF", [128, 2944], F32)
    CBT = sbuf("CBT", [128, 1152], BF16)
    IDF = sbuf("IDF", [128, 128], F32)
    ONF = sbuf("ONF", [1, 128], F32)
    ONC = sbuf("ONC", [128, 1], F32)

    pb = [es.enter_context(nc.psum_tensor("pb%d" % i, [128, 512], F32)).ap()
          for i in range(7)]
    pbt = es.enter_context(nc.psum_tensor("pbt", [128, 1024], BF16)).ap()

    sems = {}
    for s in ("dS", "dG", "dW", "dO", "T", "V", "A"):
        sems[s] = es.enter_context(nc.semaphore(s))

    w = Wk(nc, sems)

    def fview(arena, byte_off, parts, cols):
        assert byte_off % 4 == 0
        return arena[0:parts, byte_off // 2: byte_off // 2 + 2 * cols].bitcast(F32)

    H128 = CBT[:, 0:128]
    HB16 = CBT[:, 128:256]
    HB64 = CBT[:, 256:384]
    IDB = CBT[:, 384:512]
    ONB = CBT[0:1, 512:640]
    IR4 = CBT[:, 640:1152]

    smf = SMF
    DTA = smf[0:64, 0:2048]
    RTEN = smf[0:64, 0:2048]
    ABR = smf[64:128, 0:2048]
    ACS = smf[0:64, 2048:2176]
    CW0 = smf[:, 2176:2180]
    CW1 = smf[:, 2180:2184]
    CB0 = smf[:, 2184:2185]
    CB1 = smf[:, 2185:2186]
    COLS = smf[:, 2186:2250]
    M1C = smf[:, 2250:2266]
    M2C = smf[:, 2266:2282]
    M2Q = smf[:, 2282:2298]
    C1R = smf[0:16, 2304:2432]
    NWC = smf[:, 2432:2464]
    VARC = smf[:, 2464:2472]
    RSTC = smf[:, 2472:2480]
    SSCAL = smf[0:1, 2480:2496]
    SCL = smf[:, 2496:2520]
    C14 = smf[:, 2520:2524]
    CI4 = smf[:, 2524:2528]
    MXT = smf[:, 2528:2656]
    C2ROW = smf[0:1, 2656:2784]
    CRW1 = smf[0:1, 0:512]
    CRW2 = smf[0:1, 512:1024]

    def col(i):
        return COLS[:, i:i + 1]

    DTT = SMB[:, 0:1024]
    DST = SMB[:, 1024:2048]
    SDT = SMB[:, 2048:3072]
    CDB = SMB[:, 3072:4096]
    DMAT = SMB[:, 4096:4160]
    CDROW = SMB[0:1, 4160:4224]

    SP0 = SST[:, 0:4096]
    SP1 = SST[:, 4096:8192]

    def load_consts():
        w.op("V", "memset", col(44), 1e-5)
        w.op("V", "memset", col(45), -1.0, inc="V")
        w.op("S", "dma_start", CBT, cb_in, inc="dS", k=16)
        w.op("S", "dma_start", IDF, idf_in, inc="dS", k=16)
        w.op("S", "dma_start", ONF, onesf_in, inc="dS", k=16)
        w.op("S", "dma_start", ONC, onesc_in, inc="dS", k=16)

    def weight_prep():
        slot = [fview(ARW, i * 8192, 128, 2048) for i in range(3)]
        tslot = [ARW[:, 12288 + i * 2048: 12288 + (i + 1) * 2048] for i in range(2)]
        asm = [AR1[:, i * 2048:(i + 1) * 2048] for i in range(2)]
        junk = fview(AR2, 0, 128, 2048)
        run, spot = col(0), col(1)

        w.op("V", "memset", run, 0.0, inc="V")
        a_read = {}
        gi = 0
        for si, wm in enumerate((w_g, w_u, w_d)):
            wf = wm if wm.shape[1] == 2048 else wm.rearrange(
                "a (b c) -> (a b) c", c=2048)
            for t in range(64):
                sl = gi % 3
                if gi >= 3:
                    w.wait("S", "A", a_read[gi - 3])
                w.op("S", "dma_start", slot[sl], wf[t * 128:(t + 1) * 128, :],
                     inc="dS", k=16)
                w.wait("A", "dS", w.c["dS"])
                a_read[gi] = w.op("A", "activation", junk, slot[sl], AF.Abs,
                                  accum_out=spot, inc="A")
                w.wait("V", "A", a_read[gi])
                w.nop("V")
                w.op("V", "tensor_tensor", run, run, spot, ALU.add, inc="V")
                gi += 1
            w.wait("T", "V", w.c["V"])
            w.op("T", "matmul", pb[0][0:1, 0:1], run, ONC, start=True, stop=True,
                 inc="T")
            w.wait("V", "T", w.c["T"])
            sc = SSCAL[:, si:si + 1]
            w.op("V", "tensor_scalar", sc, pb[0][0:1, 0:1], 1.0 / (IFF * DM), 1e-5,
                 ALU.mult, ALU.max)
            w.nop("V")
            w.op("V", "reciprocal", SSCAL[:, 8:9], sc)
            w.nop("V")
            w.op("V", "memset", run, 0.0, inc="V")
            w.wait("T", "V", w.c["V"])
            w.nop("T")
            w.op("T", "matmul", pb[1][:, 0:1], ONF, SSCAL[:, 8:9], start=True,
                 stop=True, inc="T")
            w.wait("V", "T", w.c["T"])
            w.op("V", "tensor_copy", col(2 + si), pb[1][:, 0:1], inc="V")

        for si, tgt in ((0, SCL[:, 0:1]), (1, SCL[:, 1:2]), (2, SCL[:, 4:5])):
            w.wait("T", "V", w.c["V"])
            w.op("T", "matmul", pb[1][:, 1:2], ONF, SSCAL[:, si:si + 1], start=True,
                 stop=True, inc="T")
            w.wait("V", "T", w.c["T"])
            w.op("V", "tensor_copy", tgt, pb[1][:, 1:2], inc="V")
        w.nop("V")
        w.op("V", "tensor_tensor", SCL[:, 2:3], SCL[:, 1:2], SCL[:, 4:5], ALU.mult)
        w.nop("V")
        w.op("V", "tensor_scalar", SCL[:, 2:3], SCL[:, 2:3],
             float(1.0 / (127.0 * np.sqrt(8192.0))), 0.0, ALU.mult, ALU.add)
        w.op("V", "reciprocal", SCL[:, 3:4], SCL[:, 1:2])
        w.nop("V")
        w.op("V", "tensor_scalar", SCL[:, 3:4], SCL[:, 3:4],
             float(np.sqrt(8192.0) * 1e-5), 0.0, ALU.mult, ALU.add, inc="V")

        jobs = []
        for si, wm, dstT in ((0, w_g, s_wgT), (1, w_u, s_wuT)):
            for t in range(64):
                jobs.append(("tern", si, wm[t * 128:(t + 1) * 128, :], 128,
                             dstT[:, t * 128:(t + 1) * 128]
                             .rearrange("(k p) c -> p k c", k=16)))
        for q in range(4):
            for j in range(16):
                jobs.append(("tern", 2,
                             w_d[j * 128:(j + 1) * 128, q * 2048:(q + 1) * 2048], 128,
                             s_wdT[q * 2048:(q + 1) * 2048, j * 128:(j + 1) * 128]
                             .rearrange("(k p) c -> p k c", k=16)))
        for j in range(80):
            jobs.append(("cast", None, w_in[j * 128:(j + 1) * 128, :], 128,
                         s_winT[:, j * 128:(j + 1) * 128]
                         .rearrange("(k p) c -> p k c", k=16)))
        jobs.append(("cast", None, w_in[10240:10304, :], 64,
                     s_winT[:, 10240:10304].rearrange("(k p) c -> p k c", k=16)))
        for hf in range(2):
            for j in range(16):
                jobs.append(("cast", None,
                             w_out[j * 128:(j + 1) * 128, hf * 2048:(hf + 1) * 2048],
                             128,
                             s_woutT[hf * 2048:(hf + 1) * 2048, j * 128:(j + 1) * 128]
                             .rearrange("(k p) c -> p k c", k=16)))

        v_cons = {}
        v_ev = {}
        t_tp = {}
        g_wr = {}
        for t, (kind, si, srcap, rows, dstap) in enumerate(jobs):
            sl = t % 3
            ts = t % 2
            if t >= 3:
                w.wait("S", "V", v_cons[t - 3])
            w.op("S", "dma_start", slot[sl][0:rows, :], srcap, inc="dS", k=16)
            ld = w.c["dS"]
            w.wait("V", "dS", ld)
            if t >= 2:
                w.wait("V", "T", t_tp[t - 2])
            if t == 0:
                w.nop("V")
            sv = slot[sl][0:rows, :]
            tv = tslot[ts][0:rows, :]
            if kind == "tern":
                invc = col(2 + si)[0:rows, :]
                v_cons[t] = w.op("V", "tensor_scalar", junk[0:rows, :], sv, invc,
                                 -1.0, ALU.mult, ALU.max, inc="V")
                w.op("V", "tensor_scalar", junk[0:rows, :], junk[0:rows, :], 1.0,
                     RND, ALU.min, ALU.add)
                w.op("V", "tensor_scalar", tv, junk[0:rows, :], -RND, 0.0, ALU.add,
                     ALU.add, inc="V")
            else:
                v_cons[t] = w.op("V", "tensor_copy", tv, sv, inc="V")
            vt = w.c["V"]
            idn = IDB if rows == 128 else IDB[0:64, 0:64]
            pw = 128 if rows == 128 else 64
            for r in range(2):
                w.wait("T", "V", vt if r == 0 else w.c["V"])
                if t >= 1 or r >= 1:
                    w.wait("T", "V", w.c["V"])
                for k in range(8):
                    w.op("T", "transpose", pbt[:, k * pw:(k + 1) * pw],
                         tv[:, (r * 8 + k) * 128:(r * 8 + k + 1) * 128], idn,
                         inc=("T" if k == 7 else None))
                w.wait("V", "T", w.c["T"])
                if t >= 2:
                    w.wait("V", "dG", g_wr[t - 2])
                w.op("V", "tensor_copy",
                     asm[ts][:, r * 8 * pw:(r + 1) * 8 * pw], pbt[:, 0:8 * pw],
                     inc="V")
            t_tp[t] = w.c["T"]
            v_ev[t] = w.c["V"]
            w.wait("G", "V", v_ev[t])
            g_wr[t] = w.op("G", "dma_start", dstap,
                           asm[ts][:, 0:16 * pw].rearrange("p (k c) -> p k c", k=16),
                           inc="dG", k=16)

    def layernorm():
        xslot = [fview(AR2, 0, 128, 2048), fview(AR2, 8192, 128, 2048)]
        xnn = AR2[:, 8192:10240]
        sqs = fview(ARW, 0, 128, 2048)
        xnT = AR1[:, 0:32768]
        xnTv = xnT.rearrange("p (a c) -> p a c", a=16)
        for tt in range(16):
            sl = tt % 2
            w.wait("S", "V", w.c["V"])
            w.wait("S", "A", w.c["A"])
            w.op("S", "dma_start", xslot[sl], x_in[tt * 128:(tt + 1) * 128, :],
                 inc="dS", k=16)
            sumc, ssq, mc, vc, rc, nb = (col(8 + i) for i in range(6))
            w.wait("V", "dS", w.c["dS"])
            w.op("V", "reduce_sum", sumc, xslot[sl], AX.X)
            w.nop("V")
            w.op("V", "engine_nop", inc="V")
            w.wait("A", "dS", w.c["dS"])
            w.op("A", "activation", sqs, xslot[sl], AF.Square, accum_out=ssq,
                 inc="A")
            w.wait("V", "A", w.c["A"])
            w.nop("V")
            w.op("V", "tensor_scalar", mc, sumc, 1.0 / DM, 0.0, ALU.mult, ALU.add)
            w.op("V", "tensor_scalar", vc, ssq, 1.0 / DM, 0.0, ALU.mult, ALU.add)
            w.nop("V")
            w.op("V", "tensor_tensor", nb, mc, mc, ALU.mult)
            w.nop("V")
            w.op("V", "tensor_tensor", vc, vc, nb, ALU.subtract, inc="V")
            w.wait("A", "V", w.c["V"])
            w.nop("A")
            w.op("A", "activation", vc, vc, AF.Sqrt, bias=col(44), inc="A")
            w.wait("V", "A", w.c["A"])
            w.nop("V")
            w.op("V", "reciprocal", rc, vc)
            w.nop("V")
            w.op("V", "tensor_scalar", xnn, xslot[sl], mc, rc, ALU.subtract,
                 ALU.mult, inc="V")
            w.wait("T", "V", w.c["V"])
            for q in range(4):
                w.wait("T", "V", w.c["V"])
                for j in range(4):
                    d = q * 4 + j
                    w.op("T", "transpose", pbt[:, j * 128:(j + 1) * 128],
                         xnn[:, d * 128:(d + 1) * 128], IDB,
                         inc=("T" if j == 3 else None))
                w.wait("V", "T", w.c["T"])
                w.op("V", "tensor_copy",
                     xnTv[:, q * 4:(q + 1) * 4, tt * 128:(tt + 1) * 128],
                     pbt[:, 0:512].rearrange("p (a c) -> p a c", a=4), inc="V")
        return xnT

    def build_xq(xnT):
        zap = ARW[:, 0:2048]
        z1s = fview(ARW, 6144, 128, 8192)
        mx = fview(ARW, 4096, 128, 512)
        xnv = xnT.rearrange("p (a c) -> p a c", a=16)
        for d in range(16):
            w.wait("V", "dS", w.c["dS"])
            for q in range(4):
                bank = pb[q % 2]
                w.wait("T", "V", w.c["V"])
                w.op("T", "matmul", bank[:, 0:512], H128,
                     xnv[:, d, q * 512:(q + 1) * 512], start=True, stop=True, inc="T")
                w.wait("V", "T", w.c["T"])
                w.op("V", "tensor_copy", zap[:, q * 512:(q + 1) * 512], bank[:, 0:512],
                     inc="V")
            w.wait("G", "V", w.c["V"])
            w.op("G", "dma_start", s_za[d * 128:(d + 1) * 128, :], zap, inc="dS", k=16)
        zb = AR2[:, 0:8192]
        zbv = zb.rearrange("p (a c) -> p a c", a=16)
        xqs = AR2[:, 8192:16384]
        za_r = s_za.rearrange("(dh d3 dl) t -> (dh d3) dl t", dh=16, d3=8, dl=16)
        xq_w = s_xq.rearrange("(dh d3 dl) t -> (dh d3) dl t", dh=16, d3=8, dl=16)
        for tc in range(4):
            w.wait("S", "dS", w.c["dS"])
            w.wait("S", "T", w.c["T"])
            w.op("S", "dma_start", zbv, za_r[:, :, tc * 512:(tc + 1) * 512],
                 inc="dS", k=16)
            w.wait("T", "dS", w.c["dS"])
            for k in range(16):
                bank = pb[2 + (k % 2)]
                w.wait("T", "V", w.c["V"])
                w.op("T", "matmul", bank[:, 0:512], HB16, zb[:, k * 512:(k + 1) * 512],
                     start=True, stop=True, inc="T")
                w.wait("V", "T", w.c["T"])
                w.op("V", "tensor_copy", z1s[:, k * 512:(k + 1) * 512], bank[:, 0:512],
                     inc="V")
            w.op("V", "tensor_reduce", mx,
                 z1s.rearrange("p (a c) -> p c a", a=16), AX.X, ALU.max,
                 apply_absolute_value=True, inc="V")
            w.wait("T", "V", w.c["V"])
            for j in range(4):
                w.op("T", "transpose", pb[4][:, j * 128:(j + 1) * 128],
                     mx[:, j * 128:(j + 1) * 128], IDF, inc=("T" if j == 3 else None))
            w.wait("V", "T", w.c["T"])
            for j in range(4):
                w.op("V", "reduce_max", M1C[:, tc * 4 + j: tc * 4 + j + 1],
                     pb[4][:, j * 128:(j + 1) * 128], AX.X)
            mm4 = M1C[:, tc * 4: tc * 4 + 4]
            w.nop("V")
            w.op("V", "tensor_scalar", mm4, mm4, float(np.sqrt(2048.0) * 1e-5), 0.0,
                 ALU.max, ALU.add)
            w.nop("V")
            w.op("V", "reciprocal", C14, mm4)
            w.nop("V")
            w.op("V", "tensor_scalar", C14, C14, 127.0, 0.0, ALU.mult, ALU.add)
            w.op("V", "tensor_scalar", CI4, mm4, float(1.0 / (127.0 * np.sqrt(2048.0))),
                 0.0, ALU.mult, ALU.add, inc="V")
            w.wait("T", "V", w.c["V"])
            w.nop("T")
            for j in range(4):
                w.op("T", "transpose", pb[4][0:1, j * 128:(j + 1) * 128],
                     C14[:, j:j + 1], IDF)
                w.op("T", "transpose", pb[6][0:1, j * 128:(j + 1) * 128],
                     CI4[:, j:j + 1], IDF, inc=("T" if j == 3 else None))
            w.wait("V", "T", w.c["T"])
            w.op("V", "tensor_copy", CRW1, pb[4][0:1, 0:512])
            w.op("V", "tensor_copy", CRW2, pb[6][0:1, 0:512], inc="V")
            w.wait("T", "V", w.c["V"])
            w.nop("T")
            for j in range(4):
                w.op("T", "matmul", pb[5][:, j * 128:(j + 1) * 128], ONF,
                     CRW1[0:1, j * 128:(j + 1) * 128], start=True, stop=True)
                w.op("T", "matmul", pb[6][:, j * 128:(j + 1) * 128], ONF,
                     CRW2[0:1, j * 128:(j + 1) * 128], start=True, stop=True,
                     inc=("T" if j == 3 else None))
            w.wait("V", "T", w.c["T"])
            w.wait("V", "dS", w.c["dS"])
            z3 = z1s.rearrange("p (a c) -> p a c", a=16)
            xq3 = xqs.rearrange("p (a c) -> p a c", a=16)
            for j in range(4):
                w.op("V", "tensor_tensor", z3[:, :, j * 128:(j + 1) * 128],
                     z3[:, :, j * 128:(j + 1) * 128],
                     pb[5][:, j * 128:(j + 1) * 128].unsqueeze(1)
                     .broadcast_to([128, 16, 128]), ALU.mult)
            w.op("V", "tensor_scalar", z1s, z1s, RND, -RND, ALU.add, ALU.add)
            for j in range(4):
                w.op("V", "tensor_tensor", xq3[:, :, j * 128:(j + 1) * 128],
                     z3[:, :, j * 128:(j + 1) * 128],
                     pb[6][:, j * 128:(j + 1) * 128].unsqueeze(1)
                     .broadcast_to([128, 16, 128]), ALU.mult, inc=("V" if j == 3
                                                                   else None))
            w.wait("G", "V", w.c["V"])
            w.op("G", "dma_start", xq_w[:, :, tc * 512:(tc + 1) * 512],
                 xqs.rearrange("p (a c) -> p a c", a=16), inc="dS", k=16)
        xq = AR2[:, 0:32768]
        w.wait("S", "dS", w.c["dS"])
        xqv = xq.rearrange("p (a c) -> p a c", a=16)
        for k in range(16):
            w.op("S", "dma_start", xqv[:, k, :], s_xq[k * 128:(k + 1) * 128, :],
                 inc="dS", k=16)
        return xq

    def ffn_f1(xq):
        wslot = [ARW[:, 0:4096], ARW[:, 4096:8192]]
        silg = [ARW[:, 8192 + i * 512: 8192 + (i + 1) * 512] for i in range(2)]
        y1b = [ARW[:, 9216 + i * 512: 9216 + (i + 1) * 512] for i in range(2)]
        yab = [ARW[:, 10240 + i * 512: 10240 + (i + 1) * 512] for i in range(2)]
        xqv = xq.rearrange("p (a c) -> p a c", a=16)
        t_mm = {}
        t_sa = {}
        a_done = {}
        v_y1 = {}
        v_evac = {}
        s_store = {}
        t_wcons = {}

        def emit_stage_a(jt):
            pa = pb[4 + (jt % 2)]
            w.wait("T", "V", v_y1[jt])
            if jt >= 2:
                w.wait("T", "V", v_evac[jt - 2])
            t_sa[jt] = w.op("T", "matmul", pa[:, 0:512], H128, y1b[jt % 2],
                            start=True, stop=True, inc="T")
            w.wait("V", "T", t_sa[jt])
            if jt >= 2:
                w.wait("V", "dS", s_store[jt - 2])
            v_evac[jt] = w.op("V", "tensor_copy", yab[jt % 2], pa[:, 0:512], inc="V")
            w.wait("G", "V", v_evac[jt])
            o_, tc_ = jt // 4, jt % 4
            s_store[jt] = w.op("G", "dma_start",
                               s_ya[o_ * 128:(o_ + 1) * 128, tc_ * 512:(tc_ + 1) * 512],
                               yab[jt % 2], inc="dS", k=16)

        for o in range(64):
            sl = o % 2
            if o >= 2:
                w.wait("S", "T", t_wcons[o - 2])
            w.op("S", "dma_start",
                 wslot[sl][:, 0:2048].rearrange("p (k c) -> p k c", k=16),
                 s_wgT[:, o * 128:(o + 1) * 128]
                 .rearrange("(k p) c -> p k c", k=16), inc="dW", k=16)
            w.op("S", "dma_start",
                 wslot[sl][:, 2048:4096].rearrange("p (k c) -> p k c", k=16),
                 s_wuT[:, o * 128:(o + 1) * 128]
                 .rearrange("(k p) c -> p k c", k=16), inc="dW", k=16)
            wz = w.c["dW"]
            for tc in range(4):
                it = o * 4 + tc
                pg = pb[0 + (tc % 2)]
                pu = pb[2 + (tc % 2)]
                if tc == 0:
                    w.wait("T", "dW", wz)
                if it >= 2:
                    w.wait("T", "A", a_done[it - 2])
                    w.wait("T", "V", v_y1[it - 2])
                for k in range(16):
                    w.op("T", "matmul", pg[:, 0:512],
                         wslot[sl][:, k * 128:(k + 1) * 128],
                         xqv[:, k, tc * 512:(tc + 1) * 512], start=(k == 0),
                         stop=(k == 15))
                    w.op("T", "matmul", pu[:, 0:512],
                         wslot[sl][:, 2048 + k * 128: 2048 + (k + 1) * 128],
                         xqv[:, k, tc * 512:(tc + 1) * 512], start=(k == 0),
                         stop=(k == 15), inc=("T" if k == 15 else None))
                t_mm[it] = w.c["T"]
                w.wait("A", "T", t_mm[it])
                if it >= 2:
                    w.wait("A", "V", v_y1[it - 2])
                a_done[it] = w.op("A", "activation", silg[tc % 2], pg[:, 0:512],
                                  AF.Silu, scale=SCL[:, 0:1], inc="A")
                w.wait("V", "A", a_done[it])
                v_y1[it] = w.op("V", "tensor_tensor", y1b[tc % 2], silg[tc % 2],
                                pu[:, 0:512], ALU.mult, inc="V")
                if it >= 1:
                    emit_stage_a(it - 1)
            t_wcons[o] = w.c["T"]
        emit_stage_a(255)

    def ffn_f2a():
        zbs = [AR2[:, 0:8192], SST[:, 0:8192]]
        z2s = [fview(ARW, 0, 128, 8192), fview(AR2, 32768, 128, 8192)]
        r2b = AR2[:, 8192:16384]
        r2v = r2b.rearrange("p (a c) -> p a c", a=64)
        mxts = [MXT, smf[:, 2784:2912]]
        c2cs = [col(30), col(31)]
        c2rows = [C2ROW, smf[0:1, 0:128]]
        ya_r = s_ya.rearrange("(oh b6 ol) t -> (oh b6) ol t", oh=64, b6=2, ol=64)
        r2_w = s_r2.rearrange("(oh b6 ol) t -> (oh b6) ol t", oh=64, b6=2, ol=64)
        t_zb = {}
        q_done = {}
        qa_done = {}
        ld_done = {}

        def mm_part(tc):
            zb = zbs[tc % 2]
            zbv = zb.rearrange("p (a c) -> p a c", a=64)
            zs = z2s[tc % 2]
            if tc >= 2:
                w.wait("S", "T", t_zb[tc - 2])
            for part in range(4):
                w.op("S", "dma_start", zbv[:, part * 16:(part + 1) * 16, :],
                     ya_r[:, part * 16:(part + 1) * 16, tc * 128:(tc + 1) * 128],
                     inc="dS", k=16)
            ld_done[tc] = w.c["dS"]
            w.wait("T", "dS", ld_done[tc])
            if tc >= 2:
                w.wait("T", "V", q_done[tc - 2])
                w.wait("T", "A", qa_done.get(tc - 2, 0))
            for k in range(16):
                bank = pb[(tc % 2) * 2 + (k % 2)]
                if k >= 2:
                    w.wait("T", "V", w.c["V"])
                    w.wait("T", "A", w.c["A"])
                for j in range(4):
                    w.op("T", "matmul", bank[:, j * 128:(j + 1) * 128], HB64,
                         zb[:, (k * 4 + j) * 128:(k * 4 + j + 1) * 128],
                         start=True, stop=True, inc=("T" if j == 3 else None))
                ee = "V" if k % 2 == 0 else "A"
                w.wait(ee, "T", w.c["T"])
                if ee == "V":
                    w.op("V", "tensor_copy", zs[:, k * 512:(k + 1) * 512],
                         bank[:, 0:512], inc="V")
                else:
                    w.op("A", "copy", zs[:, k * 512:(k + 1) * 512], bank[:, 0:512],
                         inc="A")
            t_zb[tc] = w.c["T"]

        def quant_part(tc):
            zs = z2s[tc % 2]
            z3 = zs.rearrange("p (a c) -> p a c", a=64)
            mxt = mxts[tc % 2]
            c2c = c2cs[tc % 2]
            c2r = c2rows[tc % 2]
            tb = pb[4 + (tc % 2)]
            zt = zs.rearrange("p (a c) -> p c a", a=64)
            w.wait("V", "A", w.c["A"])
            w.op("V", "tensor_reduce", mxt, zt, AX.X, ALU.max,
                 apply_absolute_value=True, inc="V")
            w.wait("T", "V", w.c["V"])
            w.op("T", "transpose", tb[:, 0:128], mxt, IDF, inc="T")
            w.wait("V", "T", w.c["T"])
            mcol = M2C[:, tc:tc + 1]
            w.op("V", "reduce_max", mcol, tb[:, 0:128], AX.X)
            w.nop("V")
            w.op("V", "tensor_tensor", mcol, mcol, SCL[:, 3:4], ALU.max)
            w.nop("V")
            w.op("V", "reciprocal", c2c, mcol)
            w.nop("V")
            w.op("V", "tensor_scalar", c2c, c2c, 127.0, 0.0, ALU.mult, ALU.add,
                 inc="V")
            w.wait("T", "V", w.c["V"])
            w.nop("T")
            w.op("T", "transpose", tb[0:1, 0:128], c2c, IDF, inc="T")
            w.wait("V", "T", w.c["T"])
            w.op("V", "tensor_copy", c2r, tb[0:1, 0:128], inc="V")
            w.wait("T", "V", w.c["V"])
            w.nop("T")
            w.op("T", "matmul", tb[:, 128:256], ONF, c2r, start=True, stop=True,
                 inc="T")
            w.wait("V", "T", w.c["T"])
            w.wait("V", "dS", w.c["dS"])
            w.op("V", "tensor_tensor", z3, z3,
                 tb[:, 128:256].unsqueeze(1).broadcast_to([128, 64, 128]), ALU.mult)
            w.op("V", "tensor_scalar", r2v, z3, RND, -RND, ALU.add, ALU.add, inc="V")
            q_done[tc] = w.c["V"]
            qa_done[tc] = w.c["A"]
            w.wait("G", "V", q_done[tc])
            for part in range(4):
                w.op("G", "dma_start",
                     r2_w[:, part * 16:(part + 1) * 16, tc * 128:(tc + 1) * 128],
                     r2v[:, part * 16:(part + 1) * 16, :], inc="dS", k=16)

        for tc in range(16):
            mm_part(tc)
            if tc >= 1:
                quant_part(tc - 1)
        quant_part(15)
        w.nop("V")
        w.op("V", "tensor_tensor", M2Q, M2C, SCL[:, 2:3].broadcast_to([128, 16]),
             ALU.mult, inc="V")

    def dt_family():
        dtspb = ARW[:, 12294:14342][0:64, :]
        sdch = ARW[:, 14342:14470][0:64, :]
        dsch = ARW[:, 14470:14598][0:64, :]
        T1 = fview(ARW, 31744, 64, 2048)
        ABR2 = T1
        w.wait("A", "V", w.c["V"])
        w.op("A", "activation", T1, DTA, AF.Abs)
        w.op("A", "activation", T1, T1, AF.Exp, scale=col(45)[0:64, :])
        w.op("A", "activation", T1, T1, AF.Ln, bias=1.0, inc="A")
        w.wait("V", "A", w.c["A"])
        w.op("V", "tensor_scalar", DTA, DTA, 0.0, 0.0, ALU.max, ALU.add)
        w.op("V", "tensor_tensor", DTA, DTA, T1, ALU.add, inc="V")
        w.wait("A", "V", w.c["V"])
        w.op("A", "copy", dtspb, DTA, inc="A")
        w.wait("T", "A", w.c["A"])
        for c in range(NCH):
            if c % 4 == 0:
                w.wait("T", "V", w.c["V"])
            w.op("T", "transpose", pbt[:, (c % 4) * 128:(c % 4) * 128 + 64],
                 dtspb[:, c * 128:(c + 1) * 128], IDB[0:64, 0:64],
                 inc=("T" if c % 4 == 3 else None))
            if c % 4 == 3:
                w.wait("V", "T", w.c["T"])
                w.op("V", "tensor_copy",
                     DTT[:, (c - 3) * 64:(c + 1) * 64].rearrange(
                         "p (a c) -> p a c", a=4),
                     pbt[:, 0:512].rearrange("p (a c) -> p a c", a=4)[:, :, 0:64],
                     inc="V")
        w.wait("A", "dS", w.c["dS"])
        w.wait("A", "V", w.c["V"])
        w.op("A", "activation", col(41)[0:64, :], col(41)[0:64, :], AF.Exp,
             inc="A")
        w.wait("V", "A", w.c["A"])
        w.nop("V")
        w.op("V", "tensor_scalar", col(41)[0:64, :], col(41)[0:64, :], -1.0, 0.0,
             ALU.mult, ALU.add)
        w.nop("V")
        w.op("V", "tensor_scalar", ABR2, DTA, col(41)[0:64, :], 0.0, ALU.mult,
             ALU.add, inc="V")
        zc = col(43)
        w.op("V", "memset", zc, 0.0)
        w.nop("V")
        w.op("V", "engine_nop", inc="V")
        for c in range(NCH):
            w.op("V", "tensor_tensor_scan", ACS, ABR2[:, c * 128:(c + 1) * 128],
                 zc[0:64, :].broadcast_to([64, 128]), 0.0, ALU.add, ALU.add, inc="V")
            w.wait("A", "V", w.c["V"])
            w.nop("A")
            w.op("A", "activation", sdch, ACS, AF.Exp)
            w.op("A", "activation", dsch, ACS, AF.Exp, scale=col(45)[0:64, :],
                 bias=ACS[:, 127:128], inc="A")
            w.wait("T", "A", w.c["A"])
            w.nop("T")
            w.wait("T", "V", w.c["V"])
            w.op("T", "transpose", pbt[:, 0:64], sdch, IDB[0:64, 0:64])
            w.op("T", "transpose", pbt[:, 128:192], dsch, IDB[0:64, 0:64], inc="T")
            w.op("T", "transpose", pbt[0:1, 256:320], sdch[:, 127:128],
                 IDB[0:64, 0:64], inc="T")
            w.wait("V", "T", w.c["T"])
            w.op("V", "tensor_copy", SDT[:, c * 64:(c + 1) * 64], pbt[:, 0:64])
            w.op("V", "tensor_copy", DST[:, c * 64:(c + 1) * 64], pbt[:, 128:192])
            w.op("V", "tensor_copy", CDROW, pbt[0:1, 256:320], inc="V")
            w.wait("T", "V", w.c["V"])
            w.nop("T")
            w.op("T", "matmul", pb[3][:, 0:64], ONB, CDROW, start=True, stop=True,
                 inc="T")
            w.wait("V", "T", w.c["T"])
            w.op("V", "tensor_copy", CDB[:, c * 64:(c + 1) * 64], pb[3][:, 0:64],
                 inc="V")
        w.wait("A", "V", w.c["V"])
        w.op("A", "activation", RTEN, ABR2, AF.Exp, inc="A")
        w.wait("V", "A", w.c["A"])
        for c in range(NCH):
            w.op("V", "memset", RTEN[:, c * 128: c * 128 + 1], 0.0,
                 inc=("V" if c == NCH - 1 else None))
        w.wait("T", "V", w.c["V"])
        w.op("T", "transpose", pb[3][0:1, 0:128], col(42), IDF, inc="T")
        w.wait("V", "T", w.c["T"])
        w.op("V", "tensor_copy", C1R[0:1, 0:128], pb[3][0:1, 0:128], inc="V")
        w.wait("T", "V", w.c["V"])
        w.nop("T")
        w.op("T", "matmul", pb[3][:, 0:64], ONF, C1R[0:1, 0:64], start=True,
             stop=True, inc="T")
        w.wait("V", "T", w.c["T"])
        w.op("V", "tensor_copy", DMAT, pb[3][:, 0:64], inc="V")

    def mixer_m1(xnT):
        wslot = [ARW[:, 0:4096], ARW[:, 4096:8192]]
        cvin = [ARW[:, 8192:10243], ARW[:, 10243:12294]]
        hout = ARW[:, 12294:14342]
        gpiece = ARW[:, 14342:14854]
        xnv = xnT.rearrange("p (a c) -> p a c", a=16)
        w.op("S", "dma_start", col(40)[0:64, :],
             dt_bias.rearrange("(a b) -> a b", b=1), inc="dS", k=16)
        w.op("S", "dma_start", col(41)[0:64, :],
             a_log.rearrange("(a b) -> a b", b=1), inc="dS", k=16)
        w.op("S", "dma_start", col(42)[0:64, :],
             dvec.rearrange("(a b) -> a b", b=1), inc="dS", k=16)
        w.op("V", "memset", cvin[0][:, 0:3], 0.0)
        w.op("V", "memset", cvin[1][:, 0:3], 0.0, inc="V")

        groups = [("dt", [80])]
        groups += [("hbc", list(range(32 + 2 * i, 34 + 2 * i))) for i in range(24)]
        groups += [("gate", list(range(2 * i, 2 * i + 2))) for i in range(16)]

        t_after = {}
        ev = {}
        cnt = 0
        hout2 = [hout, ARW[:, 14854:16902]]
        gp2 = [gpiece, ARW[:, 16902:17414]]
        hb_done = {}
        gp_done = {}
        cv_done = {}
        sl_done = {}
        hi = 0
        gpi = 0
        for gi, (kind, js) in enumerate(groups):
            sl = gi % 2
            j0 = js[0]
            rows = 64 if kind == "dt" else 256
            if gi >= 2:
                w.wait("S", "T", t_after[gi - 2])
            w.op("S", "dma_start",
                 wslot[sl][:, 0:4096].rearrange("p (k c) -> p k c", k=16)[:, :, 0:rows],
                 s_winT[:, j0 * 128: j0 * 128 + rows]
                 .rearrange("(k p) c -> p k c", k=16), inc="dW", k=16)
            w.wait("T", "dW", w.c["dW"])
            for jj, j in enumerate(js):
                mrows = 64 if kind == "dt" else 128
                for tc in range(4):
                    bank = pb[cnt % 2]
                    if cnt >= 2:
                        eng, val = ev[cnt - 2]
                        w.wait("T", eng, val)
                    for k in range(16):
                        w.op("T", "matmul", bank[0:mrows, 0:512],
                             wslot[sl][:, k * 256 + jj * 128: k * 256 + jj * 128 + mrows],
                             xnv[:, k, tc * 512:(tc + 1) * 512],
                             start=(k == 0), stop=(k == 15),
                             inc=("T" if k == 15 else None))
                    tmm = w.c["T"]
                    if kind == "dt":
                        w.wait("V", "T", tmm)
                        w.wait("V", "dS", w.c["dS"])
                        ev[cnt] = ("V", w.op("V", "tensor_scalar",
                                             DTA[:, tc * 512:(tc + 1) * 512],
                                             bank[0:64, 0:512], col(40)[0:64, :], 0.0,
                                             ALU.add, ALU.add, inc="V"))
                    elif kind == "hbc":
                        w.wait("A", "T", tmm)
                        if hi >= 2 and tc == 0:
                            w.wait("A", "V", cv_done[hi - 2])
                        ev[cnt] = ("A", w.op("A", "copy",
                                             cvin[jj % 2][:, 3 + tc * 512: 3 + (tc + 1) * 512],
                                             bank[:, 0:512], inc="A"))
                    else:
                        w.wait("A", "T", tmm)
                        if gpi >= 2:
                            w.wait("A", "dS", gp_done[gpi - 2])
                        ev[cnt] = ("A", w.op("A", "copy", gp2[gpi % 2],
                                             bank[:, 0:512], inc="A"))
                        w.wait("S", "A", w.c["A"])
                        gp_done[gpi] = w.op("S", "dma_start",
                                            s_gate[j * 128:(j + 1) * 128,
                                                   tc * 512:(tc + 1) * 512],
                                            gp2[gpi % 2], inc="dS", k=16)
                        gpi += 1
                    cnt += 1
                if kind == "hbc":
                    ci = j - 32
                    cwsp = CW0 if jj % 2 == 0 else CW1
                    cbsp = CB0 if jj % 2 == 0 else CB1
                    if hi >= 2:
                        w.wait("G", "V", cv_done[hi - 2])
                        w.wait("G", "A", sl_done[hi - 2])
                    w.op("G", "dma_start", cwsp, conv_w[ci * 128:(ci + 1) * 128, :],
                         inc="dG", k=16)
                    w.op("G", "dma_start", cbsp,
                         conv_b[ci * 128:(ci + 1) * 128].rearrange("(a b) -> a b", b=1),
                         inc="dG", k=16)
                    w.wait("V", "dG", w.c["dG"])
                    w.wait("V", "A", w.c["A"])
                    if hi >= 2:
                        w.wait("V", "dS", hb_done[hi - 2])
                    src_ = cvin[jj % 2]
                    ho = hout2[hi % 2]
                    w.op("V", "tensor_scalar", ho, src_[:, 0:2048], cwsp[:, 0:1],
                         0.0, ALU.mult, ALU.add)
                    for k in range(1, 4):
                        w.op("V", "scalar_tensor_tensor", ho, src_[:, k:2048 + k],
                             cwsp[:, k:k + 1], ho, ALU.mult, ALU.add,
                             inc=("V" if k == 3 else None))
                    cv_done[hi] = w.c["V"]
                    w.wait("A", "V", w.c["V"])
                    sl_done[hi] = w.op("A", "activation", ho, ho, AF.Silu, bias=cbsp,
                                       inc="A")
                    w.wait("S", "A", w.c["A"])
                    hb_done[hi] = w.op("S", "dma_start",
                                       s_hbc[ci * 128:(ci + 1) * 128, :], ho,
                                       inc="dS", k=16)
                    hi += 1
            if kind == "dt":
                dt_family()
            t_after[gi] = w.c["T"]

    def ssm():
        INS = []
        INS.append({"XT": AR2[:, 0:4096], "GT": AR2[:, 4096:8192],
                    "BT": AR2[:, 8192:9216], "BCF": AR2[:, 9216:11264]})
        INS.append({"XT": AR1[:, 9216:13312], "GT": AR1[:, 13312:17408],
                    "BT": AR1[:, 17408:18432], "BCF": AR1[:, 18432:20480]})
        XDT = AR2[:, 11264:15360]
        XDD = AR2[:, 15360:19456]
        SILG = AR2[:, 19456:23552]
        GGB = AR2[:, 23552:27648]
        ONT = AR2[:, 27648:31744]
        LMK = [AR1[:, 0:1024], AR1[:, 2048:3072]]
        MPR = [AR1[:, 1024:2048], AR1[:, 3072:4096]]
        YTMP = AR1[:, 4096:4608]
        Y2B = AR1[:, 4608:5120]
        XDB = AR1[:, 5120:9216]
        SQS = fview(ARW, 0, 128, 4096)
        w.op("V", "memset", SP0, 0.0, inc="V")

        t1d = {}
        t2d = {}
        v1d = {}
        vscan = {}
        v2e = {}
        v3d = {}
        ldd = {}
        vpro = {}
        asil = {}

        def load_chunk(c):
            sd = INS[c % 2]
            if c >= 1:
                w.wait("S", "T", w.c["T"])
                w.wait("S", "V", w.c["V"])
                w.wait("S", "A", w.c["A"])
            w.op("S", "dma_start_transpose", sd["XT"],
                 s_hbc[0:4096, c * 128:(c + 1) * 128], inc="dS", k=16)
            w.op("S", "dma_start_transpose", sd["GT"],
                 s_gate[0:4096, c * 128:(c + 1) * 128], inc="dS", k=16)
            w.op("S", "dma_start_transpose", sd["BT"],
                 s_hbc[4096:5120, c * 128:(c + 1) * 128], inc="dS", k=16)
            w.op("S", "dma_start", sd["BCF"].rearrange("p (a c) -> p a c", a=16),
                 s_hbc[4096:6144, c * 128:(c + 1) * 128]
                 .rearrange("(a p) c -> p a c", a=16), inc="dS", k=16)
            ldd[c] = w.c["dS"]

        def t1(c, g):
            gi = c * 8 + g
            sd = INS[c % 2]
            if gi >= 2:
                w.wait("T", "V", v1d.get(gi - 2, 0))
            if gi >= 1:
                w.wait("T", "V", vscan.get(gi - 1, 0))
            if g == 0:
                w.wait("T", "dS", ldd[c])
            w.op("T", "matmul", pb[0][:, (g % 2) * 128:(g % 2) * 128 + 128],
                 sd["BCF"][:, g * 128:(g + 1) * 128],
                 sd["BCF"][:, 1024 + g * 128: 1024 + (g + 1) * 128], start=True,
                 stop=True)
            for i in range(8):
                bank = pb[1] if i < 4 else pb[2]
                w.op("T", "matmul", bank[:, (i % 4) * 128:(i % 4 + 1) * 128],
                     IDF[0:64, 8 * g + i: 8 * g + i + 1].broadcast_to([64, 128]),
                     RTEN[:, c * 128:(c + 1) * 128],
                     start=True, stop=True, inc=("T" if i == 7 else None))
            t1d[gi] = w.c["T"]

        def v1(c, g):
            gi = c * 8 + g
            w.wait("V", "T", t1d[gi])
            if gi >= 2:
                w.wait("V", "T", t2d.get(gi - 2, 0))
            lm = LMK[gi % 2]
            mp = MPR[gi % 2]
            for half in range(2):
                w.op("V", "tensor_tensor_scan", lm[:, half * 512:(half + 1) * 512],
                     pb[1 + half][:, 0:512], IR4, 0.0, ALU.mult, ALU.add,
                     inc=("V" if half == 1 else None))
            vscan[gi] = w.c["V"]
            for half in range(2):
                w.op("V", "tensor_tensor",
                     mp[:, half * 512:(half + 1) * 512]
                     .rearrange("p (a c) -> p a c", a=4),
                     pb[0][:, (g % 2) * 128:(g % 2) * 128 + 128].unsqueeze(1)
                     .broadcast_to([128, 4, 128]),
                     lm[:, half * 512:(half + 1) * 512]
                     .rearrange("p (a c) -> p a c", a=4), ALU.mult,
                     inc=("V" if half == 1 else None))
            v1d[gi] = w.c["V"]

        def t2(c, g):
            gi = c * 8 + g
            sd = INS[c % 2]
            Sold = SP0 if c % 2 == 0 else SP1
            mp = MPR[gi % 2]
            w.wait("T", "V", v1d[gi])
            if gi >= 1:
                w.wait("T", "V", v2e.get(gi - 1, 0))
            for i in range(8):
                w.op("T", "matmul", pb[3][:, i * 64:(i + 1) * 64],
                     mp[:, i * 128:(i + 1) * 128],
                     XDT[:, (8 * g + i) * 64:(8 * g + i + 1) * 64],
                     start=True, stop=True)
            w.op("T", "matmul", pb[4][:, 0:512], sd["BT"][:, g * 128:(g + 1) * 128],
                 XDD[:, g * 512:(g + 1) * 512], start=True, stop=True)
            w.op("T", "matmul", pb[5][:, 0:512],
                 sd["BCF"][:, 1024 + g * 128:1024 + (g + 1) * 128],
                 Sold[:, g * 512:(g + 1) * 512], start=True, stop=True, inc="T")
            t2d[gi] = w.c["T"]

        def v2(c, g):
            gi = c * 8 + g
            Sold = SP0 if c % 2 == 0 else SP1
            Snew = SP1 if c % 2 == 0 else SP0
            w.wait("V", "T", t2d[gi])
            w.op("V", "tensor_tensor",
                 Snew[:, g * 512:(g + 1) * 512].rearrange("p (a c) -> p a c", a=8),
                 Sold[:, g * 512:(g + 1) * 512].rearrange("p (a c) -> p a c", a=8),
                 CDB[:, c * 64 + g * 8: c * 64 + (g + 1) * 8].unsqueeze(2)
                 .broadcast_to([128, 8, 64]), ALU.mult)
            w.op("V", "tensor_tensor", Snew[:, g * 512:(g + 1) * 512],
                 Snew[:, g * 512:(g + 1) * 512], pb[4][:, 0:512], ALU.add)
            w.op("V", "tensor_tensor", YTMP.rearrange("p (a c) -> p a c", a=8),
                 pb[5][:, 0:512].rearrange("p (a c) -> p a c", a=8),
                 SDT[:, c * 64 + g * 8: c * 64 + (g + 1) * 8].unsqueeze(2)
                 .broadcast_to([128, 8, 64]), ALU.mult)
            w.op("V", "scalar_tensor_tensor", Y2B, pb[3][:, 0:512], 1.0, YTMP,
                 ALU.mult, ALU.add, inc="V")
            v2e[gi] = w.c["V"]

        def v3(c, g):
            gi = c * 8 + g
            w.op("V", "tensor_tensor", Y2B, Y2B, XDB[:, g * 512:(g + 1) * 512],
                 ALU.add)
            w.op("V", "tensor_tensor", GGB[:, g * 512:(g + 1) * 512], Y2B,
                 SILG[:, g * 512:(g + 1) * 512], ALU.mult, inc="V")
            v3d[gi] = w.c["V"]
            w.wait("A", "V", v3d[gi])
            w.op("A", "activation", SQS[:, g * 512:(g + 1) * 512],
                 GGB[:, g * 512:(g + 1) * 512], AF.Square,
                 accum_out=VARC[:, g:g + 1], inc="A")

        for c in range(NCH):
            load_chunk(c)
            sd = INS[c % 2]
            w.wait("V", "dS", ldd[c])
            if c >= 1:
                w.wait("V", "T", t2d.get((c - 1) * 8 + 7, 0))
                w.wait("V", "A", w.c["A"])
            xtv = sd["XT"].rearrange("p (h q) -> p h q", h=64)
            w.op("V", "tensor_tensor", XDT.rearrange("p (h q) -> p h q", h=64), xtv,
                 DTT[:, c * 64:(c + 1) * 64].unsqueeze(2).broadcast_to([128, 64, 64]),
                 ALU.mult)
            w.op("V", "tensor_tensor", XDD.rearrange("p (h q) -> p h q", h=64),
                 XDT.rearrange("p (h q) -> p h q", h=64),
                 DST[:, c * 64:(c + 1) * 64].unsqueeze(2).broadcast_to([128, 64, 64]),
                 ALU.mult)
            w.op("V", "tensor_tensor", XDB.rearrange("p (h q) -> p h q", h=64), xtv,
                 DMAT.unsqueeze(2).broadcast_to([128, 64, 64]), ALU.mult, inc="V")
            vpro[c] = w.c["V"]
            w.wait("A", "dS", ldd[c])
            w.wait("A", "V", w.c["V"])
            asil[c] = w.op("A", "activation", SILG, sd["GT"], AF.Silu, inc="A")
            for g in range(8):
                t1(c, g)
                v1(c, g)
                t2(c, g)
                v2(c, g)
                v3(c, g)
            w.wait("V", "A", w.c["A"])
            w.op("V", "tensor_scalar", VARC, VARC, 1.0 / 512.0, 0.0, ALU.mult,
                 ALU.add, inc="V")
            w.wait("A", "V", w.c["V"])
            w.nop("A")
            w.op("A", "activation", VARC, VARC, AF.Sqrt, bias=col(44), inc="A")
            w.wait("V", "A", w.c["A"])
            w.nop("V")
            w.op("V", "reciprocal", RSTC, VARC)
            w.nop("V")
            w.wait("V", "dS", w.c["dS"])
            w.op("V", "tensor_tensor", ONT.rearrange("p (a c) -> p a c", a=8),
                 GGB.rearrange("p (a c) -> p a c", a=8),
                 RSTC.unsqueeze(2).broadcast_to([128, 8, 512]), ALU.mult, inc="V")
            w.wait("S", "V", w.c["V"])
            w.op("S", "dma_start", s_on[c * 128:(c + 1) * 128, :], ONT, inc="dS",
                 k=16)

    def wout_phase():
        w.op("S", "dma_start", fview(ARW, 0, 32, 128),
             norm_w.rearrange("(a b) -> a b", b=128), inc="dS", k=16)
        w.wait("T", "dS", w.c["dS"])
        w.op("T", "transpose", pb[0][:, 0:32], fview(ARW, 0, 32, 128),
             IDF[0:32, 0:32], inc="T")
        w.wait("V", "T", w.c["T"])
        w.op("V", "tensor_copy", NWC, pb[0][:, 0:32])
        w.nop("V")
        w.op("V", "engine_nop", inc="V")

        onfm = AR1[:, 0:16384]
        onv = onfm.rearrange("p (a c) -> p a c", a=32)
        wc = [AR2[:, 0:8192], AR2[:, 8192:16384]]
        xsl = [fview(ARW, 4096 + i * 1024, 128, 256) for i in range(4)]
        opiece = [fview(ARW, 8192 + i * 1024, 128, 256) for i in range(4)]
        t_on = {}
        t_wc = {}
        v_evac = {}
        o_dma = {}
        v_scale = {}
        for tq in range(4):
            if tq >= 1:
                w.wait("S", "T", t_on[tq - 1])
                w.wait("S", "V", v_scale[tq - 1])
            for i in range(32):
                w.op("S", "dma_start_transpose", onv[:, i, :],
                     s_on[tq * 512:(tq + 1) * 512, i * 128:(i + 1) * 128],
                     inc="dS", k=16)
            w.wait("V", "dS", w.c["dS"])
            for i in range(32):
                w.op("V", "tensor_scalar", onv[:, i, :], onv[:, i, :],
                     NWC[:, i:i + 1], 0.0, ALU.mult, ALU.add,
                     inc=("V" if i == 31 else None))
            v_scale[tq] = w.c["V"]
            for dg in range(8):
                it = tq * 8 + dg
                sl = dg % 2
                if it >= 2:
                    w.wait("S", "T", t_wc[it - 2])
                for half in range(2):
                    w.op("S", "dma_start",
                         wc[sl].rearrange("p (a c) -> p a c", a=32)
                         [:, half * 16:(half + 1) * 16, :],
                         s_woutT[:, dg * 256:(dg + 1) * 256]
                         .rearrange("(k p) c -> p k c", k=32)
                         [:, half * 16:(half + 1) * 16, :], inc="dW", k=16)
                w.wait("T", "dW", w.c["dW"])
                w.wait("T", "V", v_scale[tq])
                wv = wc[sl].rearrange("p (a c) -> p a c", a=32)
                for ts in range(4):
                    pidx = (tq, dg, ts)
                    tsg = tq * 4 + ts
                    bank = pb[ts % 4]
                    prevv = (tq, dg - 1, ts) if dg >= 1 else ((tq - 1, 7, ts) if tq else None)
                    if prevv in o_dma:
                        w.wait("S", "dO", o_dma[prevv])
                    w.op("S", "dma_start", xsl[ts],
                         x_in[tsg * 128:(tsg + 1) * 128, dg * 256:(dg + 1) * 256],
                         inc="dS", k=16)
                    x_ld = w.c["dS"]
                    if prevv in v_evac:
                        w.wait("T", "V", v_evac[prevv])
                    for k in range(32):
                        w.op("T", "matmul", bank[:, 0:256],
                             onv[:, k, ts * 128:(ts + 1) * 128], wv[:, k, :],
                             start=(k == 0), stop=(k == 31),
                             inc=("T" if k == 31 else None))
                    w.wait("V", "T", w.c["T"])
                    w.wait("V", "dS", x_ld)
                    if prevv in o_dma:
                        w.wait("V", "dO", o_dma[prevv])
                    v_evac[pidx] = w.op("V", "tensor_tensor", opiece[ts],
                                        bank[:, 0:256], xsl[ts], ALU.add, inc="V")
                    w.wait("S", "V", v_evac[pidx])
                    o_dma[pidx] = w.op("S", "dma_start",
                                       out_hbm[tsg * 128:(tsg + 1) * 128,
                                               dg * 256:(dg + 1) * 256],
                                       opiece[ts], inc="dO", k=16)
                t_wc[it] = w.c["T"]
            t_on[tq] = w.c["T"]

    def down_phase():
        rres = AR1[:, 0:32768]
        rv = rres.rearrange("p (a c) -> p a c", a=64)
        wc = [AR2[:, 0:16384], AR2[:, 16384:32768]]
        opiece = [fview(ARW, i * 1024, 128, 256) for i in range(4)]
        t_r = {}
        t_wc = {}
        v_evac = {}
        o_dma = {}
        for tq in range(4):
            if tq >= 1:
                w.wait("G", "T", t_r[tq - 1])
            for i in range(64):
                w.op("G", "dma_start", rv[:, i, :],
                     s_r2[i * 128:(i + 1) * 128, tq * 512:(tq + 1) * 512],
                     inc="dS", k=16)
            rq_loaded = w.c["dS"]
            for dg in range(8):
                it = tq * 8 + dg
                sl = dg % 2
                if it >= 2:
                    w.wait("S", "T", t_wc[it - 2])
                for half in range(4):
                    w.op("S", "dma_start",
                         wc[sl].rearrange("p (a c) -> p a c", a=64)
                         [:, half * 16:(half + 1) * 16, :],
                         s_wdT[:, dg * 256:(dg + 1) * 256]
                         .rearrange("(k p) c -> p k c", k=64)
                         [:, half * 16:(half + 1) * 16, :], inc="dW", k=16)
                w.wait("T", "dW", w.c["dW"])
                w.wait("T", "dS", rq_loaded)
                wv = wc[sl].rearrange("p (a c) -> p a c", a=64)
                for ts in range(4):
                    pidx = (tq, dg, ts)
                    bank = pb[ts % 4]
                    prev = (tq, dg - 1, ts) if dg >= 1 else ((tq - 1, 7, ts) if tq else None)
                    if prev in v_evac:
                        w.wait("T", "V", v_evac[prev])
                    for k in range(64):
                        w.op("T", "matmul", bank[:, 0:256],
                             rv[:, k, ts * 128:(ts + 1) * 128], wv[:, k, :],
                             start=(k == 0), stop=(k == 63),
                             inc=("T" if k == 63 else None))
                    w.wait("V", "T", w.c["T"])
                    prev2 = (tq, dg - 1, ts) if dg >= 1 else ((tq - 1, 7, ts) if tq else None)
                    if prev2 in o_dma:
                        w.wait("V", "dO", o_dma[prev2])
                    tsg = tq * 4 + ts
                    v_evac[pidx] = w.op("V", "tensor_scalar", opiece[ts], bank[:, 0:256],
                                        M2Q[:, tsg:tsg + 1], 0.0, ALU.mult, ALU.add,
                                        inc="V")
                    w.wait("G", "V", v_evac[pidx])
                    o_dma[pidx] = w.op("G", "dma_start",
                                       out_hbm[tsg * 128:(tsg + 1) * 128,
                                               dg * 256:(dg + 1) * 256],
                                       opiece[ts], accum_op=ALU.add, inc="dO", k=16)
                t_wc[it] = w.c["T"]
            t_r[tq] = w.c["T"]

    load_consts()
    w.barrier()
    weight_prep()
    w.barrier()
    xnT = layernorm()
    w.barrier()
    xq = build_xq(xnT)
    w.barrier()
    ffn_f1(xq)
    w.barrier()
    ffn_f2a()
    w.barrier()
    mixer_m1(xnT)
    w.barrier()
    ssm()
    w.barrier()
    wout_phase()
    w.barrier()
    down_phase()
    w.barrier()

    if DEBUG:
        for nm, src in [("d_za", s_za), ("d_xq", s_xq), ("d_gate", s_gate),
                        ("d_hbc", s_hbc), ("d_on", s_on), ("d_ya", s_ya),
                        ("d_r2", s_r2)]:
            w.op("G", "dma_start", dbg[nm], src, inc="dO", k=16)
        w.op("G", "dma_start", dbg["d_smf"], SMF, inc="dO", k=16)

    w.wait("G", "dO", w.c["dO"])
    w.wait("G", "dS", w.c["dS"])
    w.wait("G", "dG", w.c["dG"])
    w.wait("G", "dW", w.c["dW"])

    with nc.Block() as block:
        @block.tensor
        def _(t):
            for f in w.st["T"]:
                f(t)

        @block.vector
        def _(v):
            for f in w.st["V"]:
                f(v)

        @block.scalar
        def _(a):
            for f in w.st["A"]:
                f(a)

        @block.sync
        def _(s):
            for f in w.st["S"]:
                f(s)

        @block.gpsimd
        def _(g):
            for f in w.st["G"]:
                f(g)

    es.close()
    return nc


def _consts():
    h128 = _hadamard(128)
    hb16 = np.kron(_hadamard(16), np.eye(8, dtype=np.float32))
    hb64 = np.kron(_hadamard(64), np.eye(2, dtype=np.float32))
    i128 = np.eye(128, dtype=np.float32)
    ones_blk = np.zeros((128, 128), np.float32)
    ones_blk[0, :] = 1.0
    cb = np.concatenate([h128, hb16, hb64, i128, ones_blk,
                         np.tile(i128, (1, 4))], axis=1)
    return {
        "cb": cb.astype(ml_dtypes.bfloat16),
        "ones_f": np.ones((1, 128), np.float32),
        "ones_c": np.ones((128, 1), np.float32),
        "id_f": i128,
    }


def kernel(**inp):
    f32 = lambda a: np.ascontiguousarray(np.asarray(a, np.float32))
    x = f32(inp["L0_attn_hidden_states"])
    shared = {
        "w_in": f32(inp["L0_attn_W_in"]),
        "conv_w": f32(inp["L0_attn_conv_w"]),
        "conv_b": f32(inp["L0_attn_conv_b"]),
        "dt_bias": f32(inp["L0_attn_dt_bias"]),
        "a_log": f32(inp["L0_attn_A_log"]),
        "dvec": f32(inp["L0_attn_D"]),
        "norm_w": f32(inp["L0_attn_norm_weight"]),
        "w_out": f32(inp["L0_attn_W_out"]),
        "w_g": f32(inp["L0_ffn_w_gate"]),
        "w_u": f32(inp["L0_ffn_w_up"]),
        "w_d": f32(inp["L0_ffn_w_down"]),
    }
    shared.update(_consts())
    nc = build_program()
    in_maps = [dict(shared, x=x[b]) for b in range(8)]
    res = run_bass_kernel_spmd(nc, in_maps, core_ids=list(range(8)))
    out = np.stack([np.asarray(res.results[b]["out"]) for b in range(8)], axis=0)
    return out.astype(np.float32)


if __name__ == "__main__":
    nc = build_program()
    print("instructions:", nc.n_instructions())


# version 18
# speedup vs baseline: 1.1780x; 1.0061x over previous
import numpy as np
import ml_dtypes
import concourse.bass as bass
import concourse.mybir as mybir
from concourse.bass_utils import run_bass_kernel_spmd
from contextlib import ExitStack

F32 = mybir.dt.float32
BF16 = mybir.dt.bfloat16
AF = mybir.ActivationFunctionType
ALU = mybir.AxisListType
ALU = mybir.AluOpType
AX = mybir.AxisListType

L = 2048
DM = 2048
HH, PP, NNS, GG = 64, 64, 128, 8
INTER = 4096
CONV = 6144
PROJ = 10304
IFF = 8192
CS = 128
NCH = L // CS
KC = 4
RND = 3.0 * 2.0 ** 22
EPS = 1e-5
DEBUG = False

ENGS = "TVASG"


def _hadamard(n):
    h = np.array([[1.0]], np.float32)
    while h.shape[0] < n:
        h = np.block([[h, h], [h, -h]])
    return h


class Wk:
    def __init__(self, nc, semh):
        self.nc = nc
        self.semh = semh
        self.c = {k: 0 for k in semh}
        self.st = {e: [] for e in ENGS}

    def wait(self, eng, sem, val=None):
        v = self.c[sem] if val is None else val
        if v <= 0:
            return
        h = self.semh[sem]
        self.st[eng].append(lambda e, h=h, v=v: e.wait_ge(h, v))

    def op(self, eng, meth, *args, inc=None, k=1, **kw):
        h = None
        kk = 0
        if inc is not None:
            self.c[inc] += k
            h, kk = self.semh[inc], k

        import traceback
        site = traceback.extract_stack(limit=3)[0]

        def f(e, m=meth, a=args, kw=kw, h=h, kk=kk, site=site):
            try:
                ins = getattr(e, m)(*a, **kw)
            except Exception:
                print("FAILING OP:", m, "emitted at", site.filename, site.lineno,
                      [getattr(x, "shape", None) for x in a])
                raise
            if h is not None:
                ins.then_inc(h, kk)

        self.st[eng].append(f)
        return self.c[inc] if inc is not None else None

    def nop(self, eng, cyc=1024):
        self.st[eng].append(lambda e, c=cyc: e.nop(cycle_cnt=c))

    def fence(self, eng, sem):
        self.c[sem] += 1
        h = self.semh[sem]
        self.st[eng].append(lambda e, h=h: e.nop(cycle_cnt=1024).then_inc(h, 1))
        return self.c[sem]

    def barrier(self):
        snap = dict(self.c)
        for e in ENGS:
            for s, v in snap.items():
                if v > 0:
                    self.wait(e, s, v)


def build_program():
    nc = bass.Bass("TRN2", target_bir_lowering=False)
    dt_ = nc.dram_tensor

    x_in = dt_("x", [L, DM], F32, kind="ExternalInput").ap()
    w_in = dt_("w_in", [PROJ, DM], F32, kind="ExternalInput").ap()
    conv_w = dt_("conv_w", [CONV, KC], F32, kind="ExternalInput").ap()
    conv_b = dt_("conv_b", [CONV], F32, kind="ExternalInput").ap()
    dt_bias = dt_("dt_bias", [HH], F32, kind="ExternalInput").ap()
    a_log = dt_("a_log", [HH], F32, kind="ExternalInput").ap()
    dvec = dt_("dvec", [HH], F32, kind="ExternalInput").ap()
    norm_w = dt_("norm_w", [INTER], F32, kind="ExternalInput").ap()
    w_out = dt_("w_out", [DM, INTER], F32, kind="ExternalInput").ap()
    w_g = dt_("w_g", [IFF, DM], F32, kind="ExternalInput").ap()
    w_u = dt_("w_u", [IFF, DM], F32, kind="ExternalInput").ap()
    w_d = dt_("w_d", [DM, IFF], F32, kind="ExternalInput").ap()
    cb_in = dt_("cb", [128, 1152], BF16, kind="ExternalInput").ap()
    onesf_in = dt_("ones_f", [1, 128], F32, kind="ExternalInput").ap()
    onesc_in = dt_("ones_c", [128, 1], F32, kind="ExternalInput").ap()
    idf_in = dt_("id_f", [128, 128], F32, kind="ExternalInput").ap()

    out_hbm = dt_("out", [L, DM], F32, kind="ExternalOutput").ap()

    s_winT = dt_("s_winT", [DM, PROJ], BF16, kind="Internal").ap()
    s_woutT = dt_("s_woutT", [INTER, DM], BF16, kind="Internal").ap()
    s_wgT = dt_("s_wgT", [DM, IFF], BF16, kind="Internal").ap()
    s_wuT = dt_("s_wuT", [DM, IFF], BF16, kind="Internal").ap()
    s_wdT = dt_("s_wdT", [IFF, DM], BF16, kind="Internal").ap()
    s_za = dt_("s_za", [DM, L], BF16, kind="Internal").ap()
    s_xq = dt_("s_xq", [DM, L], BF16, kind="Internal").ap()
    s_gate = dt_("s_gate", [INTER, L], BF16, kind="Internal").ap()
    s_hbc = dt_("s_hbc", [CONV, L], BF16, kind="Internal").ap()
    s_on = dt_("s_on", [L, INTER], BF16, kind="Internal").ap()
    s_ya = dt_("s_ya", [IFF, L], BF16, kind="Internal").ap()
    s_r2 = dt_("s_r2", [IFF, L], BF16, kind="Internal").ap()

    dbg = {}
    if DEBUG:
        for nm, shp in [("d_za", [DM, L]), ("d_xq", [DM, L]), ("d_gate", [INTER, L]),
                        ("d_hbc", [CONV, L]), ("d_on", [L, INTER]), ("d_ya", [IFF, L]),
                        ("d_r2", [IFF, L])]:
            dbg[nm] = dt_(nm, shp, BF16, kind="ExternalOutput").ap()
        dbg["d_smf"] = dt_("d_smf", [128, 2944], F32, kind="ExternalOutput").ap()

    es = ExitStack()

    def sbuf(name, shape, dty):
        return es.enter_context(nc.sbuf_tensor(name, shape, dty)).ap()

    AR1 = sbuf("AR1", [128, 32768], BF16)
    AR2 = sbuf("AR2", [128, 32768], BF16)
    ARW = sbuf("ARW", [128, 19968], BF16)
    SST = sbuf("SST", [128, 8192], BF16)
    SMB = sbuf("SMB", [128, 4224], BF16)
    SMF = sbuf("SMF", [128, 2944], F32)
    CBT = sbuf("CBT", [128, 1152], BF16)
    IDF = sbuf("IDF", [128, 128], F32)
    ONF = sbuf("ONF", [1, 128], F32)
    ONC = sbuf("ONC", [128, 1], F32)

    pb = [es.enter_context(nc.psum_tensor("pb%d" % i, [128, 512], F32)).ap()
          for i in range(7)]
    pbt = es.enter_context(nc.psum_tensor("pbt", [128, 1024], BF16)).ap()

    sems = {}
    for s in ("dS", "dG", "dW", "dO", "T", "V", "A"):
        sems[s] = es.enter_context(nc.semaphore(s))

    w = Wk(nc, sems)

    def fview(arena, byte_off, parts, cols):
        assert byte_off % 4 == 0
        return arena[0:parts, byte_off // 2: byte_off // 2 + 2 * cols].bitcast(F32)

    H128 = CBT[:, 0:128]
    HB16 = CBT[:, 128:256]
    HB64 = CBT[:, 256:384]
    IDB = CBT[:, 384:512]
    ONB = CBT[0:1, 512:640]
    IR4 = CBT[:, 640:1152]

    smf = SMF
    DTA = smf[0:64, 0:2048]
    RTEN = smf[0:64, 0:2048]
    ABR = smf[64:128, 0:2048]
    ACS = smf[0:64, 2048:2176]
    CW0 = smf[:, 2176:2180]
    CW1 = smf[:, 2180:2184]
    CB0 = smf[:, 2184:2185]
    CB1 = smf[:, 2185:2186]
    COLS = smf[:, 2186:2250]
    M1C = smf[:, 2250:2266]
    M2C = smf[:, 2266:2282]
    M2Q = smf[:, 2282:2298]
    C1R = smf[0:16, 2304:2432]
    NWC = smf[:, 2432:2464]
    VARC = smf[:, 2464:2472]
    RSTC = smf[:, 2472:2480]
    SSCAL = smf[0:1, 2480:2496]
    SCL = smf[:, 2496:2520]
    C14 = smf[:, 2520:2524]
    CI4 = smf[:, 2524:2528]
    MXT = smf[:, 2528:2656]
    C2ROW = smf[0:1, 2656:2784]
    CRW1 = smf[0:1, 0:512]
    CRW2 = smf[0:1, 512:1024]

    def col(i):
        return COLS[:, i:i + 1]

    DTT = SMB[:, 0:1024]
    DST = SMB[:, 1024:2048]
    SDT = SMB[:, 2048:3072]
    CDB = SMB[:, 3072:4096]
    DMAT = SMB[:, 4096:4160]
    CDROW = SMB[0:1, 4160:4224]

    SP0 = SST[:, 0:4096]
    SP1 = SST[:, 4096:8192]

    def load_consts():
        w.op("V", "memset", col(44), 1e-5)
        w.op("V", "memset", col(45), -1.0, inc="V")
        w.op("S", "dma_start", CBT, cb_in, inc="dS", k=16)
        w.op("S", "dma_start", IDF, idf_in, inc="dS", k=16)
        w.op("S", "dma_start", ONF, onesf_in, inc="dS", k=16)
        w.op("S", "dma_start", ONC, onesc_in, inc="dS", k=16)

    def weight_prep():
        slot = [fview(ARW, i * 8192, 128, 2048) for i in range(3)]
        tslot = [ARW[:, 12288 + i * 2048: 12288 + (i + 1) * 2048] for i in range(2)]
        asm = [AR1[:, i * 2048:(i + 1) * 2048] for i in range(2)]
        junk = fview(AR2, 0, 128, 2048)
        run, spot = col(0), col(1)

        w.op("V", "memset", run, 0.0, inc="V")
        a_read = {}
        gi = 0
        for si, wm in enumerate((w_g, w_u, w_d)):
            wf = wm if wm.shape[1] == 2048 else wm.rearrange(
                "a (b c) -> (a b) c", c=2048)
            for t in range(64):
                sl = gi % 3
                if gi >= 3:
                    w.wait("S", "A", a_read[gi - 3])
                w.op("S", "dma_start", slot[sl], wf[t * 128:(t + 1) * 128, :],
                     inc="dS", k=16)
                w.wait("A", "dS", w.c["dS"])
                a_read[gi] = w.op("A", "activation", junk, slot[sl], AF.Abs,
                                  accum_out=spot, inc="A")
                w.wait("V", "A", a_read[gi])
                w.nop("V")
                w.op("V", "tensor_tensor", run, run, spot, ALU.add, inc="V")
                gi += 1
            w.wait("T", "V", w.c["V"])
            w.op("T", "matmul", pb[0][0:1, 0:1], run, ONC, start=True, stop=True,
                 inc="T")
            w.wait("V", "T", w.c["T"])
            sc = SSCAL[:, si:si + 1]
            w.op("V", "tensor_scalar", sc, pb[0][0:1, 0:1], 1.0 / (IFF * DM), 1e-5,
                 ALU.mult, ALU.max)
            w.nop("V")
            w.op("V", "reciprocal", SSCAL[:, 8:9], sc)
            w.nop("V")
            w.op("V", "memset", run, 0.0, inc="V")
            w.wait("T", "V", w.c["V"])
            w.nop("T")
            w.op("T", "matmul", pb[1][:, 0:1], ONF, SSCAL[:, 8:9], start=True,
                 stop=True, inc="T")
            w.wait("V", "T", w.c["T"])
            w.op("V", "tensor_copy", col(2 + si), pb[1][:, 0:1], inc="V")

        for si, tgt in ((0, SCL[:, 0:1]), (1, SCL[:, 1:2]), (2, SCL[:, 4:5])):
            w.wait("T", "V", w.c["V"])
            w.op("T", "matmul", pb[1][:, 1:2], ONF, SSCAL[:, si:si + 1], start=True,
                 stop=True, inc="T")
            w.wait("V", "T", w.c["T"])
            w.op("V", "tensor_copy", tgt, pb[1][:, 1:2], inc="V")
        w.nop("V")
        w.op("V", "tensor_tensor", SCL[:, 2:3], SCL[:, 1:2], SCL[:, 4:5], ALU.mult)
        w.nop("V")
        w.op("V", "tensor_scalar", SCL[:, 2:3], SCL[:, 2:3],
             float(1.0 / (127.0 * np.sqrt(8192.0))), 0.0, ALU.mult, ALU.add)
        w.op("V", "reciprocal", SCL[:, 3:4], SCL[:, 1:2])
        w.nop("V")
        w.op("V", "tensor_scalar", SCL[:, 3:4], SCL[:, 3:4],
             float(np.sqrt(8192.0) * 1e-5), 0.0, ALU.mult, ALU.add, inc="V")

        jobs = []
        for si, wm, dstT in ((0, w_g, s_wgT), (1, w_u, s_wuT)):
            for t in range(64):
                jobs.append(("tern", si, wm[t * 128:(t + 1) * 128, :], 128,
                             dstT[:, t * 128:(t + 1) * 128]
                             .rearrange("(k p) c -> p k c", k=16)))
        for q in range(4):
            for j in range(16):
                jobs.append(("tern", 2,
                             w_d[j * 128:(j + 1) * 128, q * 2048:(q + 1) * 2048], 128,
                             s_wdT[q * 2048:(q + 1) * 2048, j * 128:(j + 1) * 128]
                             .rearrange("(k p) c -> p k c", k=16)))
        for j in range(80):
            jobs.append(("cast", None, w_in[j * 128:(j + 1) * 128, :], 128,
                         s_winT[:, j * 128:(j + 1) * 128]
                         .rearrange("(k p) c -> p k c", k=16)))
        jobs.append(("cast", None, w_in[10240:10304, :], 64,
                     s_winT[:, 10240:10304].rearrange("(k p) c -> p k c", k=16)))
        for hf in range(2):
            for j in range(16):
                jobs.append(("cast", None,
                             w_out[j * 128:(j + 1) * 128, hf * 2048:(hf + 1) * 2048],
                             128,
                             s_woutT[hf * 2048:(hf + 1) * 2048, j * 128:(j + 1) * 128]
                             .rearrange("(k p) c -> p k c", k=16)))

        v_cons = {}
        v_ev = {}
        t_tp = {}
        g_wr = {}
        for t, (kind, si, srcap, rows, dstap) in enumerate(jobs):
            sl = t % 3
            ts = t % 2
            if t >= 3:
                w.wait("S", "V", v_cons[t - 3])
            w.op("S", "dma_start", slot[sl][0:rows, :], srcap, inc="dS", k=16)
            ld = w.c["dS"]
            w.wait("V", "dS", ld)
            if t >= 2:
                w.wait("V", "T", t_tp[t - 2])
            if t == 0:
                w.nop("V")
            sv = slot[sl][0:rows, :]
            tv = tslot[ts][0:rows, :]
            if kind == "tern":
                invc = col(2 + si)[0:rows, :]
                v_cons[t] = w.op("V", "tensor_scalar", junk[0:rows, :], sv, invc,
                                 -1.0, ALU.mult, ALU.max, inc="V")
                w.op("V", "tensor_scalar", junk[0:rows, :], junk[0:rows, :], 1.0,
                     RND, ALU.min, ALU.add)
                w.op("V", "tensor_scalar", tv, junk[0:rows, :], -RND, 0.0, ALU.add,
                     ALU.add, inc="V")
            else:
                v_cons[t] = w.op("V", "tensor_copy", tv, sv, inc="V")
            vt = w.c["V"]
            idn = IDB if rows == 128 else IDB[0:64, 0:64]
            pw = 128 if rows == 128 else 64
            for r in range(2):
                w.wait("T", "V", vt if r == 0 else w.c["V"])
                if t >= 1 or r >= 1:
                    w.wait("T", "V", w.c["V"])
                for k in range(8):
                    w.op("T", "transpose", pbt[:, k * pw:(k + 1) * pw],
                         tv[:, (r * 8 + k) * 128:(r * 8 + k + 1) * 128], idn,
                         inc=("T" if k == 7 else None))
                w.wait("V", "T", w.c["T"])
                if t >= 2:
                    w.wait("V", "dG", g_wr[t - 2])
                w.op("V", "tensor_copy",
                     asm[ts][:, r * 8 * pw:(r + 1) * 8 * pw], pbt[:, 0:8 * pw],
                     inc="V")
            t_tp[t] = w.c["T"]
            v_ev[t] = w.c["V"]
            w.wait("G", "V", v_ev[t])
            g_wr[t] = w.op("G", "dma_start", dstap,
                           asm[ts][:, 0:16 * pw].rearrange("p (k c) -> p k c", k=16),
                           inc="dG", k=16)

    def layernorm():
        xslot = [fview(AR2, 0, 128, 2048), fview(AR2, 8192, 128, 2048)]
        xnn = AR2[:, 8192:10240]
        sqs = fview(ARW, 0, 128, 2048)
        xnT = AR1[:, 0:32768]
        xnTv = xnT.rearrange("p (a c) -> p a c", a=16)
        for tt in range(16):
            sl = tt % 2
            w.wait("S", "V", w.c["V"])
            w.wait("S", "A", w.c["A"])
            w.op("S", "dma_start", xslot[sl], x_in[tt * 128:(tt + 1) * 128, :],
                 inc="dS", k=16)
            sumc, ssq, mc, vc, rc, nb = (col(8 + i) for i in range(6))
            w.wait("V", "dS", w.c["dS"])
            w.op("V", "reduce_sum", sumc, xslot[sl], AX.X)
            w.nop("V")
            w.op("V", "engine_nop", inc="V")
            w.wait("A", "dS", w.c["dS"])
            w.op("A", "activation", sqs, xslot[sl], AF.Square, accum_out=ssq,
                 inc="A")
            w.wait("V", "A", w.c["A"])
            w.nop("V")
            w.op("V", "tensor_scalar", mc, sumc, 1.0 / DM, 0.0, ALU.mult, ALU.add)
            w.op("V", "tensor_scalar", vc, ssq, 1.0 / DM, 0.0, ALU.mult, ALU.add)
            w.nop("V")
            w.op("V", "tensor_tensor", nb, mc, mc, ALU.mult)
            w.nop("V")
            w.op("V", "tensor_tensor", vc, vc, nb, ALU.subtract, inc="V")
            w.wait("A", "V", w.c["V"])
            w.nop("A")
            w.op("A", "activation", vc, vc, AF.Sqrt, bias=col(44), inc="A")
            w.wait("V", "A", w.c["A"])
            w.nop("V")
            w.op("V", "reciprocal", rc, vc)
            w.nop("V")
            w.op("V", "tensor_scalar", xnn, xslot[sl], mc, rc, ALU.subtract,
                 ALU.mult, inc="V")
            w.wait("T", "V", w.c["V"])
            for q in range(4):
                w.wait("T", "V", w.c["V"])
                for j in range(4):
                    d = q * 4 + j
                    w.op("T", "transpose", pbt[:, j * 128:(j + 1) * 128],
                         xnn[:, d * 128:(d + 1) * 128], IDB,
                         inc=("T" if j == 3 else None))
                w.wait("V", "T", w.c["T"])
                w.op("V", "tensor_copy",
                     xnTv[:, q * 4:(q + 1) * 4, tt * 128:(tt + 1) * 128],
                     pbt[:, 0:512].rearrange("p (a c) -> p a c", a=4), inc="V")
        return xnT

    def build_xq(xnT):
        zap = ARW[:, 0:2048]
        z1s = fview(ARW, 6144, 128, 8192)
        mx = fview(ARW, 4096, 128, 512)
        xnv = xnT.rearrange("p (a c) -> p a c", a=16)
        for d in range(16):
            w.wait("V", "dS", w.c["dS"])
            for q in range(4):
                bank = pb[q % 2]
                w.wait("T", "V", w.c["V"])
                w.op("T", "matmul", bank[:, 0:512], H128,
                     xnv[:, d, q * 512:(q + 1) * 512], start=True, stop=True, inc="T")
                w.wait("V", "T", w.c["T"])
                w.op("V", "tensor_copy", zap[:, q * 512:(q + 1) * 512], bank[:, 0:512],
                     inc="V")
            w.wait("G", "V", w.c["V"])
            w.op("G", "dma_start", s_za[d * 128:(d + 1) * 128, :], zap, inc="dS", k=16)
        zb = AR2[:, 0:8192]
        zbv = zb.rearrange("p (a c) -> p a c", a=16)
        xqs = AR2[:, 8192:16384]
        za_r = s_za.rearrange("(dh d3 dl) t -> (dh d3) dl t", dh=16, d3=8, dl=16)
        xq_w = s_xq.rearrange("(dh d3 dl) t -> (dh d3) dl t", dh=16, d3=8, dl=16)
        for tc in range(4):
            w.wait("S", "dS", w.c["dS"])
            w.wait("S", "T", w.c["T"])
            w.op("S", "dma_start", zbv, za_r[:, :, tc * 512:(tc + 1) * 512],
                 inc="dS", k=16)
            w.wait("T", "dS", w.c["dS"])
            for k in range(16):
                bank = pb[2 + (k % 2)]
                w.wait("T", "V", w.c["V"])
                w.op("T", "matmul", bank[:, 0:512], HB16, zb[:, k * 512:(k + 1) * 512],
                     start=True, stop=True, inc="T")
                w.wait("V", "T", w.c["T"])
                w.op("V", "tensor_copy", z1s[:, k * 512:(k + 1) * 512], bank[:, 0:512],
                     inc="V")
            w.op("V", "tensor_reduce", mx,
                 z1s.rearrange("p (a c) -> p c a", a=16), AX.X, ALU.max,
                 apply_absolute_value=True, inc="V")
            w.wait("T", "V", w.c["V"])
            for j in range(4):
                w.op("T", "transpose", pb[4][:, j * 128:(j + 1) * 128],
                     mx[:, j * 128:(j + 1) * 128], IDF, inc=("T" if j == 3 else None))
            w.wait("V", "T", w.c["T"])
            for j in range(4):
                w.op("V", "reduce_max", M1C[:, tc * 4 + j: tc * 4 + j + 1],
                     pb[4][:, j * 128:(j + 1) * 128], AX.X)
            mm4 = M1C[:, tc * 4: tc * 4 + 4]
            w.nop("V")
            w.op("V", "tensor_scalar", mm4, mm4, float(np.sqrt(2048.0) * 1e-5), 0.0,
                 ALU.max, ALU.add)
            w.nop("V")
            w.op("V", "reciprocal", C14, mm4)
            w.nop("V")
            w.op("V", "tensor_scalar", C14, C14, 127.0, 0.0, ALU.mult, ALU.add)
            w.op("V", "tensor_scalar", CI4, mm4, float(1.0 / (127.0 * np.sqrt(2048.0))),
                 0.0, ALU.mult, ALU.add, inc="V")
            w.wait("T", "V", w.c["V"])
            w.nop("T")
            for j in range(4):
                w.op("T", "transpose", pb[4][0:1, j * 128:(j + 1) * 128],
                     C14[:, j:j + 1], IDF)
                w.op("T", "transpose", pb[6][0:1, j * 128:(j + 1) * 128],
                     CI4[:, j:j + 1], IDF, inc=("T" if j == 3 else None))
            w.wait("V", "T", w.c["T"])
            w.op("V", "tensor_copy", CRW1, pb[4][0:1, 0:512])
            w.op("V", "tensor_copy", CRW2, pb[6][0:1, 0:512], inc="V")
            w.wait("T", "V", w.c["V"])
            w.nop("T")
            for j in range(4):
                w.op("T", "matmul", pb[5][:, j * 128:(j + 1) * 128], ONF,
                     CRW1[0:1, j * 128:(j + 1) * 128], start=True, stop=True)
                w.op("T", "matmul", pb[6][:, j * 128:(j + 1) * 128], ONF,
                     CRW2[0:1, j * 128:(j + 1) * 128], start=True, stop=True,
                     inc=("T" if j == 3 else None))
            w.wait("V", "T", w.c["T"])
            w.wait("V", "dS", w.c["dS"])
            z3 = z1s.rearrange("p (a c) -> p a c", a=16)
            xq3 = xqs.rearrange("p (a c) -> p a c", a=16)
            for j in range(4):
                w.op("V", "tensor_tensor", z3[:, :, j * 128:(j + 1) * 128],
                     z3[:, :, j * 128:(j + 1) * 128],
                     pb[5][:, j * 128:(j + 1) * 128].unsqueeze(1)
                     .broadcast_to([128, 16, 128]), ALU.mult)
            w.op("V", "tensor_scalar", z1s, z1s, RND, -RND, ALU.add, ALU.add)
            for j in range(4):
                w.op("V", "tensor_tensor", xq3[:, :, j * 128:(j + 1) * 128],
                     z3[:, :, j * 128:(j + 1) * 128],
                     pb[6][:, j * 128:(j + 1) * 128].unsqueeze(1)
                     .broadcast_to([128, 16, 128]), ALU.mult, inc=("V" if j == 3
                                                                   else None))
            w.wait("G", "V", w.c["V"])
            w.op("G", "dma_start", xq_w[:, :, tc * 512:(tc + 1) * 512],
                 xqs.rearrange("p (a c) -> p a c", a=16), inc="dS", k=16)
        xq = AR2[:, 0:32768]
        w.wait("S", "dS", w.c["dS"])
        xqv = xq.rearrange("p (a c) -> p a c", a=16)
        for k in range(16):
            w.op("S", "dma_start", xqv[:, k, :], s_xq[k * 128:(k + 1) * 128, :],
                 inc="dS", k=16)
        return xq

    def ffn_f1(xq):
        wslot = [ARW[:, 0:4096], ARW[:, 4096:8192]]
        silg = [ARW[:, 8192 + i * 512: 8192 + (i + 1) * 512] for i in range(2)]
        y1b = [ARW[:, 9216 + i * 512: 9216 + (i + 1) * 512] for i in range(2)]
        yab = [ARW[:, 10240 + i * 512: 10240 + (i + 1) * 512] for i in range(2)]
        xqv = xq.rearrange("p (a c) -> p a c", a=16)
        t_mm = {}
        t_sa = {}
        a_done = {}
        v_y1 = {}
        v_evac = {}
        s_store = {}
        t_wcons = {}

        def emit_stage_a(jt):
            pa = pb[4 + (jt % 2)]
            w.wait("T", "V", v_y1[jt])
            if jt >= 2:
                w.wait("T", "V", v_evac[jt - 2])
            t_sa[jt] = w.op("T", "matmul", pa[:, 0:512], H128, y1b[jt % 2],
                            start=True, stop=True, inc="T")
            w.wait("V", "T", t_sa[jt])
            if jt >= 2:
                w.wait("V", "dS", s_store[jt - 2])
            v_evac[jt] = w.op("V", "tensor_copy", yab[jt % 2], pa[:, 0:512], inc="V")
            w.wait("G", "V", v_evac[jt])
            o_, tc_ = jt // 4, jt % 4
            s_store[jt] = w.op("G", "dma_start",
                               s_ya[o_ * 128:(o_ + 1) * 128, tc_ * 512:(tc_ + 1) * 512],
                               yab[jt % 2], inc="dS", k=16)

        for o in range(64):
            sl = o % 2
            if o >= 2:
                w.wait("S", "T", t_wcons[o - 2])
            w.op("S", "dma_start",
                 wslot[sl][:, 0:2048].rearrange("p (k c) -> p k c", k=16),
                 s_wgT[:, o * 128:(o + 1) * 128]
                 .rearrange("(k p) c -> p k c", k=16), inc="dW", k=16)
            w.op("S", "dma_start",
                 wslot[sl][:, 2048:4096].rearrange("p (k c) -> p k c", k=16),
                 s_wuT[:, o * 128:(o + 1) * 128]
                 .rearrange("(k p) c -> p k c", k=16), inc="dW", k=16)
            wz = w.c["dW"]
            for tc in range(4):
                it = o * 4 + tc
                pg = pb[0 + (tc % 2)]
                pu = pb[2 + (tc % 2)]
                if tc == 0:
                    w.wait("T", "dW", wz)
                if it >= 2:
                    w.wait("T", "A", a_done[it - 2])
                    w.wait("T", "V", v_y1[it - 2])
                for k in range(16):
                    w.op("T", "matmul", pg[:, 0:512],
                         wslot[sl][:, k * 128:(k + 1) * 128],
                         xqv[:, k, tc * 512:(tc + 1) * 512], start=(k == 0),
                         stop=(k == 15))
                    w.op("T", "matmul", pu[:, 0:512],
                         wslot[sl][:, 2048 + k * 128: 2048 + (k + 1) * 128],
                         xqv[:, k, tc * 512:(tc + 1) * 512], start=(k == 0),
                         stop=(k == 15), inc=("T" if k == 15 else None))
                t_mm[it] = w.c["T"]
                w.wait("A", "T", t_mm[it])
                if it >= 2:
                    w.wait("A", "V", v_y1[it - 2])
                a_done[it] = w.op("A", "activation", silg[tc % 2], pg[:, 0:512],
                                  AF.Silu, scale=SCL[:, 0:1], inc="A")
                w.wait("V", "A", a_done[it])
                v_y1[it] = w.op("V", "tensor_tensor", y1b[tc % 2], silg[tc % 2],
                                pu[:, 0:512], ALU.mult, inc="V")
                if it >= 1:
                    emit_stage_a(it - 1)
            t_wcons[o] = w.c["T"]
        emit_stage_a(255)

    def ffn_f2a():
        zbs = [AR2[:, 0:8192], SST[:, 0:8192]]
        z2s = [fview(ARW, 0, 128, 8192), fview(AR2, 32768, 128, 8192)]
        r2b = AR2[:, 8192:16384]
        r2v = r2b.rearrange("p (a c) -> p a c", a=64)
        mxts = [MXT, smf[:, 2784:2912]]
        c2cs = [col(30), col(31)]
        c2rows = [C2ROW, smf[0:1, 0:128]]
        ya_r = s_ya.rearrange("(oh b6 ol) t -> (oh b6) ol t", oh=64, b6=2, ol=64)
        r2_w = s_r2.rearrange("(oh b6 ol) t -> (oh b6) ol t", oh=64, b6=2, ol=64)
        t_zb = {}
        q_done = {}
        qa_done = {}
        ld_done = {}

        def mm_part(tc):
            zb = zbs[tc % 2]
            zbv = zb.rearrange("p (a c) -> p a c", a=64)
            zs = z2s[tc % 2]
            if tc >= 2:
                w.wait("S", "T", t_zb[tc - 2])
            for part in range(4):
                w.op("S", "dma_start", zbv[:, part * 16:(part + 1) * 16, :],
                     ya_r[:, part * 16:(part + 1) * 16, tc * 128:(tc + 1) * 128],
                     inc="dS", k=16)
            ld_done[tc] = w.c["dS"]
            w.wait("T", "dS", ld_done[tc])
            if tc >= 2:
                w.wait("T", "V", q_done[tc - 2])
                w.wait("T", "A", qa_done.get(tc - 2, 0))
            for k in range(16):
                bank = pb[(tc % 2) * 2 + (k % 2)]
                if k >= 2:
                    w.wait("T", "V", w.c["V"])
                    w.wait("T", "A", w.c["A"])
                for j in range(4):
                    w.op("T", "matmul", bank[:, j * 128:(j + 1) * 128], HB64,
                         zb[:, (k * 4 + j) * 128:(k * 4 + j + 1) * 128],
                         start=True, stop=True, inc=("T" if j == 3 else None))
                ee = "V" if k % 2 == 0 else "A"
                w.wait(ee, "T", w.c["T"])
                if ee == "V":
                    w.op("V", "tensor_copy", zs[:, k * 512:(k + 1) * 512],
                         bank[:, 0:512], inc="V")
                else:
                    w.op("A", "copy", zs[:, k * 512:(k + 1) * 512], bank[:, 0:512],
                         inc="A")
            t_zb[tc] = w.c["T"]

        def quant_part(tc):
            zs = z2s[tc % 2]
            z3 = zs.rearrange("p (a c) -> p a c", a=64)
            mxt = mxts[tc % 2]
            c2c = c2cs[tc % 2]
            c2r = c2rows[tc % 2]
            tb = pb[4 + (tc % 2)]
            zt = zs.rearrange("p (a c) -> p c a", a=64)
            w.wait("V", "A", w.c["A"])
            w.op("V", "tensor_reduce", mxt, zt, AX.X, ALU.max,
                 apply_absolute_value=True, inc="V")
            w.wait("T", "V", w.c["V"])
            w.op("T", "transpose", tb[:, 0:128], mxt, IDF, inc="T")
            w.wait("V", "T", w.c["T"])
            mcol = M2C[:, tc:tc + 1]
            w.op("V", "reduce_max", mcol, tb[:, 0:128], AX.X)
            w.nop("V")
            w.op("V", "tensor_tensor", mcol, mcol, SCL[:, 3:4], ALU.max)
            w.nop("V")
            w.op("V", "reciprocal", c2c, mcol)
            w.nop("V")
            w.op("V", "tensor_scalar", c2c, c2c, 127.0, 0.0, ALU.mult, ALU.add,
                 inc="V")
            w.wait("T", "V", w.c["V"])
            w.nop("T")
            w.op("T", "transpose", tb[0:1, 0:128], c2c, IDF, inc="T")
            w.wait("V", "T", w.c["T"])
            w.op("V", "tensor_copy", c2r, tb[0:1, 0:128], inc="V")
            w.wait("T", "V", w.c["V"])
            w.nop("T")
            w.op("T", "matmul", tb[:, 128:256], ONF, c2r, start=True, stop=True,
                 inc="T")
            w.wait("V", "T", w.c["T"])
            w.wait("V", "dS", w.c["dS"])
            w.op("V", "tensor_tensor", z3, z3,
                 tb[:, 128:256].unsqueeze(1).broadcast_to([128, 64, 128]), ALU.mult)
            w.op("V", "tensor_scalar", r2v, z3, RND, -RND, ALU.add, ALU.add, inc="V")
            q_done[tc] = w.c["V"]
            qa_done[tc] = w.c["A"]
            w.wait("G", "V", q_done[tc])
            for part in range(4):
                w.op("G", "dma_start",
                     r2_w[:, part * 16:(part + 1) * 16, tc * 128:(tc + 1) * 128],
                     r2v[:, part * 16:(part + 1) * 16, :], inc="dS", k=16)

        for tc in range(16):
            mm_part(tc)
            if tc >= 1:
                quant_part(tc - 1)
        quant_part(15)
        w.nop("V")
        w.op("V", "tensor_tensor", M2Q, M2C, SCL[:, 2:3].broadcast_to([128, 16]),
             ALU.mult, inc="V")

    def dt_family():
        dtspb = ARW[:, 12294:14342][0:64, :]
        sdch = ARW[:, 14342:14470][0:64, :]
        dsch = ARW[:, 14470:14598][0:64, :]
        T1 = fview(ARW, 31744, 64, 2048)
        ABR2 = T1
        w.wait("A", "V", w.c["V"])
        w.op("A", "activation", T1, DTA, AF.Abs)
        w.op("A", "activation", T1, T1, AF.Exp, scale=col(45)[0:64, :])
        w.op("A", "activation", T1, T1, AF.Ln, bias=1.0, inc="A")
        w.wait("V", "A", w.c["A"])
        w.op("V", "tensor_scalar", DTA, DTA, 0.0, 0.0, ALU.max, ALU.add)
        w.op("V", "tensor_tensor", DTA, DTA, T1, ALU.add, inc="V")
        w.wait("A", "V", w.c["V"])
        w.op("A", "copy", dtspb, DTA, inc="A")
        w.wait("T", "A", w.c["A"])
        for c in range(NCH):
            if c % 4 == 0:
                w.wait("T", "V", w.c["V"])
            w.op("T", "transpose", pbt[:, (c % 4) * 128:(c % 4) * 128 + 64],
                 dtspb[:, c * 128:(c + 1) * 128], IDB[0:64, 0:64],
                 inc=("T" if c % 4 == 3 else None))
            if c % 4 == 3:
                w.wait("V", "T", w.c["T"])
                w.op("V", "tensor_copy",
                     DTT[:, (c - 3) * 64:(c + 1) * 64].rearrange(
                         "p (a c) -> p a c", a=4),
                     pbt[:, 0:512].rearrange("p (a c) -> p a c", a=4)[:, :, 0:64],
                     inc="V")
        w.wait("A", "dS", w.c["dS"])
        w.wait("A", "V", w.c["V"])
        w.op("A", "activation", col(41)[0:64, :], col(41)[0:64, :], AF.Exp,
             inc="A")
        w.wait("V", "A", w.c["A"])
        w.nop("V")
        w.op("V", "tensor_scalar", col(41)[0:64, :], col(41)[0:64, :], -1.0, 0.0,
             ALU.mult, ALU.add)
        w.nop("V")
        w.op("V", "tensor_scalar", ABR2, DTA, col(41)[0:64, :], 0.0, ALU.mult,
             ALU.add, inc="V")
        zc = col(43)
        w.op("V", "memset", zc, 0.0)
        w.nop("V")
        w.op("V", "engine_nop", inc="V")
        for c in range(NCH):
            w.op("V", "tensor_tensor_scan", ACS, ABR2[:, c * 128:(c + 1) * 128],
                 zc[0:64, :].broadcast_to([64, 128]), 0.0, ALU.add, ALU.add, inc="V")
            w.wait("A", "V", w.c["V"])
            w.nop("A")
            w.op("A", "activation", sdch, ACS, AF.Exp)
            w.op("A", "activation", dsch, ACS, AF.Exp, scale=col(45)[0:64, :],
                 bias=ACS[:, 127:128], inc="A")
            w.wait("T", "A", w.c["A"])
            w.nop("T")
            w.wait("T", "V", w.c["V"])
            w.op("T", "transpose", pbt[:, 0:64], sdch, IDB[0:64, 0:64])
            w.op("T", "transpose", pbt[:, 128:192], dsch, IDB[0:64, 0:64], inc="T")
            w.op("T", "transpose", pbt[0:1, 256:320], sdch[:, 127:128],
                 IDB[0:64, 0:64], inc="T")
            w.wait("V", "T", w.c["T"])
            w.op("V", "tensor_copy", SDT[:, c * 64:(c + 1) * 64], pbt[:, 0:64])
            w.op("V", "tensor_copy", DST[:, c * 64:(c + 1) * 64], pbt[:, 128:192])
            w.op("V", "tensor_copy", CDROW, pbt[0:1, 256:320], inc="V")
            w.wait("T", "V", w.c["V"])
            w.nop("T")
            w.op("T", "matmul", pb[3][:, 0:64], ONB, CDROW, start=True, stop=True,
                 inc="T")
            w.wait("V", "T", w.c["T"])
            w.op("V", "tensor_copy", CDB[:, c * 64:(c + 1) * 64], pb[3][:, 0:64],
                 inc="V")
        w.wait("A", "V", w.c["V"])
        w.op("A", "activation", RTEN, ABR2, AF.Exp, inc="A")
        w.wait("V", "A", w.c["A"])
        for c in range(NCH):
            w.op("V", "memset", RTEN[:, c * 128: c * 128 + 1], 0.0,
                 inc=("V" if c == NCH - 1 else None))
        w.wait("T", "V", w.c["V"])
        w.op("T", "transpose", pb[3][0:1, 0:128], col(42), IDF, inc="T")
        w.wait("V", "T", w.c["T"])
        w.op("V", "tensor_copy", C1R[0:1, 0:128], pb[3][0:1, 0:128], inc="V")
        w.wait("T", "V", w.c["V"])
        w.nop("T")
        w.op("T", "matmul", pb[3][:, 0:64], ONF, C1R[0:1, 0:64], start=True,
             stop=True, inc="T")
        w.wait("V", "T", w.c["T"])
        w.op("V", "tensor_copy", DMAT, pb[3][:, 0:64], inc="V")

    def mixer_m1(xnT):
        wslot = [ARW[:, 0:4096], ARW[:, 4096:8192]]
        cvin = [ARW[:, 8192:10243], ARW[:, 10243:12294]]
        hout = ARW[:, 12294:14342]
        gpiece = ARW[:, 14342:14854]
        xnv = xnT.rearrange("p (a c) -> p a c", a=16)
        w.op("S", "dma_start", col(40)[0:64, :],
             dt_bias.rearrange("(a b) -> a b", b=1), inc="dS", k=16)
        w.op("S", "dma_start", col(41)[0:64, :],
             a_log.rearrange("(a b) -> a b", b=1), inc="dS", k=16)
        w.op("S", "dma_start", col(42)[0:64, :],
             dvec.rearrange("(a b) -> a b", b=1), inc="dS", k=16)
        w.op("V", "memset", cvin[0][:, 0:3], 0.0)
        w.op("V", "memset", cvin[1][:, 0:3], 0.0, inc="V")

        groups = [("dt", [80])]
        groups += [("hbc", list(range(32 + 2 * i, 34 + 2 * i))) for i in range(24)]
        groups += [("gate", list(range(2 * i, 2 * i + 2))) for i in range(16)]

        t_after = {}
        ev = {}
        cnt = 0
        hout2 = [hout, ARW[:, 14854:16902]]
        gp2 = [gpiece, ARW[:, 16902:17414]]
        hb_done = {}
        gp_done = {}
        cv_done = {}
        sl_done = {}
        hi = 0
        gpi = 0
        for gi, (kind, js) in enumerate(groups):
            sl = gi % 2
            j0 = js[0]
            rows = 64 if kind == "dt" else 256
            if gi >= 2:
                w.wait("S", "T", t_after[gi - 2])
            w.op("S", "dma_start",
                 wslot[sl][:, 0:4096].rearrange("p (k c) -> p k c", k=16)[:, :, 0:rows],
                 s_winT[:, j0 * 128: j0 * 128 + rows]
                 .rearrange("(k p) c -> p k c", k=16), inc="dW", k=16)
            w.wait("T", "dW", w.c["dW"])
            for jj, j in enumerate(js):
                mrows = 64 if kind == "dt" else 128
                for tc in range(4):
                    bank = pb[cnt % 2]
                    if cnt >= 2:
                        eng, val = ev[cnt - 2]
                        w.wait("T", eng, val)
                    for k in range(16):
                        w.op("T", "matmul", bank[0:mrows, 0:512],
                             wslot[sl][:, k * 256 + jj * 128: k * 256 + jj * 128 + mrows],
                             xnv[:, k, tc * 512:(tc + 1) * 512],
                             start=(k == 0), stop=(k == 15),
                             inc=("T" if k == 15 else None))
                    tmm = w.c["T"]
                    if kind == "dt":
                        w.wait("V", "T", tmm)
                        w.wait("V", "dS", w.c["dS"])
                        ev[cnt] = ("V", w.op("V", "tensor_scalar",
                                             DTA[:, tc * 512:(tc + 1) * 512],
                                             bank[0:64, 0:512], col(40)[0:64, :], 0.0,
                                             ALU.add, ALU.add, inc="V"))
                    elif kind == "hbc":
                        w.wait("A", "T", tmm)
                        if hi >= 2 and tc == 0:
                            w.wait("A", "V", cv_done[hi - 2])
                        ev[cnt] = ("A", w.op("A", "copy",
                                             cvin[jj % 2][:, 3 + tc * 512: 3 + (tc + 1) * 512],
                                             bank[:, 0:512], inc="A"))
                    else:
                        w.wait("A", "T", tmm)
                        if gpi >= 2:
                            w.wait("A", "dS", gp_done[gpi - 2])
                        ev[cnt] = ("A", w.op("A", "copy", gp2[gpi % 2],
                                             bank[:, 0:512], inc="A"))
                        w.wait("S", "A", w.c["A"])
                        gp_done[gpi] = w.op("S", "dma_start",
                                            s_gate[j * 128:(j + 1) * 128,
                                                   tc * 512:(tc + 1) * 512],
                                            gp2[gpi % 2], inc="dS", k=16)
                        gpi += 1
                    cnt += 1
                if kind == "hbc":
                    ci = j - 32
                    cwsp = CW0 if jj % 2 == 0 else CW1
                    cbsp = CB0 if jj % 2 == 0 else CB1
                    if hi >= 2:
                        w.wait("G", "V", cv_done[hi - 2])
                        w.wait("G", "A", sl_done[hi - 2])
                    w.op("G", "dma_start", cwsp, conv_w[ci * 128:(ci + 1) * 128, :],
                         inc="dG", k=16)
                    w.op("G", "dma_start", cbsp,
                         conv_b[ci * 128:(ci + 1) * 128].rearrange("(a b) -> a b", b=1),
                         inc="dG", k=16)
                    w.wait("V", "dG", w.c["dG"])
                    w.wait("V", "A", w.c["A"])
                    if hi >= 2:
                        w.wait("V", "dS", hb_done[hi - 2])
                    src_ = cvin[jj % 2]
                    ho = hout2[hi % 2]
                    w.op("V", "tensor_scalar", ho, src_[:, 0:2048], cwsp[:, 0:1],
                         0.0, ALU.mult, ALU.add)
                    for k in range(1, 4):
                        w.op("V", "scalar_tensor_tensor", ho, src_[:, k:2048 + k],
                             cwsp[:, k:k + 1], ho, ALU.mult, ALU.add,
                             inc=("V" if k == 3 else None))
                    cv_done[hi] = w.c["V"]
                    w.wait("A", "V", w.c["V"])
                    sl_done[hi] = w.op("A", "activation", ho, ho, AF.Silu, bias=cbsp,
                                       inc="A")
                    w.wait("S", "A", w.c["A"])
                    hb_done[hi] = w.op("S", "dma_start",
                                       s_hbc[ci * 128:(ci + 1) * 128, :], ho,
                                       inc="dS", k=16)
                    hi += 1
            if kind == "dt":
                dt_family()
            t_after[gi] = w.c["T"]

    def ssm():
        INS = []
        INS.append({"XT": AR2[:, 0:4096], "GT": AR2[:, 4096:8192],
                    "BT": AR2[:, 8192:9216], "BCF": AR2[:, 9216:11264]})
        INS.append({"XT": AR1[:, 9216:13312], "GT": AR1[:, 13312:17408],
                    "BT": AR1[:, 17408:18432], "BCF": AR1[:, 18432:20480]})
        XDT = AR2[:, 11264:15360]
        XDD = AR2[:, 15360:19456]
        SILG = AR2[:, 19456:23552]
        GGB = AR2[:, 23552:27648]
        ONT = AR2[:, 27648:31744]
        LMK = [AR1[:, 0:1024], AR1[:, 2048:3072]]
        MPR = [AR1[:, 1024:2048], AR1[:, 3072:4096]]
        YTMP = AR1[:, 4096:4608]
        Y2B = AR1[:, 4608:5120]
        XDB = AR1[:, 5120:9216]
        SQS = fview(ARW, 0, 128, 4096)
        w.op("V", "memset", SP0, 0.0, inc="V")

        t1d = {}
        t2d = {}
        v1d = {}
        vscan = {}
        v2e = {}
        v3d = {}
        ldd = {}
        vpro = {}
        asil = {}

        def load_chunk(c):
            sd = INS[c % 2]
            if c >= 2:
                w.wait("S", "T", t2d.get((c - 1) * 8 + 7, 0))
                w.wait("S", "V", vpro.get(c - 1, 0))
                w.wait("S", "A", asil.get(c - 1, 0))
            w.op("S", "dma_start_transpose", sd["XT"],
                 s_hbc[0:4096, c * 128:(c + 1) * 128], inc="dS", k=16)
            w.op("S", "dma_start_transpose", sd["GT"],
                 s_gate[0:4096, c * 128:(c + 1) * 128], inc="dS", k=16)
            w.op("S", "dma_start_transpose", sd["BT"],
                 s_hbc[4096:5120, c * 128:(c + 1) * 128], inc="dS", k=16)
            w.op("S", "dma_start", sd["BCF"].rearrange("p (a c) -> p a c", a=16),
                 s_hbc[4096:6144, c * 128:(c + 1) * 128]
                 .rearrange("(a p) c -> p a c", a=16), inc="dS", k=16)
            ldd[c] = w.c["dS"]

        def t1(c, g):
            gi = c * 8 + g
            sd = INS[c % 2]
            if gi >= 2:
                w.wait("T", "V", v1d.get(gi - 2, 0))
            if gi >= 1:
                w.wait("T", "V", vscan.get(gi - 1, 0))
            if g == 0:
                w.wait("T", "dS", ldd[c])
            w.op("T", "matmul", pb[0][:, (g % 2) * 128:(g % 2) * 128 + 128],
                 sd["BCF"][:, g * 128:(g + 1) * 128],
                 sd["BCF"][:, 1024 + g * 128: 1024 + (g + 1) * 128], start=True,
                 stop=True)
            for i in range(8):
                bank = pb[1] if i < 4 else pb[2]
                w.op("T", "matmul", bank[:, (i % 4) * 128:(i % 4 + 1) * 128],
                     IDF[0:64, 8 * g + i: 8 * g + i + 1].broadcast_to([64, 128]),
                     RTEN[:, c * 128:(c + 1) * 128],
                     start=True, stop=True, inc=("T" if i == 7 else None))
            t1d[gi] = w.c["T"]

        def v1(c, g):
            gi = c * 8 + g
            w.wait("V", "T", t1d[gi])
            if gi >= 2:
                w.wait("V", "T", t2d.get(gi - 2, 0))
            lm = LMK[gi % 2]
            mp = MPR[gi % 2]
            for half in range(2):
                w.op("V", "tensor_tensor_scan", lm[:, half * 512:(half + 1) * 512],
                     pb[1 + half][:, 0:512], IR4, 0.0, ALU.mult, ALU.add,
                     inc=("V" if half == 1 else None))
            vscan[gi] = w.c["V"]
            for half in range(2):
                w.op("V", "tensor_tensor",
                     mp[:, half * 512:(half + 1) * 512]
                     .rearrange("p (a c) -> p a c", a=4),
                     pb[0][:, (g % 2) * 128:(g % 2) * 128 + 128].unsqueeze(1)
                     .broadcast_to([128, 4, 128]),
                     lm[:, half * 512:(half + 1) * 512]
                     .rearrange("p (a c) -> p a c", a=4), ALU.mult,
                     inc=("V" if half == 1 else None))
            v1d[gi] = w.c["V"]

        def t2(c, g):
            gi = c * 8 + g
            sd = INS[c % 2]
            Sold = SP0 if c % 2 == 0 else SP1
            mp = MPR[gi % 2]
            w.wait("T", "V", v1d[gi])
            if gi >= 1:
                w.wait("T", "V", v2e.get(gi - 1, 0))
            for i in range(8):
                w.op("T", "matmul", pb[3][:, i * 64:(i + 1) * 64],
                     mp[:, i * 128:(i + 1) * 128],
                     XDT[:, (8 * g + i) * 64:(8 * g + i + 1) * 64],
                     start=True, stop=True)
            w.op("T", "matmul", pb[4][:, 0:512], sd["BT"][:, g * 128:(g + 1) * 128],
                 XDD[:, g * 512:(g + 1) * 512], start=True, stop=True)
            w.op("T", "matmul", pb[5][:, 0:512],
                 sd["BCF"][:, 1024 + g * 128:1024 + (g + 1) * 128],
                 Sold[:, g * 512:(g + 1) * 512], start=True, stop=True, inc="T")
            t2d[gi] = w.c["T"]

        def v2(c, g):
            gi = c * 8 + g
            Sold = SP0 if c % 2 == 0 else SP1
            Snew = SP1 if c % 2 == 0 else SP0
            w.wait("V", "T", t2d[gi])
            w.op("V", "tensor_tensor",
                 Snew[:, g * 512:(g + 1) * 512].rearrange("p (a c) -> p a c", a=8),
                 Sold[:, g * 512:(g + 1) * 512].rearrange("p (a c) -> p a c", a=8),
                 CDB[:, c * 64 + g * 8: c * 64 + (g + 1) * 8].unsqueeze(2)
                 .broadcast_to([128, 8, 64]), ALU.mult)
            w.op("V", "tensor_tensor", Snew[:, g * 512:(g + 1) * 512],
                 Snew[:, g * 512:(g + 1) * 512], pb[4][:, 0:512], ALU.add)
            w.op("V", "tensor_tensor", YTMP.rearrange("p (a c) -> p a c", a=8),
                 pb[5][:, 0:512].rearrange("p (a c) -> p a c", a=8),
                 SDT[:, c * 64 + g * 8: c * 64 + (g + 1) * 8].unsqueeze(2)
                 .broadcast_to([128, 8, 64]), ALU.mult)
            w.op("V", "scalar_tensor_tensor", Y2B, pb[3][:, 0:512], 1.0, YTMP,
                 ALU.mult, ALU.add, inc="V")
            v2e[gi] = w.c["V"]

        def v3(c, g):
            gi = c * 8 + g
            w.op("V", "tensor_tensor", Y2B, Y2B, XDB[:, g * 512:(g + 1) * 512],
                 ALU.add)
            w.op("V", "tensor_tensor", GGB[:, g * 512:(g + 1) * 512], Y2B,
                 SILG[:, g * 512:(g + 1) * 512], ALU.mult, inc="V")
            v3d[gi] = w.c["V"]
            w.wait("A", "V", v3d[gi])
            w.op("A", "activation", SQS[:, g * 512:(g + 1) * 512],
                 GGB[:, g * 512:(g + 1) * 512], AF.Square,
                 accum_out=VARC[:, g:g + 1], inc="A")

        load_chunk(0)
        for c in range(NCH):
            sd = INS[c % 2]
            w.wait("V", "dS", ldd[c])
            if c >= 1:
                w.wait("V", "T", t2d.get((c - 1) * 8 + 7, 0))
                w.wait("V", "A", w.c["A"])
            xtv = sd["XT"].rearrange("p (h q) -> p h q", h=64)
            w.op("V", "tensor_tensor", XDT.rearrange("p (h q) -> p h q", h=64), xtv,
                 DTT[:, c * 64:(c + 1) * 64].unsqueeze(2).broadcast_to([128, 64, 64]),
                 ALU.mult)
            w.op("V", "tensor_tensor", XDD.rearrange("p (h q) -> p h q", h=64),
                 XDT.rearrange("p (h q) -> p h q", h=64),
                 DST[:, c * 64:(c + 1) * 64].unsqueeze(2).broadcast_to([128, 64, 64]),
                 ALU.mult)
            w.op("V", "tensor_tensor", XDB.rearrange("p (h q) -> p h q", h=64), xtv,
                 DMAT.unsqueeze(2).broadcast_to([128, 64, 64]), ALU.mult, inc="V")
            vpro[c] = w.c["V"]
            w.wait("A", "dS", ldd[c])
            w.wait("A", "V", w.c["V"])
            asil[c] = w.op("A", "activation", SILG, sd["GT"], AF.Silu, inc="A")
            if c + 1 < NCH:
                load_chunk(c + 1)
            for g in range(8):
                t1(c, g)
                v1(c, g)
                t2(c, g)
                v2(c, g)
                v3(c, g)
            w.wait("V", "A", w.c["A"])
            w.op("V", "tensor_scalar", VARC, VARC, 1.0 / 512.0, 0.0, ALU.mult,
                 ALU.add, inc="V")
            w.wait("A", "V", w.c["V"])
            w.nop("A")
            w.op("A", "activation", VARC, VARC, AF.Sqrt, bias=col(44), inc="A")
            w.wait("V", "A", w.c["A"])
            w.nop("V")
            w.op("V", "reciprocal", RSTC, VARC)
            w.nop("V")
            w.wait("V", "dS", w.c["dS"])
            w.op("V", "tensor_tensor", ONT.rearrange("p (a c) -> p a c", a=8),
                 GGB.rearrange("p (a c) -> p a c", a=8),
                 RSTC.unsqueeze(2).broadcast_to([128, 8, 512]), ALU.mult, inc="V")
            w.wait("S", "V", w.c["V"])
            w.op("S", "dma_start", s_on[c * 128:(c + 1) * 128, :], ONT, inc="dS",
                 k=16)

    def wout_phase():
        w.op("S", "dma_start", fview(ARW, 0, 32, 128),
             norm_w.rearrange("(a b) -> a b", b=128), inc="dS", k=16)
        w.wait("T", "dS", w.c["dS"])
        w.op("T", "transpose", pb[0][:, 0:32], fview(ARW, 0, 32, 128),
             IDF[0:32, 0:32], inc="T")
        w.wait("V", "T", w.c["T"])
        w.op("V", "tensor_copy", NWC, pb[0][:, 0:32])
        w.nop("V")
        w.op("V", "engine_nop", inc="V")

        onfm = AR1[:, 0:16384]
        onv = onfm.rearrange("p (a c) -> p a c", a=32)
        wc = [AR2[:, 0:8192], AR2[:, 8192:16384]]
        xsl = [fview(ARW, 4096 + i * 1024, 128, 256) for i in range(4)]
        opiece = [fview(ARW, 8192 + i * 1024, 128, 256) for i in range(4)]
        t_on = {}
        t_wc = {}
        v_evac = {}
        o_dma = {}
        v_scale = {}
        for tq in range(4):
            if tq >= 1:
                w.wait("S", "T", t_on[tq - 1])
                w.wait("S", "V", v_scale[tq - 1])
            for i in range(32):
                w.op("S", "dma_start_transpose", onv[:, i, :],
                     s_on[tq * 512:(tq + 1) * 512, i * 128:(i + 1) * 128],
                     inc="dS", k=16)
            w.wait("V", "dS", w.c["dS"])
            for i in range(32):
                w.op("V", "tensor_scalar", onv[:, i, :], onv[:, i, :],
                     NWC[:, i:i + 1], 0.0, ALU.mult, ALU.add,
                     inc=("V" if i == 31 else None))
            v_scale[tq] = w.c["V"]
            for dg in range(8):
                it = tq * 8 + dg
                sl = dg % 2
                if it >= 2:
                    w.wait("S", "T", t_wc[it - 2])
                for half in range(2):
                    w.op("S", "dma_start",
                         wc[sl].rearrange("p (a c) -> p a c", a=32)
                         [:, half * 16:(half + 1) * 16, :],
                         s_woutT[:, dg * 256:(dg + 1) * 256]
                         .rearrange("(k p) c -> p k c", k=32)
                         [:, half * 16:(half + 1) * 16, :], inc="dW", k=16)
                w.wait("T", "dW", w.c["dW"])
                w.wait("T", "V", v_scale[tq])
                wv = wc[sl].rearrange("p (a c) -> p a c", a=32)
                for ts in range(4):
                    pidx = (tq, dg, ts)
                    tsg = tq * 4 + ts
                    bank = pb[ts % 4]
                    prevv = (tq, dg - 1, ts) if dg >= 1 else ((tq - 1, 7, ts) if tq else None)
                    if prevv in o_dma:
                        w.wait("S", "dO", o_dma[prevv])
                    w.op("S", "dma_start", xsl[ts],
                         x_in[tsg * 128:(tsg + 1) * 128, dg * 256:(dg + 1) * 256],
                         inc="dS", k=16)
                    x_ld = w.c["dS"]
                    if prevv in v_evac:
                        w.wait("T", "V", v_evac[prevv])
                    for k in range(32):
                        w.op("T", "matmul", bank[:, 0:256],
                             onv[:, k, ts * 128:(ts + 1) * 128], wv[:, k, :],
                             start=(k == 0), stop=(k == 31),
                             inc=("T" if k == 31 else None))
                    w.wait("V", "T", w.c["T"])
                    w.wait("V", "dS", x_ld)
                    if prevv in o_dma:
                        w.wait("V", "dO", o_dma[prevv])
                    v_evac[pidx] = w.op("V", "tensor_tensor", opiece[ts],
                                        bank[:, 0:256], xsl[ts], ALU.add, inc="V")
                    w.wait("S", "V", v_evac[pidx])
                    o_dma[pidx] = w.op("S", "dma_start",
                                       out_hbm[tsg * 128:(tsg + 1) * 128,
                                               dg * 256:(dg + 1) * 256],
                                       opiece[ts], inc="dO", k=16)
                t_wc[it] = w.c["T"]
            t_on[tq] = w.c["T"]

    def down_phase():
        rres = AR1[:, 0:32768]
        rv = rres.rearrange("p (a c) -> p a c", a=64)
        wc = [AR2[:, 0:16384], AR2[:, 16384:32768]]
        opiece = [fview(ARW, i * 1024, 128, 256) for i in range(4)]
        t_r = {}
        t_wc = {}
        v_evac = {}
        o_dma = {}
        for tq in range(4):
            if tq >= 1:
                w.wait("G", "T", t_r[tq - 1])
            for i in range(64):
                w.op("G", "dma_start", rv[:, i, :],
                     s_r2[i * 128:(i + 1) * 128, tq * 512:(tq + 1) * 512],
                     inc="dS", k=16)
            rq_loaded = w.c["dS"]
            for dg in range(8):
                it = tq * 8 + dg
                sl = dg % 2
                if it >= 2:
                    w.wait("S", "T", t_wc[it - 2])
                for half in range(4):
                    w.op("S", "dma_start",
                         wc[sl].rearrange("p (a c) -> p a c", a=64)
                         [:, half * 16:(half + 1) * 16, :],
                         s_wdT[:, dg * 256:(dg + 1) * 256]
                         .rearrange("(k p) c -> p k c", k=64)
                         [:, half * 16:(half + 1) * 16, :], inc="dW", k=16)
                w.wait("T", "dW", w.c["dW"])
                w.wait("T", "dS", rq_loaded)
                wv = wc[sl].rearrange("p (a c) -> p a c", a=64)
                for ts in range(4):
                    pidx = (tq, dg, ts)
                    bank = pb[ts % 4]
                    prev = (tq, dg - 1, ts) if dg >= 1 else ((tq - 1, 7, ts) if tq else None)
                    if prev in v_evac:
                        w.wait("T", "V", v_evac[prev])
                    for k in range(64):
                        w.op("T", "matmul", bank[:, 0:256],
                             rv[:, k, ts * 128:(ts + 1) * 128], wv[:, k, :],
                             start=(k == 0), stop=(k == 63),
                             inc=("T" if k == 63 else None))
                    w.wait("V", "T", w.c["T"])
                    prev2 = (tq, dg - 1, ts) if dg >= 1 else ((tq - 1, 7, ts) if tq else None)
                    if prev2 in o_dma:
                        w.wait("V", "dO", o_dma[prev2])
                    tsg = tq * 4 + ts
                    v_evac[pidx] = w.op("V", "tensor_scalar", opiece[ts], bank[:, 0:256],
                                        M2Q[:, tsg:tsg + 1], 0.0, ALU.mult, ALU.add,
                                        inc="V")
                    w.wait("G", "V", v_evac[pidx])
                    o_dma[pidx] = w.op("G", "dma_start",
                                       out_hbm[tsg * 128:(tsg + 1) * 128,
                                               dg * 256:(dg + 1) * 256],
                                       opiece[ts], accum_op=ALU.add, inc="dO", k=16)
                t_wc[it] = w.c["T"]
            t_r[tq] = w.c["T"]

    load_consts()
    w.barrier()
    weight_prep()
    w.barrier()
    xnT = layernorm()
    w.barrier()
    xq = build_xq(xnT)
    w.barrier()
    ffn_f1(xq)
    w.barrier()
    ffn_f2a()
    w.barrier()
    mixer_m1(xnT)
    w.barrier()
    ssm()
    w.barrier()
    wout_phase()
    w.barrier()
    down_phase()
    w.barrier()

    if DEBUG:
        for nm, src in [("d_za", s_za), ("d_xq", s_xq), ("d_gate", s_gate),
                        ("d_hbc", s_hbc), ("d_on", s_on), ("d_ya", s_ya),
                        ("d_r2", s_r2)]:
            w.op("G", "dma_start", dbg[nm], src, inc="dO", k=16)
        w.op("G", "dma_start", dbg["d_smf"], SMF, inc="dO", k=16)

    w.wait("G", "dO", w.c["dO"])
    w.wait("G", "dS", w.c["dS"])
    w.wait("G", "dG", w.c["dG"])
    w.wait("G", "dW", w.c["dW"])

    with nc.Block() as block:
        @block.tensor
        def _(t):
            for f in w.st["T"]:
                f(t)

        @block.vector
        def _(v):
            for f in w.st["V"]:
                f(v)

        @block.scalar
        def _(a):
            for f in w.st["A"]:
                f(a)

        @block.sync
        def _(s):
            for f in w.st["S"]:
                f(s)

        @block.gpsimd
        def _(g):
            for f in w.st["G"]:
                f(g)

    es.close()
    return nc


def _consts():
    h128 = _hadamard(128)
    hb16 = np.kron(_hadamard(16), np.eye(8, dtype=np.float32))
    hb64 = np.kron(_hadamard(64), np.eye(2, dtype=np.float32))
    i128 = np.eye(128, dtype=np.float32)
    ones_blk = np.zeros((128, 128), np.float32)
    ones_blk[0, :] = 1.0
    cb = np.concatenate([h128, hb16, hb64, i128, ones_blk,
                         np.tile(i128, (1, 4))], axis=1)
    return {
        "cb": cb.astype(ml_dtypes.bfloat16),
        "ones_f": np.ones((1, 128), np.float32),
        "ones_c": np.ones((128, 1), np.float32),
        "id_f": i128,
    }


def kernel(**inp):
    f32 = lambda a: np.ascontiguousarray(np.asarray(a, np.float32))
    x = f32(inp["L0_attn_hidden_states"])
    shared = {
        "w_in": f32(inp["L0_attn_W_in"]),
        "conv_w": f32(inp["L0_attn_conv_w"]),
        "conv_b": f32(inp["L0_attn_conv_b"]),
        "dt_bias": f32(inp["L0_attn_dt_bias"]),
        "a_log": f32(inp["L0_attn_A_log"]),
        "dvec": f32(inp["L0_attn_D"]),
        "norm_w": f32(inp["L0_attn_norm_weight"]),
        "w_out": f32(inp["L0_attn_W_out"]),
        "w_g": f32(inp["L0_ffn_w_gate"]),
        "w_u": f32(inp["L0_ffn_w_up"]),
        "w_d": f32(inp["L0_ffn_w_down"]),
    }
    shared.update(_consts())
    nc = build_program()
    in_maps = [dict(shared, x=x[b]) for b in range(8)]
    res = run_bass_kernel_spmd(nc, in_maps, core_ids=list(range(8)))
    out = np.stack([np.asarray(res.results[b]["out"]) for b in range(8)], axis=0)
    return out.astype(np.float32)


if __name__ == "__main__":
    nc = build_program()
    print("instructions:", nc.n_instructions())


# version 19
# speedup vs baseline: 1.2505x; 1.0616x over previous
import numpy as np
import ml_dtypes
import concourse.bass as bass
import concourse.mybir as mybir
from concourse.bass_utils import run_bass_kernel_spmd
from contextlib import ExitStack

F32 = mybir.dt.float32
BF16 = mybir.dt.bfloat16
AF = mybir.ActivationFunctionType
ALU = mybir.AxisListType
ALU = mybir.AluOpType
AX = mybir.AxisListType

L = 2048
DM = 2048
HH, PP, NNS, GG = 64, 64, 128, 8
INTER = 4096
CONV = 6144
PROJ = 10304
IFF = 8192
CS = 128
NCH = L // CS
KC = 4
RND = 3.0 * 2.0 ** 22
EPS = 1e-5
DEBUG = False

ENGS = "TVASG"


def _hadamard(n):
    h = np.array([[1.0]], np.float32)
    while h.shape[0] < n:
        h = np.block([[h, h], [h, -h]])
    return h


class Wk:
    def __init__(self, nc, semh):
        self.nc = nc
        self.semh = semh
        self.c = {k: 0 for k in semh}
        self.st = {e: [] for e in ENGS}

    def wait(self, eng, sem, val=None):
        v = self.c[sem] if val is None else val
        if v <= 0:
            return
        h = self.semh[sem]
        self.st[eng].append(lambda e, h=h, v=v: e.wait_ge(h, v))

    def op(self, eng, meth, *args, inc=None, k=1, **kw):
        h = None
        kk = 0
        if inc is not None:
            self.c[inc] += k
            h, kk = self.semh[inc], k

        import traceback
        site = traceback.extract_stack(limit=3)[0]

        def f(e, m=meth, a=args, kw=kw, h=h, kk=kk, site=site):
            try:
                ins = getattr(e, m)(*a, **kw)
            except Exception:
                print("FAILING OP:", m, "emitted at", site.filename, site.lineno,
                      [getattr(x, "shape", None) for x in a])
                raise
            if h is not None:
                ins.then_inc(h, kk)

        self.st[eng].append(f)
        return self.c[inc] if inc is not None else None

    def nop(self, eng, cyc=1024):
        self.st[eng].append(lambda e, c=cyc: e.nop(cycle_cnt=c))

    def fence(self, eng, sem):
        self.c[sem] += 1
        h = self.semh[sem]
        self.st[eng].append(lambda e, h=h: e.nop(cycle_cnt=1024).then_inc(h, 1))
        return self.c[sem]

    def barrier(self):
        snap = dict(self.c)
        for e in ENGS:
            for s, v in snap.items():
                if v > 0:
                    self.wait(e, s, v)


def build_program():
    nc = bass.Bass("TRN2", target_bir_lowering=False)
    dt_ = nc.dram_tensor

    x_in = dt_("x", [L, DM], F32, kind="ExternalInput").ap()
    w_in = dt_("w_in", [PROJ, DM], F32, kind="ExternalInput").ap()
    conv_w = dt_("conv_w", [CONV, KC], F32, kind="ExternalInput").ap()
    conv_b = dt_("conv_b", [CONV], F32, kind="ExternalInput").ap()
    dt_bias = dt_("dt_bias", [HH], F32, kind="ExternalInput").ap()
    a_log = dt_("a_log", [HH], F32, kind="ExternalInput").ap()
    dvec = dt_("dvec", [HH], F32, kind="ExternalInput").ap()
    norm_w = dt_("norm_w", [INTER], F32, kind="ExternalInput").ap()
    w_out = dt_("w_out", [DM, INTER], F32, kind="ExternalInput").ap()
    w_g = dt_("w_g", [IFF, DM], F32, kind="ExternalInput").ap()
    w_u = dt_("w_u", [IFF, DM], F32, kind="ExternalInput").ap()
    w_d = dt_("w_d", [DM, IFF], F32, kind="ExternalInput").ap()
    cb_in = dt_("cb", [128, 1152], BF16, kind="ExternalInput").ap()
    onesf_in = dt_("ones_f", [1, 128], F32, kind="ExternalInput").ap()
    onesc_in = dt_("ones_c", [128, 1], F32, kind="ExternalInput").ap()
    idf_in = dt_("id_f", [128, 128], F32, kind="ExternalInput").ap()

    out_hbm = dt_("out", [L, DM], F32, kind="ExternalOutput").ap()

    s_winT = dt_("s_winT", [DM, PROJ], BF16, kind="Internal").ap()
    s_woutT = dt_("s_woutT", [INTER, DM], BF16, kind="Internal").ap()
    s_wgT = dt_("s_wgT", [DM, IFF], BF16, kind="Internal").ap()
    s_wuT = dt_("s_wuT", [DM, IFF], BF16, kind="Internal").ap()
    s_wdT = dt_("s_wdT", [IFF, DM], BF16, kind="Internal").ap()
    s_za = dt_("s_za", [DM, L], BF16, kind="Internal").ap()
    s_xq = dt_("s_xq", [DM, L], BF16, kind="Internal").ap()
    s_gate = dt_("s_gate", [INTER, L], BF16, kind="Internal").ap()
    s_hbc = dt_("s_hbc", [CONV, L], BF16, kind="Internal").ap()
    s_on = dt_("s_on", [L, INTER], BF16, kind="Internal").ap()
    s_ya = dt_("s_ya", [IFF, L], BF16, kind="Internal").ap()
    s_r2 = dt_("s_r2", [IFF, L], BF16, kind="Internal").ap()

    dbg = {}
    if DEBUG:
        for nm, shp in [("d_za", [DM, L]), ("d_xq", [DM, L]), ("d_gate", [INTER, L]),
                        ("d_hbc", [CONV, L]), ("d_on", [L, INTER]), ("d_ya", [IFF, L]),
                        ("d_r2", [IFF, L])]:
            dbg[nm] = dt_(nm, shp, BF16, kind="ExternalOutput").ap()
        dbg["d_smf"] = dt_("d_smf", [128, 2944], F32, kind="ExternalOutput").ap()

    es = ExitStack()

    def sbuf(name, shape, dty):
        return es.enter_context(nc.sbuf_tensor(name, shape, dty)).ap()

    AR1 = sbuf("AR1", [128, 32768], BF16)
    AR2 = sbuf("AR2", [128, 32768], BF16)
    ARW = sbuf("ARW", [128, 19968], BF16)
    SST = sbuf("SST", [128, 8192], BF16)
    SMB = sbuf("SMB", [128, 4224], BF16)
    SMF = sbuf("SMF", [128, 2944], F32)
    CBT = sbuf("CBT", [128, 1152], BF16)
    IDF = sbuf("IDF", [128, 128], F32)
    ONF = sbuf("ONF", [1, 128], F32)
    ONC = sbuf("ONC", [128, 1], F32)

    pb = [es.enter_context(nc.psum_tensor("pb%d" % i, [128, 512], F32)).ap()
          for i in range(6)]
    pbt = es.enter_context(nc.psum_tensor("pbt", [128, 1024], BF16)).ap()
    pbt2 = es.enter_context(nc.psum_tensor("pbt2", [128, 1024], BF16)).ap()

    sems = {}
    for s in ("dS", "dG", "dW", "dO", "T", "V", "A"):
        sems[s] = es.enter_context(nc.semaphore(s))

    w = Wk(nc, sems)

    def fview(arena, byte_off, parts, cols):
        assert byte_off % 4 == 0
        return arena[0:parts, byte_off // 2: byte_off // 2 + 2 * cols].bitcast(F32)

    H128 = CBT[:, 0:128]
    HB16 = CBT[:, 128:256]
    HB64 = CBT[:, 256:384]
    IDB = CBT[:, 384:512]
    ONB = CBT[0:1, 512:640]
    IR4 = CBT[:, 640:1152]

    smf = SMF
    DTA = smf[0:64, 0:2048]
    RTEN = smf[0:64, 0:2048]
    ABR = smf[64:128, 0:2048]
    ACS = smf[0:64, 2048:2176]
    CW0 = smf[:, 2176:2180]
    CW1 = smf[:, 2180:2184]
    CB0 = smf[:, 2184:2185]
    CB1 = smf[:, 2185:2186]
    COLS = smf[:, 2186:2250]
    M1C = smf[:, 2250:2266]
    M2C = smf[:, 2266:2282]
    M2Q = smf[:, 2282:2298]
    C1R = smf[0:16, 2304:2432]
    NWC = smf[:, 2432:2464]
    VARC = smf[:, 2464:2472]
    RSTC = smf[:, 2472:2480]
    SSCAL = smf[0:1, 2480:2496]
    SCL = smf[:, 2496:2520]
    C14 = smf[:, 2520:2524]
    CI4 = smf[:, 2524:2528]
    MXT = smf[:, 2528:2656]
    C2ROW = smf[0:1, 2656:2784]
    CRW1 = smf[0:1, 0:512]
    CRW2 = smf[0:1, 512:1024]

    def col(i):
        return COLS[:, i:i + 1]

    DTT = SMB[:, 0:1024]
    DST = SMB[:, 1024:2048]
    SDT = SMB[:, 2048:3072]
    CDB = SMB[:, 3072:4096]
    DMAT = SMB[:, 4096:4160]
    CDROW = SMB[0:1, 4160:4224]

    SP0 = SST[:, 0:4096]
    SP1 = SST[:, 4096:8192]

    def load_consts():
        w.op("V", "memset", col(44), 1e-5)
        w.op("V", "memset", col(45), -1.0, inc="V")
        w.op("S", "dma_start", CBT, cb_in, inc="dS", k=16)
        w.op("S", "dma_start", IDF, idf_in, inc="dS", k=16)
        w.op("S", "dma_start", ONF, onesf_in, inc="dS", k=16)
        w.op("S", "dma_start", ONC, onesc_in, inc="dS", k=16)

    def weight_prep():
        slot = [fview(ARW, i * 8192, 128, 2048) for i in range(3)]
        tslot = [ARW[:, 12288 + i * 2048: 12288 + (i + 1) * 2048] for i in range(2)]
        asm = [AR1[:, i * 2048:(i + 1) * 2048] for i in range(2)]
        junk = fview(AR2, 0, 128, 2048)
        run, spot = col(0), col(1)

        w.op("V", "memset", run, 0.0, inc="V")
        a_read = {}
        gi = 0
        for si, wm in enumerate((w_g, w_u, w_d)):
            wf = wm if wm.shape[1] == 2048 else wm.rearrange(
                "a (b c) -> (a b) c", c=2048)
            for t in range(64):
                sl = gi % 3
                if gi >= 3:
                    w.wait("S", "A", a_read[gi - 3])
                w.op("S", "dma_start", slot[sl], wf[t * 128:(t + 1) * 128, :],
                     inc="dS", k=16)
                w.wait("A", "dS", w.c["dS"])
                a_read[gi] = w.op("A", "activation", junk, slot[sl], AF.Abs,
                                  accum_out=spot, inc="A")
                w.wait("V", "A", a_read[gi])
                w.nop("V")
                w.op("V", "tensor_tensor", run, run, spot, ALU.add, inc="V")
                gi += 1
            w.wait("T", "V", w.c["V"])
            w.op("T", "matmul", pb[0][0:1, 0:1], run, ONC, start=True, stop=True,
                 inc="T")
            w.wait("V", "T", w.c["T"])
            sc = SSCAL[:, si:si + 1]
            w.op("V", "tensor_scalar", sc, pb[0][0:1, 0:1], 1.0 / (IFF * DM), 1e-5,
                 ALU.mult, ALU.max)
            w.nop("V")
            w.op("V", "reciprocal", SSCAL[:, 8:9], sc)
            w.nop("V")
            w.op("V", "memset", run, 0.0, inc="V")
            w.wait("T", "V", w.c["V"])
            w.nop("T")
            w.op("T", "matmul", pb[1][:, 0:1], ONF, SSCAL[:, 8:9], start=True,
                 stop=True, inc="T")
            w.wait("V", "T", w.c["T"])
            w.op("V", "tensor_copy", col(2 + si), pb[1][:, 0:1], inc="V")

        for si, tgt in ((0, SCL[:, 0:1]), (1, SCL[:, 1:2]), (2, SCL[:, 4:5])):
            w.wait("T", "V", w.c["V"])
            w.op("T", "matmul", pb[1][:, 1:2], ONF, SSCAL[:, si:si + 1], start=True,
                 stop=True, inc="T")
            w.wait("V", "T", w.c["T"])
            w.op("V", "tensor_copy", tgt, pb[1][:, 1:2], inc="V")
        w.nop("V")
        w.op("V", "tensor_tensor", SCL[:, 2:3], SCL[:, 1:2], SCL[:, 4:5], ALU.mult)
        w.nop("V")
        w.op("V", "tensor_scalar", SCL[:, 2:3], SCL[:, 2:3],
             float(1.0 / (127.0 * np.sqrt(8192.0))), 0.0, ALU.mult, ALU.add)
        w.op("V", "reciprocal", SCL[:, 3:4], SCL[:, 1:2])
        w.nop("V")
        w.op("V", "tensor_scalar", SCL[:, 3:4], SCL[:, 3:4],
             float(np.sqrt(8192.0) * 1e-5), 0.0, ALU.mult, ALU.add, inc="V")

        jobs = []
        for si, wm, dstT in ((0, w_g, s_wgT), (1, w_u, s_wuT)):
            for t in range(64):
                jobs.append(("tern", si, wm[t * 128:(t + 1) * 128, :], 128,
                             dstT[:, t * 128:(t + 1) * 128]
                             .rearrange("(k p) c -> p k c", k=16)))
        for q in range(4):
            for j in range(16):
                jobs.append(("tern", 2,
                             w_d[j * 128:(j + 1) * 128, q * 2048:(q + 1) * 2048], 128,
                             s_wdT[q * 2048:(q + 1) * 2048, j * 128:(j + 1) * 128]
                             .rearrange("(k p) c -> p k c", k=16)))
        for j in range(80):
            jobs.append(("cast", None, w_in[j * 128:(j + 1) * 128, :], 128,
                         s_winT[:, j * 128:(j + 1) * 128]
                         .rearrange("(k p) c -> p k c", k=16)))
        jobs.append(("cast", None, w_in[10240:10304, :], 64,
                     s_winT[:, 10240:10304].rearrange("(k p) c -> p k c", k=16)))
        for hf in range(2):
            for j in range(16):
                jobs.append(("cast", None,
                             w_out[j * 128:(j + 1) * 128, hf * 2048:(hf + 1) * 2048],
                             128,
                             s_woutT[hf * 2048:(hf + 1) * 2048, j * 128:(j + 1) * 128]
                             .rearrange("(k p) c -> p k c", k=16)))

        v_cons = {}
        ev_d = {}
        t_tp = {}
        g_wr = {}
        for t, (kind, si, srcap, rows, dstap) in enumerate(jobs):
            sl = t % 3
            ts = t % 2
            if t >= 3:
                w.wait("S", "V", v_cons[t - 3])
            w.op("S", "dma_start", slot[sl][0:rows, :], srcap, inc="dS", k=16)
            ld = w.c["dS"]
            w.wait("V", "dS", ld)
            if t >= 2:
                w.wait("V", "T", t_tp[t - 2])
            sv = slot[sl][0:rows, :]
            tv = tslot[ts][0:rows, :]
            if kind == "tern":
                invc = col(2 + si)[0:rows, :]
                v_cons[t] = w.op("V", "tensor_scalar", junk[0:rows, :], sv, invc,
                                 -1.0, ALU.mult, ALU.max, inc="V")
                w.op("V", "tensor_scalar", junk[0:rows, :], junk[0:rows, :], 1.0,
                     RND, ALU.min, ALU.add)
                w.op("V", "tensor_scalar", tv, junk[0:rows, :], -RND, 0.0, ALU.add,
                     ALU.add, inc="V")
            else:
                v_cons[t] = w.op("V", "tensor_copy", tv, sv, inc="V")
            vt = w.c["V"]
            idn = IDB if rows == 128 else IDB[0:64, 0:64]
            pw = 128 if rows == 128 else 64
            w.wait("T", "V", vt)
            for r in range(2):
                pp = pbt if r == 0 else pbt2
                if t >= 1:
                    w.wait("T", "A", ev_d.get((t - 1, r), 0))
                for k in range(8):
                    w.op("T", "transpose", pp[:, k * pw:(k + 1) * pw],
                         tv[:, (r * 8 + k) * 128:(r * 8 + k + 1) * 128], idn,
                         inc=("T" if k == 7 else None))
                w.wait("A", "T", w.c["T"])
                if t >= 2 and r == 0:
                    w.wait("A", "dG", g_wr[t - 2])
                ev_d[(t, r)] = w.op("A", "copy",
                                    asm[ts][:, r * 8 * pw:(r + 1) * 8 * pw],
                                    pp[:, 0:8 * pw], inc="A")
            t_tp[t] = w.c["T"]
            w.wait("G", "A", ev_d[(t, 1)])
            g_wr[t] = w.op("G", "dma_start", dstap,
                           asm[ts][:, 0:16 * pw].rearrange("p (k c) -> p k c", k=16),
                           inc="dG", k=16)

    def layernorm():
        xslot = [fview(AR2, 0, 128, 2048), fview(AR2, 8192, 128, 2048)]
        xnn = AR2[:, 8192:10240]
        sqs = fview(ARW, 0, 128, 2048)
        xnT = AR1[:, 0:32768]
        xnTv = xnT.rearrange("p (a c) -> p a c", a=16)
        for tt in range(16):
            sl = tt % 2
            w.wait("S", "V", w.c["V"])
            w.wait("S", "A", w.c["A"])
            w.op("S", "dma_start", xslot[sl], x_in[tt * 128:(tt + 1) * 128, :],
                 inc="dS", k=16)
            sumc, ssq, mc, vc, rc, nb = (col(8 + i) for i in range(6))
            w.wait("V", "dS", w.c["dS"])
            w.op("V", "reduce_sum", sumc, xslot[sl], AX.X)
            w.nop("V")
            w.op("V", "engine_nop", inc="V")
            w.wait("A", "dS", w.c["dS"])
            w.op("A", "activation", sqs, xslot[sl], AF.Square, accum_out=ssq,
                 inc="A")
            w.wait("V", "A", w.c["A"])
            w.nop("V")
            w.op("V", "tensor_scalar", mc, sumc, 1.0 / DM, 0.0, ALU.mult, ALU.add)
            w.op("V", "tensor_scalar", vc, ssq, 1.0 / DM, 0.0, ALU.mult, ALU.add)
            w.nop("V")
            w.op("V", "tensor_tensor", nb, mc, mc, ALU.mult)
            w.nop("V")
            w.op("V", "tensor_tensor", vc, vc, nb, ALU.subtract, inc="V")
            w.wait("A", "V", w.c["V"])
            w.nop("A")
            w.op("A", "activation", vc, vc, AF.Sqrt, bias=col(44), inc="A")
            w.wait("V", "A", w.c["A"])
            w.nop("V")
            w.op("V", "reciprocal", rc, vc)
            w.nop("V")
            w.op("V", "tensor_scalar", xnn, xslot[sl], mc, rc, ALU.subtract,
                 ALU.mult, inc="V")
            w.wait("T", "V", w.c["V"])
            for q in range(4):
                w.wait("T", "V", w.c["V"])
                for j in range(4):
                    d = q * 4 + j
                    w.op("T", "transpose", pbt[:, j * 128:(j + 1) * 128],
                         xnn[:, d * 128:(d + 1) * 128], IDB,
                         inc=("T" if j == 3 else None))
                w.wait("V", "T", w.c["T"])
                w.op("V", "tensor_copy",
                     xnTv[:, q * 4:(q + 1) * 4, tt * 128:(tt + 1) * 128],
                     pbt[:, 0:512].rearrange("p (a c) -> p a c", a=4), inc="V")
        return xnT

    def build_xq(xnT):
        zap = ARW[:, 0:2048]
        z1s = fview(ARW, 6144, 128, 8192)
        mx = fview(ARW, 4096, 128, 512)
        xnv = xnT.rearrange("p (a c) -> p a c", a=16)
        for d in range(16):
            w.wait("V", "dS", w.c["dS"])
            for q in range(4):
                bank = pb[q % 2]
                w.wait("T", "V", w.c["V"])
                w.op("T", "matmul", bank[:, 0:512], H128,
                     xnv[:, d, q * 512:(q + 1) * 512], start=True, stop=True, inc="T")
                w.wait("V", "T", w.c["T"])
                w.op("V", "tensor_copy", zap[:, q * 512:(q + 1) * 512], bank[:, 0:512],
                     inc="V")
            w.wait("G", "V", w.c["V"])
            w.op("G", "dma_start", s_za[d * 128:(d + 1) * 128, :], zap, inc="dS", k=16)
        zb = AR2[:, 0:8192]
        zbv = zb.rearrange("p (a c) -> p a c", a=16)
        xqs = AR2[:, 8192:16384]
        za_r = s_za.rearrange("(dh d3 dl) t -> (dh d3) dl t", dh=16, d3=8, dl=16)
        xq_w = s_xq.rearrange("(dh d3 dl) t -> (dh d3) dl t", dh=16, d3=8, dl=16)
        for tc in range(4):
            w.wait("S", "dS", w.c["dS"])
            w.wait("S", "T", w.c["T"])
            w.op("S", "dma_start", zbv, za_r[:, :, tc * 512:(tc + 1) * 512],
                 inc="dS", k=16)
            w.wait("T", "dS", w.c["dS"])
            for k in range(16):
                bank = pb[2 + (k % 2)]
                w.wait("T", "V", w.c["V"])
                w.op("T", "matmul", bank[:, 0:512], HB16, zb[:, k * 512:(k + 1) * 512],
                     start=True, stop=True, inc="T")
                w.wait("V", "T", w.c["T"])
                w.op("V", "tensor_copy", z1s[:, k * 512:(k + 1) * 512], bank[:, 0:512],
                     inc="V")
            w.op("V", "tensor_reduce", mx,
                 z1s.rearrange("p (a c) -> p c a", a=16), AX.X, ALU.max,
                 apply_absolute_value=True, inc="V")
            w.wait("T", "V", w.c["V"])
            for j in range(4):
                w.op("T", "transpose", pb[4][:, j * 128:(j + 1) * 128],
                     mx[:, j * 128:(j + 1) * 128], IDF, inc=("T" if j == 3 else None))
            w.wait("V", "T", w.c["T"])
            for j in range(4):
                w.op("V", "reduce_max", M1C[:, tc * 4 + j: tc * 4 + j + 1],
                     pb[4][:, j * 128:(j + 1) * 128], AX.X)
            mm4 = M1C[:, tc * 4: tc * 4 + 4]
            w.nop("V")
            w.op("V", "tensor_scalar", mm4, mm4, float(np.sqrt(2048.0) * 1e-5), 0.0,
                 ALU.max, ALU.add)
            w.nop("V")
            w.op("V", "reciprocal", C14, mm4)
            w.nop("V")
            w.op("V", "tensor_scalar", C14, C14, 127.0, 0.0, ALU.mult, ALU.add)
            w.op("V", "tensor_scalar", CI4, mm4, float(1.0 / (127.0 * np.sqrt(2048.0))),
                 0.0, ALU.mult, ALU.add, inc="V")
            w.wait("T", "V", w.c["V"])
            w.nop("T")
            for j in range(4):
                w.op("T", "transpose", pb[4][0:1, j * 128:(j + 1) * 128],
                     C14[:, j:j + 1], IDF)
                w.op("T", "transpose", pb[5][0:1, j * 128:(j + 1) * 128],
                     CI4[:, j:j + 1], IDF, inc=("T" if j == 3 else None))
            w.wait("V", "T", w.c["T"])
            w.op("V", "tensor_copy", CRW1, pb[4][0:1, 0:512])
            w.op("V", "tensor_copy", CRW2, pb[5][0:1, 0:512], inc="V")
            w.wait("T", "V", w.c["V"])
            w.nop("T")
            for j in range(4):
                w.op("T", "matmul", pb[5][:, j * 128:(j + 1) * 128], ONF,
                     CRW1[0:1, j * 128:(j + 1) * 128], start=True, stop=True)
                w.op("T", "matmul", pb[4][:, j * 128:(j + 1) * 128], ONF,
                     CRW2[0:1, j * 128:(j + 1) * 128], start=True, stop=True,
                     inc=("T" if j == 3 else None))
            w.wait("V", "T", w.c["T"])
            w.wait("V", "dS", w.c["dS"])
            z3 = z1s.rearrange("p (a c) -> p a c", a=16)
            xq3 = xqs.rearrange("p (a c) -> p a c", a=16)
            for j in range(4):
                w.op("V", "tensor_tensor", z3[:, :, j * 128:(j + 1) * 128],
                     z3[:, :, j * 128:(j + 1) * 128],
                     pb[5][:, j * 128:(j + 1) * 128].unsqueeze(1)
                     .broadcast_to([128, 16, 128]), ALU.mult)
            w.op("V", "tensor_scalar", z1s, z1s, RND, -RND, ALU.add, ALU.add)
            for j in range(4):
                w.op("V", "tensor_tensor", xq3[:, :, j * 128:(j + 1) * 128],
                     z3[:, :, j * 128:(j + 1) * 128],
                     pb[4][:, j * 128:(j + 1) * 128].unsqueeze(1)
                     .broadcast_to([128, 16, 128]), ALU.mult, inc=("V" if j == 3
                                                                   else None))
            w.wait("G", "V", w.c["V"])
            w.op("G", "dma_start", xq_w[:, :, tc * 512:(tc + 1) * 512],
                 xqs.rearrange("p (a c) -> p a c", a=16), inc="dS", k=16)
        xq = AR2[:, 0:32768]
        w.wait("S", "dS", w.c["dS"])
        xqv = xq.rearrange("p (a c) -> p a c", a=16)
        for k in range(16):
            w.op("S", "dma_start", xqv[:, k, :], s_xq[k * 128:(k + 1) * 128, :],
                 inc="dS", k=16)
        return xq

    def ffn_f1(xq):
        wslot = [ARW[:, 0:4096], ARW[:, 4096:8192]]
        silg = [ARW[:, 8192 + i * 512: 8192 + (i + 1) * 512] for i in range(2)]
        y1b = [ARW[:, 9216 + i * 512: 9216 + (i + 1) * 512] for i in range(2)]
        yab = [ARW[:, 10240 + i * 512: 10240 + (i + 1) * 512] for i in range(2)]
        xqv = xq.rearrange("p (a c) -> p a c", a=16)
        t_mm = {}
        t_sa = {}
        a_done = {}
        v_y1 = {}
        v_evac = {}
        s_store = {}
        t_wcons = {}

        def emit_stage_a(jt):
            pa = pb[4 + (jt % 2)]
            w.wait("T", "V", v_y1[jt])
            if jt >= 2:
                w.wait("T", "V", v_evac[jt - 2])
            t_sa[jt] = w.op("T", "matmul", pa[:, 0:512], H128, y1b[jt % 2],
                            start=True, stop=True, inc="T")
            w.wait("V", "T", t_sa[jt])
            if jt >= 2:
                w.wait("V", "dS", s_store[jt - 2])
            v_evac[jt] = w.op("V", "tensor_copy", yab[jt % 2], pa[:, 0:512], inc="V")
            w.wait("G", "V", v_evac[jt])
            o_, tc_ = jt // 4, jt % 4
            s_store[jt] = w.op("G", "dma_start",
                               s_ya[o_ * 128:(o_ + 1) * 128, tc_ * 512:(tc_ + 1) * 512],
                               yab[jt % 2], inc="dS", k=16)

        for o in range(64):
            sl = o % 2
            if o >= 2:
                w.wait("S", "T", t_wcons[o - 2])
            w.op("S", "dma_start",
                 wslot[sl][:, 0:2048].rearrange("p (k c) -> p k c", k=16),
                 s_wgT[:, o * 128:(o + 1) * 128]
                 .rearrange("(k p) c -> p k c", k=16), inc="dW", k=16)
            w.op("S", "dma_start",
                 wslot[sl][:, 2048:4096].rearrange("p (k c) -> p k c", k=16),
                 s_wuT[:, o * 128:(o + 1) * 128]
                 .rearrange("(k p) c -> p k c", k=16), inc="dW", k=16)
            wz = w.c["dW"]
            for tc in range(4):
                it = o * 4 + tc
                pg = pb[0 + (tc % 2)]
                pu = pb[2 + (tc % 2)]
                if tc == 0:
                    w.wait("T", "dW", wz)
                if it >= 2:
                    w.wait("T", "A", a_done[it - 2])
                    w.wait("T", "V", v_y1[it - 2])
                for k in range(16):
                    w.op("T", "matmul", pg[:, 0:512],
                         wslot[sl][:, k * 128:(k + 1) * 128],
                         xqv[:, k, tc * 512:(tc + 1) * 512], start=(k == 0),
                         stop=(k == 15))
                    w.op("T", "matmul", pu[:, 0:512],
                         wslot[sl][:, 2048 + k * 128: 2048 + (k + 1) * 128],
                         xqv[:, k, tc * 512:(tc + 1) * 512], start=(k == 0),
                         stop=(k == 15), inc=("T" if k == 15 else None))
                t_mm[it] = w.c["T"]
                w.wait("A", "T", t_mm[it])
                if it >= 2:
                    w.wait("A", "V", v_y1[it - 2])
                a_done[it] = w.op("A", "activation", silg[tc % 2], pg[:, 0:512],
                                  AF.Silu, scale=SCL[:, 0:1], inc="A")
                w.wait("V", "A", a_done[it])
                v_y1[it] = w.op("V", "tensor_tensor", y1b[tc % 2], silg[tc % 2],
                                pu[:, 0:512], ALU.mult, inc="V")
                if it >= 1:
                    emit_stage_a(it - 1)
            t_wcons[o] = w.c["T"]
        emit_stage_a(255)

    def ffn_f2a():
        zbs = [AR2[:, 0:8192], SST[:, 0:8192]]
        z2s = [fview(ARW, 0, 128, 8192), fview(AR2, 32768, 128, 8192)]
        r2b = AR2[:, 8192:16384]
        r2v = r2b.rearrange("p (a c) -> p a c", a=64)
        mxts = [MXT, smf[:, 2784:2912]]
        c2cs = [col(30), col(31)]
        c2rows = [C2ROW, smf[0:1, 0:128]]
        ya_r = s_ya.rearrange("(oh b6 ol) t -> (oh b6) ol t", oh=64, b6=2, ol=64)
        r2_w = s_r2.rearrange("(oh b6 ol) t -> (oh b6) ol t", oh=64, b6=2, ol=64)
        t_zb = {}
        q_done = {}
        qa_done = {}
        ld_done = {}

        def mm_part(tc):
            zb = zbs[tc % 2]
            zbv = zb.rearrange("p (a c) -> p a c", a=64)
            zs = z2s[tc % 2]
            if tc >= 2:
                w.wait("S", "T", t_zb[tc - 2])
            for part in range(4):
                w.op("S", "dma_start", zbv[:, part * 16:(part + 1) * 16, :],
                     ya_r[:, part * 16:(part + 1) * 16, tc * 128:(tc + 1) * 128],
                     inc="dS", k=16)
            ld_done[tc] = w.c["dS"]
            w.wait("T", "dS", ld_done[tc])
            if tc >= 2:
                w.wait("T", "V", q_done[tc - 2])
                w.wait("T", "A", qa_done.get(tc - 2, 0))
            for k in range(16):
                bank = pb[(tc % 2) * 2 + (k % 2)]
                if k >= 2:
                    w.wait("T", "V", w.c["V"])
                    w.wait("T", "A", w.c["A"])
                for j in range(4):
                    w.op("T", "matmul", bank[:, j * 128:(j + 1) * 128], HB64,
                         zb[:, (k * 4 + j) * 128:(k * 4 + j + 1) * 128],
                         start=True, stop=True, inc=("T" if j == 3 else None))
                ee = "V" if k % 2 == 0 else "A"
                w.wait(ee, "T", w.c["T"])
                if ee == "V":
                    w.op("V", "tensor_copy", zs[:, k * 512:(k + 1) * 512],
                         bank[:, 0:512], inc="V")
                else:
                    w.op("A", "copy", zs[:, k * 512:(k + 1) * 512], bank[:, 0:512],
                         inc="A")
            t_zb[tc] = w.c["T"]

        def quant_part(tc):
            zs = z2s[tc % 2]
            z3 = zs.rearrange("p (a c) -> p a c", a=64)
            mxt = mxts[tc % 2]
            c2c = c2cs[tc % 2]
            c2r = c2rows[tc % 2]
            tb = pb[4 + (tc % 2)]
            zt = zs.rearrange("p (a c) -> p c a", a=64)
            w.wait("V", "A", w.c["A"])
            w.op("V", "tensor_reduce", mxt, zt, AX.X, ALU.max,
                 apply_absolute_value=True, inc="V")
            w.wait("T", "V", w.c["V"])
            w.op("T", "transpose", tb[:, 0:128], mxt, IDF, inc="T")
            w.wait("V", "T", w.c["T"])
            mcol = M2C[:, tc:tc + 1]
            w.op("V", "reduce_max", mcol, tb[:, 0:128], AX.X)
            w.nop("V")
            w.op("V", "tensor_tensor", mcol, mcol, SCL[:, 3:4], ALU.max)
            w.nop("V")
            w.op("V", "reciprocal", c2c, mcol)
            w.nop("V")
            w.op("V", "tensor_scalar", c2c, c2c, 127.0, 0.0, ALU.mult, ALU.add,
                 inc="V")
            w.wait("T", "V", w.c["V"])
            w.nop("T")
            w.op("T", "transpose", tb[0:1, 0:128], c2c, IDF, inc="T")
            w.wait("V", "T", w.c["T"])
            w.op("V", "tensor_copy", c2r, tb[0:1, 0:128], inc="V")
            w.wait("T", "V", w.c["V"])
            w.nop("T")
            w.op("T", "matmul", tb[:, 128:256], ONF, c2r, start=True, stop=True,
                 inc="T")
            w.wait("V", "T", w.c["T"])
            w.wait("V", "dS", w.c["dS"])
            w.op("V", "tensor_tensor", z3, z3,
                 tb[:, 128:256].unsqueeze(1).broadcast_to([128, 64, 128]), ALU.mult)
            w.op("V", "tensor_scalar", r2v, z3, RND, -RND, ALU.add, ALU.add, inc="V")
            q_done[tc] = w.c["V"]
            qa_done[tc] = w.c["A"]
            w.wait("G", "V", q_done[tc])
            for part in range(4):
                w.op("G", "dma_start",
                     r2_w[:, part * 16:(part + 1) * 16, tc * 128:(tc + 1) * 128],
                     r2v[:, part * 16:(part + 1) * 16, :], inc="dS", k=16)

        for tc in range(16):
            mm_part(tc)
            if tc >= 1:
                quant_part(tc - 1)
        quant_part(15)
        w.nop("V")
        w.op("V", "tensor_tensor", M2Q, M2C, SCL[:, 2:3].broadcast_to([128, 16]),
             ALU.mult, inc="V")

    def dt_family():
        dtspb = ARW[:, 12294:14342][0:64, :]
        sdch = ARW[:, 14342:14470][0:64, :]
        dsch = ARW[:, 14470:14598][0:64, :]
        T1 = fview(ARW, 31744, 64, 2048)
        ABR2 = T1
        w.wait("A", "V", w.c["V"])
        w.op("A", "activation", T1, DTA, AF.Abs)
        w.op("A", "activation", T1, T1, AF.Exp, scale=col(45)[0:64, :])
        w.op("A", "activation", T1, T1, AF.Ln, bias=1.0, inc="A")
        w.wait("V", "A", w.c["A"])
        w.op("V", "tensor_scalar", DTA, DTA, 0.0, 0.0, ALU.max, ALU.add)
        w.op("V", "tensor_tensor", DTA, DTA, T1, ALU.add, inc="V")
        w.wait("A", "V", w.c["V"])
        w.op("A", "copy", dtspb, DTA, inc="A")
        w.wait("T", "A", w.c["A"])
        for c in range(NCH):
            if c % 4 == 0:
                w.wait("T", "V", w.c["V"])
            w.op("T", "transpose", pbt[:, (c % 4) * 128:(c % 4) * 128 + 64],
                 dtspb[:, c * 128:(c + 1) * 128], IDB[0:64, 0:64],
                 inc=("T" if c % 4 == 3 else None))
            if c % 4 == 3:
                w.wait("V", "T", w.c["T"])
                w.op("V", "tensor_copy",
                     DTT[:, (c - 3) * 64:(c + 1) * 64].rearrange(
                         "p (a c) -> p a c", a=4),
                     pbt[:, 0:512].rearrange("p (a c) -> p a c", a=4)[:, :, 0:64],
                     inc="V")
        w.wait("A", "dS", w.c["dS"])
        w.wait("A", "V", w.c["V"])
        w.op("A", "activation", col(41)[0:64, :], col(41)[0:64, :], AF.Exp,
             inc="A")
        w.wait("V", "A", w.c["A"])
        w.nop("V")
        w.op("V", "tensor_scalar", col(41)[0:64, :], col(41)[0:64, :], -1.0, 0.0,
             ALU.mult, ALU.add)
        w.nop("V")
        w.op("V", "tensor_scalar", ABR2, DTA, col(41)[0:64, :], 0.0, ALU.mult,
             ALU.add, inc="V")
        zc = col(43)
        w.op("V", "memset", zc, 0.0)
        w.nop("V")
        w.op("V", "engine_nop", inc="V")
        for c in range(NCH):
            w.op("V", "tensor_tensor_scan", ACS, ABR2[:, c * 128:(c + 1) * 128],
                 zc[0:64, :].broadcast_to([64, 128]), 0.0, ALU.add, ALU.add, inc="V")
            w.wait("A", "V", w.c["V"])
            w.nop("A")
            w.op("A", "activation", sdch, ACS, AF.Exp)
            w.op("A", "activation", dsch, ACS, AF.Exp, scale=col(45)[0:64, :],
                 bias=ACS[:, 127:128], inc="A")
            w.wait("T", "A", w.c["A"])
            w.nop("T")
            w.wait("T", "V", w.c["V"])
            w.op("T", "transpose", pbt[:, 0:64], sdch, IDB[0:64, 0:64])
            w.op("T", "transpose", pbt[:, 128:192], dsch, IDB[0:64, 0:64], inc="T")
            w.op("T", "transpose", pbt[0:1, 256:320], sdch[:, 127:128],
                 IDB[0:64, 0:64], inc="T")
            w.wait("V", "T", w.c["T"])
            w.op("V", "tensor_copy", SDT[:, c * 64:(c + 1) * 64], pbt[:, 0:64])
            w.op("V", "tensor_copy", DST[:, c * 64:(c + 1) * 64], pbt[:, 128:192])
            w.op("V", "tensor_copy", CDROW, pbt[0:1, 256:320], inc="V")
            w.wait("T", "V", w.c["V"])
            w.nop("T")
            w.op("T", "matmul", pb[3][:, 0:64], ONB, CDROW, start=True, stop=True,
                 inc="T")
            w.wait("V", "T", w.c["T"])
            w.op("V", "tensor_copy", CDB[:, c * 64:(c + 1) * 64], pb[3][:, 0:64],
                 inc="V")
        w.wait("A", "V", w.c["V"])
        w.op("A", "activation", RTEN, ABR2, AF.Exp, inc="A")
        w.wait("V", "A", w.c["A"])
        for c in range(NCH):
            w.op("V", "memset", RTEN[:, c * 128: c * 128 + 1], 0.0,
                 inc=("V" if c == NCH - 1 else None))
        w.wait("T", "V", w.c["V"])
        w.op("T", "transpose", pb[3][0:1, 0:128], col(42), IDF, inc="T")
        w.wait("V", "T", w.c["T"])
        w.op("V", "tensor_copy", C1R[0:1, 0:128], pb[3][0:1, 0:128], inc="V")
        w.wait("T", "V", w.c["V"])
        w.nop("T")
        w.op("T", "matmul", pb[3][:, 0:64], ONF, C1R[0:1, 0:64], start=True,
             stop=True, inc="T")
        w.wait("V", "T", w.c["T"])
        w.op("V", "tensor_copy", DMAT, pb[3][:, 0:64], inc="V")

    def mixer_m1(xnT):
        wslot = [ARW[:, 0:4096], ARW[:, 4096:8192]]
        cvin = [ARW[:, 8192:10243], ARW[:, 10243:12294]]
        hout = ARW[:, 12294:14342]
        gpiece = ARW[:, 14342:14854]
        xnv = xnT.rearrange("p (a c) -> p a c", a=16)
        w.op("S", "dma_start", col(40)[0:64, :],
             dt_bias.rearrange("(a b) -> a b", b=1), inc="dS", k=16)
        w.op("S", "dma_start", col(41)[0:64, :],
             a_log.rearrange("(a b) -> a b", b=1), inc="dS", k=16)
        w.op("S", "dma_start", col(42)[0:64, :],
             dvec.rearrange("(a b) -> a b", b=1), inc="dS", k=16)
        w.op("V", "memset", cvin[0][:, 0:3], 0.0)
        w.op("V", "memset", cvin[1][:, 0:3], 0.0, inc="V")

        groups = [("dt", [80])]
        groups += [("hbc", list(range(32 + 2 * i, 34 + 2 * i))) for i in range(24)]
        groups += [("gate", list(range(2 * i, 2 * i + 2))) for i in range(16)]

        t_after = {}
        ev = {}
        cnt = 0
        hout2 = [hout, ARW[:, 14854:16902]]
        gp2 = [gpiece, ARW[:, 16902:17414]]
        hb_done = {}
        gp_done = {}
        cv_done = {}
        sl_done = {}
        hi = 0
        gpi = 0
        for gi, (kind, js) in enumerate(groups):
            sl = gi % 2
            j0 = js[0]
            rows = 64 if kind == "dt" else 256
            if gi >= 2:
                w.wait("S", "T", t_after[gi - 2])
            w.op("S", "dma_start",
                 wslot[sl][:, 0:4096].rearrange("p (k c) -> p k c", k=16)[:, :, 0:rows],
                 s_winT[:, j0 * 128: j0 * 128 + rows]
                 .rearrange("(k p) c -> p k c", k=16), inc="dW", k=16)
            w.wait("T", "dW", w.c["dW"])
            for jj, j in enumerate(js):
                mrows = 64 if kind == "dt" else 128
                for tc in range(4):
                    bank = pb[cnt % 2]
                    if cnt >= 2:
                        eng, val = ev[cnt - 2]
                        w.wait("T", eng, val)
                    for k in range(16):
                        w.op("T", "matmul", bank[0:mrows, 0:512],
                             wslot[sl][:, k * 256 + jj * 128: k * 256 + jj * 128 + mrows],
                             xnv[:, k, tc * 512:(tc + 1) * 512],
                             start=(k == 0), stop=(k == 15),
                             inc=("T" if k == 15 else None))
                    tmm = w.c["T"]
                    if kind == "dt":
                        w.wait("V", "T", tmm)
                        w.wait("V", "dS", w.c["dS"])
                        ev[cnt] = ("V", w.op("V", "tensor_scalar",
                                             DTA[:, tc * 512:(tc + 1) * 512],
                                             bank[0:64, 0:512], col(40)[0:64, :], 0.0,
                                             ALU.add, ALU.add, inc="V"))
                    elif kind == "hbc":
                        w.wait("A", "T", tmm)
                        if hi >= 2 and tc == 0:
                            w.wait("A", "V", cv_done[hi - 2])
                        ev[cnt] = ("A", w.op("A", "copy",
                                             cvin[jj % 2][:, 3 + tc * 512: 3 + (tc + 1) * 512],
                                             bank[:, 0:512], inc="A"))
                    else:
                        w.wait("A", "T", tmm)
                        if gpi >= 2:
                            w.wait("A", "dS", gp_done[gpi - 2])
                        ev[cnt] = ("A", w.op("A", "copy", gp2[gpi % 2],
                                             bank[:, 0:512], inc="A"))
                        w.wait("S", "A", w.c["A"])
                        gp_done[gpi] = w.op("S", "dma_start",
                                            s_gate[j * 128:(j + 1) * 128,
                                                   tc * 512:(tc + 1) * 512],
                                            gp2[gpi % 2], inc="dS", k=16)
                        gpi += 1
                    cnt += 1
                if kind == "hbc":
                    ci = j - 32
                    cwsp = CW0 if jj % 2 == 0 else CW1
                    cbsp = CB0 if jj % 2 == 0 else CB1
                    if hi >= 2:
                        w.wait("G", "V", cv_done[hi - 2])
                        w.wait("G", "A", sl_done[hi - 2])
                    w.op("G", "dma_start", cwsp, conv_w[ci * 128:(ci + 1) * 128, :],
                         inc="dG", k=16)
                    w.op("G", "dma_start", cbsp,
                         conv_b[ci * 128:(ci + 1) * 128].rearrange("(a b) -> a b", b=1),
                         inc="dG", k=16)
                    w.wait("V", "dG", w.c["dG"])
                    w.wait("V", "A", w.c["A"])
                    if hi >= 2:
                        w.wait("V", "dS", hb_done[hi - 2])
                    src_ = cvin[jj % 2]
                    ho = hout2[hi % 2]
                    w.op("V", "tensor_scalar", ho, src_[:, 0:2048], cwsp[:, 0:1],
                         0.0, ALU.mult, ALU.add)
                    for k in range(1, 4):
                        w.op("V", "scalar_tensor_tensor", ho, src_[:, k:2048 + k],
                             cwsp[:, k:k + 1], ho, ALU.mult, ALU.add,
                             inc=("V" if k == 3 else None))
                    cv_done[hi] = w.c["V"]
                    w.wait("A", "V", w.c["V"])
                    sl_done[hi] = w.op("A", "activation", ho, ho, AF.Silu, bias=cbsp,
                                       inc="A")
                    w.wait("S", "A", w.c["A"])
                    hb_done[hi] = w.op("S", "dma_start",
                                       s_hbc[ci * 128:(ci + 1) * 128, :], ho,
                                       inc="dS", k=16)
                    hi += 1
            if kind == "dt":
                dt_family()
            t_after[gi] = w.c["T"]

    def ssm():
        INS = []
        INS.append({"XT": AR2[:, 0:4096], "GT": AR2[:, 4096:8192],
                    "BT": AR2[:, 8192:9216], "BCF": AR2[:, 9216:11264]})
        INS.append({"XT": AR1[:, 9216:13312], "GT": AR1[:, 13312:17408],
                    "BT": AR1[:, 17408:18432], "BCF": AR1[:, 18432:20480]})
        XDT = AR2[:, 11264:15360]
        XDD = AR2[:, 15360:19456]
        SILG = AR2[:, 19456:23552]
        GGB = AR2[:, 23552:27648]
        ONT = AR2[:, 27648:31744]
        LMK = [AR1[:, 0:1024], AR1[:, 2048:3072]]
        MPR = [AR1[:, 1024:2048], AR1[:, 3072:4096]]
        YTMP = AR1[:, 4096:4608]
        Y2B = AR1[:, 4608:5120]
        XDB = AR1[:, 5120:9216]
        SQS = fview(ARW, 0, 128, 4096)
        w.op("V", "memset", SP0, 0.0, inc="V")

        t1d = {}
        t2d = {}
        v1d = {}
        vscan = {}
        v2e = {}
        v3d = {}
        ldd = {}
        vpro = {}
        asil = {}

        def load_chunk(c):
            sd = INS[c % 2]
            if c >= 2:
                w.wait("S", "T", t2d.get((c - 1) * 8 + 7, 0))
                w.wait("S", "V", vpro.get(c - 1, 0))
                w.wait("S", "A", asil.get(c - 1, 0))
            w.op("S", "dma_start_transpose", sd["XT"],
                 s_hbc[0:4096, c * 128:(c + 1) * 128], inc="dS", k=16)
            w.op("S", "dma_start_transpose", sd["GT"],
                 s_gate[0:4096, c * 128:(c + 1) * 128], inc="dS", k=16)
            w.op("S", "dma_start_transpose", sd["BT"],
                 s_hbc[4096:5120, c * 128:(c + 1) * 128], inc="dS", k=16)
            w.op("S", "dma_start", sd["BCF"].rearrange("p (a c) -> p a c", a=16),
                 s_hbc[4096:6144, c * 128:(c + 1) * 128]
                 .rearrange("(a p) c -> p a c", a=16), inc="dS", k=16)
            ldd[c] = w.c["dS"]

        def t1(c, g):
            gi = c * 8 + g
            sd = INS[c % 2]
            if gi >= 2:
                w.wait("T", "V", v1d.get(gi - 2, 0))
            if gi >= 1:
                w.wait("T", "V", vscan.get(gi - 1, 0))
            if g == 0:
                w.wait("T", "dS", ldd[c])
            w.op("T", "matmul", pb[0][:, (g % 2) * 128:(g % 2) * 128 + 128],
                 sd["BCF"][:, g * 128:(g + 1) * 128],
                 sd["BCF"][:, 1024 + g * 128: 1024 + (g + 1) * 128], start=True,
                 stop=True)
            for i in range(8):
                bank = pb[1] if i < 4 else pb[2]
                w.op("T", "matmul", bank[:, (i % 4) * 128:(i % 4 + 1) * 128],
                     IDF[0:64, 8 * g + i: 8 * g + i + 1].broadcast_to([64, 128]),
                     RTEN[:, c * 128:(c + 1) * 128],
                     start=True, stop=True, inc=("T" if i == 7 else None))
            t1d[gi] = w.c["T"]

        def v1(c, g):
            gi = c * 8 + g
            w.wait("V", "T", t1d[gi])
            if gi >= 2:
                w.wait("V", "T", t2d.get(gi - 2, 0))
            lm = LMK[gi % 2]
            mp = MPR[gi % 2]
            for half in range(2):
                w.op("V", "tensor_tensor_scan", lm[:, half * 512:(half + 1) * 512],
                     pb[1 + half][:, 0:512], IR4, 0.0, ALU.mult, ALU.add,
                     inc=("V" if half == 1 else None))
            vscan[gi] = w.c["V"]
            for half in range(2):
                w.op("V", "tensor_tensor",
                     mp[:, half * 512:(half + 1) * 512]
                     .rearrange("p (a c) -> p a c", a=4),
                     pb[0][:, (g % 2) * 128:(g % 2) * 128 + 128].unsqueeze(1)
                     .broadcast_to([128, 4, 128]),
                     lm[:, half * 512:(half + 1) * 512]
                     .rearrange("p (a c) -> p a c", a=4), ALU.mult,
                     inc=("V" if half == 1 else None))
            v1d[gi] = w.c["V"]

        def t2(c, g):
            gi = c * 8 + g
            sd = INS[c % 2]
            Sold = SP0 if c % 2 == 0 else SP1
            mp = MPR[gi % 2]
            w.wait("T", "V", v1d[gi])
            if gi >= 1:
                w.wait("T", "V", v2e.get(gi - 1, 0))
            for i in range(8):
                w.op("T", "matmul", pb[3][:, i * 64:(i + 1) * 64],
                     mp[:, i * 128:(i + 1) * 128],
                     XDT[:, (8 * g + i) * 64:(8 * g + i + 1) * 64],
                     start=True, stop=True)
            w.op("T", "matmul", pb[4][:, 0:512], sd["BT"][:, g * 128:(g + 1) * 128],
                 XDD[:, g * 512:(g + 1) * 512], start=True, stop=True)
            w.op("T", "matmul", pb[5][:, 0:512],
                 sd["BCF"][:, 1024 + g * 128:1024 + (g + 1) * 128],
                 Sold[:, g * 512:(g + 1) * 512], start=True, stop=True, inc="T")
            t2d[gi] = w.c["T"]

        def v2(c, g):
            gi = c * 8 + g
            Sold = SP0 if c % 2 == 0 else SP1
            Snew = SP1 if c % 2 == 0 else SP0
            w.wait("V", "T", t2d[gi])
            w.op("V", "tensor_tensor",
                 Snew[:, g * 512:(g + 1) * 512].rearrange("p (a c) -> p a c", a=8),
                 Sold[:, g * 512:(g + 1) * 512].rearrange("p (a c) -> p a c", a=8),
                 CDB[:, c * 64 + g * 8: c * 64 + (g + 1) * 8].unsqueeze(2)
                 .broadcast_to([128, 8, 64]), ALU.mult)
            w.op("V", "tensor_tensor", Snew[:, g * 512:(g + 1) * 512],
                 Snew[:, g * 512:(g + 1) * 512], pb[4][:, 0:512], ALU.add)
            w.op("V", "tensor_tensor", YTMP.rearrange("p (a c) -> p a c", a=8),
                 pb[5][:, 0:512].rearrange("p (a c) -> p a c", a=8),
                 SDT[:, c * 64 + g * 8: c * 64 + (g + 1) * 8].unsqueeze(2)
                 .broadcast_to([128, 8, 64]), ALU.mult)
            w.op("V", "scalar_tensor_tensor", Y2B, pb[3][:, 0:512], 1.0, YTMP,
                 ALU.mult, ALU.add, inc="V")
            v2e[gi] = w.c["V"]

        def v3(c, g):
            gi = c * 8 + g
            w.op("V", "tensor_tensor", Y2B, Y2B, XDB[:, g * 512:(g + 1) * 512],
                 ALU.add)
            w.op("V", "tensor_tensor", GGB[:, g * 512:(g + 1) * 512], Y2B,
                 SILG[:, g * 512:(g + 1) * 512], ALU.mult, inc="V")
            v3d[gi] = w.c["V"]
            w.wait("A", "V", v3d[gi])
            w.op("A", "activation", SQS[:, g * 512:(g + 1) * 512],
                 GGB[:, g * 512:(g + 1) * 512], AF.Square,
                 accum_out=VARC[:, g:g + 1], inc="A")

        load_chunk(0)
        for c in range(NCH):
            sd = INS[c % 2]
            w.wait("V", "dS", ldd[c])
            if c >= 1:
                w.wait("V", "T", t2d.get((c - 1) * 8 + 7, 0))
                w.wait("V", "A", w.c["A"])
            xtv = sd["XT"].rearrange("p (h q) -> p h q", h=64)
            w.op("V", "tensor_tensor", XDT.rearrange("p (h q) -> p h q", h=64), xtv,
                 DTT[:, c * 64:(c + 1) * 64].unsqueeze(2).broadcast_to([128, 64, 64]),
                 ALU.mult)
            w.op("V", "tensor_tensor", XDD.rearrange("p (h q) -> p h q", h=64),
                 XDT.rearrange("p (h q) -> p h q", h=64),
                 DST[:, c * 64:(c + 1) * 64].unsqueeze(2).broadcast_to([128, 64, 64]),
                 ALU.mult)
            w.op("V", "tensor_tensor", XDB.rearrange("p (h q) -> p h q", h=64), xtv,
                 DMAT.unsqueeze(2).broadcast_to([128, 64, 64]), ALU.mult, inc="V")
            vpro[c] = w.c["V"]
            w.wait("A", "dS", ldd[c])
            w.wait("A", "V", w.c["V"])
            asil[c] = w.op("A", "activation", SILG, sd["GT"], AF.Silu, inc="A")
            if c + 1 < NCH:
                load_chunk(c + 1)
            for g in range(8):
                t1(c, g)
                v1(c, g)
                t2(c, g)
                v2(c, g)
                v3(c, g)
            w.wait("V", "A", w.c["A"])
            w.op("V", "tensor_scalar", VARC, VARC, 1.0 / 512.0, 0.0, ALU.mult,
                 ALU.add, inc="V")
            w.wait("A", "V", w.c["V"])
            w.nop("A")
            w.op("A", "activation", VARC, VARC, AF.Sqrt, bias=col(44), inc="A")
            w.wait("V", "A", w.c["A"])
            w.nop("V")
            w.op("V", "reciprocal", RSTC, VARC)
            w.nop("V")
            w.wait("V", "dS", w.c["dS"])
            w.op("V", "tensor_tensor", ONT.rearrange("p (a c) -> p a c", a=8),
                 GGB.rearrange("p (a c) -> p a c", a=8),
                 RSTC.unsqueeze(2).broadcast_to([128, 8, 512]), ALU.mult, inc="V")
            w.wait("S", "V", w.c["V"])
            w.op("S", "dma_start", s_on[c * 128:(c + 1) * 128, :], ONT, inc="dS",
                 k=16)

    def wout_phase():
        w.op("S", "dma_start", fview(ARW, 0, 32, 128),
             norm_w.rearrange("(a b) -> a b", b=128), inc="dS", k=16)
        w.wait("T", "dS", w.c["dS"])
        w.op("T", "transpose", pb[0][:, 0:32], fview(ARW, 0, 32, 128),
             IDF[0:32, 0:32], inc="T")
        w.wait("V", "T", w.c["T"])
        w.op("V", "tensor_copy", NWC, pb[0][:, 0:32])
        w.nop("V")
        w.op("V", "engine_nop", inc="V")

        onq = [AR1[:, 0:16384], AR1[:, 16384:32768]]
        wc = [AR2[:, 0:8192], AR2[:, 8192:16384]]
        xrow = [fview(ARW, 4096 + i * 8192, 128, 2048) for i in range(4)]
        opiece = [fview(ARW, 36864 + i * 1024, 128, 256) for i in range(2)]
        plist = []
        t_on = {}
        t_wc = {}
        v_evac = {}
        o_dma = {}
        v_scale = {}
        ld_q = {}

        def load_q_part(tq, part):
            ov = onq[tq % 2].rearrange("p (a c) -> p a c", a=32)
            if tq >= 2 and part == 0:
                w.wait("S", "T", t_on[tq - 2])
            for i in range(part * 8, (part + 1) * 8):
                w.op("S", "dma_start_transpose", ov[:, i, :],
                     s_on[tq * 512:(tq + 1) * 512, i * 128:(i + 1) * 128],
                     inc="dS", k=16)
            if part == 3:
                ld_q[tq] = w.c["dS"]

        def scale_q(tq):
            ov = onq[tq % 2].rearrange("p (a c) -> p a c", a=32)
            w.wait("V", "dS", ld_q[tq])
            if tq >= 2:
                w.wait("V", "T", t_on[tq - 2])
            for i in range(32):
                w.op("V", "tensor_scalar", ov[:, i, :], ov[:, i, :],
                     NWC[:, i:i + 1], 0.0, ALU.mult, ALU.add,
                     inc=("V" if i == 31 else None))
            v_scale[tq] = w.c["V"]

        for part in range(4):
            load_q_part(0, part)
        scale_q(0)
        for tq in range(4):
            ov = onq[tq % 2].rearrange("p (a c) -> p a c", a=32)
            for dg in range(8):
                it = tq * 8 + dg
                sl = dg % 2
                if it >= 2:
                    w.wait("S", "T", t_wc[it - 2])
                for half in range(2):
                    w.op("S", "dma_start",
                         wc[sl].rearrange("p (a c) -> p a c", a=32)
                         [:, half * 16:(half + 1) * 16, :],
                         s_woutT[:, dg * 256:(dg + 1) * 256]
                         .rearrange("(k p) c -> p k c", k=32)
                         [:, half * 16:(half + 1) * 16, :], inc="dW", k=16)
                if tq + 1 < 4 and dg < 4:
                    load_q_part(tq + 1, dg)
                if tq + 1 < 4 and dg == 4:
                    scale_q(tq + 1)
                w.wait("T", "dW", w.c["dW"])
                w.wait("T", "V", v_scale[tq])
                wv = wc[sl].rearrange("p (a c) -> p a c", a=32)
                for ts in range(4):
                    pidx = (tq, dg, ts)
                    tsg = tq * 4 + ts
                    bank = pb[ts % 4]
                    if dg == 0:
                        pv = (tq - 1, 7, ts)
                        if pv in v_evac:
                            w.wait("S", "V", v_evac[pv])
                        w.op("S", "dma_start", xrow[ts],
                             x_in[tsg * 128:(tsg + 1) * 128, :], inc="dS", k=16)
                        ld_q[("x", tsg)] = w.c["dS"]
                    prevv = (tq, dg - 1, ts) if dg >= 1 else ((tq - 1, 7, ts) if tq else None)
                    if prevv in v_evac:
                        w.wait("T", "V", v_evac[prevv])
                    for k in range(32):
                        w.op("T", "matmul", bank[:, 0:256],
                             ov[:, k, ts * 128:(ts + 1) * 128], wv[:, k, :],
                             start=(k == 0), stop=(k == 31),
                             inc=("T" if k == 31 else None))
                    w.wait("V", "T", w.c["T"])
                    w.wait("V", "dS", ld_q[("x", tsg)])
                    if len(plist) >= 2:
                        w.wait("V", "dO", o_dma[plist[-2]])
                    sltp = len(plist) % 2
                    v_evac[pidx] = w.op("V", "tensor_tensor", opiece[sltp],
                                        bank[:, 0:256],
                                        xrow[ts][:, dg * 256:(dg + 1) * 256],
                                        ALU.add, inc="V")
                    w.wait("S", "V", v_evac[pidx])
                    o_dma[pidx] = w.op("S", "dma_start",
                                       out_hbm[tsg * 128:(tsg + 1) * 128,
                                               dg * 256:(dg + 1) * 256],
                                       opiece[sltp], inc="dO", k=16)
                    plist.append(pidx)
                t_wc[it] = w.c["T"]
            t_on[tq] = w.c["T"]

    def down_phase():
        rres = AR1[:, 0:32768]
        rv = rres.rearrange("p (a c) -> p a c", a=64)
        wc = [AR2[:, 0:16384], AR2[:, 16384:32768]]
        opiece = [fview(ARW, i * 1024, 128, 256) for i in range(4)]
        t_r = {}
        t_wc = {}
        v_evac = {}
        o_dma = {}
        for tq in range(4):
            if tq >= 1:
                w.wait("G", "T", t_r[tq - 1])
            for i in range(64):
                w.op("G", "dma_start", rv[:, i, :],
                     s_r2[i * 128:(i + 1) * 128, tq * 512:(tq + 1) * 512],
                     inc="dS", k=16)
            rq_loaded = w.c["dS"]
            for dg in range(8):
                it = tq * 8 + dg
                sl = dg % 2
                if it >= 2:
                    w.wait("S", "T", t_wc[it - 2])
                for half in range(4):
                    w.op("S", "dma_start",
                         wc[sl].rearrange("p (a c) -> p a c", a=64)
                         [:, half * 16:(half + 1) * 16, :],
                         s_wdT[:, dg * 256:(dg + 1) * 256]
                         .rearrange("(k p) c -> p k c", k=64)
                         [:, half * 16:(half + 1) * 16, :], inc="dW", k=16)
                w.wait("T", "dW", w.c["dW"])
                w.wait("T", "dS", rq_loaded)
                wv = wc[sl].rearrange("p (a c) -> p a c", a=64)
                for ts in range(4):
                    pidx = (tq, dg, ts)
                    bank = pb[ts % 4]
                    prev = (tq, dg - 1, ts) if dg >= 1 else ((tq - 1, 7, ts) if tq else None)
                    if prev in v_evac:
                        w.wait("T", "V", v_evac[prev])
                    for k in range(64):
                        w.op("T", "matmul", bank[:, 0:256],
                             rv[:, k, ts * 128:(ts + 1) * 128], wv[:, k, :],
                             start=(k == 0), stop=(k == 63),
                             inc=("T" if k == 63 else None))
                    w.wait("V", "T", w.c["T"])
                    prev2 = (tq, dg - 1, ts) if dg >= 1 else ((tq - 1, 7, ts) if tq else None)
                    if prev2 in o_dma:
                        w.wait("V", "dO", o_dma[prev2])
                    tsg = tq * 4 + ts
                    v_evac[pidx] = w.op("V", "tensor_scalar", opiece[ts], bank[:, 0:256],
                                        M2Q[:, tsg:tsg + 1], 0.0, ALU.mult, ALU.add,
                                        inc="V")
                    w.wait("G", "V", v_evac[pidx])
                    o_dma[pidx] = w.op("G", "dma_start",
                                       out_hbm[tsg * 128:(tsg + 1) * 128,
                                               dg * 256:(dg + 1) * 256],
                                       opiece[ts], accum_op=ALU.add, inc="dO", k=16)
                t_wc[it] = w.c["T"]
            t_r[tq] = w.c["T"]

    load_consts()
    w.barrier()
    weight_prep()
    w.barrier()
    xnT = layernorm()
    w.barrier()
    xq = build_xq(xnT)
    w.barrier()
    ffn_f1(xq)
    w.barrier()
    ffn_f2a()
    w.barrier()
    mixer_m1(xnT)
    w.barrier()
    ssm()
    w.barrier()
    wout_phase()
    w.barrier()
    down_phase()
    w.barrier()

    if DEBUG:
        for nm, src in [("d_za", s_za), ("d_xq", s_xq), ("d_gate", s_gate),
                        ("d_hbc", s_hbc), ("d_on", s_on), ("d_ya", s_ya),
                        ("d_r2", s_r2)]:
            w.op("G", "dma_start", dbg[nm], src, inc="dO", k=16)
        w.op("G", "dma_start", dbg["d_smf"], SMF, inc="dO", k=16)

    w.wait("G", "dO", w.c["dO"])
    w.wait("G", "dS", w.c["dS"])
    w.wait("G", "dG", w.c["dG"])
    w.wait("G", "dW", w.c["dW"])

    with nc.Block() as block:
        @block.tensor
        def _(t):
            for f in w.st["T"]:
                f(t)

        @block.vector
        def _(v):
            for f in w.st["V"]:
                f(v)

        @block.scalar
        def _(a):
            for f in w.st["A"]:
                f(a)

        @block.sync
        def _(s):
            for f in w.st["S"]:
                f(s)

        @block.gpsimd
        def _(g):
            for f in w.st["G"]:
                f(g)

    es.close()
    return nc


def _consts():
    h128 = _hadamard(128)
    hb16 = np.kron(_hadamard(16), np.eye(8, dtype=np.float32))
    hb64 = np.kron(_hadamard(64), np.eye(2, dtype=np.float32))
    i128 = np.eye(128, dtype=np.float32)
    ones_blk = np.zeros((128, 128), np.float32)
    ones_blk[0, :] = 1.0
    cb = np.concatenate([h128, hb16, hb64, i128, ones_blk,
                         np.tile(i128, (1, 4))], axis=1)
    return {
        "cb": cb.astype(ml_dtypes.bfloat16),
        "ones_f": np.ones((1, 128), np.float32),
        "ones_c": np.ones((128, 1), np.float32),
        "id_f": i128,
    }


def kernel(**inp):
    f32 = lambda a: np.ascontiguousarray(np.asarray(a, np.float32))
    x = f32(inp["L0_attn_hidden_states"])
    shared = {
        "w_in": f32(inp["L0_attn_W_in"]),
        "conv_w": f32(inp["L0_attn_conv_w"]),
        "conv_b": f32(inp["L0_attn_conv_b"]),
        "dt_bias": f32(inp["L0_attn_dt_bias"]),
        "a_log": f32(inp["L0_attn_A_log"]),
        "dvec": f32(inp["L0_attn_D"]),
        "norm_w": f32(inp["L0_attn_norm_weight"]),
        "w_out": f32(inp["L0_attn_W_out"]),
        "w_g": f32(inp["L0_ffn_w_gate"]),
        "w_u": f32(inp["L0_ffn_w_up"]),
        "w_d": f32(inp["L0_ffn_w_down"]),
    }
    shared.update(_consts())
    nc = build_program()
    in_maps = [dict(shared, x=x[b]) for b in range(8)]
    res = run_bass_kernel_spmd(nc, in_maps, core_ids=list(range(8)))
    out = np.stack([np.asarray(res.results[b]["out"]) for b in range(8)], axis=0)
    return out.astype(np.float32)


if __name__ == "__main__":
    nc = build_program()
    print("instructions:", nc.n_instructions())


# version 20
# speedup vs baseline: 1.2951x; 1.0357x over previous
import numpy as np
import ml_dtypes
import concourse.bass as bass
import concourse.mybir as mybir
from concourse.bass_utils import run_bass_kernel_spmd
from contextlib import ExitStack

F32 = mybir.dt.float32
BF16 = mybir.dt.bfloat16
AF = mybir.ActivationFunctionType
ALU = mybir.AxisListType
ALU = mybir.AluOpType
AX = mybir.AxisListType

L = 2048
DM = 2048
HH, PP, NNS, GG = 64, 64, 128, 8
INTER = 4096
CONV = 6144
PROJ = 10304
IFF = 8192
CS = 128
NCH = L // CS
KC = 4
RND = 3.0 * 2.0 ** 22
EPS = 1e-5
DEBUG = False

ENGS = "TVASG"


def _hadamard(n):
    h = np.array([[1.0]], np.float32)
    while h.shape[0] < n:
        h = np.block([[h, h], [h, -h]])
    return h


class Wk:
    def __init__(self, nc, semh):
        self.nc = nc
        self.semh = semh
        self.c = {k: 0 for k in semh}
        self.st = {e: [] for e in ENGS}

    def wait(self, eng, sem, val=None):
        v = self.c[sem] if val is None else val
        if v <= 0:
            return
        h = self.semh[sem]
        self.st[eng].append(lambda e, h=h, v=v: e.wait_ge(h, v))

    def op(self, eng, meth, *args, inc=None, k=1, **kw):
        h = None
        kk = 0
        if inc is not None:
            self.c[inc] += k
            h, kk = self.semh[inc], k

        import traceback
        site = traceback.extract_stack(limit=3)[0]

        def f(e, m=meth, a=args, kw=kw, h=h, kk=kk, site=site):
            try:
                ins = getattr(e, m)(*a, **kw)
            except Exception:
                print("FAILING OP:", m, "emitted at", site.filename, site.lineno,
                      [getattr(x, "shape", None) for x in a])
                raise
            if h is not None:
                ins.then_inc(h, kk)

        self.st[eng].append(f)
        return self.c[inc] if inc is not None else None

    def nop(self, eng, cyc=1024):
        self.st[eng].append(lambda e, c=cyc: e.nop(cycle_cnt=c))

    def fence(self, eng, sem):
        self.c[sem] += 1
        h = self.semh[sem]
        self.st[eng].append(lambda e, h=h: e.nop(cycle_cnt=1024).then_inc(h, 1))
        return self.c[sem]

    def barrier(self):
        snap = dict(self.c)
        for e in ENGS:
            for s, v in snap.items():
                if v > 0:
                    self.wait(e, s, v)


def build_program():
    nc = bass.Bass("TRN2", target_bir_lowering=False)
    dt_ = nc.dram_tensor

    x_in = dt_("x", [L, DM], F32, kind="ExternalInput").ap()
    w_in = dt_("w_in", [PROJ, DM], F32, kind="ExternalInput").ap()
    conv_w = dt_("conv_w", [CONV, KC], F32, kind="ExternalInput").ap()
    conv_b = dt_("conv_b", [CONV], F32, kind="ExternalInput").ap()
    dt_bias = dt_("dt_bias", [HH], F32, kind="ExternalInput").ap()
    a_log = dt_("a_log", [HH], F32, kind="ExternalInput").ap()
    dvec = dt_("dvec", [HH], F32, kind="ExternalInput").ap()
    norm_w = dt_("norm_w", [INTER], F32, kind="ExternalInput").ap()
    w_out = dt_("w_out", [DM, INTER], F32, kind="ExternalInput").ap()
    w_g = dt_("w_g", [IFF, DM], F32, kind="ExternalInput").ap()
    w_u = dt_("w_u", [IFF, DM], F32, kind="ExternalInput").ap()
    w_d = dt_("w_d", [DM, IFF], F32, kind="ExternalInput").ap()
    cb_in = dt_("cb", [128, 1152], BF16, kind="ExternalInput").ap()
    onesf_in = dt_("ones_f", [1, 128], F32, kind="ExternalInput").ap()
    onesc_in = dt_("ones_c", [128, 1], F32, kind="ExternalInput").ap()
    idf_in = dt_("id_f", [128, 128], F32, kind="ExternalInput").ap()

    out_hbm = dt_("out", [L, DM], F32, kind="ExternalOutput").ap()

    s_winT = dt_("s_winT", [DM, PROJ], BF16, kind="Internal").ap()
    s_woutT = dt_("s_woutT", [INTER, DM], BF16, kind="Internal").ap()
    s_wgT = dt_("s_wgT", [DM, IFF], BF16, kind="Internal").ap()
    s_wuT = dt_("s_wuT", [DM, IFF], BF16, kind="Internal").ap()
    s_wdT = dt_("s_wdT", [IFF, DM], BF16, kind="Internal").ap()
    s_za = dt_("s_za", [DM, L], BF16, kind="Internal").ap()
    s_xq = dt_("s_xq", [DM, L], BF16, kind="Internal").ap()
    s_gate = dt_("s_gate", [INTER, L], BF16, kind="Internal").ap()
    s_hbc = dt_("s_hbc", [CONV, L], BF16, kind="Internal").ap()
    s_on = dt_("s_on", [L, INTER], BF16, kind="Internal").ap()
    s_ya = dt_("s_ya", [IFF, L], BF16, kind="Internal").ap()
    s_r2 = dt_("s_r2", [IFF, L], BF16, kind="Internal").ap()

    dbg = {}
    if DEBUG:
        for nm, shp in [("d_za", [DM, L]), ("d_xq", [DM, L]), ("d_gate", [INTER, L]),
                        ("d_hbc", [CONV, L]), ("d_on", [L, INTER]), ("d_ya", [IFF, L]),
                        ("d_r2", [IFF, L])]:
            dbg[nm] = dt_(nm, shp, BF16, kind="ExternalOutput").ap()
        dbg["d_smf"] = dt_("d_smf", [128, 2944], F32, kind="ExternalOutput").ap()

    es = ExitStack()

    def sbuf(name, shape, dty):
        return es.enter_context(nc.sbuf_tensor(name, shape, dty)).ap()

    AR1 = sbuf("AR1", [128, 32768], BF16)
    AR2 = sbuf("AR2", [128, 32768], BF16)
    ARW = sbuf("ARW", [128, 19968], BF16)
    SST = sbuf("SST", [128, 8192], BF16)
    SMB = sbuf("SMB", [128, 4224], BF16)
    SMF = sbuf("SMF", [128, 2944], F32)
    CBT = sbuf("CBT", [128, 1152], BF16)
    IDF = sbuf("IDF", [128, 128], F32)
    ONF = sbuf("ONF", [1, 128], F32)
    ONC = sbuf("ONC", [128, 1], F32)

    pb = [es.enter_context(nc.psum_tensor("pb%d" % i, [128, 512], F32)).ap()
          for i in range(6)]
    pbt = es.enter_context(nc.psum_tensor("pbt", [128, 1024], BF16)).ap()
    pbt2 = es.enter_context(nc.psum_tensor("pbt2", [128, 1024], BF16)).ap()

    sems = {}
    for s in ("dS", "dG", "dW", "dO", "T", "V", "A"):
        sems[s] = es.enter_context(nc.semaphore(s))

    w = Wk(nc, sems)

    def fview(arena, byte_off, parts, cols):
        assert byte_off % 4 == 0
        return arena[0:parts, byte_off // 2: byte_off // 2 + 2 * cols].bitcast(F32)

    H128 = CBT[:, 0:128]
    HB16 = CBT[:, 128:256]
    HB64 = CBT[:, 256:384]
    IDB = CBT[:, 384:512]
    ONB = CBT[0:1, 512:640]
    IR4 = CBT[:, 640:1152]

    smf = SMF
    DTA = smf[0:64, 0:2048]
    RTEN = smf[0:64, 0:2048]
    ABR = smf[64:128, 0:2048]
    ACS = smf[0:64, 2048:2176]
    CW0 = smf[:, 2176:2180]
    CW1 = smf[:, 2180:2184]
    CB0 = smf[:, 2184:2185]
    CB1 = smf[:, 2185:2186]
    COLS = smf[:, 2186:2250]
    M1C = smf[:, 2250:2266]
    M2C = smf[:, 2266:2282]
    M2Q = smf[:, 2282:2298]
    C1R = smf[0:16, 2304:2432]
    NWC = smf[:, 2432:2464]
    VARC = smf[:, 2464:2472]
    RSTC = smf[:, 2472:2480]
    SSCAL = smf[0:1, 2480:2496]
    SCL = smf[:, 2496:2520]
    C14 = smf[:, 2520:2524]
    CI4 = smf[:, 2524:2528]
    MXT = smf[:, 2528:2656]
    C2ROW = smf[0:1, 2656:2784]
    CRW1 = smf[0:1, 0:512]
    CRW2 = smf[0:1, 512:1024]

    def col(i):
        return COLS[:, i:i + 1]

    DTT = SMB[:, 0:1024]
    DST = SMB[:, 1024:2048]
    SDT = SMB[:, 2048:3072]
    CDB = SMB[:, 3072:4096]
    DMAT = SMB[:, 4096:4160]
    CDROW = SMB[0:1, 4160:4224]

    SP0 = SST[:, 0:4096]
    SP1 = SST[:, 4096:8192]

    def load_consts():
        w.op("V", "memset", col(44), 1e-5)
        w.op("V", "memset", col(45), -1.0, inc="V")
        w.op("S", "dma_start", CBT, cb_in, inc="dS", k=16)
        w.op("S", "dma_start", IDF, idf_in, inc="dS", k=16)
        w.op("S", "dma_start", ONF, onesf_in, inc="dS", k=16)
        w.op("S", "dma_start", ONC, onesc_in, inc="dS", k=16)

    def wp_pass1():
        slot = [fview(ARW, i * 8192, 128, 2048) for i in range(3)]
        tslot = [ARW[:, 12288 + i * 2048: 12288 + (i + 1) * 2048] for i in range(2)]
        asm = [SST[:, 0:2048], SST[:, 2048:4096]]
        junk = fview(SST.bitcast(BF16) if False else SST, 8192, 128, 2048)
        run, spot = col(0), col(1)

        w.op("V", "memset", run, 0.0, inc="V")
        a_read = {}
        gi = 0
        for si, wm in enumerate((w_g, w_u, w_d)):
            wf = wm if wm.shape[1] == 2048 else wm.rearrange(
                "a (b c) -> (a b) c", c=2048)
            for t in range(64):
                sl = gi % 3
                if gi >= 3:
                    w.wait("S", "A", a_read[gi - 3])
                w.op("S", "dma_start", slot[sl], wf[t * 128:(t + 1) * 128, :],
                     inc="dS", k=16)
                w.wait("A", "dS", w.c["dS"])
                a_read[gi] = w.op("A", "activation", junk, slot[sl], AF.Abs,
                                  accum_out=spot, inc="A")
                w.wait("V", "A", a_read[gi])
                w.nop("V")
                w.op("V", "tensor_tensor", run, run, spot, ALU.add, inc="V")
                gi += 1
                yield
            w.wait("T", "V", w.c["V"])
            w.op("T", "matmul", pb[0][0:1, 0:1], run, ONC, start=True, stop=True,
                 inc="T")
            w.wait("V", "T", w.c["T"])
            sc = SSCAL[:, si:si + 1]
            w.op("V", "tensor_scalar", sc, pb[0][0:1, 0:1], 1.0 / (IFF * DM), 1e-5,
                 ALU.mult, ALU.max)
            w.nop("V")
            w.op("V", "reciprocal", SSCAL[:, 8:9], sc)
            w.nop("V")
            w.op("V", "memset", run, 0.0, inc="V")
            w.wait("T", "V", w.c["V"])
            w.nop("T")
            w.op("T", "matmul", pb[1][:, 0:1], ONF, SSCAL[:, 8:9], start=True,
                 stop=True, inc="T")
            w.wait("V", "T", w.c["T"])
            w.op("V", "tensor_copy", col(2 + si), pb[1][:, 0:1], inc="V")

        for si, tgt in ((0, SCL[:, 0:1]), (1, SCL[:, 1:2]), (2, SCL[:, 4:5])):
            w.wait("T", "V", w.c["V"])
            w.op("T", "matmul", pb[1][:, 1:2], ONF, SSCAL[:, si:si + 1], start=True,
                 stop=True, inc="T")
            w.wait("V", "T", w.c["T"])
            w.op("V", "tensor_copy", tgt, pb[1][:, 1:2], inc="V")
        w.nop("V")
        w.op("V", "tensor_tensor", SCL[:, 2:3], SCL[:, 1:2], SCL[:, 4:5], ALU.mult)
        w.nop("V")
        w.op("V", "tensor_scalar", SCL[:, 2:3], SCL[:, 2:3],
             float(1.0 / (127.0 * np.sqrt(8192.0))), 0.0, ALU.mult, ALU.add)
        w.op("V", "reciprocal", SCL[:, 3:4], SCL[:, 1:2])
        w.nop("V")
        w.op("V", "tensor_scalar", SCL[:, 3:4], SCL[:, 3:4],
             float(np.sqrt(8192.0) * 1e-5), 0.0, ALU.mult, ALU.add, inc="V")

    def wp_pass2():
        slot = [fview(ARW, i * 8192, 128, 2048) for i in range(3)]
        tslot = [ARW[:, 12288 + i * 2048: 12288 + (i + 1) * 2048] for i in range(2)]
        asm = [SST[:, 0:2048], SST[:, 2048:4096]]
        junk = fview(SST, 8192, 128, 2048)
        jobs = []
        for si, wm, dstT in ((0, w_g, s_wgT), (1, w_u, s_wuT)):
            for t in range(64):
                jobs.append(("tern", si, wm[t * 128:(t + 1) * 128, :], 128,
                             dstT[:, t * 128:(t + 1) * 128]
                             .rearrange("(k p) c -> p k c", k=16)))
        for q in range(4):
            for j in range(16):
                jobs.append(("tern", 2,
                             w_d[j * 128:(j + 1) * 128, q * 2048:(q + 1) * 2048], 128,
                             s_wdT[q * 2048:(q + 1) * 2048, j * 128:(j + 1) * 128]
                             .rearrange("(k p) c -> p k c", k=16)))
        for j in range(80):
            jobs.append(("cast", None, w_in[j * 128:(j + 1) * 128, :], 128,
                         s_winT[:, j * 128:(j + 1) * 128]
                         .rearrange("(k p) c -> p k c", k=16)))
        jobs.append(("cast", None, w_in[10240:10304, :], 64,
                     s_winT[:, 10240:10304].rearrange("(k p) c -> p k c", k=16)))
        for hf in range(2):
            for j in range(16):
                jobs.append(("cast", None,
                             w_out[j * 128:(j + 1) * 128, hf * 2048:(hf + 1) * 2048],
                             128,
                             s_woutT[hf * 2048:(hf + 1) * 2048, j * 128:(j + 1) * 128]
                             .rearrange("(k p) c -> p k c", k=16)))

        v_cons = {}
        ev_d = {}
        t_tp = {}
        g_wr = {}
        for t, (kind, si, srcap, rows, dstap) in enumerate(jobs):
            sl = t % 3
            ts = t % 2
            if t >= 3:
                w.wait("S", "V", v_cons[t - 3])
            w.op("S", "dma_start", slot[sl][0:rows, :], srcap, inc="dS", k=16)
            ld = w.c["dS"]
            w.wait("V", "dS", ld)
            if t >= 2:
                w.wait("V", "T", t_tp[t - 2])
            sv = slot[sl][0:rows, :]
            tv = tslot[ts][0:rows, :]
            if kind == "tern":
                invc = col(2 + si)[0:rows, :]
                v_cons[t] = w.op("V", "tensor_scalar", junk[0:rows, :], sv, invc,
                                 -1.0, ALU.mult, ALU.max, inc="V")
                w.op("V", "tensor_scalar", junk[0:rows, :], junk[0:rows, :], 1.0,
                     RND, ALU.min, ALU.add)
                w.op("V", "tensor_scalar", tv, junk[0:rows, :], -RND, 0.0, ALU.add,
                     ALU.add, inc="V")
            else:
                v_cons[t] = w.op("V", "tensor_copy", tv, sv, inc="V")
            vt = w.c["V"]
            idn = IDB if rows == 128 else IDB[0:64, 0:64]
            pw = 128 if rows == 128 else 64
            w.wait("T", "V", vt)
            for r in range(2):
                pp = pbt if r == 0 else pbt2
                if t >= 1:
                    w.wait("T", "A", ev_d.get((t - 1, r), 0))
                for k in range(8):
                    w.op("T", "transpose", pp[:, k * pw:(k + 1) * pw],
                         tv[:, (r * 8 + k) * 128:(r * 8 + k + 1) * 128], idn,
                         inc=("T" if k == 7 else None))
                w.wait("A", "T", w.c["T"])
                if t >= 2 and r == 0:
                    w.wait("A", "dG", g_wr[t - 2])
                ev_d[(t, r)] = w.op("A", "copy",
                                    asm[ts][:, r * 8 * pw:(r + 1) * 8 * pw],
                                    pp[:, 0:8 * pw], inc="A")
            t_tp[t] = w.c["T"]
            w.wait("G", "A", ev_d[(t, 1)])
            g_wr[t] = w.op("G", "dma_start", dstap,
                           asm[ts][:, 0:16 * pw].rearrange("p (k c) -> p k c", k=16),
                           inc="dG", k=16)
            yield

    def layernorm():
        xslot = [fview(AR2, 0, 128, 2048), fview(AR2, 8192, 128, 2048)]
        xnn = AR2[:, 8192:10240]
        sqs = fview(AR2, 20480, 128, 2048)
        xnT = AR1[:, 0:32768]
        xnTv = xnT.rearrange("p (a c) -> p a c", a=16)
        for tt in range(16):
            sl = tt % 2
            w.wait("S", "V", w.c["V"])
            w.wait("S", "A", w.c["A"])
            w.op("S", "dma_start", xslot[sl], x_in[tt * 128:(tt + 1) * 128, :],
                 inc="dS", k=16)
            sumc, ssq, mc, vc, rc, nb = (col(8 + i) for i in range(6))
            w.wait("V", "dS", w.c["dS"])
            w.op("V", "reduce_sum", sumc, xslot[sl], AX.X)
            w.nop("V")
            w.op("V", "engine_nop", inc="V")
            w.wait("A", "dS", w.c["dS"])
            w.op("A", "activation", sqs, xslot[sl], AF.Square, accum_out=ssq,
                 inc="A")
            w.wait("V", "A", w.c["A"])
            w.nop("V")
            w.op("V", "tensor_scalar", mc, sumc, 1.0 / DM, 0.0, ALU.mult, ALU.add)
            w.op("V", "tensor_scalar", vc, ssq, 1.0 / DM, 0.0, ALU.mult, ALU.add)
            w.nop("V")
            w.op("V", "tensor_tensor", nb, mc, mc, ALU.mult)
            w.nop("V")
            w.op("V", "tensor_tensor", vc, vc, nb, ALU.subtract, inc="V")
            w.wait("A", "V", w.c["V"])
            w.nop("A")
            w.op("A", "activation", vc, vc, AF.Sqrt, bias=col(44), inc="A")
            w.wait("V", "A", w.c["A"])
            w.nop("V")
            w.op("V", "reciprocal", rc, vc)
            w.nop("V")
            w.op("V", "tensor_scalar", xnn, xslot[sl], mc, rc, ALU.subtract,
                 ALU.mult, inc="V")
            w.wait("T", "V", w.c["V"])
            for q in range(4):
                w.wait("T", "V", w.c["V"])
                for j in range(4):
                    d = q * 4 + j
                    w.op("T", "transpose", pbt[:, j * 128:(j + 1) * 128],
                         xnn[:, d * 128:(d + 1) * 128], IDB,
                         inc=("T" if j == 3 else None))
                w.wait("V", "T", w.c["T"])
                w.op("V", "tensor_copy",
                     xnTv[:, q * 4:(q + 1) * 4, tt * 128:(tt + 1) * 128],
                     pbt[:, 0:512].rearrange("p (a c) -> p a c", a=4), inc="V")
            yield

    def build_xq(xnT):
        zap = AR2[:, 24576:26624]
        z1s = fview(AR2, 16384, 128, 8192)
        mx = fview(AR2, 53248, 128, 512)
        xnv = xnT.rearrange("p (a c) -> p a c", a=16)
        for d in range(16):
            w.wait("V", "dS", w.c["dS"])
            for q in range(4):
                bank = pb[q % 2]
                w.wait("T", "V", w.c["V"])
                w.op("T", "matmul", bank[:, 0:512], H128,
                     xnv[:, d, q * 512:(q + 1) * 512], start=True, stop=True, inc="T")
                w.wait("V", "T", w.c["T"])
                w.op("V", "tensor_copy", zap[:, q * 512:(q + 1) * 512], bank[:, 0:512],
                     inc="V")
            w.wait("G", "V", w.c["V"])
            w.op("G", "dma_start", s_za[d * 128:(d + 1) * 128, :], zap, inc="dS", k=16)
            yield
        zb = AR2[:, 0:8192]
        zbv = zb.rearrange("p (a c) -> p a c", a=16)
        xqs = AR2[:, 24576:32768]
        za_r = s_za.rearrange("(dh d3 dl) t -> (dh d3) dl t", dh=16, d3=8, dl=16)
        xq_w = s_xq.rearrange("(dh d3 dl) t -> (dh d3) dl t", dh=16, d3=8, dl=16)
        for tc in range(4):
            w.wait("S", "dS", w.c["dS"])
            w.wait("S", "T", w.c["T"])
            w.op("S", "dma_start", zbv, za_r[:, :, tc * 512:(tc + 1) * 512],
                 inc="dS", k=16)
            w.wait("T", "dS", w.c["dS"])
            for k in range(16):
                bank = pb[2 + (k % 2)]
                w.wait("T", "V", w.c["V"])
                w.op("T", "matmul", bank[:, 0:512], HB16, zb[:, k * 512:(k + 1) * 512],
                     start=True, stop=True, inc="T")
                w.wait("V", "T", w.c["T"])
                w.op("V", "tensor_copy", z1s[:, k * 512:(k + 1) * 512], bank[:, 0:512],
                     inc="V")
            w.op("V", "tensor_reduce", mx,
                 z1s.rearrange("p (a c) -> p c a", a=16), AX.X, ALU.max,
                 apply_absolute_value=True, inc="V")
            w.wait("T", "V", w.c["V"])
            for j in range(4):
                w.op("T", "transpose", pb[4][:, j * 128:(j + 1) * 128],
                     mx[:, j * 128:(j + 1) * 128], IDF, inc=("T" if j == 3 else None))
            w.wait("V", "T", w.c["T"])
            for j in range(4):
                w.op("V", "reduce_max", M1C[:, tc * 4 + j: tc * 4 + j + 1],
                     pb[4][:, j * 128:(j + 1) * 128], AX.X)
            mm4 = M1C[:, tc * 4: tc * 4 + 4]
            w.nop("V")
            w.op("V", "tensor_scalar", mm4, mm4, float(np.sqrt(2048.0) * 1e-5), 0.0,
                 ALU.max, ALU.add)
            w.nop("V")
            w.op("V", "reciprocal", C14, mm4)
            w.nop("V")
            w.op("V", "tensor_scalar", C14, C14, 127.0, 0.0, ALU.mult, ALU.add)
            w.op("V", "tensor_scalar", CI4, mm4, float(1.0 / (127.0 * np.sqrt(2048.0))),
                 0.0, ALU.mult, ALU.add, inc="V")
            w.wait("T", "V", w.c["V"])
            w.nop("T")
            for j in range(4):
                w.op("T", "transpose", pb[4][0:1, j * 128:(j + 1) * 128],
                     C14[:, j:j + 1], IDF)
                w.op("T", "transpose", pb[5][0:1, j * 128:(j + 1) * 128],
                     CI4[:, j:j + 1], IDF, inc=("T" if j == 3 else None))
            w.wait("V", "T", w.c["T"])
            w.op("V", "tensor_copy", CRW1, pb[4][0:1, 0:512])
            w.op("V", "tensor_copy", CRW2, pb[5][0:1, 0:512], inc="V")
            w.wait("T", "V", w.c["V"])
            w.nop("T")
            for j in range(4):
                w.op("T", "matmul", pb[5][:, j * 128:(j + 1) * 128], ONF,
                     CRW1[0:1, j * 128:(j + 1) * 128], start=True, stop=True)
                w.op("T", "matmul", pb[4][:, j * 128:(j + 1) * 128], ONF,
                     CRW2[0:1, j * 128:(j + 1) * 128], start=True, stop=True,
                     inc=("T" if j == 3 else None))
            w.wait("V", "T", w.c["T"])
            w.wait("V", "dS", w.c["dS"])
            z3 = z1s.rearrange("p (a c) -> p a c", a=16)
            xq3 = xqs.rearrange("p (a c) -> p a c", a=16)
            for j in range(4):
                w.op("V", "tensor_tensor", z3[:, :, j * 128:(j + 1) * 128],
                     z3[:, :, j * 128:(j + 1) * 128],
                     pb[5][:, j * 128:(j + 1) * 128].unsqueeze(1)
                     .broadcast_to([128, 16, 128]), ALU.mult)
            w.op("V", "tensor_scalar", z1s, z1s, RND, -RND, ALU.add, ALU.add)
            for j in range(4):
                w.op("V", "tensor_tensor", xq3[:, :, j * 128:(j + 1) * 128],
                     z3[:, :, j * 128:(j + 1) * 128],
                     pb[4][:, j * 128:(j + 1) * 128].unsqueeze(1)
                     .broadcast_to([128, 16, 128]), ALU.mult, inc=("V" if j == 3
                                                                   else None))
            w.wait("G", "V", w.c["V"])
            w.op("G", "dma_start", xq_w[:, :, tc * 512:(tc + 1) * 512],
                 xqs.rearrange("p (a c) -> p a c", a=16), inc="dS", k=16)
            yield
        xq = AR2[:, 0:32768]
        w.wait("S", "dS", w.c["dS"])
        xqv = xq.rearrange("p (a c) -> p a c", a=16)
        for k in range(16):
            w.op("S", "dma_start", xqv[:, k, :], s_xq[k * 128:(k + 1) * 128, :],
                 inc="dS", k=16)
        yield

    def ffn_f1(xq):
        wslot = [ARW[:, 0:4096], ARW[:, 4096:8192]]
        silg = [ARW[:, 8192 + i * 512: 8192 + (i + 1) * 512] for i in range(2)]
        y1b = [ARW[:, 9216 + i * 512: 9216 + (i + 1) * 512] for i in range(2)]
        yab = [ARW[:, 10240 + i * 512: 10240 + (i + 1) * 512] for i in range(2)]
        xqv = xq.rearrange("p (a c) -> p a c", a=16)
        t_mm = {}
        t_sa = {}
        a_done = {}
        v_y1 = {}
        v_evac = {}
        s_store = {}
        t_wcons = {}

        def emit_stage_a(jt):
            pa = pb[4 + (jt % 2)]
            w.wait("T", "V", v_y1[jt])
            if jt >= 2:
                w.wait("T", "V", v_evac[jt - 2])
            t_sa[jt] = w.op("T", "matmul", pa[:, 0:512], H128, y1b[jt % 2],
                            start=True, stop=True, inc="T")
            w.wait("V", "T", t_sa[jt])
            if jt >= 2:
                w.wait("V", "dS", s_store[jt - 2])
            v_evac[jt] = w.op("V", "tensor_copy", yab[jt % 2], pa[:, 0:512], inc="V")
            w.wait("G", "V", v_evac[jt])
            o_, tc_ = jt // 4, jt % 4
            s_store[jt] = w.op("G", "dma_start",
                               s_ya[o_ * 128:(o_ + 1) * 128, tc_ * 512:(tc_ + 1) * 512],
                               yab[jt % 2], inc="dS", k=16)

        for o in range(64):
            sl = o % 2
            if o >= 2:
                w.wait("S", "T", t_wcons[o - 2])
            w.op("S", "dma_start",
                 wslot[sl][:, 0:2048].rearrange("p (k c) -> p k c", k=16),
                 s_wgT[:, o * 128:(o + 1) * 128]
                 .rearrange("(k p) c -> p k c", k=16), inc="dW", k=16)
            w.op("S", "dma_start",
                 wslot[sl][:, 2048:4096].rearrange("p (k c) -> p k c", k=16),
                 s_wuT[:, o * 128:(o + 1) * 128]
                 .rearrange("(k p) c -> p k c", k=16), inc="dW", k=16)
            wz = w.c["dW"]
            for tc in range(4):
                it = o * 4 + tc
                pg = pb[0 + (tc % 2)]
                pu = pb[2 + (tc % 2)]
                if tc == 0:
                    w.wait("T", "dW", wz)
                if it >= 2:
                    w.wait("T", "A", a_done[it - 2])
                    w.wait("T", "V", v_y1[it - 2])
                for k in range(16):
                    w.op("T", "matmul", pg[:, 0:512],
                         wslot[sl][:, k * 128:(k + 1) * 128],
                         xqv[:, k, tc * 512:(tc + 1) * 512], start=(k == 0),
                         stop=(k == 15))
                    w.op("T", "matmul", pu[:, 0:512],
                         wslot[sl][:, 2048 + k * 128: 2048 + (k + 1) * 128],
                         xqv[:, k, tc * 512:(tc + 1) * 512], start=(k == 0),
                         stop=(k == 15), inc=("T" if k == 15 else None))
                t_mm[it] = w.c["T"]
                w.wait("A", "T", t_mm[it])
                if it >= 2:
                    w.wait("A", "V", v_y1[it - 2])
                a_done[it] = w.op("A", "activation", silg[tc % 2], pg[:, 0:512],
                                  AF.Silu, scale=SCL[:, 0:1], inc="A")
                w.wait("V", "A", a_done[it])
                v_y1[it] = w.op("V", "tensor_tensor", y1b[tc % 2], silg[tc % 2],
                                pu[:, 0:512], ALU.mult, inc="V")
                if it >= 1:
                    emit_stage_a(it - 1)
            t_wcons[o] = w.c["T"]
        emit_stage_a(255)

    def ffn_f2a():
        zbs = [AR2[:, 0:8192], SST[:, 0:8192]]
        z2s = [fview(ARW, 0, 128, 8192), fview(AR2, 32768, 128, 8192)]
        r2b = AR2[:, 8192:16384]
        r2v = r2b.rearrange("p (a c) -> p a c", a=64)
        mxts = [MXT, smf[:, 2784:2912]]
        c2cs = [col(30), col(31)]
        c2rows = [C2ROW, smf[0:1, 0:128]]
        ya_r = s_ya.rearrange("(oh b6 ol) t -> (oh b6) ol t", oh=64, b6=2, ol=64)
        r2_w = s_r2.rearrange("(oh b6 ol) t -> (oh b6) ol t", oh=64, b6=2, ol=64)
        t_zb = {}
        q_done = {}
        qa_done = {}
        ld_done = {}

        def mm_part(tc):
            zb = zbs[tc % 2]
            zbv = zb.rearrange("p (a c) -> p a c", a=64)
            zs = z2s[tc % 2]
            if tc >= 2:
                w.wait("S", "T", t_zb[tc - 2])
            for part in range(4):
                w.op("S", "dma_start", zbv[:, part * 16:(part + 1) * 16, :],
                     ya_r[:, part * 16:(part + 1) * 16, tc * 128:(tc + 1) * 128],
                     inc="dS", k=16)
            ld_done[tc] = w.c["dS"]
            w.wait("T", "dS", ld_done[tc])
            if tc >= 2:
                w.wait("T", "V", q_done[tc - 2])
                w.wait("T", "A", qa_done.get(tc - 2, 0))
            for k in range(16):
                bank = pb[(tc % 2) * 2 + (k % 2)]
                if k >= 2:
                    w.wait("T", "V", w.c["V"])
                    w.wait("T", "A", w.c["A"])
                for j in range(4):
                    w.op("T", "matmul", bank[:, j * 128:(j + 1) * 128], HB64,
                         zb[:, (k * 4 + j) * 128:(k * 4 + j + 1) * 128],
                         start=True, stop=True, inc=("T" if j == 3 else None))
                ee = "V" if k % 2 == 0 else "A"
                w.wait(ee, "T", w.c["T"])
                if ee == "V":
                    w.op("V", "tensor_copy", zs[:, k * 512:(k + 1) * 512],
                         bank[:, 0:512], inc="V")
                else:
                    w.op("A", "copy", zs[:, k * 512:(k + 1) * 512], bank[:, 0:512],
                         inc="A")
            t_zb[tc] = w.c["T"]

        def quant_part(tc):
            zs = z2s[tc % 2]
            z3 = zs.rearrange("p (a c) -> p a c", a=64)
            mxt = mxts[tc % 2]
            c2c = c2cs[tc % 2]
            c2r = c2rows[tc % 2]
            tb = pb[4 + (tc % 2)]
            zt = zs.rearrange("p (a c) -> p c a", a=64)
            w.wait("V", "A", w.c["A"])
            w.op("V", "tensor_reduce", mxt, zt, AX.X, ALU.max,
                 apply_absolute_value=True, inc="V")
            w.wait("T", "V", w.c["V"])
            w.op("T", "transpose", tb[:, 0:128], mxt, IDF, inc="T")
            w.wait("V", "T", w.c["T"])
            mcol = M2C[:, tc:tc + 1]
            w.op("V", "reduce_max", mcol, tb[:, 0:128], AX.X)
            w.nop("V")
            w.op("V", "tensor_tensor", mcol, mcol, SCL[:, 3:4], ALU.max)
            w.nop("V")
            w.op("V", "reciprocal", c2c, mcol)
            w.nop("V")
            w.op("V", "tensor_scalar", c2c, c2c, 127.0, 0.0, ALU.mult, ALU.add,
                 inc="V")
            w.wait("T", "V", w.c["V"])
            w.nop("T")
            w.op("T", "transpose", tb[0:1, 0:128], c2c, IDF, inc="T")
            w.wait("V", "T", w.c["T"])
            w.op("V", "tensor_copy", c2r, tb[0:1, 0:128], inc="V")
            w.wait("T", "V", w.c["V"])
            w.nop("T")
            w.op("T", "matmul", tb[:, 128:256], ONF, c2r, start=True, stop=True,
                 inc="T")
            w.wait("V", "T", w.c["T"])
            w.wait("V", "dS", w.c["dS"])
            w.op("V", "tensor_tensor", z3, z3,
                 tb[:, 128:256].unsqueeze(1).broadcast_to([128, 64, 128]), ALU.mult)
            w.op("V", "tensor_scalar", r2v, z3, RND, -RND, ALU.add, ALU.add, inc="V")
            q_done[tc] = w.c["V"]
            qa_done[tc] = w.c["A"]
            w.wait("G", "V", q_done[tc])
            for part in range(4):
                w.op("G", "dma_start",
                     r2_w[:, part * 16:(part + 1) * 16, tc * 128:(tc + 1) * 128],
                     r2v[:, part * 16:(part + 1) * 16, :], inc="dS", k=16)

        for tc in range(16):
            mm_part(tc)
            if tc >= 1:
                quant_part(tc - 1)
        quant_part(15)
        w.nop("V")
        w.op("V", "tensor_tensor", M2Q, M2C, SCL[:, 2:3].broadcast_to([128, 16]),
             ALU.mult, inc="V")

    def dt_family():
        dtspb = ARW[:, 12294:14342][0:64, :]
        sdch = ARW[:, 14342:14470][0:64, :]
        dsch = ARW[:, 14470:14598][0:64, :]
        T1 = fview(ARW, 31744, 64, 2048)
        ABR2 = T1
        w.wait("A", "V", w.c["V"])
        w.op("A", "activation", T1, DTA, AF.Abs)
        w.op("A", "activation", T1, T1, AF.Exp, scale=col(45)[0:64, :])
        w.op("A", "activation", T1, T1, AF.Ln, bias=1.0, inc="A")
        w.wait("V", "A", w.c["A"])
        w.op("V", "tensor_scalar", DTA, DTA, 0.0, 0.0, ALU.max, ALU.add)
        w.op("V", "tensor_tensor", DTA, DTA, T1, ALU.add, inc="V")
        w.wait("A", "V", w.c["V"])
        w.op("A", "copy", dtspb, DTA, inc="A")
        w.wait("T", "A", w.c["A"])
        for c in range(NCH):
            if c % 4 == 0:
                w.wait("T", "V", w.c["V"])
            w.op("T", "transpose", pbt[:, (c % 4) * 128:(c % 4) * 128 + 64],
                 dtspb[:, c * 128:(c + 1) * 128], IDB[0:64, 0:64],
                 inc=("T" if c % 4 == 3 else None))
            if c % 4 == 3:
                w.wait("V", "T", w.c["T"])
                w.op("V", "tensor_copy",
                     DTT[:, (c - 3) * 64:(c + 1) * 64].rearrange(
                         "p (a c) -> p a c", a=4),
                     pbt[:, 0:512].rearrange("p (a c) -> p a c", a=4)[:, :, 0:64],
                     inc="V")
        w.wait("A", "dS", w.c["dS"])
        w.wait("A", "V", w.c["V"])
        w.op("A", "activation", col(41)[0:64, :], col(41)[0:64, :], AF.Exp,
             inc="A")
        w.wait("V", "A", w.c["A"])
        w.nop("V")
        w.op("V", "tensor_scalar", col(41)[0:64, :], col(41)[0:64, :], -1.0, 0.0,
             ALU.mult, ALU.add)
        w.nop("V")
        w.op("V", "tensor_scalar", ABR2, DTA, col(41)[0:64, :], 0.0, ALU.mult,
             ALU.add, inc="V")
        zc = col(43)
        w.op("V", "memset", zc, 0.0)
        w.nop("V")
        w.op("V", "engine_nop", inc="V")
        for c in range(NCH):
            w.op("V", "tensor_tensor_scan", ACS, ABR2[:, c * 128:(c + 1) * 128],
                 zc[0:64, :].broadcast_to([64, 128]), 0.0, ALU.add, ALU.add, inc="V")
            w.wait("A", "V", w.c["V"])
            w.nop("A")
            w.op("A", "activation", sdch, ACS, AF.Exp)
            w.op("A", "activation", dsch, ACS, AF.Exp, scale=col(45)[0:64, :],
                 bias=ACS[:, 127:128], inc="A")
            w.wait("T", "A", w.c["A"])
            w.nop("T")
            w.wait("T", "V", w.c["V"])
            w.op("T", "transpose", pbt[:, 0:64], sdch, IDB[0:64, 0:64])
            w.op("T", "transpose", pbt[:, 128:192], dsch, IDB[0:64, 0:64], inc="T")
            w.op("T", "transpose", pbt[0:1, 256:320], sdch[:, 127:128],
                 IDB[0:64, 0:64], inc="T")
            w.wait("V", "T", w.c["T"])
            w.op("V", "tensor_copy", SDT[:, c * 64:(c + 1) * 64], pbt[:, 0:64])
            w.op("V", "tensor_copy", DST[:, c * 64:(c + 1) * 64], pbt[:, 128:192])
            w.op("V", "tensor_copy", CDROW, pbt[0:1, 256:320], inc="V")
            w.wait("T", "V", w.c["V"])
            w.nop("T")
            w.op("T", "matmul", pb[3][:, 0:64], ONB, CDROW, start=True, stop=True,
                 inc="T")
            w.wait("V", "T", w.c["T"])
            w.op("V", "tensor_copy", CDB[:, c * 64:(c + 1) * 64], pb[3][:, 0:64],
                 inc="V")
        w.wait("A", "V", w.c["V"])
        w.op("A", "activation", RTEN, ABR2, AF.Exp, inc="A")
        w.wait("V", "A", w.c["A"])
        for c in range(NCH):
            w.op("V", "memset", RTEN[:, c * 128: c * 128 + 1], 0.0,
                 inc=("V" if c == NCH - 1 else None))
        w.wait("T", "V", w.c["V"])
        w.op("T", "transpose", pb[3][0:1, 0:128], col(42), IDF, inc="T")
        w.wait("V", "T", w.c["T"])
        w.op("V", "tensor_copy", C1R[0:1, 0:128], pb[3][0:1, 0:128], inc="V")
        w.wait("T", "V", w.c["V"])
        w.nop("T")
        w.op("T", "matmul", pb[3][:, 0:64], ONF, C1R[0:1, 0:64], start=True,
             stop=True, inc="T")
        w.wait("V", "T", w.c["T"])
        w.op("V", "tensor_copy", DMAT, pb[3][:, 0:64], inc="V")

    def mixer_m1(xnT):
        wslot = [ARW[:, 0:4096], ARW[:, 4096:8192]]
        cvin = [ARW[:, 8192:10243], ARW[:, 10243:12294]]
        hout = ARW[:, 12294:14342]
        gpiece = ARW[:, 14342:14854]
        xnv = xnT.rearrange("p (a c) -> p a c", a=16)
        w.op("S", "dma_start", col(40)[0:64, :],
             dt_bias.rearrange("(a b) -> a b", b=1), inc="dS", k=16)
        w.op("S", "dma_start", col(41)[0:64, :],
             a_log.rearrange("(a b) -> a b", b=1), inc="dS", k=16)
        w.op("S", "dma_start", col(42)[0:64, :],
             dvec.rearrange("(a b) -> a b", b=1), inc="dS", k=16)
        w.op("V", "memset", cvin[0][:, 0:3], 0.0)
        w.op("V", "memset", cvin[1][:, 0:3], 0.0, inc="V")

        groups = [("dt", [80])]
        groups += [("hbc", list(range(32 + 2 * i, 34 + 2 * i))) for i in range(24)]
        groups += [("gate", list(range(2 * i, 2 * i + 2))) for i in range(16)]

        t_after = {}
        ev = {}
        cnt = 0
        hout2 = [hout, ARW[:, 14854:16902]]
        gp2 = [gpiece, ARW[:, 16902:17414]]
        hb_done = {}
        gp_done = {}
        cv_done = {}
        sl_done = {}
        hi = 0
        gpi = 0
        for gi, (kind, js) in enumerate(groups):
            sl = gi % 2
            j0 = js[0]
            rows = 64 if kind == "dt" else 256
            if gi >= 2:
                w.wait("S", "T", t_after[gi - 2])
            w.op("S", "dma_start",
                 wslot[sl][:, 0:4096].rearrange("p (k c) -> p k c", k=16)[:, :, 0:rows],
                 s_winT[:, j0 * 128: j0 * 128 + rows]
                 .rearrange("(k p) c -> p k c", k=16), inc="dW", k=16)
            w.wait("T", "dW", w.c["dW"])
            for jj, j in enumerate(js):
                mrows = 64 if kind == "dt" else 128
                for tc in range(4):
                    bank = pb[cnt % 2]
                    if cnt >= 2:
                        eng, val = ev[cnt - 2]
                        w.wait("T", eng, val)
                    for k in range(16):
                        w.op("T", "matmul", bank[0:mrows, 0:512],
                             wslot[sl][:, k * 256 + jj * 128: k * 256 + jj * 128 + mrows],
                             xnv[:, k, tc * 512:(tc + 1) * 512],
                             start=(k == 0), stop=(k == 15),
                             inc=("T" if k == 15 else None))
                    tmm = w.c["T"]
                    if kind == "dt":
                        w.wait("V", "T", tmm)
                        w.wait("V", "dS", w.c["dS"])
                        ev[cnt] = ("V", w.op("V", "tensor_scalar",
                                             DTA[:, tc * 512:(tc + 1) * 512],
                                             bank[0:64, 0:512], col(40)[0:64, :], 0.0,
                                             ALU.add, ALU.add, inc="V"))
                    elif kind == "hbc":
                        w.wait("A", "T", tmm)
                        if hi >= 2 and tc == 0:
                            w.wait("A", "V", cv_done[hi - 2])
                        ev[cnt] = ("A", w.op("A", "copy",
                                             cvin[jj % 2][:, 3 + tc * 512: 3 + (tc + 1) * 512],
                                             bank[:, 0:512], inc="A"))
                    else:
                        w.wait("A", "T", tmm)
                        if gpi >= 2:
                            w.wait("A", "dS", gp_done[gpi - 2])
                        ev[cnt] = ("A", w.op("A", "copy", gp2[gpi % 2],
                                             bank[:, 0:512], inc="A"))
                        w.wait("S", "A", w.c["A"])
                        gp_done[gpi] = w.op("S", "dma_start",
                                            s_gate[j * 128:(j + 1) * 128,
                                                   tc * 512:(tc + 1) * 512],
                                            gp2[gpi % 2], inc="dS", k=16)
                        gpi += 1
                    cnt += 1
                if kind == "hbc":
                    ci = j - 32
                    cwsp = CW0 if jj % 2 == 0 else CW1
                    cbsp = CB0 if jj % 2 == 0 else CB1
                    if hi >= 2:
                        w.wait("G", "V", cv_done[hi - 2])
                        w.wait("G", "A", sl_done[hi - 2])
                    w.op("G", "dma_start", cwsp, conv_w[ci * 128:(ci + 1) * 128, :],
                         inc="dG", k=16)
                    w.op("G", "dma_start", cbsp,
                         conv_b[ci * 128:(ci + 1) * 128].rearrange("(a b) -> a b", b=1),
                         inc="dG", k=16)
                    w.wait("V", "dG", w.c["dG"])
                    w.wait("V", "A", w.c["A"])
                    if hi >= 2:
                        w.wait("V", "dS", hb_done[hi - 2])
                    src_ = cvin[jj % 2]
                    ho = hout2[hi % 2]
                    w.op("V", "tensor_scalar", ho, src_[:, 0:2048], cwsp[:, 0:1],
                         0.0, ALU.mult, ALU.add)
                    for k in range(1, 4):
                        w.op("V", "scalar_tensor_tensor", ho, src_[:, k:2048 + k],
                             cwsp[:, k:k + 1], ho, ALU.mult, ALU.add,
                             inc=("V" if k == 3 else None))
                    cv_done[hi] = w.c["V"]
                    w.wait("A", "V", w.c["V"])
                    sl_done[hi] = w.op("A", "activation", ho, ho, AF.Silu, bias=cbsp,
                                       inc="A")
                    w.wait("S", "A", w.c["A"])
                    hb_done[hi] = w.op("S", "dma_start",
                                       s_hbc[ci * 128:(ci + 1) * 128, :], ho,
                                       inc="dS", k=16)
                    hi += 1
            if kind == "dt":
                dt_family()
            t_after[gi] = w.c["T"]

    def ssm():
        INS = []
        INS.append({"XT": AR2[:, 0:4096], "GT": AR2[:, 4096:8192],
                    "BT": AR2[:, 8192:9216], "BCF": AR2[:, 9216:11264]})
        INS.append({"XT": AR1[:, 9216:13312], "GT": AR1[:, 13312:17408],
                    "BT": AR1[:, 17408:18432], "BCF": AR1[:, 18432:20480]})
        XDT = AR2[:, 11264:15360]
        XDD = AR2[:, 15360:19456]
        SILG = AR2[:, 19456:23552]
        GGB = AR2[:, 23552:27648]
        ONT = AR2[:, 27648:31744]
        LMK = [AR1[:, 0:1024], AR1[:, 2048:3072]]
        MPR = [AR1[:, 1024:2048], AR1[:, 3072:4096]]
        YTMP = AR1[:, 4096:4608]
        Y2B = AR1[:, 4608:5120]
        XDB = AR1[:, 5120:9216]
        SQS = fview(ARW, 0, 128, 4096)
        w.op("V", "memset", SP0, 0.0, inc="V")

        t1d = {}
        t2d = {}
        v1d = {}
        vscan = {}
        v2e = {}
        v3d = {}
        ldd = {}
        vpro = {}
        asil = {}

        def load_chunk(c):
            sd = INS[c % 2]
            if c >= 2:
                w.wait("S", "T", t2d.get((c - 1) * 8 + 7, 0))
                w.wait("S", "V", vpro.get(c - 1, 0))
                w.wait("S", "A", asil.get(c - 1, 0))
            w.op("S", "dma_start_transpose", sd["XT"],
                 s_hbc[0:4096, c * 128:(c + 1) * 128], inc="dS", k=16)
            w.op("S", "dma_start_transpose", sd["GT"],
                 s_gate[0:4096, c * 128:(c + 1) * 128], inc="dS", k=16)
            w.op("S", "dma_start_transpose", sd["BT"],
                 s_hbc[4096:5120, c * 128:(c + 1) * 128], inc="dS", k=16)
            w.op("S", "dma_start", sd["BCF"].rearrange("p (a c) -> p a c", a=16),
                 s_hbc[4096:6144, c * 128:(c + 1) * 128]
                 .rearrange("(a p) c -> p a c", a=16), inc="dS", k=16)
            ldd[c] = w.c["dS"]

        def t1(c, g):
            gi = c * 8 + g
            sd = INS[c % 2]
            if gi >= 2:
                w.wait("T", "V", v1d.get(gi - 2, 0))
            if gi >= 1:
                w.wait("T", "V", vscan.get(gi - 1, 0))
            if g == 0:
                w.wait("T", "dS", ldd[c])
            w.op("T", "matmul", pb[0][:, (g % 2) * 128:(g % 2) * 128 + 128],
                 sd["BCF"][:, g * 128:(g + 1) * 128],
                 sd["BCF"][:, 1024 + g * 128: 1024 + (g + 1) * 128], start=True,
                 stop=True)
            for i in range(8):
                bank = pb[1] if i < 4 else pb[2]
                w.op("T", "matmul", bank[:, (i % 4) * 128:(i % 4 + 1) * 128],
                     IDF[0:64, 8 * g + i: 8 * g + i + 1].broadcast_to([64, 128]),
                     RTEN[:, c * 128:(c + 1) * 128],
                     start=True, stop=True, inc=("T" if i == 7 else None))
            t1d[gi] = w.c["T"]

        def v1(c, g):
            gi = c * 8 + g
            w.wait("V", "T", t1d[gi])
            if gi >= 2:
                w.wait("V", "T", t2d.get(gi - 2, 0))
            lm = LMK[gi % 2]
            mp = MPR[gi % 2]
            for half in range(2):
                w.op("V", "tensor_tensor_scan", lm[:, half * 512:(half + 1) * 512],
                     pb[1 + half][:, 0:512], IR4, 0.0, ALU.mult, ALU.add,
                     inc=("V" if half == 1 else None))
            vscan[gi] = w.c["V"]
            for half in range(2):
                w.op("V", "tensor_tensor",
                     mp[:, half * 512:(half + 1) * 512]
                     .rearrange("p (a c) -> p a c", a=4),
                     pb[0][:, (g % 2) * 128:(g % 2) * 128 + 128].unsqueeze(1)
                     .broadcast_to([128, 4, 128]),
                     lm[:, half * 512:(half + 1) * 512]
                     .rearrange("p (a c) -> p a c", a=4), ALU.mult,
                     inc=("V" if half == 1 else None))
            v1d[gi] = w.c["V"]

        def t2(c, g):
            gi = c * 8 + g
            sd = INS[c % 2]
            Sold = SP0 if c % 2 == 0 else SP1
            mp = MPR[gi % 2]
            w.wait("T", "V", v1d[gi])
            if gi >= 1:
                w.wait("T", "V", v2e.get(gi - 1, 0))
            for i in range(8):
                w.op("T", "matmul", pb[3][:, i * 64:(i + 1) * 64],
                     mp[:, i * 128:(i + 1) * 128],
                     XDT[:, (8 * g + i) * 64:(8 * g + i + 1) * 64],
                     start=True, stop=True)
            w.op("T", "matmul", pb[4][:, 0:512], sd["BT"][:, g * 128:(g + 1) * 128],
                 XDD[:, g * 512:(g + 1) * 512], start=True, stop=True)
            w.op("T", "matmul", pb[5][:, 0:512],
                 sd["BCF"][:, 1024 + g * 128:1024 + (g + 1) * 128],
                 Sold[:, g * 512:(g + 1) * 512], start=True, stop=True, inc="T")
            t2d[gi] = w.c["T"]

        def v2(c, g):
            gi = c * 8 + g
            Sold = SP0 if c % 2 == 0 else SP1
            Snew = SP1 if c % 2 == 0 else SP0
            w.wait("V", "T", t2d[gi])
            w.op("V", "tensor_tensor",
                 Snew[:, g * 512:(g + 1) * 512].rearrange("p (a c) -> p a c", a=8),
                 Sold[:, g * 512:(g + 1) * 512].rearrange("p (a c) -> p a c", a=8),
                 CDB[:, c * 64 + g * 8: c * 64 + (g + 1) * 8].unsqueeze(2)
                 .broadcast_to([128, 8, 64]), ALU.mult)
            w.op("V", "tensor_tensor", Snew[:, g * 512:(g + 1) * 512],
                 Snew[:, g * 512:(g + 1) * 512], pb[4][:, 0:512], ALU.add)
            w.op("V", "tensor_tensor", YTMP.rearrange("p (a c) -> p a c", a=8),
                 pb[5][:, 0:512].rearrange("p (a c) -> p a c", a=8),
                 SDT[:, c * 64 + g * 8: c * 64 + (g + 1) * 8].unsqueeze(2)
                 .broadcast_to([128, 8, 64]), ALU.mult)
            w.op("V", "scalar_tensor_tensor", Y2B, pb[3][:, 0:512], 1.0, YTMP,
                 ALU.mult, ALU.add, inc="V")
            v2e[gi] = w.c["V"]

        def v3(c, g):
            gi = c * 8 + g
            w.op("V", "tensor_tensor", Y2B, Y2B, XDB[:, g * 512:(g + 1) * 512],
                 ALU.add)
            w.op("V", "tensor_tensor", GGB[:, g * 512:(g + 1) * 512], Y2B,
                 SILG[:, g * 512:(g + 1) * 512], ALU.mult, inc="V")
            v3d[gi] = w.c["V"]
            w.wait("A", "V", v3d[gi])
            w.op("A", "activation", SQS[:, g * 512:(g + 1) * 512],
                 GGB[:, g * 512:(g + 1) * 512], AF.Square,
                 accum_out=VARC[:, g:g + 1], inc="A")

        load_chunk(0)
        for c in range(NCH):
            sd = INS[c % 2]
            w.wait("V", "dS", ldd[c])
            if c >= 1:
                w.wait("V", "T", t2d.get((c - 1) * 8 + 7, 0))
                w.wait("V", "A", w.c["A"])
            xtv = sd["XT"].rearrange("p (h q) -> p h q", h=64)
            w.op("V", "tensor_tensor", XDT.rearrange("p (h q) -> p h q", h=64), xtv,
                 DTT[:, c * 64:(c + 1) * 64].unsqueeze(2).broadcast_to([128, 64, 64]),
                 ALU.mult)
            w.op("V", "tensor_tensor", XDD.rearrange("p (h q) -> p h q", h=64),
                 XDT.rearrange("p (h q) -> p h q", h=64),
                 DST[:, c * 64:(c + 1) * 64].unsqueeze(2).broadcast_to([128, 64, 64]),
                 ALU.mult)
            w.op("V", "tensor_tensor", XDB.rearrange("p (h q) -> p h q", h=64), xtv,
                 DMAT.unsqueeze(2).broadcast_to([128, 64, 64]), ALU.mult, inc="V")
            vpro[c] = w.c["V"]
            w.wait("A", "dS", ldd[c])
            w.wait("A", "V", w.c["V"])
            asil[c] = w.op("A", "activation", SILG, sd["GT"], AF.Silu, inc="A")
            if c + 1 < NCH:
                load_chunk(c + 1)
            for g in range(8):
                t1(c, g)
                v1(c, g)
                t2(c, g)
                v2(c, g)
                v3(c, g)
            w.wait("V", "A", w.c["A"])
            w.op("V", "tensor_scalar", VARC, VARC, 1.0 / 512.0, 0.0, ALU.mult,
                 ALU.add, inc="V")
            w.wait("A", "V", w.c["V"])
            w.nop("A")
            w.op("A", "activation", VARC, VARC, AF.Sqrt, bias=col(44), inc="A")
            w.wait("V", "A", w.c["A"])
            w.nop("V")
            w.op("V", "reciprocal", RSTC, VARC)
            w.nop("V")
            w.wait("V", "dS", w.c["dS"])
            w.op("V", "tensor_tensor", ONT.rearrange("p (a c) -> p a c", a=8),
                 GGB.rearrange("p (a c) -> p a c", a=8),
                 RSTC.unsqueeze(2).broadcast_to([128, 8, 512]), ALU.mult, inc="V")
            w.wait("S", "V", w.c["V"])
            w.op("S", "dma_start", s_on[c * 128:(c + 1) * 128, :], ONT, inc="dS",
                 k=16)

    def wout_phase():
        w.op("S", "dma_start", fview(ARW, 0, 32, 128),
             norm_w.rearrange("(a b) -> a b", b=128), inc="dS", k=16)
        w.wait("T", "dS", w.c["dS"])
        w.op("T", "transpose", pb[0][:, 0:32], fview(ARW, 0, 32, 128),
             IDF[0:32, 0:32], inc="T")
        w.wait("V", "T", w.c["T"])
        w.op("V", "tensor_copy", NWC, pb[0][:, 0:32])
        w.nop("V")
        w.op("V", "engine_nop", inc="V")

        onq = [AR1[:, 0:16384], AR1[:, 16384:32768]]
        wc = [AR2[:, 0:8192], AR2[:, 8192:16384]]
        xrow = [fview(ARW, 4096 + i * 8192, 128, 2048) for i in range(4)]
        opiece = [fview(ARW, 36864 + i * 1024, 128, 256) for i in range(2)]
        plist = []
        t_on = {}
        t_wc = {}
        v_evac = {}
        o_dma = {}
        v_scale = {}
        ld_q = {}

        def load_q_part(tq, part):
            ov = onq[tq % 2].rearrange("p (a c) -> p a c", a=32)
            if tq >= 2 and part == 0:
                w.wait("S", "T", t_on[tq - 2])
            for i in range(part * 8, (part + 1) * 8):
                w.op("S", "dma_start_transpose", ov[:, i, :],
                     s_on[tq * 512:(tq + 1) * 512, i * 128:(i + 1) * 128],
                     inc="dS", k=16)
            if part == 3:
                ld_q[tq] = w.c["dS"]

        def scale_q(tq):
            ov = onq[tq % 2].rearrange("p (a c) -> p a c", a=32)
            w.wait("V", "dS", ld_q[tq])
            if tq >= 2:
                w.wait("V", "T", t_on[tq - 2])
            for i in range(32):
                w.op("V", "tensor_scalar", ov[:, i, :], ov[:, i, :],
                     NWC[:, i:i + 1], 0.0, ALU.mult, ALU.add,
                     inc=("V" if i == 31 else None))
            v_scale[tq] = w.c["V"]

        for part in range(4):
            load_q_part(0, part)
        scale_q(0)
        for tq in range(4):
            ov = onq[tq % 2].rearrange("p (a c) -> p a c", a=32)
            for dg in range(8):
                it = tq * 8 + dg
                sl = dg % 2
                if it >= 2:
                    w.wait("S", "T", t_wc[it - 2])
                for half in range(2):
                    w.op("S", "dma_start",
                         wc[sl].rearrange("p (a c) -> p a c", a=32)
                         [:, half * 16:(half + 1) * 16, :],
                         s_woutT[:, dg * 256:(dg + 1) * 256]
                         .rearrange("(k p) c -> p k c", k=32)
                         [:, half * 16:(half + 1) * 16, :], inc="dW", k=16)
                if tq + 1 < 4 and dg < 4:
                    load_q_part(tq + 1, dg)
                if tq + 1 < 4 and dg == 4:
                    scale_q(tq + 1)
                w.wait("T", "dW", w.c["dW"])
                w.wait("T", "V", v_scale[tq])
                wv = wc[sl].rearrange("p (a c) -> p a c", a=32)
                for ts in range(4):
                    pidx = (tq, dg, ts)
                    tsg = tq * 4 + ts
                    bank = pb[ts % 4]
                    if dg == 0:
                        pv = (tq - 1, 7, ts)
                        if pv in v_evac:
                            w.wait("S", "V", v_evac[pv])
                        w.op("S", "dma_start", xrow[ts],
                             x_in[tsg * 128:(tsg + 1) * 128, :], inc="dS", k=16)
                        ld_q[("x", tsg)] = w.c["dS"]
                    prevv = (tq, dg - 1, ts) if dg >= 1 else ((tq - 1, 7, ts) if tq else None)
                    if prevv in v_evac:
                        w.wait("T", "V", v_evac[prevv])
                    for k in range(32):
                        w.op("T", "matmul", bank[:, 0:256],
                             ov[:, k, ts * 128:(ts + 1) * 128], wv[:, k, :],
                             start=(k == 0), stop=(k == 31),
                             inc=("T" if k == 31 else None))
                    w.wait("V", "T", w.c["T"])
                    w.wait("V", "dS", ld_q[("x", tsg)])
                    if len(plist) >= 2:
                        w.wait("V", "dO", o_dma[plist[-2]])
                    sltp = len(plist) % 2
                    v_evac[pidx] = w.op("V", "tensor_tensor", opiece[sltp],
                                        bank[:, 0:256],
                                        xrow[ts][:, dg * 256:(dg + 1) * 256],
                                        ALU.add, inc="V")
                    w.wait("S", "V", v_evac[pidx])
                    o_dma[pidx] = w.op("S", "dma_start",
                                       out_hbm[tsg * 128:(tsg + 1) * 128,
                                               dg * 256:(dg + 1) * 256],
                                       opiece[sltp], inc="dO", k=16)
                    plist.append(pidx)
                t_wc[it] = w.c["T"]
            t_on[tq] = w.c["T"]

    def down_phase():
        rres = AR1[:, 0:32768]
        rv = rres.rearrange("p (a c) -> p a c", a=64)
        wc = [AR2[:, 0:16384], AR2[:, 16384:32768]]
        opiece = [fview(ARW, i * 1024, 128, 256) for i in range(4)]
        t_r = {}
        t_wc = {}
        v_evac = {}
        o_dma = {}
        for tq in range(4):
            if tq >= 1:
                w.wait("G", "T", t_r[tq - 1])
            for i in range(64):
                w.op("G", "dma_start", rv[:, i, :],
                     s_r2[i * 128:(i + 1) * 128, tq * 512:(tq + 1) * 512],
                     inc="dS", k=16)
            rq_loaded = w.c["dS"]
            for dg in range(8):
                it = tq * 8 + dg
                sl = dg % 2
                if it >= 2:
                    w.wait("S", "T", t_wc[it - 2])
                for half in range(4):
                    w.op("S", "dma_start",
                         wc[sl].rearrange("p (a c) -> p a c", a=64)
                         [:, half * 16:(half + 1) * 16, :],
                         s_wdT[:, dg * 256:(dg + 1) * 256]
                         .rearrange("(k p) c -> p k c", k=64)
                         [:, half * 16:(half + 1) * 16, :], inc="dW", k=16)
                w.wait("T", "dW", w.c["dW"])
                w.wait("T", "dS", rq_loaded)
                wv = wc[sl].rearrange("p (a c) -> p a c", a=64)
                for ts in range(4):
                    pidx = (tq, dg, ts)
                    bank = pb[ts % 4]
                    prev = (tq, dg - 1, ts) if dg >= 1 else ((tq - 1, 7, ts) if tq else None)
                    if prev in v_evac:
                        w.wait("T", "V", v_evac[prev])
                    for k in range(64):
                        w.op("T", "matmul", bank[:, 0:256],
                             rv[:, k, ts * 128:(ts + 1) * 128], wv[:, k, :],
                             start=(k == 0), stop=(k == 63),
                             inc=("T" if k == 63 else None))
                    w.wait("V", "T", w.c["T"])
                    prev2 = (tq, dg - 1, ts) if dg >= 1 else ((tq - 1, 7, ts) if tq else None)
                    if prev2 in o_dma:
                        w.wait("V", "dO", o_dma[prev2])
                    tsg = tq * 4 + ts
                    v_evac[pidx] = w.op("V", "tensor_scalar", opiece[ts], bank[:, 0:256],
                                        M2Q[:, tsg:tsg + 1], 0.0, ALU.mult, ALU.add,
                                        inc="V")
                    w.wait("G", "V", v_evac[pidx])
                    o_dma[pidx] = w.op("G", "dma_start",
                                       out_hbm[tsg * 128:(tsg + 1) * 128,
                                               dg * 256:(dg + 1) * 256],
                                       opiece[ts], accum_op=ALU.add, inc="dO", k=16)
                t_wc[it] = w.c["T"]
            t_r[tq] = w.c["T"]

    load_consts()
    w.barrier()
    xnT = AR1[:, 0:32768]
    xq = AR2[:, 0:32768]
    g1 = wp_pass1()
    gl = layernorm()
    i = 0
    for _ in g1:
        i += 1
        if i % 12 == 0:
            next(gl, None)
    for _ in gl:
        pass
    g2 = wp_pass2()
    gx = build_xq(xnT)
    i = 0
    for _ in g2:
        i += 1
        if i % 15 == 0:
            next(gx, None)
    for _ in gx:
        pass
    w.barrier()
    ffn_f1(xq)
    w.barrier()
    ffn_f2a()
    w.barrier()
    mixer_m1(xnT)
    w.barrier()
    ssm()
    w.barrier()
    wout_phase()
    w.barrier()
    down_phase()
    w.barrier()

    if DEBUG:
        for nm, src in [("d_za", s_za), ("d_xq", s_xq), ("d_gate", s_gate),
                        ("d_hbc", s_hbc), ("d_on", s_on), ("d_ya", s_ya),
                        ("d_r2", s_r2)]:
            w.op("G", "dma_start", dbg[nm], src, inc="dO", k=16)
        w.op("G", "dma_start", dbg["d_smf"], SMF, inc="dO", k=16)

    w.wait("G", "dO", w.c["dO"])
    w.wait("G", "dS", w.c["dS"])
    w.wait("G", "dG", w.c["dG"])
    w.wait("G", "dW", w.c["dW"])

    with nc.Block() as block:
        @block.tensor
        def _(t):
            for f in w.st["T"]:
                f(t)

        @block.vector
        def _(v):
            for f in w.st["V"]:
                f(v)

        @block.scalar
        def _(a):
            for f in w.st["A"]:
                f(a)

        @block.sync
        def _(s):
            for f in w.st["S"]:
                f(s)

        @block.gpsimd
        def _(g):
            for f in w.st["G"]:
                f(g)

    es.close()
    return nc


def _consts():
    h128 = _hadamard(128)
    hb16 = np.kron(_hadamard(16), np.eye(8, dtype=np.float32))
    hb64 = np.kron(_hadamard(64), np.eye(2, dtype=np.float32))
    i128 = np.eye(128, dtype=np.float32)
    ones_blk = np.zeros((128, 128), np.float32)
    ones_blk[0, :] = 1.0
    cb = np.concatenate([h128, hb16, hb64, i128, ones_blk,
                         np.tile(i128, (1, 4))], axis=1)
    return {
        "cb": cb.astype(ml_dtypes.bfloat16),
        "ones_f": np.ones((1, 128), np.float32),
        "ones_c": np.ones((128, 1), np.float32),
        "id_f": i128,
    }


def kernel(**inp):
    f32 = lambda a: np.ascontiguousarray(np.asarray(a, np.float32))
    x = f32(inp["L0_attn_hidden_states"])
    shared = {
        "w_in": f32(inp["L0_attn_W_in"]),
        "conv_w": f32(inp["L0_attn_conv_w"]),
        "conv_b": f32(inp["L0_attn_conv_b"]),
        "dt_bias": f32(inp["L0_attn_dt_bias"]),
        "a_log": f32(inp["L0_attn_A_log"]),
        "dvec": f32(inp["L0_attn_D"]),
        "norm_w": f32(inp["L0_attn_norm_weight"]),
        "w_out": f32(inp["L0_attn_W_out"]),
        "w_g": f32(inp["L0_ffn_w_gate"]),
        "w_u": f32(inp["L0_ffn_w_up"]),
        "w_d": f32(inp["L0_ffn_w_down"]),
    }
    shared.update(_consts())
    nc = build_program()
    in_maps = [dict(shared, x=x[b]) for b in range(8)]
    res = run_bass_kernel_spmd(nc, in_maps, core_ids=list(range(8)))
    out = np.stack([np.asarray(res.results[b]["out"]) for b in range(8)], axis=0)
    return out.astype(np.float32)


if __name__ == "__main__":
    nc = build_program()
    print("instructions:", nc.n_instructions())
